# version 10
# baseline (speedup 1.0000x reference)
LAST_RES = None
"""Single-launch raw-Bass megakernel for mk_seq_postln_d2 on 8 TRN2 NeuronCores.

Module (depth 2, post-LN): x -> LN(x + MLA(x)) -> LN(. + fp4FFN(.)) ->
LN(. + tanh-scan mixer(.)) -> LN(. + sq-relu FFN(.)).   B=16, T=2048, C=1024.

Per core: 2 sequences (batch shard).  Feature-major stream [C -> 8x128 partitions,
4096 tokens free].  Stream spills to HBM between phases (compute-bound module, HBM
has ~35x headroom).  bf16 matmul operands / fp32 PSUM accumulation and elementwise.
LayerNorms run in dedicated LNFIX phases (isolates the sqrt activation table).
The L1 tanh-scan is head-resharded across cores via AllGather (2 heads x 16 seqs per
core) so each scan step is one 128-contraction matmul + one tanh.
fp4 quant-dequant: exact fp32 rounding ladder in DVE 32x32-transposed space.
"""
import numpy as np
import ml_dtypes
import concourse.bass as bass
import concourse.mybir as mybir
from concourse.bass_utils import run_bass_kernel_spmd

F32 = mybir.dt.float32
BF16 = mybir.dt.bfloat16
I32 = mybir.dt.int32
AL = mybir.AluOpType
AF = mybir.ActivationFunctionType
AX = mybir.AxisListType.X
AP = bass.AP

NCORES = 8
SEQ, T, C, NCH = 2, 2048, 1024, 8
H, HD, LAT = 16, 64, 32
IFF, ICH = 2816, 22
NTOK = SEQ * T
TILE = 512
NT = NTOK // TILE
TPS = T // TILE
KTN = T // 128
EPS_LN, EPS_RMS = 1e-5, 1e-4
C_HALF = float(1.5 * 2 ** 22)
C_INT = float(1.5 * 2 ** 23)
ISQL = float(LAT) ** -0.5


class Ctx:
    NY, NGD = 12, 4
    ENGS = ("T", "V", "S", "G") + tuple(f"Y{i}" for i in range(12)) + tuple(f"GD{i}" for i in range(4))

    def __init__(self, nc):
        self.nc = nc
        self.count = {e: 0 for e in self.ENGS}
        self.ops = {e: [] for e in ("T", "V", "S", "G", "Y")}
        self.last_w = {}
        self.readers = {}
        self.rr = {"Y": 0, "GD": 0}

    def op(self, eng, fn, r=(), w=()):
        q = eng in ("Y", "GD")
        if q:
            stream = "Y" if eng == "Y" else "G"
            sem = f"{eng}{self.rr[eng]}"
            self.rr[eng] = (self.rr[eng] + 1) % (self.NY if eng == "Y" else self.NGD)
        else:
            stream = eng
            sem = eng
        need = {}
        for tag in r:
            lw = self.last_w.get(tag)
            if lw:
                need[lw[0]] = max(need.get(lw[0], 0), lw[1])
        for tag in w:
            lw = self.last_w.get(tag)
            if lw and (q or lw[0] != sem):
                need[lw[0]] = max(need.get(lw[0], 0), lw[1])
            for (e2, v) in self.readers.get(tag, ()):
                if q or e2 != sem:
                    need[e2] = max(need.get(e2, 0), v)
        if q and self.count[sem] > 0:
            need[sem] = max(need.get(sem, 0), self.count[sem])
        inc = 16 if q else 1
        self.count[sem] += inc
        val = self.count[sem]
        self.ops[stream].append((need, sem, fn, val))
        for tag in w:
            self.last_w[tag] = (sem, val)
            self.readers[tag] = []
        for tag in r:
            self.readers.setdefault(tag, []).append((sem, val))

    def barrier(self):
        snap = dict(self.count)
        for stream in self.ops:
            self.ops[stream].append(("B", snap))
        self.last_w = {}
        self.readers = {}

    def emit(self, block, sems):
        def body_of(stream):
            def body(h):
                wm = {e: 0 for e in self.ENGS}
                for item in self.ops[stream]:
                    if item[0] == "B":
                        for e, v in item[1].items():
                            if v > wm[e]:
                                h.wait_ge(sems[e], v)
                                wm[e] = v
                        continue
                    need, eng, fn, val = item
                    for e, v in need.items():
                        if v > wm[e]:
                            h.wait_ge(sems[e], v)
                            wm[e] = v
                    inst = fn(h)
                    if inst is not None:
                        inst.then_inc(sems[eng], 16 if eng not in ("T", "V", "S", "G") else 1)
            return body
        block.sync(body_of("Y"))
        block.tensor(body_of("T"))
        block.vector(body_of("V"))
        block.scalar(body_of("S"))
        block.gpsimd(body_of("G"))


class Buf:
    def __init__(self, slab, slab_w, off, n, dt, slab_dt=BF16):
        self.slab, self.sw, self.off, self.dt = slab, slab_w, off, dt
        self.native = (dt == slab_dt) or (mybir.dt.size(dt) == mybir.dt.size(slab_dt))
        self.esz = 1 if self.native else 2
        self.n = n // self.esz

    def _mk(self, par_pair, o, dims):
        if dims is None:
            dims = [[1, self.n]]
        if self.native:
            a = AP(self.slab, self.off + par_pair[2] + o, [list(par_pair[:2])] + [list(d) for d in dims])
            if a.dtype != self.dt:
                a = a.bitcast(self.dt)
            return a
        nd = [[s * 2, c] for s, c in dims[:-1]]
        s_in, c_in = dims[-1]
        assert s_in == 1, f"fp32-on-bf16 innermost must be contiguous, got step {s_in}"
        nd.append([1, 2 * c_in])
        a = AP(self.slab, self.off + par_pair[2] + o * 2, [list(par_pair[:2])] + nd)
        return a.bitcast(F32)

    def ap(self, o=0, dims=None):
        return self._mk((self.sw, 128, 0), o, dims)

    def app(self, p0, np_, o=0, dims=None):
        return self._mk((self.sw, np_, p0 * self.sw), o, dims)


def build_nc(stop=4):
    nc = bass.Bass("TRN2", target_bir_lowering=False, debug=False)

    def din(name, shape, dt=F32):
        return nc.dram_tensor(name, shape, dt, kind="ExternalInput").ap()

    x_in = din("x_in", [C, NTOK])
    cattn_wT = din("cattn_wT", [C, 3 * C])
    cattn_b = din("cattn_b", [3 * C])
    q2l_wT = din("q2l_wT", [HD, LAT])
    q2l_b = din("q2l_b", [LAT])
    kv2l_wT = din("kv2l_wT", [HD, LAT])
    kv2l_b = din("kv2l_b", [LAT])
    l2o_wT = din("l2o_wT", [LAT, HD])
    l2o_b = din("l2o_b", [HD])
    wqk_wT = din("wqk_wT", [LAT, LAT])
    wqk_b = din("wqk_b", [LAT])
    cproj_wT = din("cproj_wT", [C, C])
    cproj_b = din("cproj_b", [C])
    wg_T = din("wg_T", [C, IFF])
    wu_T = din("wu_T", [C, IFF])
    wd_T = din("wd_T", [IFF, C])
    norm_w = din("norm_w", [C])
    a_sh = din("a_sh", [2, 64, 64])
    b_sh = din("b_sh", [2, 64, 64])
    c_sh = din("c_sh", [2, 64, 64])
    skip_sh = din("skip_sh", [128])
    wout_T = din("wout_T", [C, C])
    bout_v = din("bout_v", [C])
    wr_T = din("wr_T", [C, C])
    wk_T = din("wk_T", [C, C])
    wv_T = din("wv_T", [C, C])
    mu_r = din("mu_r", [C])
    mu_k = din("mu_k", [C])
    altsgn = din("altsgn", [128])
    cos4 = din("cos4", [128, T])
    sin4 = din("sin4", [128, T])
    core_id_in = din("core_id_in", [1, 1], I32)

    out_d = nc.dram_tensor("out_d", [C, NTOK], F32, kind="ExternalOutput").ap()


    def dsc(name, shape, dt=F32, **kw):
        return nc.dram_tensor(name, shape, dt, kind="Internal", **kw).ap()

    y1_d = dsc("y1_d", [C, NTOK])
    x1_d = dsc("x1_d", [C, NTOK])
    y2_d = dsc("y2_d", [C, NTOK])
    x2_d = dsc("x2_d", [C, NTOK])
    y3_d = dsc("y3_d", [C, NTOK])
    x3_d = dsc("x3_d", [C, NTOK])
    y4_d = dsc("y4_d", [C, NTOK])
    s_d = [dsc(f"s{k}_d", [2, NTOK]) for k in range(4)]
    cf_d = [dsc(f"cf{k}_d", [4, NTOK], BF16) for k in range(4)]
    cfx_d = dsc("cfx_d", [4, NTOK], BF16)
    WSEG = 1024 * 352
    WDSEG = 2816 * 128
    wgs_d = dsc("wgs_d", [C, 352])
    wus_d = dsc("wus_d", [C, 352])
    wds_d = dsc("wds_d", [IFF, 128])
    wqin_d = dsc("wqin_d", [2 * WSEG + WDSEG], BF16)
    wq_g = dsc("wq_g", [NCORES * (2 * WSEG + WDSEG)], BF16)
    wgq_d = dsc("wgq_d", [C, IFF], BF16)
    wuq_d = dsc("wuq_d", [C, IFF], BF16)
    wdq_d = dsc("wdq_d", [IFF, C], BF16)
    xn_d = dsc("xn_d", [C, NTOK], BF16)
    xn_g = dsc("xn_g", [NCORES * C, NTOK], BF16)
    mx_d = dsc("mx_d", [128, H * T], BF16)
    mx_g = dsc("mx_g", [NCORES * 128, H * T], BF16)

    ct = Ctx(nc)
    sems = {e: nc.semaphore(f"sem{e}").__enter__() for e in Ctx.ENGS}
    PF = 2300
    PW = 8000
    WB = 91648

    with (
        nc.sbuf_tensor("persf", [128, PF], F32) as persf_t,
        nc.sbuf_tensor("pers", [128, PW], BF16) as pers_t,
        nc.sbuf_tensor("arena", [128, WB], BF16) as arena_t,
        nc.psum_tensor("pb0", [128, 512], F32) as _p0,
        nc.psum_tensor("pb1", [128, 512], F32) as _p1,
        nc.psum_tensor("pb2", [128, 512], F32) as _p2,
        nc.psum_tensor("pb3", [128, 512], F32) as _p3,
        nc.psum_tensor("pb4", [128, 512], F32) as _p4,
        nc.psum_tensor("pb5", [128, 512], F32) as _p5,
        nc.psum_tensor("pb6", [128, 512], F32) as _p6,
        nc.psum_tensor("pb7", [128, 512], F32) as _p7,
        nc.Block() as block,
    ):
        pass_sems = sems
        PB = [_p0, _p1, _p2, _p3, _p4, _p5, _p6, _p7]

        def PF32(off, n, dt=F32):
            return Buf(persf_t, PF, off, n, dt, slab_dt=F32)

        def P(off, n, dt=BF16):
            return Buf(pers_t, PW, off, n, dt)

        def A(off, n, dt=BF16):
            return Buf(arena_t, WB, off, n, dt)

        cattnb = PF32(0, 24)
        cprojb = PF32(24, 8)
        boutb = PF32(32, 8)
        normwb = PF32(40, 8)
        murb = PF32(48, 8)
        mukb = PF32(56, 8)
        omrb = PF32(64, 8)
        omkb = PF32(72, 8)
        q2lb4 = PF32(80, 1)
        q2lbr4 = PF32(81, 1)
        kvb4 = PF32(82, 1)
        kvbr4 = PF32(83, 1)
        wqkb4 = PF32(84, 1)
        l2ob2 = PF32(85, 1)
        skipb = PF32(86, 1)
        cidb = PF32(87, 1, I32)
        kvrow = PF32(88, 32)
        kvbias = PF32(120, 128)
        abdf = PF32(248, 128)
        dn = PF32(376, 1024)
        strow = PF32(1400, 512)
        statw = PF32(1912, 160)
        qt_am = PF32(2072, 32)
        qt_inv = PF32(2104, 32)
        qt_s6 = PF32(2136, 32)
        altsg = PF32(2170, 1)
        constc = PF32(2168, 2)
        qtB_am = PF32(2172, 32)
        qtB_inv = PF32(2204, 32)
        qtB_s6 = PF32(2236, 32)

        nc.const_aps.register(F32, EPS_LN, constc.ap(0, [[1, 1]]))
        nc.const_aps.register(F32, EPS_RMS, constc.ap(1, [[1, 1]]))
        q2lw = P(0, 32)
        q2lrw = P(32, 32)
        kv2lw = P(64, 32)
        kv2lrw = P(96, 32)
        wqkw = P(128, 32)
        l2ow = P(160, 64)
        cosb = P(224, T)
        sinb = P(2272, T)
        abd = P(4320, 128)
        bbd = P(4448, 128)
        cbd = P(4576, 128)
        ones1 = P(4704, 2)
        onesr = P(4706, 128)
        dnh = P(5858, 1024)
        dnl = P(6882, 1024)
        y2buf = P(4834, 512)
        ybf = P(5346, 512)

        cattn = A(0, 24576)
        qT = A(24576, 16384)
        kT = A(40960, 16384)
        vSB = A(57344, 17888)
        qkvS = A(75232, 12288)
        xbS = A(87520, 4096)
        rt1 = A(87520, 512)
        rt2 = A(88544, 512)

        def voff(s2, hh2, kt2):
            return (s2 * H + hh2) * 559 + (0 if kt2 == 0 else 64 + (kt2 - 1) * 33)
        qt_x = A(75232, 1024, F32)
        qt_t = A(76256, 1024, F32)
        qt_xn = A(77280, 1024, F32)
        qt_u = A(78304, 1024, F32)
        qt_r2 = A(79328, 1024, F32)
        qtB_t = A(56832, 1024, F32)
        qtB_xn = A(57856, 1024, F32)
        qtB_u = A(58880, 1024, F32)
        qtB_r2 = A(59904, 1024, F32)
        qtB_r1 = A(60928, 512)
        qtB_a5 = A(61440, 512)
        qtB_sg = A(61952, 512)
        qtB_t2 = A(62464, 512)
        qtB_qs = A(62976, 512)
        qtB_qT = A(63488, 512)

        qt_r1 = A(80352, 512)
        qt_a5 = A(80864, 512)
        qt_sg = A(81376, 512)
        qt_t2 = A(81888, 512)
        qt_qs = A(82400, 512)
        qt_qT = A(82912, 512)
        qt_q = A(83424, 512)
        QTS = [
            {"t": qt_t, "xn": qt_xn, "u": qt_u, "r2": qt_r2, "am": qt_am, "inv": qt_inv,
             "s6": qt_s6, "r1": qt_r1, "a5": qt_a5, "sg": qt_sg, "t2": qt_t2, "qs": qt_qs, "qT": qt_qT},
            {"t": qtB_t, "xn": qtB_xn, "u": qtB_u, "r2": qtB_r2, "am": qtB_am, "inv": qtB_inv,
             "s6": qtB_s6, "r1": qtB_r1, "a5": qtB_a5, "sg": qtB_sg, "t2": qtB_t2, "qs": qtB_qs, "qT": qtB_qT},
        ]
        x0S = A(0, 8192, F32)
        cprojS = A(8192, 8192)
        eS = A(16384, 8192)
        yrawS = A(16384, 8192, F32)
        yvS = A(83936, 2048)
        attnS = A(85984, 4096)

        x1t = A(0, 8192, F32)
        x1tB = A(64000, 8192, F32)
        x1tp = [x1t, x1tB]
        xqS = A(8192, 4096)
        htS = A(12288, 22528, F32)
        hqS = A(34816, 11264)
        wgbuf = [A(46080 + i * 1024, 1024) for i in range(2)]
        wubuf = [A(48128 + i * 1024, 1024) for i in range(2)]
        wdbuf = [A(50176 + i * 2816, 2816) for i in range(2)]
        silS = A(55808, 512)
        xnb = A(56320, 512)

        lyt = A(0, 8192, F32)
        lcf = A(8192, 4096)
        lxn = A(12288, 4096)

        xsc = A(0, 32768)
        stS = A(32768, 1024 + 16)
        mxacc = [A(33808 + i * 4096, 4096) for i in range(2)]
        mxfm = A(0, 32768)
        woutS = A(32768, 8192)
        x2t = A(40960, 8192, F32)

        wrS = A(0, 8192)
        wkS = A(8192, 8192)
        wvS = A(16384, 8192)
        x3t = A(24576, 8192, F32)
        xpt = A(32768, 8192, F32)
        xrS = A(40960, 4096)
        xkS = A(45056, 4096)
        rbS = A(49152, 4096)
        kaS = A(53248, 4096)
        p4tmp = A(57344, 1024, F32)

        def init_phase():
            ct.op("Y", lambda h: h.dma_start(out=cattnb.ap(), in_=AP(cattn_b.tensor, 0, [[1, 128], [128, 24]]),
                                             allow_slow_non_contiguous=True), w=["cattnb"])
            for dst, src, nm in ((cprojb, cproj_b, "cprojb"), (boutb, bout_v, "boutb"),
                                 (normwb, norm_w, "normwb"), (murb, mu_r, "murb"), (mukb, mu_k, "mukb")):
                ct.op("Y", lambda h, d=dst, s=src: h.dma_start(out=d.ap(), in_=AP(s.tensor, 0, [[1, 128], [128, 8]]),
                                                               allow_slow_non_contiguous=True), w=[nm])
            ct.op("Y", lambda h: h.dma_start(out=skipb.ap(), in_=AP(skip_sh.tensor, 0, [[1, 128], [1, 1]])), w=["skipb"])
            ct.op("Y", lambda h: h.dma_start(out=cidb.app(0, 1, dims=[[1, 1]]), in_=core_id_in), w=["cid"])
            for half in range(2):
                ct.op("GD", lambda h, p=64 * half: h.dma_start(out=q2lw.app(p, 64), in_=q2l_wT), w=["q2lw"])
                ct.op("GD", lambda h, p=64 * half: h.dma_start(out=kv2lw.app(p, 64), in_=kv2l_wT), w=["kv2lw"])
                ct.op("GD", lambda h, p=64 * half: h.dma_start(out=l2ow.app(p, 32, dims=[[1, 64]]), in_=l2o_wT), w=["l2ow"])
            for g in range(4):
                ct.op("GD", lambda h, p=32 * g: h.dma_start(out=wqkw.app(p, 32), in_=wqk_wT), w=["wqkw"])
            ct.op("GD", lambda h: h.dma_start(out=cosb.ap(), in_=cos4), w=["cosb"])
            ct.op("GD", lambda h: h.dma_start(out=sinb.ap(), in_=sin4), w=["sinb"])
            ct.op("GD", lambda h: h.dma_start(
                out=cattn.ap(), in_=AP(cattn_wT.tensor, 0, [[3 * C, 128], [128 * 3 * C, 8], [128, 24], [1, 128]])),
                w=["cattn"])
            for g in range(4):
                ct.op("Y", lambda h, p=32 * g: h.dma_start(out=q2lb4.app(p, 32, dims=[[1, 1]]), in_=AP(q2l_b.tensor, 0, [[1, 32], [1, 1]])), w=["q2lb4"])
                ct.op("Y", lambda h, p=32 * g: h.dma_start(out=kvb4.app(p, 32, dims=[[1, 1]]), in_=AP(kv2l_b.tensor, 0, [[1, 32], [1, 1]])), w=["kvb4"])
                ct.op("Y", lambda h, p=32 * g: h.dma_start(out=wqkb4.app(p, 32, dims=[[1, 1]]), in_=AP(wqk_b.tensor, 0, [[1, 32], [1, 1]])), w=["wqkb4"])
            for half in range(2):
                ct.op("Y", lambda h, p=64 * half: h.dma_start(out=l2ob2.app(p, 64, dims=[[1, 1]]), in_=AP(l2o_b.tensor, 0, [[1, 64], [1, 1]])), w=["l2ob2"])
            ct.op("V", lambda h: h.memset(ones1.ap(), 1.0), w=["ones1"])
            ct.op("V", lambda h: h.memset(constc.ap(0, [[1, 1]]), EPS_LN), w=["constc"])
            ct.op("V", lambda h: h.memset(constc.ap(1, [[1, 1]]), EPS_RMS), w=["constc"])
            ct.op("V", lambda h: h.memset(onesr.ap(0, [[1, 128]]), 1.0), w=["onesr"])
            ct.op("V", lambda h: h.memset(dn.ap(), 1.0), w=["dn"])
            for dst, src, nm, nmr in ((q2lrw, q2lw, "q2lw", "q2lrw"), (kv2lrw, kv2lw, "kv2lw", "kv2lrw")):
                ct.op("V", lambda h, d=dst, s=src: h.tensor_scalar_mul(
                    d.ap(0, [[2, 16]]), s.ap(1, [[2, 16]]), -1.0), r=[nm], w=[nmr])
                ct.op("V", lambda h, d=dst, s=src: h.tensor_scalar_mul(
                    d.ap(1, [[2, 16]]), s.ap(0, [[2, 16]]), 1.0), r=[nm], w=[nmr])
            ct.op("Y", lambda h: h.dma_start(out=altsg.ap(), in_=AP(altsgn.tensor, 0, [[1, 128], [1, 1]])), w=["altsg"])
            pair_mask = [i + 1 if i % 2 == 0 else i - 1 for i in range(32)]
            for dst, src, nm, nmr in ((q2lbr4, q2lb4, "q2lb4", "q2lbr4"), (kvbr4, kvb4, "kvb4", "kvbr4")):
                ct.op("V", lambda h, d=dst, s=src: h.stream_shuffle(d.ap(), s.ap(), pair_mask), r=[nm], w=[nmr])
                ct.op("V", lambda h, d=dst: h.tensor_scalar(d.ap(), d.ap(), altsg.ap(), None, AL.mult),
                      r=[nmr, "altsg"], w=[nmr])
            ct.op("V", lambda h: h.transpose(
                kvrow.app(0, 32, dims=[[1, 32]]),
                AP(persf_t, kvb4.off, [[PF, 32], [0, 32]])), r=["kvb4"], w=["kvrow"])
            ct.op("V", lambda h: h.tensor_copy(dnh.app(0, 1, dims=[[1, 32]]), kvrow.app(0, 1, dims=[[1, 32]])),
                  r=["kvrow"], w=["dnh"])
            ct.op("V", lambda h: h.tensor_tensor(dnl.app(0, 1, dims=[[1, 32]]), kvrow.app(0, 1, dims=[[1, 32]]),
                                                 dnh.app(0, 1, dims=[[1, 32]]), AL.subtract),
                  r=["kvrow", "dnh"], w=["dnl"])
            for lohi, buf in ((0, dnh), (1, dnl)):
                ct.op("T", lambda h, b2=buf, st=(lohi == 0): h.matmul(
                    AP(PB[0], 0, [[512, 128], [1, 32]]), onesr.app(0, 1, dims=[[1, 128]]),
                    b2.app(0, 1, dims=[[1, 32]]), start=st, stop=(lohi == 1), tile_position=(0, 0)),
                    r=["dnh", "dnl", "onesr"], w=["pb0"])
            for rep in range(4):
                ct.op("V", lambda h, rp=rep: h.tensor_copy(
                    kvbias.ap(32 * rp, [[1, 32]]), AP(PB[0], 0, [[512, 128], [1, 32]])),
                    r=["pb0"], w=["kvbias"])
            ct.op("V", lambda h: h.memset(abdf.ap(), 0.0), w=["abdf"])
            ct.op("GD", lambda h: h.dma_start(out=abdf.app(0, 64, dims=[[1, 64]]), in_=a_sh[0]), w=["abdf"])
            ct.op("GD", lambda h: h.dma_start(out=abdf.app(64, 64, o=64, dims=[[1, 64]]), in_=a_sh[1]), w=["abdf"])
            ct.op("S", lambda h: h.activation(abd.ap(), abdf.ap(), AF.Tanh), r=["abdf"], w=["abd"])
            for wbd, wsrc, nm in ((bbd, b_sh, "bbd"), (cbd, c_sh, "cbd")):
                ct.op("V", lambda h, d=wbd: h.memset(d.ap(), 0.0), w=[nm])
                ct.op("GD", lambda h, d=wbd, s=wsrc: h.dma_start(out=d.app(0, 64, dims=[[1, 64]]), in_=s[0]), w=[nm])
                ct.op("GD", lambda h, d=wbd, s=wsrc: h.dma_start(out=d.app(64, 64, o=64, dims=[[1, 64]]), in_=s[1]), w=[nm])
            ct.op("V", lambda h: h.tensor_scalar(omrb.ap(), murb.ap(), -1.0, 1.0, AL.mult, AL.add), r=["murb"], w=["omrb"])
            ct.op("V", lambda h: h.tensor_scalar(omkb.ap(), mukb.ap(), -1.0, 1.0, AL.mult, AL.add), r=["mukb"], w=["omkb"])
            ct.op("V", lambda h: h.memset(vSB.ap(), 0.0), w=["vSB"])
            for s2 in range(SEQ):
                for hh2 in range(H):
                    base = (s2 * H + hh2) * 559
                    ct.op("V", lambda h, b2=base: h.memset(vSB.ap(b2 + 32, [[1, 1]]), 1.0), w=["vSB"])
                    ct.op("V", lambda h, b2=base: h.memset(vSB.ap(b2 + 96, [[33, 15], [1, 1]]), 1.0), w=["vSB"])

        init_phase()
        ct.barrier()

        def qstage_a(src_ap, n, qs_, r_extra=()):
            nb = n // 16
            B = QTS[qs_]
            tg = f"qtmp{qs_}"
            ct.op("V", lambda h: h.transpose(B["t"].ap(0, [[1, n]]), src_ap), r=list(r_extra) + [tg], w=[tg])
            ct.op("V", lambda h: h.tensor_reduce(B["am"].ap(0, [[1, nb]]), B["t"].ap(0, [[16, nb], [1, 16]]),
                                                 AX, AL.max, apply_absolute_value=True), r=[tg], w=[tg])
            ct.op("V", lambda h: h.tensor_scalar(B["s6"].ap(0, [[1, nb]]), B["am"].ap(0, [[1, nb]]),
                                                 1.0 / 6.0, 1e-12, AL.mult, AL.max), r=[tg], w=[tg])
            ct.op("V", lambda h: h.reciprocal(B["inv"].ap(0, [[1, nb]]), B["s6"].ap(0, [[1, nb]])), r=[tg], w=[tg])
            ct.op("V", lambda h: h.tensor_tensor(B["xn"].ap(0, [[16, nb], [1, 16]]), B["t"].ap(0, [[16, nb], [1, 16]]),
                                                 B["inv"].ap(0, [[1, nb], [0, 16]]), AL.mult), r=[tg], w=[tg])

        def qstage_s(n, qs_):
            B = QTS[qs_]
            tg = f"qtmp{qs_}"
            sg_t = f"qsg{qs_}"
            ct.op("S", lambda h: h.activation(B["u"].ap(0, [[1, n]]), B["xn"].ap(0, [[1, n]]), AF.Abs),
                  r=[tg], w=[tg])
            ct.op("S", lambda h: h.activation(B["sg"].ap(0, [[1, n]]), B["xn"].ap(0, [[1, n]]), AF.Sign), r=[tg], w=[sg_t])

        def qstage_b(dst_ap, n, qs_, w_extra=()):
            nb = n // 16
            B = QTS[qs_]
            tg = f"qtmp{qs_}"
            sg_t = f"qsg{qs_}"
            ct.op("V", lambda h: h.tensor_scalar(B["r2"].ap(0, [[1, n]]), B["u"].ap(0, [[1, n]]), 0.5, 4.0, AL.mult, AL.add),
                  r=[tg], w=[tg])
            ct.op("V", lambda h: h.scalar_tensor_tensor(B["xn"].ap(0, [[1, n]]), B["u"].ap(0, [[1, n]]), 2.0,
                                                        B["r2"].ap(0, [[1, n]]), AL.add, AL.min),
                  r=[tg, sg_t], w=[tg])
            ct.op("V", lambda h: h.scalar_tensor_tensor(B["r2"].ap(0, [[1, n]]), B["u"].ap(0, [[1, n]]), 2.0,
                                                        B["xn"].ap(0, [[1, n]]), AL.mult, AL.min),
                  r=[tg], w=[tg])
            ct.op("V", lambda h: h.tensor_scalar(B["r1"].ap(0, [[1, n]]), B["r2"].ap(0, [[1, n]]), C_INT, C_INT, AL.add, AL.subtract),
                  r=[tg], w=[tg])
            ct.op("V", lambda h: h.tensor_scalar(B["a5"].ap(0, [[1, n]]), B["r1"].ap(0, [[1, n]]), 2.0, 8.0, AL.mult, AL.subtract),
                  r=[tg], w=[tg])
            ct.op("V", lambda h: h.scalar_tensor_tensor(B["t2"].ap(0, [[1, n]]), B["r1"].ap(0, [[1, n]]), 2.0,
                                                        B["a5"].ap(0, [[1, n]]), AL.subtract, AL.max),
                  r=[tg], w=[tg])
            ct.op("V", lambda h: h.scalar_tensor_tensor(B["qs"].ap(0, [[1, n]]), B["r1"].ap(0, [[1, n]]), 0.5,
                                                        B["t2"].ap(0, [[1, n]]), AL.mult, AL.max),
                  r=[tg], w=[tg])
            ct.op("V", lambda h: h.tensor_tensor(B["t2"].ap(0, [[1, n]]), B["qs"].ap(0, [[1, n]]),
                                                 B["sg"].ap(0, [[1, n]]), AL.mult),
                  r=[tg, sg_t], w=[tg])
            ct.op("V", lambda h: h.tensor_tensor(B["qT"].ap(0, [[16, nb], [1, 16]]), B["t2"].ap(0, [[16, nb], [1, 16]]),
                                                 B["s6"].ap(0, [[1, nb], [0, 16]]), AL.mult), r=[tg], w=[tg])
            ct.op("V", lambda h: h.transpose(dst_ap, B["qT"].ap(0, [[1, n]])), r=[tg], w=list(w_extra) + [tg])

        def quant_block(src_ap, dst_ap, n, r_extra=(), w_extra=(), qs_=0):
            qstage_a(src_ap, n, qs_, r_extra=r_extra)
            qstage_s(n, qs_)
            qstage_b(dst_ap, n, qs_, w_extra=w_extra)

        lnw0 = strow.app(0, 1, dims=[[1, 512]])
        lnw1 = dn.app(0, 1, o=0, dims=[[1, 512]])
        lnw2 = dn.app(0, 1, o=512, dims=[[1, 512]])
        rhi = ybf.app(0, 1, dims=[[1, 512]])
        rlo = y2buf.app(0, 1, dims=[[1, 512]])
        nhi = dnh.app(0, 1, o=0, dims=[[1, 512]])
        nlo = dnh.app(0, 1, o=512, dims=[[1, 512]])

        def ln_coef(rms=False):
            ct.op("V", lambda h: h.tensor_scalar_mul(lnw0, lnw0, 1.0 / C), r=["strow"], w=["strow"])
            ct.op("V", lambda h: h.tensor_scalar_mul(lnw1, lnw1, 1.0 / C), r=["strow", "dn"], w=["dn"])
            ct.op("V", lambda h: h.tensor_tensor(lnw2, lnw0, lnw0, AL.mult), r=["strow", "dn"], w=["dn"])
            ct.op("V", lambda h: h.tensor_tensor(lnw1, lnw1, lnw2, AL.subtract), r=["dn"], w=["dn"])
            if rms:
                ct.op("V", lambda h: h.tensor_copy(lnw2, lnw1), r=["dn"], w=["dn"])
            ct.op("S", lambda h: h.activation(lnw1, lnw1, AF.Sqrt, bias=EPS_LN), r=["dn"], w=["dn"])
            ct.op("V", lambda h: h.reciprocal(lnw1, lnw1), r=["dn"], w=["dn"])
            ct.op("V", lambda h: h.tensor_tensor(lnw0, lnw0, lnw1, AL.mult), r=["strow", "dn"], w=["strow"])

        def ln_hilo_mms(src_r, src_nb, bR, btagR, bNB, btagNB):
            ct.op("V", lambda h, s=src_r: h.tensor_copy(rhi, s), r=["strow", "dn"], w=["ybf"])
            ct.op("V", lambda h, s=src_r: h.tensor_tensor(rlo, s, rhi, AL.subtract),
                  r=["strow", "dn", "ybf"], w=["y2b"])
            ct.op("V", lambda h, s=src_nb: h.tensor_copy(nhi, s), r=["strow", "dn", "dnh"], w=["dnh"])
            ct.op("V", lambda h, s=src_nb: h.tensor_tensor(nlo, s, nhi, AL.subtract),
                  r=["strow", "dn", "dnh"], w=["dnh"])
            ct.op("T", lambda h, b=bR: h.matmul(b[:, :], onesr.app(0, 1, dims=[[1, 128]]), rhi,
                                                start=True, stop=False, tile_position=(0, 0)),
                  r=["ybf", "onesr"], w=[btagR])
            ct.op("T", lambda h, b=bR: h.matmul(b[:, :], onesr.app(0, 1, dims=[[1, 128]]), rlo,
                                                start=False, stop=True, tile_position=(0, 0)),
                  r=["y2b", "onesr"], w=[btagR])
            ct.op("T", lambda h, b=bNB: h.matmul(b[:, :], onesr.app(0, 1, dims=[[1, 128]]), nhi,
                                                 start=True, stop=False, tile_position=(0, 0)),
                  r=["dnh", "onesr"], w=[btagNB])
            ct.op("T", lambda h, b=bNB: h.matmul(b[:, :], onesr.app(0, 1, dims=[[1, 128]]), nlo,
                                                 start=False, stop=True, tile_position=(0, 0)),
                  r=["dnh", "onesr"], w=[btagNB])

        def wq_slice_fetch():
            ct.op("GD", lambda h: h.dma_start(
                out=wgs_d, in_=AP(wg_T.tensor, nc.partition_id([mybir.EngineType.Pool]) * 352,
                                  [[IFF, C], [1, 352]])), r=[], w=["wgs"])
            ct.op("GD", lambda h: h.dma_start(
                out=wus_d, in_=AP(wu_T.tensor, nc.partition_id([mybir.EngineType.Pool]) * 352,
                                  [[IFF, C], [1, 352]])), r=[], w=["wus"])
            ct.op("GD", lambda h: h.dma_start(
                out=wds_d, in_=AP(wd_T.tensor, nc.partition_id([mybir.EngineType.Pool]) * 128,
                                  [[C, IFF], [1, 128]])), r=[], w=["wds"])

        def wq_blocklist():
            out = []
            for (w_src, seg0, rows, cols, nm) in ((wgs_d, 0, C, 352, "wgs"),
                                                  (wus_d, WSEG, C, 352, "wus"),
                                                  (wds_d, 2 * WSEG, IFF, 128, "wds")):
                for rc in range(rows // 128):
                    out.append((w_src, seg0, cols, nm, rc))
            return out

        WQB = wq_blocklist()
        wqi = [0]

        def emit_wq(kcount):
            for _ in range(kcount):
                if wqi[0] >= len(WQB):
                    return
                w_src, seg0, n, nm, rc = WQB[wqi[0]]
                wqi[0] += 1
                src = AP(w_src.tensor, rc * 128 * n, [[n, 128], [1, n]])
                dst = AP(wqin_d.tensor, seg0 + rc * 128 * n, [[n, 128], [1, n]])
                ct.op("Y", lambda h, s=src, n=n: h.dma_start(out=qt_x.ap(0, [[1, n]]), in_=s), r=["qtmp0", nm], w=["qt_x"])
                quant_block(qt_x.ap(0, [[1, n]]), qt_q.ap(0, [[1, n]]), n, r_extra=["qt_x"], qs_=0)
                ct.op("Y", lambda h, d=dst, n=n: h.dma_start(out=d, in_=qt_q.ap(0, [[1, n]])), r=["qtmp0"], w=["wqin"])

        def wq_ag():
            ct.op("G", lambda h: h.collective_compute(
                "AllGather", AL.bypass, replica_groups=[list(range(NCORES))],
                ins=[wqin_d.opt()], outs=[wq_g.opt()]), r=["wqin"], w=["wq_g"])

        RK = 2 * WSEG + WDSEG

        def wg_segs(oc):
            a = 128 * oc
            b = a + 128
            segs = []
            r2 = a // 352
            while a < b:
                e = min(b, 352 * (r2 + 1))
                segs.append((r2, a, e))
                a = e
                r2 += 1
            return segs

        def p0_tile(s, j):
            jj = s * TPS + j
            col0 = jj * TILE
            pos0 = j * TILE
            ct.op("GD", lambda h: h.dma_start(
                out=xbS.ap(), in_=AP(x_in.tensor, col0, [[NTOK, 128], [128 * NTOK, 8], [1, 512]])),
                r=["xbrt"], w=["xbrt"])
            for oc in range(24):
                bank, btag = PB[oc % 2], f"pb{oc % 2}"
                for kc in range(8):
                    lhsT = cattn.ap((kc * 24 + oc) * 128, [[1, 128]])
                    rhs = xbS.ap(kc * 512, [[1, 512]])
                    ct.op("T", lambda h, l=lhsT, r2=rhs, b=bank, st=(kc == 0), sp=(kc == 7):
                          h.matmul(b[:, :], l, r2, start=st, stop=sp), r=["cattn", "xbrt"], w=[btag])
                ct.op("S", lambda h, oc=oc, b=bank: h.activation(
                    qkvS.ap(oc * 512, [[1, 512]]), b[:, :], AF.Identity, bias=cattnb.ap(oc, [[1, 1]])),
                    r=[btag, "cattnb"], w=["qkv"])
            cosap = cosb.ap(pos0, [[1, 512]])
            sinap = sinb.ap(pos0, [[1, 512]])
            for g in range(4):
                for (wl, bank, btag, wtag) in ((q2lw, PB[2], "pb2", "q2lw"), (q2lrw, PB[3], "pb3", "q2lrw"),
                                               (kv2lw, PB[4], "pb4", "kv2lw"), (kv2lrw, PB[5], "pb5", "kv2lrw")):
                    srcc = 0 if wtag.startswith("q") else 8
                    for i in range(4):
                        hh = 4 * g + i
                        chunk = srcc + hh // 2
                        par = 64 * (hh % 2)
                        lhsT = wl.app(par, 64)
                        rhs = qkvS.app(par, 64, o=chunk * 512, dims=[[1, 512]])
                        ct.op("T", lambda h, l=lhsT, r2=rhs, b=bank, i=i, par=par:
                              h.matmul(AP(b, 32 * i * 512, [[512, 32], [1, 512]]), l, r2,
                                       start=True, stop=True, tile_position=(par, 32 * i)),
                              r=["qkv", wtag], w=[btag])
                ct.op("V", lambda h: h.scalar_tensor_tensor(rt1.ap(), PB[2][:, :], q2lb4.ap(), cosap, AL.add, AL.mult),
                      r=["pb2", "q2lb4", "cosb", "xbrt"], w=["xbrt1"])
                ct.op("V", lambda h: h.scalar_tensor_tensor(rt2.ap(), PB[3][:, :], q2lbr4.ap(), sinap, AL.add, AL.mult),
                      r=["pb3", "q2lbr4", "sinb", "xbrt"], w=["xbrt2"])
                ct.op("V", lambda h: h.tensor_tensor(rt1.ap(), rt1.ap(), rt2.ap(), AL.add),
                      r=["xbrt1", "xbrt2"], w=["xbrt1"])
                for i in range(4):
                    lhsT = wqkw.app(32 * i, 32)
                    rhs = rt1.app(32 * i, 32, dims=[[1, 512]])
                    ct.op("T", lambda h, l=lhsT, r2=rhs, i=i:
                          h.matmul(AP(PB[6], 32 * i * 512, [[512, 32], [1, 512]]), l, r2,
                                   start=True, stop=True, tile_position=(32 * i, 32 * i)),
                          r=["xbrt1", "wqkw"], w=["pb6"])
                qdst = qT.ap((s * 4 + g) * T + pos0, [[1, 512]])
                ct.op("S", lambda h, d=qdst: h.activation(d, PB[6][:, :], AF.Identity, bias=wqkb4.ap()),
                      r=["pb6", "wqkb4"], w=["qT"])
                kdst = kT.ap((s * 4 + g) * T + pos0, [[1, 512]])
                ct.op("V", lambda h: h.scalar_tensor_tensor(rt1.ap(), PB[4][:, :], kvb4.ap(), cosap, AL.add, AL.mult),
                      r=["pb4", "kvb4", "cosb", "xbrt", "xbrt1"], w=["xbrt1"])
                ct.op("V", lambda h: h.scalar_tensor_tensor(rt2.ap(), PB[5][:, :], kvbr4.ap(), sinap, AL.add, AL.mult),
                      r=["pb5", "kvbr4", "sinb", "xbrt", "xbrt2"], w=["xbrt2"])
                ct.op("V", lambda h, d=kdst: h.tensor_tensor(d, rt1.ap(), rt2.ap(), AL.add),
                      r=["xbrt1", "xbrt2"], w=["kT"])
            for hh in range(H):
                par = 64 * (hh % 2)
                chunk = 16 + hh // 2
                bank, btag = (PB[2], "pb2") if hh % 2 == 0 else (PB[3], "pb3")
                for ksub in range(4):
                    lhsT = qkvS.app(par, 64, o=chunk * 512 + ksub * 128, dims=[[1, 128]])
                    rhs = kv2lw.app(par, 64)
                    ct.op("T", lambda h, l=lhsT, r2=rhs, b=bank, ks=ksub, par=par:
                          h.matmul(AP(b, ks * 32, [[512, 128], [1, 32]]), l, r2,
                                   start=True, stop=True, tile_position=(par, 0)),
                          r=["qkv", "kv2lw"], w=[btag])
                if j == 0:
                    vdst0 = vSB.ap(voff(s, hh, 0), [[1, 1], [1, 32]])
                    ct.op("V", lambda h, b=bank, d=vdst0: h.tensor_tensor(
                        d, AP(b, 0, [[512, 128], [32, 1], [1, 32]]), kvbias.ap(0, [[32, 1], [1, 32]]), AL.add),
                        r=[btag, "kvbias"], w=["vSB"])
                    vdst1 = vSB.ap(voff(s, hh, 1), [[33, 3], [1, 32]])
                    ct.op("V", lambda h, b=bank, d=vdst1: h.tensor_tensor(
                        d, AP(b, 32, [[512, 128], [32, 3], [1, 32]]), kvbias.ap(32, [[32, 3], [1, 32]]), AL.add),
                        r=[btag, "kvbias"], w=["vSB"])
                else:
                    vdst = vSB.ap(voff(s, hh, j * 4), [[33, 4], [1, 32]])
                    ct.op("V", lambda h, b=bank, d=vdst: h.tensor_tensor(
                        d, AP(b, 0, [[512, 128], [32, 4], [1, 32]]), kvbias.ap(0, [[32, 4], [1, 32]]), AL.add),
                        r=[btag, "kvbias"], w=["vSB"])

        for s in range(SEQ):
            for j in range(TPS):
                p0_tile(s, j)
        ct.barrier()

        ct.op("GD", lambda h: h.dma_start(
            out=cprojS.ap(), in_=AP(cproj_wT.tensor, 0, [[C, 128], [128 * C, 8], [128, 8], [1, 128]])),
            w=["cprojw"])

        def p1_tile(s, qt):
            qlo = qt * TILE
            jj = s * TPS + qt
            col0 = jj * TILE
            ct.op("Y", lambda h: h.dma_start(
                out=x0S.ap(), in_=AP(x_in.tensor, col0, [[NTOK, 128], [128 * NTOK, 8], [1, 512]])),
                r=["x0"], w=["x0"])
            nkt = 4 * qt + 4
            for half in range(2):
                for kt in range(nkt):
                    qs = 0 if (kt + 1) * 128 <= qlo else (kt * 128 - qlo)
                    epar = kt % 2
                    for gg in range(2):
                        g = half * 2 + gg
                        for i in range(4):
                            lhsT = kT.app(32 * i, 32, o=(s * 4 + g) * T + kt * 128, dims=[[1, 128]])
                            rhs = qT.app(32 * i, 32, o=(s * 4 + g) * T + qlo + qs, dims=[[1, 512 - qs]])
                            ct.op("T", lambda h, l=lhsT, r2=rhs, b=PB[i], qs=qs, i=i:
                                  h.matmul(AP(b, qs, [[512, 128], [1, 512 - qs]]), l, r2,
                                           start=True, stop=True, tile_position=(32 * i, 0)),
                                  r=["qT", "kT"], w=[f"pb{i}"])
                        for i in range(4):
                            hrel = gg * 4 + i
                            ec = epar * 4096 + hrel * 512
                            ct.op("S", lambda h, i=i, ec=ec, qs=qs: h.activation(
                                eS.ap(ec + qs, [[1, 512 - qs]]),
                                AP(PB[i], qs, [[512, 128], [1, 512 - qs]]), AF.Exp, scale=ISQL),
                                r=[f"pb{i}"], w=[f"e{epar}"])
                            if qs > 0:
                                ct.op("V", lambda h, ec=ec, qs=qs: h.memset(eS.ap(ec, [[1, qs]]), 0.0),
                                      w=[f"e{epar}"])
                            if kt * 128 >= qlo:
                                ct.op("G", lambda h, ec=ec, qs=qs: h.affine_select(
                                    out=eS.ap(ec + qs, [[1, 128]]), in_=eS.ap(ec + qs, [[1, 128]]),
                                    compare_op=AL.is_ge, fill=0.0, base=0,
                                    pattern=[[1, 128]], channel_multiplier=-1),
                                    r=[f"e{epar}"], w=[f"e{epar}"])
                    for pr in range(4):
                        for ab in range(2):
                            hh = half * 8 + 2 * pr + ab
                            hrel = hh - half * 8
                            ec = epar * 4096 + hrel * 512
                            mdim = 64 if kt == 0 else 33
                            lhsT = vSB.ap(voff(s, hh, kt), [[1, mdim]])
                            rhs = eS.ap(ec + qs, [[1, 512 - qs]])
                            ct.op("T", lambda h, l=lhsT, r2=rhs, b=PB[4 + pr], ab=ab, st=(kt == 0), qs=qs, md=mdim:
                                  h.matmul(AP(b, 64 * ab * 512 + qs, [[512, md], [1, 512 - qs]]), l, r2,
                                           start=st, stop=False, tile_position=(0, 64 * ab)),
                                  r=["vSB", f"e{epar}"], w=[f"pb{4 + pr}"])
                for pr in range(4):
                    bank, btag = PB[4 + pr], f"pb{4 + pr}"
                    ct.op("S", lambda h, b=bank, pr=pr: h.activation(
                        yrawS.ap(pr * 512, [[1, 512]]), b[:, :], AF.Identity),
                        r=[btag], w=["e0", "e1", "yraw"])
                    ct.op("GD", lambda h, pr=pr: h.dma_start(
                        out=dn.app(32 * pr, 1, dims=[[1, 512]]),
                        in_=yrawS.app(32, 1, o=pr * 512, dims=[[1, 512]])),
                        r=["yraw"], w=["dn"])
                    ct.op("GD", lambda h, pr=pr: h.dma_start(
                        out=dn.app(32 * pr, 1, o=512, dims=[[1, 512]]),
                        in_=yrawS.app(96, 1, o=pr * 512, dims=[[1, 512]])),
                        r=["yraw"], w=["dn"])
                ct.op("V", lambda h: h.reciprocal(dn.ap(), dn.ap()), r=["dn"], w=["dn"])
                ct.op("V", lambda h: h.tensor_copy(dnh.ap(), dn.ap()), r=["dn"], w=["dnh"])
                ct.op("V", lambda h: h.tensor_tensor(dnl.ap(), dn.ap(), dnh.ap(), AL.subtract),
                      r=["dn", "dnh"], w=["dnl"])
                for pr in range(4):
                    ch = half * 4 + pr
                    bank, btag = PB[4 + pr], f"pb{4 + pr}"
                    for ab in range(2):
                        for lohi, buf, bt2 in ((0, dnh, "dnh"), (1, dnl, "dnl")):
                            ct.op("T", lambda h, b=bank, pr=pr, ab=ab, b2=buf, st=(lohi == 0):
                                  h.matmul(AP(b, 64 * ab * 512, [[512, 64], [1, 512]]),
                                           onesr.app(32 * pr, 1, dims=[[1, 64]]),
                                           b2.app(32 * pr, 1, o=512 * ab, dims=[[1, 512]]),
                                           start=st, stop=(lohi == 1), tile_position=(32 * pr, 64 * ab)),
                                  r=["dnh", "dnl", "onesr", "yraw"], w=[btag])
                    ct.op("V", lambda h, b=bank, pr=pr: h.tensor_tensor(
                        yvS.ap(pr * 512, [[1, 512]]), yrawS.ap(pr * 512, [[1, 512]]), b[:, :], AL.mult),
                        r=[btag, "yraw"], w=["yv"])
                    ab_bank, ab_tag = PB[pr % 2], f"pb{pr % 2}"
                    for ab in range(2):
                        lhsT = l2ow.app(64 * ab, 32, dims=[[1, 64]])
                        rhs = yvS.app(64 * ab, 32, o=pr * 512, dims=[[1, 512]])
                        ct.op("T", lambda h, l=lhsT, r2=rhs, b=ab_bank, ab=ab:
                              h.matmul(AP(b, 64 * ab * 512, [[512, 64], [1, 512]]), l, r2,
                                       start=True, stop=True, tile_position=(64 * ab, 64 * ab)),
                              r=["yv", "l2ow"], w=[ab_tag])
                    ct.op("S", lambda h, b=ab_bank, ch=ch: h.activation(
                        attnS.ap(ch * 512, [[1, 512]]), b[:, :], AF.Identity, bias=l2ob2.ap()),
                        r=[ab_tag, "l2ob2"], w=["attn"])
            for oc in range(8):
                bank, btag = PB[2 + oc % 2], f"pb{2 + oc % 2}"
                for kc in range(8):
                    lhsT = cprojS.ap((kc * 8 + oc) * 128, [[1, 128]])
                    rhs = attnS.ap(kc * 512, [[1, 512]])
                    ct.op("T", lambda h, l=lhsT, r2=rhs, b=bank, st=(kc == 0), sp=(kc == 7):
                          h.matmul(b[:, :], l, r2, start=st, stop=sp), r=["attn", "cprojw"], w=[btag])
                xs = x0S.ap(oc * 512, [[1, 512]])
                ct.op("V", lambda h, b=bank, xs=xs, oc=oc: h.scalar_tensor_tensor(
                    xs, b[:, :], cprojb.ap(oc, [[1, 1]]), xs, AL.add, AL.add),
                    r=[btag, "x0", "cprojb"], w=["x0"])
                ct.op("S", lambda h, xs=xs: h.activation(y2buf.ap(), xs, AF.Square), r=["x0"], w=["y2b"])
                ct.op("V", lambda h, xs=xs: h.tensor_copy(ybf.ap(), xs), r=["x0"], w=["ybf"])
                ct.op("T", lambda h, st=(oc == 0), sp=(oc == 7): h.matmul(
                    AP(PB[6], 0, [[512, 1], [1, 512]]), ones1.ap(0, [[1, 1]]), ybf.ap(), start=st, stop=sp,
                    tile_position=(0, 0)), r=["ybf", "ones1"], w=["pb6"])
                ct.op("T", lambda h, st=(oc == 0), sp=(oc == 7): h.matmul(
                    AP(PB[7], 0, [[512, 1], [1, 512]]), ones1.ap(0, [[1, 1]]), y2buf.ap(), start=st, stop=sp,
                    tile_position=(0, 0)), r=["y2b", "ones1"], w=["pb7"])
            ct.op("V", lambda h: h.tensor_copy(lnw0, AP(PB[6], 0, [[512, 1], [1, 512]])),
                  r=["pb6", "strow"], w=["strow"])
            ct.op("V", lambda h: h.tensor_copy(lnw1, AP(PB[7], 0, [[512, 1], [1, 512]])),
                  r=["pb7", "dn"], w=["dn"])
            ln_coef(rms=False)
            ln_hilo_mms(lnw1, lnw0, PB[6], "pb6", PB[7], "pb7")
            x1dst = out_d if stop == 1 else x1_d
            for oc in range(8):
                xs = x0S.ap(oc * 512, [[1, 512]])
                ct.op("V", lambda h, xs=xs: h.tensor_tensor(xs, xs, PB[6][:, :], AL.mult),
                      r=["x0", "pb6"], w=["x0"])
                ct.op("V", lambda h, xs=xs: h.tensor_tensor(xs, xs, PB[7][:, :], AL.subtract),
                      r=["x0", "pb7"], w=["x0"])
                ct.op("Y", lambda h, xs=xs, oc=oc: h.dma_start(
                    out=AP(x1dst.tensor, oc * 128 * NTOK + col0, [[NTOK, 128], [1, 512]]), in_=xs),
                    r=["x0"], w=["x1_d"])

        wq_slice_fetch()
        p1_ti = 0
        for s in range(SEQ):
            for qt in range(TPS):
                p1_tile(s, qt)
                emit_wq(10)
                p1_ti += 1
                if p1_ti == 4:
                    emit_wq(len(WQB))
                    wq_ag()
        ct.barrier()

        def lnfix(k, y_src, s_src, cf_dst, x_dst, rms_extra=False, to_out=False):
            sw = [statw.ap(i * 32, [[1, 32]]) for i in range(5)]
            ct.op("Y", lambda h: h.dma_start(
                out=statw.ap(0, [[32, 2], [1, 32]]),
                in_=AP(s_src.tensor, 0, [[32, 128], [NTOK, 2], [1, 32]])), r=["statw"], w=["statw"])
            ct.op("V", lambda h: h.tensor_scalar_mul(sw[0], sw[0], 1.0 / C), r=["statw"], w=["statw"])
            ct.op("V", lambda h: h.tensor_scalar_mul(sw[1], sw[1], 1.0 / C), r=["statw"], w=["statw"])
            ct.op("V", lambda h: h.tensor_tensor(sw[2], sw[0], sw[0], AL.mult), r=["statw"], w=["statw"])
            ct.op("V", lambda h: h.tensor_tensor(sw[1], sw[1], sw[2], AL.subtract), r=["statw"], w=["statw"])
            ct.op("S", lambda h: h.activation(sw[2], sw[1], AF.Sqrt, bias=EPS_LN), r=["statw"], w=["statw"])
            ct.op("V", lambda h: h.reciprocal(sw[2], sw[2]), r=["statw"], w=["statw"])
            ct.op("V", lambda h: h.tensor_tensor(sw[3], sw[0], sw[2], AL.mult), r=["statw"], w=["statw"])
            ct.op("V", lambda h: h.tensor_scalar_mul(sw[3], sw[3], -1.0), r=["statw"], w=["statw"])
            def hilo(dst_t, row, src_sw):
                ct.op("V", lambda h: h.tensor_copy(ybf.ap(0, [[1, 32]]), src_sw), r=["statw"], w=["ybf"])
                ct.op("V", lambda h: h.tensor_tensor(strow.ap(0, [[1, 32]]), src_sw, ybf.ap(0, [[1, 32]]),
                                                     AL.subtract), r=["statw", "ybf"], w=["strow"])
                ct.op("Y", lambda h: h.dma_start(
                    out=AP(dst_t.tensor, row * NTOK, [[32, 128], [1, 32]]), in_=ybf.ap(0, [[1, 32]])),
                    r=["ybf"], w=[dst_t.tensor.name])
                ct.op("V", lambda h: h.tensor_copy(ybf.ap(0, [[1, 32]]), strow.ap(0, [[1, 32]])),
                      r=["strow"], w=["ybf"])
                ct.op("Y", lambda h: h.dma_start(
                    out=AP(dst_t.tensor, (row + 1) * NTOK, [[32, 128], [1, 32]]), in_=ybf.ap(0, [[1, 32]])),
                    r=["ybf"], w=[dst_t.tensor.name])
            hilo(cf_dst, 0, sw[2])
            hilo(cf_dst, 2, sw[3])
            if rms_extra:
                ct.op("V", lambda h: h.tensor_tensor(sw[4], sw[2], sw[2], AL.mult), r=["statw"], w=["statw"])
                ct.op("V", lambda h: h.tensor_tensor(sw[4], sw[4], sw[1], AL.mult), r=["statw"], w=["statw"])
                ct.op("S", lambda h: h.activation(sw[4], sw[4], AF.Sqrt, bias=EPS_RMS), r=["statw"], w=["statw"])
                ct.op("V", lambda h: h.reciprocal(sw[4], sw[4]), r=["statw"], w=["statw"])
                ct.op("V", lambda h: h.tensor_tensor(sw[2], sw[2], sw[4], AL.mult), r=["statw"], w=["statw"])
                ct.op("V", lambda h: h.tensor_tensor(sw[3], sw[3], sw[4], AL.mult), r=["statw"], w=["statw"])
                hilo(cfx_d, 0, sw[2])
                hilo(cfx_d, 2, sw[3])
            for jj in range(NT):
                col0 = jj * TILE
                ct.op("Y", lambda h, c0=col0: h.dma_start(
                    out=lyt.ap(), in_=AP(y_src.tensor, c0, [[NTOK, 128], [128 * NTOK, 8], [1, 512]])),
                    r=["lyt"], w=["lyt"])
                ct.op("Y", lambda h, c0=col0: h.dma_start(
                    out=lcf.app(0, 1, dims=[[512, 4], [1, 512]]),
                    in_=AP(cf_dst.tensor, c0, [[NTOK, 1], [NTOK, 4], [1, 512]])),
                    r=["lcf", cf_dst.tensor.name], w=["lcf"])
                for part, btag in ((0, "pb0"), (2, "pb1")):
                    bank = PB[0] if part == 0 else PB[1]
                    for lohi in range(2):
                        ct.op("T", lambda h, b=bank, p=part + lohi, st=(lohi == 0):
                              h.matmul(b[:, :], onesr.app(0, 1, dims=[[1, 128]]),
                                       lcf.app(0, 1, o=p * 512, dims=[[1, 512]]), start=st, stop=(lohi == 1),
                                       tile_position=(0, 0)),
                              r=["lcf", "onesr"], w=[btag])
                if rms_extra:
                    ct.op("Y", lambda h, c0=col0: h.dma_start(
                        out=lcf.app(0, 1, o=2048, dims=[[512, 4], [1, 512]]),
                        in_=AP(cfx_d.tensor, c0, [[NTOK, 1], [NTOK, 4], [1, 512]])),
                        r=["lcf2", "cfx_d"], w=["lcf2"])
                    for part, btag in ((0, "pb2"), (2, "pb3")):
                        bank = PB[2] if part == 0 else PB[3]
                        for lohi in range(2):
                            ct.op("T", lambda h, b=bank, p=part + lohi, st=(lohi == 0):
                                  h.matmul(b[:, :], onesr.app(0, 1, dims=[[1, 128]]),
                                           lcf.app(0, 1, o=2048 + p * 512, dims=[[1, 512]]), start=st, stop=(lohi == 1),
                                           tile_position=(0, 0)),
                                  r=["lcf2", "onesr"], w=[btag])
                for oc in range(8):
                    ys = lyt.ap(oc * 512, [[1, 512]])
                    if rms_extra:
                        ct.op("V", lambda h, ys=ys: h.tensor_tensor(qt_x.ap(0, [[1, 512]]), ys, PB[2][:, :], AL.mult),
                              r=["lyt", "pb2", "qtmp0"], w=["qtmp0"])
                        ct.op("V", lambda h: h.tensor_tensor(qt_u.ap(0, [[1, 512]]), qt_x.ap(0, [[1, 512]]), PB[3][:, :], AL.add),
                              r=["qtmp0", "pb3"], w=["qtmp0"])
                        ct.op("V", lambda h, oc=oc: h.tensor_scalar(
                            lxn.ap(oc * 512, [[1, 512]]), qt_u.ap(0, [[1, 512]]), normwb.ap(oc, [[1, 1]]), None, AL.mult),
                            r=["qtmp0", "normwb"], w=["lxn"])
                    ct.op("V", lambda h, ys=ys: h.tensor_tensor(ys, ys, PB[0][:, :], AL.mult),
                          r=["lyt", "pb0"], w=["lyt"])
                    ct.op("V", lambda h, ys=ys: h.tensor_tensor(ys, ys, PB[1][:, :], AL.add),
                          r=["lyt", "pb1"], w=["lyt"])
                dst = out_d if to_out else x_dst
                ct.op("Y", lambda h, c0=col0, d=dst: h.dma_start(
                    out=AP(d.tensor, c0, [[NTOK, 128], [128 * NTOK, 8], [1, 512]]), in_=lyt.ap()),
                    r=["lyt"], w=["x_out"])
                if rms_extra:
                    ct.op("Y", lambda h, c0=col0: h.dma_start(
                        out=AP(xn_d.tensor, c0, [[NTOK, 128], [128 * NTOK, 8], [1, 512]]), in_=lxn.ap()),
                        r=["lxn"], w=["xn_d"])

        if stop == 1:
            ct.emit(block, sems)
            return nc

        def p2_tile(jj):
            col0 = jj * TILE
            xt = x1tp[jj % 2]
            xtg = f"x1t{jj % 2}"
            ct.op("Y", lambda h, xt=xt: h.dma_start(
                out=xt.ap(), in_=AP(x1_d.tensor, col0, [[NTOK, 128], [128 * NTOK, 8], [1, 512]])),
                r=[xtg], w=[xtg])
            for kc in range(8):
                qstage_a(xt.ap(kc * 512, [[1, 512]]), 512, kc % 2, r_extra=[xtg])
                qstage_s(512, kc % 2)
                if kc >= 1:
                    qstage_b(xqS.ap((kc - 1) * 512, [[1, 512]]), 512, (kc - 1) % 2, w_extra=[f"xq{kc - 1}"])
            qstage_b(xqS.ap(7 * 512, [[1, 512]]), 512, 1, w_extra=["xq7"])
            for oc in range(ICH):
                par = oc % 2
                for (r2, a, e) in wg_segs(oc):
                    off_in = r2 * RK + (a - 352 * r2)
                    ct.op("Y", lambda h, par=par, oi=off_in, do=a - 128 * oc, w_=e - a: h.dma_start(
                        out=wgbuf[par].ap(do, dims=[[128, 8], [1, w_]]),
                        in_=AP(wq_g.tensor, oi, [[352, 128], [128 * 352, 8], [1, w_]])),
                        r=[f"wgb{par}", "wq_g"], w=[f"wgb{par}"])
                    ct.op("Y", lambda h, par=par, oi=off_in + WSEG, do=a - 128 * oc, w_=e - a: h.dma_start(
                        out=wubuf[par].ap(do, dims=[[128, 8], [1, w_]]),
                        in_=AP(wq_g.tensor, oi, [[352, 128], [128 * 352, 8], [1, w_]])),
                        r=[f"wub{par}", "wq_g"], w=[f"wub{par}"])
                for kc in range(8):
                    ct.op("T", lambda h, kc=kc, par=par, st=(kc == 0), sp=(kc == 7):
                          h.matmul(PB[0][:, :], wgbuf[par].ap(kc * 128, [[1, 128]]),
                                   xqS.ap(kc * 512, [[1, 512]]), start=st, stop=sp),
                          r=[f"wgb{par}", f"xq{kc}"], w=["pb0"])
                    ct.op("T", lambda h, kc=kc, par=par, st=(kc == 0), sp=(kc == 7):
                          h.matmul(PB[1][:, :], wubuf[par].ap(kc * 128, [[1, 128]]),
                                   xqS.ap(kc * 512, [[1, 512]]), start=st, stop=sp),
                          r=[f"wub{par}", f"xq{kc}"], w=["pb1"])
                ct.op("S", lambda h: h.activation(silS.ap(), PB[0][:, :], AF.Silu), r=["pb0"], w=["sil"])
                ct.op("V", lambda h, oc=oc: h.tensor_tensor(htS.ap(oc * 512, [[1, 512]]), silS.ap(),
                                                            PB[1][:, :], AL.mult),
                      r=["sil", "pb1"], w=[f"ht{oc}"])
                qstage_a(htS.ap(oc * 512, [[1, 512]]), 512, oc % 2, r_extra=[f"ht{oc}"])
                qstage_s(512, oc % 2)
                if oc >= 1:
                    qstage_b(hqS.ap((oc - 1) * 512, [[1, 512]]), 512, (oc - 1) % 2, w_extra=[f"hq{oc - 1}"])
            qstage_b(hqS.ap((ICH - 1) * 512, [[1, 512]]), 512, (ICH - 1) % 2, w_extra=[f"hq{ICH - 1}"])
            for oc in range(8):
                par = oc % 2
                bank, btag = PB[2 + par], f"pb{2 + par}"
                ct.op("Y", lambda h, oc=oc, par=par: h.dma_start(
                    out=wdbuf[par].ap(), in_=AP(wq_g.tensor, oc * RK + 2 * WSEG, [[128, 128], [128 * 128, 22], [1, 128]])),
                    r=[f"wdb{par}", "wq_g"], w=[f"wdb{par}"])
                for kc in range(ICH):
                    ct.op("T", lambda h, kc=kc, par=par, b=bank, st=(kc == 0), sp=(kc == ICH - 1):
                          h.matmul(b[:, :], wdbuf[par].ap(kc * 128, [[1, 128]]),
                                   hqS.ap(kc * 512, [[1, 512]]), start=st, stop=sp),
                          r=[f"wdb{par}", f"hq{kc}"], w=[btag])
                xs = xt.ap(oc * 512, [[1, 512]])
                ct.op("V", lambda h, b=bank, xs=xs: h.tensor_tensor(xs, xs, b[:, :], AL.add),
                      r=[btag, xtg], w=[xtg])
                ct.op("S", lambda h, xs=xs: h.activation(y2buf.ap(), xs, AF.Square), r=[xtg], w=["y2b"])
                ct.op("V", lambda h, xs=xs: h.tensor_copy(ybf.ap(), xs), r=[xtg], w=["ybf"])
                ct.op("T", lambda h, st=(oc == 0), sp=(oc == 7): h.matmul(
                    AP(PB[6], 0, [[512, 1], [1, 512]]), ones1.ap(0, [[1, 1]]), ybf.ap(), start=st, stop=sp,
                    tile_position=(0, 0)), r=["ybf", "ones1"], w=["pb6"])
                ct.op("T", lambda h, st=(oc == 0), sp=(oc == 7): h.matmul(
                    AP(PB[7], 0, [[512, 1], [1, 512]]), ones1.ap(0, [[1, 1]]), y2buf.ap(), start=st, stop=sp,
                    tile_position=(0, 0)), r=["y2b", "ones1"], w=["pb7"])
                if stop == 12:
                    ct.op("Y", lambda h, xs=xs, oc=oc: h.dma_start(
                        out=AP(y2_d.tensor, oc * 128 * NTOK + col0, [[NTOK, 128], [1, 512]]), in_=xs),
                        r=[xtg], w=["y2_d"])
            ct.op("V", lambda h: h.tensor_copy(lnw0, AP(PB[6], 0, [[512, 1], [1, 512]])),
                  r=["pb6", "strow"], w=["strow"])
            ct.op("V", lambda h: h.tensor_copy(lnw1, AP(PB[7], 0, [[512, 1], [1, 512]])),
                  r=["pb7", "dn"], w=["dn"])
            ln_coef(rms=True)
            ln_hilo_mms(lnw1, lnw0, PB[4], "pb4", PB[5], "pb5")
            ct.op("V", lambda h: h.tensor_tensor(lnw2, lnw2, lnw1, AL.mult), r=["dn"], w=["dn"])
            ct.op("V", lambda h: h.tensor_tensor(lnw2, lnw2, lnw1, AL.mult), r=["dn"], w=["dn"])
            ct.op("S", lambda h: h.activation(lnw2, lnw2, AF.Sqrt, bias=EPS_RMS), r=["dn"], w=["dn"])
            ct.op("V", lambda h: h.reciprocal(lnw2, lnw2), r=["dn"], w=["dn"])
            ct.op("V", lambda h: h.tensor_tensor(lnw1, lnw1, lnw2, AL.mult), r=["dn"], w=["dn"])
            ct.op("V", lambda h: h.tensor_tensor(lnw0, lnw0, lnw2, AL.mult), r=["strow", "dn"], w=["strow"])
            ln_hilo_mms(lnw1, lnw0, PB[6], "pb6", PB[7], "pb7")
            x2dst = out_d if stop == 2 else x2_d
            for oc in range(8):
                ys = xt.ap(oc * 512, [[1, 512]])
                ct.op("V", lambda h, ys=ys: h.tensor_tensor(qt_x.ap(0, [[1, 512]]), ys, PB[6][:, :], AL.mult),
                      r=[xtg, "pb6", "qtmp0"], w=["qtmp0"])
                ct.op("V", lambda h: h.tensor_tensor(qt_x.ap(0, [[1, 512]]), qt_x.ap(0, [[1, 512]]),
                                                     PB[7][:, :], AL.subtract),
                      r=["qtmp0", "pb7"], w=["qtmp0"])
                ct.op("V", lambda h, oc=oc: h.tensor_scalar(
                    xnb.ap(), qt_x.ap(0, [[1, 512]]), normwb.ap(oc, [[1, 1]]), None, AL.mult),
                    r=["qtmp0", "normwb"], w=["xnb"])
                ct.op("Y", lambda h, oc=oc, c0=col0: h.dma_start(
                    out=AP(xn_d.tensor, oc * 128 * NTOK + c0, [[NTOK, 128], [1, 512]]), in_=xnb.ap()),
                    r=["xnb"], w=["xn_d"])
                ct.op("V", lambda h, ys=ys: h.tensor_tensor(ys, ys, PB[4][:, :], AL.mult),
                      r=[xtg, "pb4"], w=[xtg])
                ct.op("V", lambda h, ys=ys: h.tensor_tensor(ys, ys, PB[5][:, :], AL.subtract),
                      r=[xtg, "pb5"], w=[xtg])
                ct.op("Y", lambda h, ys=ys, oc=oc: h.dma_start(
                    out=AP(x2dst.tensor, oc * 128 * NTOK + col0, [[NTOK, 128], [1, 512]]), in_=ys),
                    r=[xtg], w=["x2_d"])

        for jj in range(NT):
            p2_tile(jj)
        ct.barrier()

        if stop == 12:
            ct.op("Y", lambda h: h.dma_start(out=out_d, in_=y2_d), r=["y2_d"], w=["x_out"])
            ct.barrier()
            ct.emit(block, sems)
            return nc

        if stop == 2:
            ct.emit(block, sems)
            return nc

        def p3_scan():
            ct.op("G", lambda h: h.collective_compute(
                "AllGather", AL.bypass, replica_groups=[list(range(NCORES))],
                ins=[xn_d.opt()], outs=[xn_g.opt()]), r=["xn_d"], w=["xn_g"])
            def extract_op(h):
                cid = nc.partition_id([mybir.EngineType.Pool])
                src = AP(xn_g.tensor, cid * (128 * NTOK),
                         [[NTOK, 128], [C * NTOK, 8], [T, 2], [1, T]])
                return h.dma_start(out=xsc.ap(0, [[2 * T, 8], [T, 2], [1, T]]), in_=src)
            ct.op("GD", extract_op, r=["xn_g", "cid"], w=["xsc"])

            ct.op("V", lambda h: h.memset(stS.ap(1024, [[1, 16]]), 0.0), w=["st_i"])
            NW = T // 32
            for w in range(NW):
                bkA, tgA = PB[w % 2], f"pb{w % 2}"
                bkB, tgB = PB[4 + w % 2], f"pb{4 + w % 2}"
                ct.op("T", lambda h, w=w, b=bkA: h.matmul(
                    b[:, 0:256], bbd.ap(), xsc.ap(w * 32, [[1, 32], [T, 8]]), start=True, stop=False),
                    r=["xsc", "bbd"], w=[tgA])
                ct.op("T", lambda h, w=w, b=bkB: h.matmul(
                    b[:, 0:256], bbd.ap(), xsc.ap(w * 32 + 8 * T, [[1, 32], [T, 8]]), start=True, stop=False),
                    r=["xsc", "bbd"], w=[tgB])
                for t_in in range(32):
                    t = w * 32 + t_in
                    slot = t % 64
                    for grp, bk, tg, off in ((0, bkA, tgA, 0), (1, bkB, tgB, 8)):
                        if t == 0:
                            prev = stS.ap(1024 + off, [[1, 8]])
                            ptag = "st_i"
                        else:
                            prev = stS.ap(((t - 1) % 64) * 16 + off, [[1, 8]])
                            ptag = f"st{grp}{((t - 1) % 64) // 32}"
                        ct.op("T", lambda h, b=bk, ti=t_in, pv=prev: h.matmul(
                            AP(b, ti * 8, [[512, 128], [1, 8]]), abd.ap(), pv, start=False, stop=True),
                            r=[ptag, "abd"], w=[tg])
                        ct.op("S", lambda h, b=bk, ti=t_in, sl=slot, off=off: h.activation(
                            stS.ap(sl * 16 + off, [[1, 8]]), AP(b, ti * 8, [[512, 128], [1, 8]]), AF.Tanh),
                            r=[tg], w=[f"st{grp}{slot // 32}"])
                s0 = (w * 32) % 64
                ct.op("T", lambda h, s0=s0: h.matmul(
                    PB[2][:, :], cbd.ap(), stS.ap(s0 * 16, [[1, 512]]), start=True, stop=True),
                    r=[f"st0{s0 // 32}", f"st1{s0 // 32}", "cbd"], w=["pb2"])
                acc = mxacc[(w // 8) % 2]
                atag = f"mxa{(w // 8) % 2}"
                ct.op("V", lambda h, w=w, acc=acc: h.scalar_tensor_tensor(
                    acc.ap((w % 8) * 32, [[1, 32], [256, 16]]),
                    xsc.ap(w * 32, [[1, 32], [T, 16]]), skipb.ap(), PB[2][:, :], AL.mult, AL.add),
                    r=["pb2", "xsc", "skipb", atag], w=[atag])
                if (w + 1) % 8 == 0:
                    wf = w - 7
                    ct.op("Y", lambda h, wf=wf, acc=acc: h.dma_start(
                        out=AP(mx_d.tensor, wf * 32, [[H * T, 128], [T, 16], [1, 256]]),
                        in_=acc.ap(0, [[256, 16], [1, 256]])), r=[atag], w=["mx_d"])
            ct.op("G", lambda h: h.collective_compute(
                "AllGather", AL.bypass, replica_groups=[list(range(NCORES))],
                ins=[mx_d.opt()], outs=[mx_g.opt()]), r=["mx_d"], w=["mx_g"])

        if stop == 14:
            ct.op("GD", lambda h: h.dma_start(out=out_d, in_=xn_d), r=["xn_d"], w=["x_out"])
            ct.barrier()
            ct.emit(block, sems)
            return nc

        p3_scan()
        ct.barrier()

        if stop == 15:
            ct.op("GD", lambda h: h.dma_start(
                out=AP(out_d.tensor, 0, [[NTOK, 128], [1, NTOK]]),
                in_=AP(mx_d.tensor, 0, [[H * T, 128], [1, NTOK]])), r=["mx_d"], w=["x_out"])
            ct.barrier()
            ct.emit(block, sems)
            return nc

        def p3b():
            def extract2_op(h):
                cid = nc.partition_id([mybir.EngineType.Pool])
                src = AP(mx_g.tensor, cid * (2 * T),
                         [[H * T, 128], [128 * H * T, 8], [1, 2 * T]])
                return h.dma_start(out=mxfm.ap(0, [[NTOK, 8], [1, 2 * T]]), in_=src)
            ct.op("GD", extract2_op, r=["mx_g", "cid"], w=["mxfm"])
            ct.op("GD", lambda h: h.dma_start(
                out=woutS.ap(), in_=AP(wout_T.tensor, 0, [[C, 128], [128 * C, 8], [128, 8], [1, 128]])),
                w=["woutw"])
            for jj in range(NT):
                col0 = jj * TILE
                ct.op("Y", lambda h, c0=col0: h.dma_start(
                    out=x2t.ap(), in_=AP(x2_d.tensor, c0, [[NTOK, 128], [128 * NTOK, 8], [1, 512]])),
                    r=["x2t"], w=["x2t"])
                for oc in range(8):
                    bank, btag = PB[2 + oc % 2], f"pb{2 + oc % 2}"
                    for kc in range(8):
                        ct.op("T", lambda h, kc=kc, oc=oc, b=bank, c0=col0, st=(kc == 0), sp=(kc == 7):
                              h.matmul(b[:, :], woutS.ap((kc * 8 + oc) * 128, [[1, 128]]),
                                       mxfm.ap(kc * NTOK + c0, [[1, 512]]), start=st, stop=sp),
                              r=["mxfm", "woutw"], w=[btag])
                    xs = x2t.ap(oc * 512, [[1, 512]])
                    ct.op("V", lambda h, b=bank, xs=xs, oc=oc: h.scalar_tensor_tensor(
                        qt_x.ap(0, [[1, 512]]), b[:, :], boutb.ap(oc, [[1, 1]]), xs, AL.add, AL.add),
                        r=[btag, "x2t", "boutb", "qtmp"], w=["qtmp"])
                    ct.op("V", lambda h, xs=xs: h.tensor_tensor(xs, xs, qt_x.ap(0, [[1, 512]]), AL.add),
                        r=["x2t", "qtmp"], w=["x2t"])
                    ct.op("S", lambda h, xs=xs: h.activation(y2buf.ap(), xs, AF.Square), r=["x2t"], w=["y2b"])
                    ct.op("V", lambda h, xs=xs: h.tensor_copy(ybf.ap(), xs), r=["x2t"], w=["ybf"])
                    ct.op("T", lambda h, st=(oc == 0), sp=(oc == 7): h.matmul(
                        AP(PB[6], 0, [[512, 1], [1, 512]]), ones1.ap(0, [[1, 1]]), ybf.ap(), start=st, stop=sp,
                        tile_position=(0, 0)), r=["ybf", "ones1"], w=["pb6"])
                    ct.op("T", lambda h, st=(oc == 0), sp=(oc == 7): h.matmul(
                        AP(PB[7], 0, [[512, 1], [1, 512]]), ones1.ap(0, [[1, 1]]), y2buf.ap(), start=st, stop=sp,
                        tile_position=(0, 0)), r=["y2b", "ones1"], w=["pb7"])
                    if stop == 13:
                        ct.op("Y", lambda h, xs=xs, oc=oc, c0=col0: h.dma_start(
                            out=AP(y3_d.tensor, oc * 128 * NTOK + c0, [[NTOK, 128], [1, 512]]), in_=xs),
                            r=["x2t"], w=["y3_d"])
                ct.op("V", lambda h: h.tensor_copy(lnw0, AP(PB[6], 0, [[512, 1], [1, 512]])),
                      r=["pb6", "strow"], w=["strow"])
                ct.op("V", lambda h: h.tensor_copy(lnw1, AP(PB[7], 0, [[512, 1], [1, 512]])),
                      r=["pb7", "dn"], w=["dn"])
                ln_coef(rms=False)
                ln_hilo_mms(lnw1, lnw0, PB[6], "pb6", PB[7], "pb7")
                x3dst = out_d if stop == 3 else x3_d
                for oc in range(8):
                    xs = x2t.ap(oc * 512, [[1, 512]])
                    ct.op("V", lambda h, xs=xs: h.tensor_tensor(xs, xs, PB[6][:, :], AL.mult),
                          r=["x2t", "pb6"], w=["x2t"])
                    ct.op("V", lambda h, xs=xs: h.tensor_tensor(xs, xs, PB[7][:, :], AL.subtract),
                          r=["x2t", "pb7"], w=["x2t"])
                    ct.op("Y", lambda h, xs=xs, oc=oc, c0=col0: h.dma_start(
                        out=AP(x3dst.tensor, oc * 128 * NTOK + c0, [[NTOK, 128], [1, 512]]), in_=xs),
                        r=["x2t"], w=["x3_d"])

        p3b()
        ct.barrier()
        if stop == 13:
            ct.op("Y", lambda h: h.dma_start(out=out_d, in_=y3_d), r=["y3_d"], w=["x_out"])
            ct.barrier()
            ct.emit(block, sems)
            return nc
        if stop == 3:
            ct.emit(block, sems)
            return nc

        for wdst, wsrc, nm in ((wrS, wr_T, "wrw"), (wkS, wk_T, "wkw"), (wvS, wv_T, "wvw")):
            ct.op("GD", lambda h, d=wdst, s=wsrc: h.dma_start(
                out=d.ap(), in_=AP(s.tensor, 0, [[C, 128], [128 * C, 8], [128, 8], [1, 128]])), w=[nm])

        def p4_tile(jj):
            col0 = jj * TILE
            ct.op("Y", lambda h: h.dma_start(
                out=x3t.ap(), in_=AP(x3_d.tensor, col0, [[NTOK, 128], [128 * NTOK, 8], [1, 512]])),
                r=["x3t"], w=["x3t"])
            if jj % TPS == 0:
                ct.op("Y", lambda h: h.dma_start(
                    out=xpt.ap(1, [[512, 8], [1, 511]]),
                    in_=AP(x3_d.tensor, col0, [[NTOK, 128], [128 * NTOK, 8], [1, 511]])),
                    r=["xpt"], w=["xpt"])
                ct.op("V", lambda h: h.memset(xpt.ap(0, [[512, 8], [1, 1]]), 0.0), r=["xpt"], w=["xpt"])
            else:
                ct.op("Y", lambda h: h.dma_start(
                    out=xpt.ap(), in_=AP(x3_d.tensor, col0 - 1, [[NTOK, 128], [128 * NTOK, 8], [1, 512]])),
                    r=["xpt"], w=["xpt"])
            for oc in range(8):
                ct.op("V", lambda h, oc=oc: h.tensor_scalar(p4tmp.ap(0, [[1, 512]]), xpt.ap(oc * 512, [[1, 512]]),
                                                            omrb.ap(oc, [[1, 1]]), None, AL.mult),
                      r=["xpt", "omrb"], w=["p4t"])
                ct.op("V", lambda h, oc=oc: h.scalar_tensor_tensor(xrS.ap(oc * 512, [[1, 512]]),
                                                                   x3t.ap(oc * 512, [[1, 512]]), murb.ap(oc, [[1, 1]]),
                                                                   p4tmp.ap(0, [[1, 512]]), AL.mult, AL.add),
                      r=["x3t", "murb", "p4t"], w=["xr"])
                ct.op("V", lambda h, oc=oc: h.tensor_scalar(p4tmp.ap(0, [[1, 512]]), xpt.ap(oc * 512, [[1, 512]]),
                                                            omkb.ap(oc, [[1, 1]]), None, AL.mult),
                      r=["xpt", "omkb"], w=["p4t"])
                ct.op("V", lambda h, oc=oc: h.scalar_tensor_tensor(xkS.ap(oc * 512, [[1, 512]]),
                                                                   x3t.ap(oc * 512, [[1, 512]]), mukb.ap(oc, [[1, 1]]),
                                                                   p4tmp.ap(0, [[1, 512]]), AL.mult, AL.add),
                      r=["x3t", "mukb", "p4t"], w=["xk"])
            for oc in range(8):
                for kc in range(8):
                    ct.op("T", lambda h, kc=kc, oc=oc, st=(kc == 0), sp=(kc == 7):
                          h.matmul(PB[0][:, :], wrS.ap((kc * 8 + oc) * 128, [[1, 128]]),
                                   xrS.ap(kc * 512, [[1, 512]]), start=st, stop=sp),
                          r=["xr", "wrw"], w=["pb0"])
                ct.op("S", lambda h, oc=oc: h.activation(rbS.ap(oc * 512, [[1, 512]]), PB[0][:, :], AF.Sigmoid),
                      r=["pb0"], w=["rb"])
                for kc in range(8):
                    ct.op("T", lambda h, kc=kc, oc=oc, st=(kc == 0), sp=(kc == 7):
                          h.matmul(PB[1][:, :], wkS.ap((kc * 8 + oc) * 128, [[1, 128]]),
                                   xkS.ap(kc * 512, [[1, 512]]), start=st, stop=sp),
                          r=["xk", "wkw"], w=["pb1"])
                ct.op("S", lambda h, oc=oc: h.activation(y2buf.ap(), PB[1][:, :], AF.Relu),
                      r=["pb1"], w=["y2b"])
                ct.op("V", lambda h, oc=oc: h.tensor_tensor(kaS.ap(oc * 512, [[1, 512]]), y2buf.ap(), y2buf.ap(), AL.mult),
                      r=["y2b"], w=["ka"])
            for oc in range(8):
                bank, btag = PB[2 + oc % 2], f"pb{2 + oc % 2}"
                for kc in range(8):
                    ct.op("T", lambda h, kc=kc, oc=oc, b=bank, st=(kc == 0), sp=(kc == 7):
                          h.matmul(b[:, :], wvS.ap((kc * 8 + oc) * 128, [[1, 128]]),
                                   kaS.ap(kc * 512, [[1, 512]]), start=st, stop=sp),
                          r=["ka", "wvw"], w=[btag])
                xs = x3t.ap(oc * 512, [[1, 512]])
                ct.op("V", lambda h, b=bank, oc=oc: h.scalar_tensor_tensor(
                    p4tmp.ap(0, [[1, 512]]), b[:, :], 1.0, rbS.ap(oc * 512, [[1, 512]]), AL.mult, AL.mult),
                    r=[btag, "rb", "p4t"], w=["p4t"])
                ct.op("V", lambda h, xs=xs: h.tensor_tensor(xs, xs, p4tmp.ap(0, [[1, 512]]), AL.add),
                      r=["x3t", "p4t"], w=["x3t"])
                ct.op("S", lambda h, xs=xs: h.activation(y2buf.ap(), xs, AF.Square), r=["x3t"], w=["y2b"])
                ct.op("V", lambda h, xs=xs: h.tensor_copy(ybf.ap(), xs), r=["x3t"], w=["ybf"])
                ct.op("T", lambda h, st=(oc == 0), sp=(oc == 7): h.matmul(
                    AP(PB[6], 0, [[512, 1], [1, 512]]), ones1.ap(0, [[1, 1]]), ybf.ap(), start=st, stop=sp,
                    tile_position=(0, 0)), r=["ybf", "ones1"], w=["pb6"])
                ct.op("T", lambda h, st=(oc == 0), sp=(oc == 7): h.matmul(
                    AP(PB[7], 0, [[512, 1], [1, 512]]), ones1.ap(0, [[1, 1]]), y2buf.ap(), start=st, stop=sp,
                    tile_position=(0, 0)), r=["y2b", "ones1"], w=["pb7"])
            ct.op("V", lambda h: h.tensor_copy(lnw0, AP(PB[6], 0, [[512, 1], [1, 512]])),
                  r=["pb6", "strow"], w=["strow"])
            ct.op("V", lambda h: h.tensor_copy(lnw1, AP(PB[7], 0, [[512, 1], [1, 512]])),
                  r=["pb7", "dn"], w=["dn"])
            ln_coef(rms=False)
            ln_hilo_mms(lnw1, lnw0, PB[6], "pb6", PB[7], "pb7")
            for oc in range(8):
                xs = x3t.ap(oc * 512, [[1, 512]])
                ct.op("V", lambda h, xs=xs: h.tensor_tensor(xs, xs, PB[6][:, :], AL.mult),
                      r=["x3t", "pb6"], w=["x3t"])
                ct.op("V", lambda h, xs=xs: h.tensor_tensor(xs, xs, PB[7][:, :], AL.subtract),
                      r=["x3t", "pb7"], w=["x3t"])
                ct.op("Y", lambda h, xs=xs, oc=oc: h.dma_start(
                    out=AP(out_d.tensor, oc * 128 * NTOK + col0, [[NTOK, 128], [1, 512]]), in_=xs),
                    r=["x3t"], w=["x_out"])

        for jj in range(NT):
            p4_tile(jj)
        ct.barrier()

        ct.emit(block, sems)

    return nc


_NC_CACHE = {}


def _get_nc(stop):
    if stop not in _NC_CACHE:
        _NC_CACHE[stop] = build_nc(stop)
    return _NC_CACHE[stop]


def _rope_tables():
    inv_freq = (1.0 / 10000.0 ** (np.arange(0, LAT, 2, dtype=np.float32) / LAT)).astype(np.float32)
    t = np.arange(T, dtype=np.float32)
    freqs = np.outer(t, inv_freq)
    cos = np.repeat(np.cos(freqs), 2, axis=-1)[:, :LAT].astype(np.float32)
    sin = np.repeat(np.sin(freqs), 2, axis=-1)[:, :LAT].astype(np.float32)
    cos4 = np.tile(np.ascontiguousarray(cos.T), (4, 1))
    sin4 = np.tile(np.ascontiguousarray(sin.T), (4, 1))
    return np.ascontiguousarray(cos4), np.ascontiguousarray(sin4)


def kernel(stop=4, **inputs):
    x = inputs["L0_attn_x"]
    cos4, sin4 = _rope_tables()
    shared = {
        "cattn_wT": np.ascontiguousarray(inputs["L0_attn_c_attn_w"].T),
        "cattn_b": np.ascontiguousarray(inputs["L0_attn_c_attn_b"]),
        "q2l_wT": np.ascontiguousarray(inputs["L0_attn_q2l_w"].T),
        "q2l_b": np.ascontiguousarray(inputs["L0_attn_q2l_b"]),
        "kv2l_wT": np.ascontiguousarray(inputs["L0_attn_kv2l_w"].T),
        "kv2l_b": np.ascontiguousarray(inputs["L0_attn_kv2l_b"]),
        "l2o_wT": np.ascontiguousarray(inputs["L0_attn_l2o_w"].T),
        "l2o_b": np.ascontiguousarray(inputs["L0_attn_l2o_b"]),
        "wqk_wT": np.ascontiguousarray(inputs["L0_attn_wqk_w"].T),
        "wqk_b": np.ascontiguousarray(inputs["L0_attn_wqk_b"]),
        "cproj_wT": np.ascontiguousarray(inputs["L0_attn_cproj_w"].T),
        "cproj_b": np.ascontiguousarray(inputs["L0_attn_cproj_b"]),
        "wg_T": np.ascontiguousarray(inputs["L0_ffn_w_gate"].T),
        "wu_T": np.ascontiguousarray(inputs["L0_ffn_w_up"].T),
        "wd_T": np.ascontiguousarray(inputs["L0_ffn_w_down"].T),
        "norm_w": np.ascontiguousarray(inputs["L1_attn_norm_w"]),
        "wout_T": np.ascontiguousarray(inputs["L1_attn_w_out"].T),
        "bout_v": np.ascontiguousarray(inputs["L1_attn_b_out"]),
        "wr_T": np.ascontiguousarray(inputs["L1_ffn_W_r"].T),
        "wk_T": np.ascontiguousarray(inputs["L1_ffn_W_k"].T),
        "wv_T": np.ascontiguousarray(inputs["L1_ffn_W_v"].T),
        "mu_r": np.ascontiguousarray(inputs["L1_ffn_mu_r"]),
        "mu_k": np.ascontiguousarray(inputs["L1_ffn_mu_k"]),
        "cos4": cos4,
        "sin4": sin4,
        "altsgn": np.tile(np.array([-1.0, 1.0], dtype=np.float32), 64),
    }
    a_full = inputs["L1_attn_a"]
    b_full = inputs["L1_attn_b"]
    c_full = inputs["L1_attn_c"]
    skip_full = inputs["L1_attn_skip"]
    in_maps = []
    for c in range(NCORES):
        m = dict(shared)
        xc = x[2 * c:2 * c + 2]
        m["x_in"] = np.ascontiguousarray(xc.transpose(2, 0, 1).reshape(C, NTOK))
        m["a_sh"] = np.ascontiguousarray(a_full[2 * c:2 * c + 2])
        m["b_sh"] = np.ascontiguousarray(b_full[2 * c:2 * c + 2])
        m["c_sh"] = np.ascontiguousarray(c_full[2 * c:2 * c + 2])
        m["skip_sh"] = np.ascontiguousarray(skip_full[2 * c:2 * c + 2].reshape(128))
        m["core_id_in"] = np.array([[c]], dtype=np.int32)
        in_maps.append(m)
    nc = _get_nc(stop)
    res = run_bass_kernel_spmd(nc, in_maps, core_ids=list(range(NCORES)))
    global LAST_RES
    LAST_RES = res
    outs = []
    for c in range(NCORES):
        o = res.results[c]["out_d"]
        outs.append(o.reshape(C, SEQ, T).transpose(1, 2, 0))
    return np.ascontiguousarray(np.concatenate(outs, axis=0))


if __name__ == "__main__":
    pass


# version 11
# speedup vs baseline: 1.0012x; 1.0012x over previous
LAST_RES = None
"""Single-launch raw-Bass megakernel for mk_seq_postln_d2 on 8 TRN2 NeuronCores.

Module (depth 2, post-LN): x -> LN(x + MLA(x)) -> LN(. + fp4FFN(.)) ->
LN(. + tanh-scan mixer(.)) -> LN(. + sq-relu FFN(.)).   B=16, T=2048, C=1024.

Per core: 2 sequences (batch shard).  Feature-major stream [C -> 8x128 partitions,
4096 tokens free].  Stream spills to HBM between phases (compute-bound module, HBM
has ~35x headroom).  bf16 matmul operands / fp32 PSUM accumulation and elementwise.
LayerNorms run in dedicated LNFIX phases (isolates the sqrt activation table).
The L1 tanh-scan is head-resharded across cores via AllGather (2 heads x 16 seqs per
core) so each scan step is one 128-contraction matmul + one tanh.
fp4 quant-dequant: exact fp32 rounding ladder in DVE 32x32-transposed space.
"""
import numpy as np
import ml_dtypes
import concourse.bass as bass
import concourse.mybir as mybir
from concourse.bass_utils import run_bass_kernel_spmd

F32 = mybir.dt.float32
BF16 = mybir.dt.bfloat16
I32 = mybir.dt.int32
AL = mybir.AluOpType
AF = mybir.ActivationFunctionType
AX = mybir.AxisListType.X
AP = bass.AP

NCORES = 8
SEQ, T, C, NCH = 2, 2048, 1024, 8
H, HD, LAT = 16, 64, 32
IFF, ICH = 2816, 22
NTOK = SEQ * T
TILE = 512
NT = NTOK // TILE
TPS = T // TILE
KTN = T // 128
EPS_LN, EPS_RMS = 1e-5, 1e-4
C_HALF = float(1.5 * 2 ** 22)
C_INT = float(1.5 * 2 ** 23)
ISQL = float(LAT) ** -0.5


class Ctx:
    NY, NGD = 12, 4
    ENGS = ("T", "V", "S", "G") + tuple(f"Y{i}" for i in range(12)) + tuple(f"GD{i}" for i in range(4))

    def __init__(self, nc):
        self.nc = nc
        self.count = {e: 0 for e in self.ENGS}
        self.ops = {e: [] for e in ("T", "V", "S", "G", "Y")}
        self.last_w = {}
        self.readers = {}
        self.rr = {"Y": 0, "GD": 0}

    def op(self, eng, fn, r=(), w=()):
        q = eng in ("Y", "GD")
        if q:
            stream = "Y" if eng == "Y" else "G"
            sem = f"{eng}{self.rr[eng]}"
            self.rr[eng] = (self.rr[eng] + 1) % (self.NY if eng == "Y" else self.NGD)
        else:
            stream = eng
            sem = eng
        need = {}
        for tag in r:
            lw = self.last_w.get(tag)
            if lw:
                need[lw[0]] = max(need.get(lw[0], 0), lw[1])
        for tag in w:
            lw = self.last_w.get(tag)
            if lw and (q or lw[0] != sem):
                need[lw[0]] = max(need.get(lw[0], 0), lw[1])
            for (e2, v) in self.readers.get(tag, ()):
                if q or e2 != sem:
                    need[e2] = max(need.get(e2, 0), v)
        if q and self.count[sem] > 0:
            need[sem] = max(need.get(sem, 0), self.count[sem])
        inc = 16 if q else 1
        self.count[sem] += inc
        val = self.count[sem]
        self.ops[stream].append((need, sem, fn, val))
        for tag in w:
            self.last_w[tag] = (sem, val)
            self.readers[tag] = []
        for tag in r:
            self.readers.setdefault(tag, []).append((sem, val))

    def barrier(self):
        snap = dict(self.count)
        for stream in self.ops:
            self.ops[stream].append(("B", snap))
        self.last_w = {}
        self.readers = {}

    def emit(self, block, sems):
        def body_of(stream):
            def body(h):
                wm = {e: 0 for e in self.ENGS}
                for item in self.ops[stream]:
                    if item[0] == "B":
                        for e, v in item[1].items():
                            if v > wm[e]:
                                h.wait_ge(sems[e], v)
                                wm[e] = v
                        continue
                    need, eng, fn, val = item
                    for e, v in need.items():
                        if v > wm[e]:
                            h.wait_ge(sems[e], v)
                            wm[e] = v
                    inst = fn(h)
                    if inst is not None:
                        inst.then_inc(sems[eng], 16 if eng not in ("T", "V", "S", "G") else 1)
            return body
        block.sync(body_of("Y"))
        block.tensor(body_of("T"))
        block.vector(body_of("V"))
        block.scalar(body_of("S"))
        block.gpsimd(body_of("G"))


class Buf:
    def __init__(self, slab, slab_w, off, n, dt, slab_dt=BF16):
        self.slab, self.sw, self.off, self.dt = slab, slab_w, off, dt
        self.native = (dt == slab_dt) or (mybir.dt.size(dt) == mybir.dt.size(slab_dt))
        self.esz = 1 if self.native else 2
        self.n = n // self.esz

    def _mk(self, par_pair, o, dims):
        if dims is None:
            dims = [[1, self.n]]
        if self.native:
            a = AP(self.slab, self.off + par_pair[2] + o, [list(par_pair[:2])] + [list(d) for d in dims])
            if a.dtype != self.dt:
                a = a.bitcast(self.dt)
            return a
        nd = [[s * 2, c] for s, c in dims[:-1]]
        s_in, c_in = dims[-1]
        assert s_in == 1, f"fp32-on-bf16 innermost must be contiguous, got step {s_in}"
        nd.append([1, 2 * c_in])
        a = AP(self.slab, self.off + par_pair[2] + o * 2, [list(par_pair[:2])] + nd)
        return a.bitcast(F32)

    def ap(self, o=0, dims=None):
        return self._mk((self.sw, 128, 0), o, dims)

    def app(self, p0, np_, o=0, dims=None):
        return self._mk((self.sw, np_, p0 * self.sw), o, dims)


def build_nc(stop=4):
    nc = bass.Bass("TRN2", target_bir_lowering=False, debug=False)

    def din(name, shape, dt=F32):
        return nc.dram_tensor(name, shape, dt, kind="ExternalInput").ap()

    x_in = din("x_in", [C, NTOK])
    cattn_wT = din("cattn_wT", [C, 3 * C])
    cattn_b = din("cattn_b", [3 * C])
    q2l_wT = din("q2l_wT", [HD, LAT])
    q2l_b = din("q2l_b", [LAT])
    kv2l_wT = din("kv2l_wT", [HD, LAT])
    kv2l_b = din("kv2l_b", [LAT])
    l2o_wT = din("l2o_wT", [LAT, HD])
    l2o_b = din("l2o_b", [HD])
    wqk_wT = din("wqk_wT", [LAT, LAT])
    wqk_b = din("wqk_b", [LAT])
    cproj_wT = din("cproj_wT", [C, C])
    cproj_b = din("cproj_b", [C])
    wg_T = din("wg_T", [C, IFF])
    wu_T = din("wu_T", [C, IFF])
    wd_T = din("wd_T", [IFF, C])
    norm_w = din("norm_w", [C])
    a_sh = din("a_sh", [2, 64, 64])
    b_sh = din("b_sh", [2, 64, 64])
    c_sh = din("c_sh", [2, 64, 64])
    skip_sh = din("skip_sh", [128])
    wout_T = din("wout_T", [C, C])
    bout_v = din("bout_v", [C])
    wr_T = din("wr_T", [C, C])
    wk_T = din("wk_T", [C, C])
    wv_T = din("wv_T", [C, C])
    mu_r = din("mu_r", [C])
    mu_k = din("mu_k", [C])
    altsgn = din("altsgn", [128])
    cos4 = din("cos4", [128, T])
    sin4 = din("sin4", [128, T])
    core_id_in = din("core_id_in", [1, 1], I32)

    out_d = nc.dram_tensor("out_d", [C, NTOK], F32, kind="ExternalOutput").ap()


    def dsc(name, shape, dt=F32, **kw):
        return nc.dram_tensor(name, shape, dt, kind="Internal", **kw).ap()

    y1_d = dsc("y1_d", [C, NTOK])
    x1_d = dsc("x1_d", [C, NTOK])
    y2_d = dsc("y2_d", [C, NTOK])
    x2_d = dsc("x2_d", [C, NTOK])
    y3_d = dsc("y3_d", [C, NTOK])
    x3_d = dsc("x3_d", [C, NTOK])
    y4_d = dsc("y4_d", [C, NTOK])
    s_d = [dsc(f"s{k}_d", [2, NTOK]) for k in range(4)]
    cf_d = [dsc(f"cf{k}_d", [4, NTOK], BF16) for k in range(4)]
    cfx_d = dsc("cfx_d", [4, NTOK], BF16)
    WSEG = 1024 * 352
    WDSEG = 2816 * 128
    wgs_d = dsc("wgs_d", [C, 352])
    wus_d = dsc("wus_d", [C, 352])
    wds_d = dsc("wds_d", [IFF, 128])
    wqin_d = dsc("wqin_d", [2 * WSEG + WDSEG], BF16)
    wq_g = dsc("wq_g", [NCORES * (2 * WSEG + WDSEG)], BF16)
    wgq_d = dsc("wgq_d", [C, IFF], BF16)
    wuq_d = dsc("wuq_d", [C, IFF], BF16)
    wdq_d = dsc("wdq_d", [IFF, C], BF16)
    xn_d = dsc("xn_d", [C, NTOK], BF16)
    xn_g = dsc("xn_g", [NCORES * C, NTOK], BF16)
    mx_d = dsc("mx_d", [128, H * T], BF16)
    mx_g = dsc("mx_g", [NCORES * 128, H * T], BF16)

    ct = Ctx(nc)
    sems = {e: nc.semaphore(f"sem{e}").__enter__() for e in Ctx.ENGS}
    PF = 2300
    PW = 8000
    WB = 91648

    with (
        nc.sbuf_tensor("persf", [128, PF], F32) as persf_t,
        nc.sbuf_tensor("pers", [128, PW], BF16) as pers_t,
        nc.sbuf_tensor("arena", [128, WB], BF16) as arena_t,
        nc.psum_tensor("pb0", [128, 512], F32) as _p0,
        nc.psum_tensor("pb1", [128, 512], F32) as _p1,
        nc.psum_tensor("pb2", [128, 512], F32) as _p2,
        nc.psum_tensor("pb3", [128, 512], F32) as _p3,
        nc.psum_tensor("pb4", [128, 512], F32) as _p4,
        nc.psum_tensor("pb5", [128, 512], F32) as _p5,
        nc.psum_tensor("pb6", [128, 512], F32) as _p6,
        nc.psum_tensor("pb7", [128, 512], F32) as _p7,
        nc.Block() as block,
    ):
        pass_sems = sems
        PB = [_p0, _p1, _p2, _p3, _p4, _p5, _p6, _p7]

        def PF32(off, n, dt=F32):
            return Buf(persf_t, PF, off, n, dt, slab_dt=F32)

        def P(off, n, dt=BF16):
            return Buf(pers_t, PW, off, n, dt)

        def A(off, n, dt=BF16):
            return Buf(arena_t, WB, off, n, dt)

        cattnb = PF32(0, 24)
        cprojb = PF32(24, 8)
        boutb = PF32(32, 8)
        normwb = PF32(40, 8)
        murb = PF32(48, 8)
        mukb = PF32(56, 8)
        omrb = PF32(64, 8)
        omkb = PF32(72, 8)
        q2lb4 = PF32(80, 1)
        q2lbr4 = PF32(81, 1)
        kvb4 = PF32(82, 1)
        kvbr4 = PF32(83, 1)
        wqkb4 = PF32(84, 1)
        l2ob2 = PF32(85, 1)
        skipb = PF32(86, 1)
        cidb = PF32(87, 1, I32)
        kvrow = PF32(88, 32)
        kvbias = PF32(120, 128)
        abdf = PF32(248, 128)
        dn = PF32(376, 1024)
        strow = PF32(1400, 512)
        statw = PF32(1912, 160)
        qt_am = PF32(2072, 32)
        qt_inv = PF32(2104, 32)
        qt_s6 = PF32(2136, 32)
        altsg = PF32(2170, 1)
        constc = PF32(2168, 2)
        qtB_am = PF32(2172, 32)
        qtB_inv = PF32(2204, 32)
        qtB_s6 = PF32(2236, 32)

        nc.const_aps.register(F32, EPS_LN, constc.ap(0, [[1, 1]]))
        nc.const_aps.register(F32, EPS_RMS, constc.ap(1, [[1, 1]]))
        q2lw = P(0, 32)
        q2lrw = P(32, 32)
        kv2lw = P(64, 32)
        kv2lrw = P(96, 32)
        wqkw = P(128, 32)
        l2ow = P(160, 64)
        cosb = P(224, T)
        sinb = P(2272, T)
        abd = P(4320, 128)
        bbd = P(4448, 128)
        cbd = P(4576, 128)
        ones1 = P(4704, 2)
        onesr = P(4706, 128)
        dnh = P(5858, 1024)
        dnl = P(6882, 1024)
        y2buf = P(4834, 512)
        ybf = P(5346, 512)

        cattn = A(0, 24576)
        qT = A(24576, 16384)
        kT = A(40960, 16384)
        vSB = A(57344, 17888)
        qkvS = A(75232, 12288)
        xbS = A(87520, 4096)
        rt1 = A(87520, 512)
        rt2 = A(88544, 512)

        def voff(s2, hh2, kt2):
            return (s2 * H + hh2) * 559 + (0 if kt2 == 0 else 64 + (kt2 - 1) * 33)
        qt_x = A(75232, 1024, F32)
        qt_t = A(76256, 1024, F32)
        qt_xn = A(77280, 1024, F32)
        qt_u = A(78304, 1024, F32)
        qt_r2 = A(79328, 1024, F32)
        qtB_t = A(56832, 1024, F32)
        qtB_xn = A(57856, 1024, F32)
        qtB_u = A(58880, 1024, F32)
        qtB_r2 = A(59904, 1024, F32)
        qtB_r1 = A(60928, 512)
        qtB_a5 = A(61440, 512)
        qtB_sg = A(61952, 512)
        qtB_t2 = A(62464, 512)
        qtB_qs = A(62976, 512)
        qtB_qT = A(63488, 512)

        qt_r1 = A(80352, 512)
        qt_a5 = A(80864, 512)
        qt_sg = A(81376, 512)
        qt_t2 = A(81888, 512)
        qt_qs = A(82400, 512)
        qt_qT = A(82912, 512)
        qt_q = A(83424, 512)
        QTS = [
            {"t": qt_t, "xn": qt_xn, "u": qt_u, "r2": qt_r2, "am": qt_am, "inv": qt_inv,
             "s6": qt_s6, "r1": qt_r1, "a5": qt_a5, "sg": qt_sg, "t2": qt_t2, "qs": qt_qs, "qT": qt_qT},
            {"t": qtB_t, "xn": qtB_xn, "u": qtB_u, "r2": qtB_r2, "am": qtB_am, "inv": qtB_inv,
             "s6": qtB_s6, "r1": qtB_r1, "a5": qtB_a5, "sg": qtB_sg, "t2": qtB_t2, "qs": qtB_qs, "qT": qtB_qT},
        ]
        x0S = A(0, 8192, F32)
        cprojS = A(8192, 8192)
        eS = A(16384, 8192)
        yrawS = A(16384, 8192, F32)
        yvS = A(83936, 2048)
        attnS = A(85984, 4096)

        x1t = A(0, 8192, F32)
        x1tB = A(64000, 8192, F32)
        x1tp = [x1t, x1tB]
        xqS = A(8192, 4096)
        htS = A(12288, 22528, F32)
        hqS = A(34816, 11264)
        wgbuf = [A(46080 + i * 1024, 1024) for i in range(2)]
        wubuf = [A(48128 + i * 1024, 1024) for i in range(2)]
        wdbuf = [A(50176 + i * 2816, 2816) for i in range(2)]
        silS = A(55808, 512)
        xnb = A(56320, 512)

        lyt = A(0, 8192, F32)
        lcf = A(8192, 4096)
        lxn = A(12288, 4096)

        xsc = A(0, 32768)
        stS = A(32768, 1024 + 16)
        mxacc = [A(33808 + i * 4096, 4096) for i in range(2)]
        mxfm = A(0, 32768)
        woutS = A(32768, 8192)
        x2t = A(40960, 8192, F32)

        wrS = A(0, 8192)
        wkS = A(8192, 8192)
        wvS = A(16384, 8192)
        x3t = A(24576, 8192, F32)
        xpt = A(32768, 8192, F32)
        xrS = A(40960, 4096)
        xkS = A(45056, 4096)
        rbS = A(49152, 4096)
        kaS = A(53248, 4096)
        p4tmp = A(57344, 1024, F32)

        def init_phase():
            ct.op("Y", lambda h: h.dma_start(out=cattnb.ap(), in_=AP(cattn_b.tensor, 0, [[1, 128], [128, 24]]),
                                             allow_slow_non_contiguous=True), w=["cattnb"])
            for dst, src, nm in ((cprojb, cproj_b, "cprojb"), (boutb, bout_v, "boutb"),
                                 (normwb, norm_w, "normwb"), (murb, mu_r, "murb"), (mukb, mu_k, "mukb")):
                ct.op("Y", lambda h, d=dst, s=src: h.dma_start(out=d.ap(), in_=AP(s.tensor, 0, [[1, 128], [128, 8]]),
                                                               allow_slow_non_contiguous=True), w=[nm])
            ct.op("Y", lambda h: h.dma_start(out=skipb.ap(), in_=AP(skip_sh.tensor, 0, [[1, 128], [1, 1]])), w=["skipb"])
            ct.op("Y", lambda h: h.dma_start(out=cidb.app(0, 1, dims=[[1, 1]]), in_=core_id_in), w=["cid"])
            for half in range(2):
                ct.op("GD", lambda h, p=64 * half: h.dma_start(out=q2lw.app(p, 64), in_=q2l_wT), w=["q2lw"])
                ct.op("GD", lambda h, p=64 * half: h.dma_start(out=kv2lw.app(p, 64), in_=kv2l_wT), w=["kv2lw"])
                ct.op("GD", lambda h, p=64 * half: h.dma_start(out=l2ow.app(p, 32, dims=[[1, 64]]), in_=l2o_wT), w=["l2ow"])
            for g in range(4):
                ct.op("GD", lambda h, p=32 * g: h.dma_start(out=wqkw.app(p, 32), in_=wqk_wT), w=["wqkw"])
            ct.op("GD", lambda h: h.dma_start(out=cosb.ap(), in_=cos4), w=["cosb"])
            ct.op("GD", lambda h: h.dma_start(out=sinb.ap(), in_=sin4), w=["sinb"])
            ct.op("GD", lambda h: h.dma_start(
                out=cattn.ap(), in_=AP(cattn_wT.tensor, 0, [[3 * C, 128], [128 * 3 * C, 8], [128, 24], [1, 128]])),
                w=["cattn"])
            for g in range(4):
                ct.op("Y", lambda h, p=32 * g: h.dma_start(out=q2lb4.app(p, 32, dims=[[1, 1]]), in_=AP(q2l_b.tensor, 0, [[1, 32], [1, 1]])), w=["q2lb4"])
                ct.op("Y", lambda h, p=32 * g: h.dma_start(out=kvb4.app(p, 32, dims=[[1, 1]]), in_=AP(kv2l_b.tensor, 0, [[1, 32], [1, 1]])), w=["kvb4"])
                ct.op("Y", lambda h, p=32 * g: h.dma_start(out=wqkb4.app(p, 32, dims=[[1, 1]]), in_=AP(wqk_b.tensor, 0, [[1, 32], [1, 1]])), w=["wqkb4"])
            for half in range(2):
                ct.op("Y", lambda h, p=64 * half: h.dma_start(out=l2ob2.app(p, 64, dims=[[1, 1]]), in_=AP(l2o_b.tensor, 0, [[1, 64], [1, 1]])), w=["l2ob2"])
            ct.op("V", lambda h: h.memset(ones1.ap(), 1.0), w=["ones1"])
            ct.op("V", lambda h: h.memset(constc.ap(0, [[1, 1]]), EPS_LN), w=["constc"])
            ct.op("V", lambda h: h.memset(constc.ap(1, [[1, 1]]), EPS_RMS), w=["constc"])
            ct.op("V", lambda h: h.memset(onesr.ap(0, [[1, 128]]), 1.0), w=["onesr"])
            ct.op("V", lambda h: h.memset(dn.ap(), 1.0), w=["dn"])
            for dst, src, nm, nmr in ((q2lrw, q2lw, "q2lw", "q2lrw"), (kv2lrw, kv2lw, "kv2lw", "kv2lrw")):
                ct.op("V", lambda h, d=dst, s=src: h.tensor_scalar_mul(
                    d.ap(0, [[2, 16]]), s.ap(1, [[2, 16]]), -1.0), r=[nm], w=[nmr])
                ct.op("V", lambda h, d=dst, s=src: h.tensor_scalar_mul(
                    d.ap(1, [[2, 16]]), s.ap(0, [[2, 16]]), 1.0), r=[nm], w=[nmr])
            ct.op("Y", lambda h: h.dma_start(out=altsg.ap(), in_=AP(altsgn.tensor, 0, [[1, 128], [1, 1]])), w=["altsg"])
            pair_mask = [i + 1 if i % 2 == 0 else i - 1 for i in range(32)]
            for dst, src, nm, nmr in ((q2lbr4, q2lb4, "q2lb4", "q2lbr4"), (kvbr4, kvb4, "kvb4", "kvbr4")):
                ct.op("V", lambda h, d=dst, s=src: h.stream_shuffle(d.ap(), s.ap(), pair_mask), r=[nm], w=[nmr])
                ct.op("V", lambda h, d=dst: h.tensor_scalar(d.ap(), d.ap(), altsg.ap(), None, AL.mult),
                      r=[nmr, "altsg"], w=[nmr])
            ct.op("V", lambda h: h.transpose(
                kvrow.app(0, 32, dims=[[1, 32]]),
                AP(persf_t, kvb4.off, [[PF, 32], [0, 32]])), r=["kvb4"], w=["kvrow"])
            ct.op("V", lambda h: h.tensor_copy(dnh.app(0, 1, dims=[[1, 32]]), kvrow.app(0, 1, dims=[[1, 32]])),
                  r=["kvrow"], w=["dnh"])
            ct.op("V", lambda h: h.tensor_tensor(dnl.app(0, 1, dims=[[1, 32]]), kvrow.app(0, 1, dims=[[1, 32]]),
                                                 dnh.app(0, 1, dims=[[1, 32]]), AL.subtract),
                  r=["kvrow", "dnh"], w=["dnl"])
            for lohi, buf in ((0, dnh), (1, dnl)):
                ct.op("T", lambda h, b2=buf, st=(lohi == 0): h.matmul(
                    AP(PB[0], 0, [[512, 128], [1, 32]]), onesr.app(0, 1, dims=[[1, 128]]),
                    b2.app(0, 1, dims=[[1, 32]]), start=st, stop=(lohi == 1), tile_position=(0, 0)),
                    r=["dnh", "dnl", "onesr"], w=["pb0"])
            for rep in range(4):
                ct.op("V", lambda h, rp=rep: h.tensor_copy(
                    kvbias.ap(32 * rp, [[1, 32]]), AP(PB[0], 0, [[512, 128], [1, 32]])),
                    r=["pb0"], w=["kvbias"])
            ct.op("V", lambda h: h.memset(abdf.ap(), 0.0), w=["abdf"])
            ct.op("GD", lambda h: h.dma_start(out=abdf.app(0, 64, dims=[[1, 64]]), in_=a_sh[0]), w=["abdf"])
            ct.op("GD", lambda h: h.dma_start(out=abdf.app(64, 64, o=64, dims=[[1, 64]]), in_=a_sh[1]), w=["abdf"])
            ct.op("S", lambda h: h.activation(abd.ap(), abdf.ap(), AF.Tanh), r=["abdf"], w=["abd"])
            for wbd, wsrc, nm in ((bbd, b_sh, "bbd"), (cbd, c_sh, "cbd")):
                ct.op("V", lambda h, d=wbd: h.memset(d.ap(), 0.0), w=[nm])
                ct.op("GD", lambda h, d=wbd, s=wsrc: h.dma_start(out=d.app(0, 64, dims=[[1, 64]]), in_=s[0]), w=[nm])
                ct.op("GD", lambda h, d=wbd, s=wsrc: h.dma_start(out=d.app(64, 64, o=64, dims=[[1, 64]]), in_=s[1]), w=[nm])
            ct.op("V", lambda h: h.tensor_scalar(omrb.ap(), murb.ap(), -1.0, 1.0, AL.mult, AL.add), r=["murb"], w=["omrb"])
            ct.op("V", lambda h: h.tensor_scalar(omkb.ap(), mukb.ap(), -1.0, 1.0, AL.mult, AL.add), r=["mukb"], w=["omkb"])
            ct.op("V", lambda h: h.memset(vSB.ap(), 0.0), w=["vSB"])
            for s2 in range(SEQ):
                for hh2 in range(H):
                    base = (s2 * H + hh2) * 559
                    ct.op("V", lambda h, b2=base: h.memset(vSB.ap(b2 + 32, [[1, 1]]), 1.0), w=["vSB"])
                    ct.op("V", lambda h, b2=base: h.memset(vSB.ap(b2 + 96, [[33, 15], [1, 1]]), 1.0), w=["vSB"])

        init_phase()
        ct.barrier()

        def qstage_a(src_ap, n, qs_, r_extra=()):
            nb = n // 16
            B = QTS[qs_]
            tg = f"qtmp{qs_}"
            ct.op("V", lambda h: h.transpose(B["t"].ap(0, [[1, n]]), src_ap), r=list(r_extra) + [tg], w=[tg])
            ct.op("V", lambda h: h.tensor_reduce(B["am"].ap(0, [[1, nb]]), B["t"].ap(0, [[16, nb], [1, 16]]),
                                                 AX, AL.max, apply_absolute_value=True), r=[tg], w=[tg])
            ct.op("V", lambda h: h.tensor_scalar(B["s6"].ap(0, [[1, nb]]), B["am"].ap(0, [[1, nb]]),
                                                 1.0 / 6.0, 1e-12, AL.mult, AL.max), r=[tg], w=[tg])
            ct.op("V", lambda h: h.reciprocal(B["inv"].ap(0, [[1, nb]]), B["s6"].ap(0, [[1, nb]])), r=[tg], w=[tg])
            ct.op("V", lambda h: h.tensor_tensor(B["xn"].ap(0, [[16, nb], [1, 16]]), B["t"].ap(0, [[16, nb], [1, 16]]),
                                                 B["inv"].ap(0, [[1, nb], [0, 16]]), AL.mult), r=[tg], w=[tg])

        def qstage_s(n, qs_):
            B = QTS[qs_]
            tg = f"qtmp{qs_}"
            sg_t = f"qsg{qs_}"
            ct.op("S", lambda h: h.activation(B["u"].ap(0, [[1, n]]), B["xn"].ap(0, [[1, n]]), AF.Abs),
                  r=[tg], w=[tg])
            ct.op("S", lambda h: h.activation(B["sg"].ap(0, [[1, n]]), B["xn"].ap(0, [[1, n]]), AF.Sign), r=[tg], w=[sg_t])

        def qstage_b(dst_ap, n, qs_, w_extra=()):
            nb = n // 16
            B = QTS[qs_]
            tg = f"qtmp{qs_}"
            sg_t = f"qsg{qs_}"
            ct.op("V", lambda h: h.tensor_scalar(B["r2"].ap(0, [[1, n]]), B["u"].ap(0, [[1, n]]), 2.0, C_HALF, AL.min, AL.add),
                  r=[tg], w=[tg])
            ct.op("V", lambda h: h.tensor_scalar(B["r1"].ap(0, [[1, n]]), B["r2"].ap(0, [[1, n]]), C_HALF, None, AL.subtract),
                  r=[tg], w=[tg])
            ct.op("V", lambda h: h.tensor_scalar(B["r2"].ap(0, [[1, n]]), B["u"].ap(0, [[1, n]]), 2.0, 4.0, AL.max, AL.min),
                  r=[tg], w=[tg])
            ct.op("V", lambda h: h.tensor_scalar(B["r2"].ap(0, [[1, n]]), B["r2"].ap(0, [[1, n]]), C_INT, C_INT, AL.add, AL.subtract),
                  r=[tg], w=[tg])
            ct.op("V", lambda h: h.tensor_scalar(B["a5"].ap(0, [[1, n]]), B["u"].ap(0, [[1, n]]), 5.0, 2.0, AL.is_gt, AL.mult),
                  r=[tg], w=[tg])
            ct.op("V", lambda h: h.tensor_tensor(B["t2"].ap(0, [[1, n]]), B["r1"].ap(0, [[1, n]]), B["r2"].ap(0, [[1, n]]), AL.add),
                  r=[tg], w=[tg])
            ct.op("V", lambda h: h.tensor_tensor(B["t2"].ap(0, [[1, n]]), B["t2"].ap(0, [[1, n]]), B["a5"].ap(0, [[1, n]]), AL.add),
                  r=[tg], w=[tg])
            ct.op("V", lambda h: h.scalar_tensor_tensor(B["qs"].ap(0, [[1, n]]), B["t2"].ap(0, [[1, n]]), 2.0,
                                                        B["sg"].ap(0, [[1, n]]), AL.subtract, AL.mult),
                  r=[tg, sg_t], w=[tg])
            ct.op("V", lambda h: h.tensor_tensor(B["qT"].ap(0, [[16, nb], [1, 16]]), B["qs"].ap(0, [[16, nb], [1, 16]]),
                                                 B["s6"].ap(0, [[1, nb], [0, 16]]), AL.mult), r=[tg], w=[tg])
            ct.op("V", lambda h: h.transpose(dst_ap, B["qT"].ap(0, [[1, n]])), r=[tg], w=list(w_extra) + [tg])

        def quant_block(src_ap, dst_ap, n, r_extra=(), w_extra=(), qs_=0):
            qstage_a(src_ap, n, qs_, r_extra=r_extra)
            qstage_s(n, qs_)
            qstage_b(dst_ap, n, qs_, w_extra=w_extra)

        lnw0 = strow.app(0, 1, dims=[[1, 512]])
        lnw1 = dn.app(0, 1, o=0, dims=[[1, 512]])
        lnw2 = dn.app(0, 1, o=512, dims=[[1, 512]])
        rhi = ybf.app(0, 1, dims=[[1, 512]])
        rlo = y2buf.app(0, 1, dims=[[1, 512]])
        nhi = dnh.app(0, 1, o=0, dims=[[1, 512]])
        nlo = dnh.app(0, 1, o=512, dims=[[1, 512]])

        def ln_coef(rms=False):
            ct.op("V", lambda h: h.tensor_scalar_mul(lnw0, lnw0, 1.0 / C), r=["strow"], w=["strow"])
            ct.op("V", lambda h: h.tensor_scalar_mul(lnw1, lnw1, 1.0 / C), r=["strow", "dn"], w=["dn"])
            ct.op("V", lambda h: h.tensor_tensor(lnw2, lnw0, lnw0, AL.mult), r=["strow", "dn"], w=["dn"])
            ct.op("V", lambda h: h.tensor_tensor(lnw1, lnw1, lnw2, AL.subtract), r=["dn"], w=["dn"])
            if rms:
                ct.op("V", lambda h: h.tensor_copy(lnw2, lnw1), r=["dn"], w=["dn"])
            ct.op("S", lambda h: h.activation(lnw1, lnw1, AF.Sqrt, bias=EPS_LN), r=["dn"], w=["dn"])
            ct.op("V", lambda h: h.reciprocal(lnw1, lnw1), r=["dn"], w=["dn"])
            ct.op("V", lambda h: h.tensor_tensor(lnw0, lnw0, lnw1, AL.mult), r=["strow", "dn"], w=["strow"])

        def ln_hilo_mms(src_r, src_nb, bR, btagR, bNB, btagNB):
            ct.op("V", lambda h, s=src_r: h.tensor_copy(rhi, s), r=["strow", "dn"], w=["ybf"])
            ct.op("V", lambda h, s=src_r: h.tensor_tensor(rlo, s, rhi, AL.subtract),
                  r=["strow", "dn", "ybf"], w=["y2b"])
            ct.op("V", lambda h, s=src_nb: h.tensor_copy(nhi, s), r=["strow", "dn", "dnh"], w=["dnh"])
            ct.op("V", lambda h, s=src_nb: h.tensor_tensor(nlo, s, nhi, AL.subtract),
                  r=["strow", "dn", "dnh"], w=["dnh"])
            ct.op("T", lambda h, b=bR: h.matmul(b[:, :], onesr.app(0, 1, dims=[[1, 128]]), rhi,
                                                start=True, stop=False, tile_position=(0, 0)),
                  r=["ybf", "onesr"], w=[btagR])
            ct.op("T", lambda h, b=bR: h.matmul(b[:, :], onesr.app(0, 1, dims=[[1, 128]]), rlo,
                                                start=False, stop=True, tile_position=(0, 0)),
                  r=["y2b", "onesr"], w=[btagR])
            ct.op("T", lambda h, b=bNB: h.matmul(b[:, :], onesr.app(0, 1, dims=[[1, 128]]), nhi,
                                                 start=True, stop=False, tile_position=(0, 0)),
                  r=["dnh", "onesr"], w=[btagNB])
            ct.op("T", lambda h, b=bNB: h.matmul(b[:, :], onesr.app(0, 1, dims=[[1, 128]]), nlo,
                                                 start=False, stop=True, tile_position=(0, 0)),
                  r=["dnh", "onesr"], w=[btagNB])

        def wq_slice_fetch():
            ct.op("GD", lambda h: h.dma_start(
                out=wgs_d, in_=AP(wg_T.tensor, nc.partition_id([mybir.EngineType.Pool]) * 352,
                                  [[IFF, C], [1, 352]])), r=[], w=["wgs"])
            ct.op("GD", lambda h: h.dma_start(
                out=wus_d, in_=AP(wu_T.tensor, nc.partition_id([mybir.EngineType.Pool]) * 352,
                                  [[IFF, C], [1, 352]])), r=[], w=["wus"])
            ct.op("GD", lambda h: h.dma_start(
                out=wds_d, in_=AP(wd_T.tensor, nc.partition_id([mybir.EngineType.Pool]) * 128,
                                  [[C, IFF], [1, 128]])), r=[], w=["wds"])

        def wq_blocklist():
            out = []
            for (w_src, seg0, rows, cols, nm) in ((wgs_d, 0, C, 352, "wgs"),
                                                  (wus_d, WSEG, C, 352, "wus"),
                                                  (wds_d, 2 * WSEG, IFF, 128, "wds")):
                for rc in range(rows // 128):
                    out.append((w_src, seg0, cols, nm, rc))
            return out

        WQB = wq_blocklist()
        wqi = [0]

        def emit_wq(kcount):
            for _ in range(kcount):
                if wqi[0] >= len(WQB):
                    return
                w_src, seg0, n, nm, rc = WQB[wqi[0]]
                wqi[0] += 1
                src = AP(w_src.tensor, rc * 128 * n, [[n, 128], [1, n]])
                dst = AP(wqin_d.tensor, seg0 + rc * 128 * n, [[n, 128], [1, n]])
                ct.op("Y", lambda h, s=src, n=n: h.dma_start(out=qt_x.ap(0, [[1, n]]), in_=s), r=["qtmp0", nm], w=["qt_x"])
                quant_block(qt_x.ap(0, [[1, n]]), qt_q.ap(0, [[1, n]]), n, r_extra=["qt_x"], qs_=0)
                ct.op("Y", lambda h, d=dst, n=n: h.dma_start(out=d, in_=qt_q.ap(0, [[1, n]])), r=["qtmp0"], w=["wqin"])

        def wq_ag():
            ct.op("G", lambda h: h.collective_compute(
                "AllGather", AL.bypass, replica_groups=[list(range(NCORES))],
                ins=[wqin_d.opt()], outs=[wq_g.opt()]), r=["wqin"], w=["wq_g"])

        RK = 2 * WSEG + WDSEG

        def wg_segs(oc):
            a = 128 * oc
            b = a + 128
            segs = []
            r2 = a // 352
            while a < b:
                e = min(b, 352 * (r2 + 1))
                segs.append((r2, a, e))
                a = e
                r2 += 1
            return segs

        def p0_tile(s, j):
            jj = s * TPS + j
            col0 = jj * TILE
            pos0 = j * TILE
            ct.op("GD", lambda h: h.dma_start(
                out=xbS.ap(), in_=AP(x_in.tensor, col0, [[NTOK, 128], [128 * NTOK, 8], [1, 512]])),
                r=["xbrt"], w=["xbrt"])
            for oc in range(24):
                bank, btag = PB[oc % 2], f"pb{oc % 2}"
                for kc in range(8):
                    lhsT = cattn.ap((kc * 24 + oc) * 128, [[1, 128]])
                    rhs = xbS.ap(kc * 512, [[1, 512]])
                    ct.op("T", lambda h, l=lhsT, r2=rhs, b=bank, st=(kc == 0), sp=(kc == 7):
                          h.matmul(b[:, :], l, r2, start=st, stop=sp), r=["cattn", "xbrt"], w=[btag])
                ct.op("S", lambda h, oc=oc, b=bank: h.activation(
                    qkvS.ap(oc * 512, [[1, 512]]), b[:, :], AF.Identity, bias=cattnb.ap(oc, [[1, 1]])),
                    r=[btag, "cattnb"], w=["qkv"])
            cosap = cosb.ap(pos0, [[1, 512]])
            sinap = sinb.ap(pos0, [[1, 512]])
            for g in range(4):
                for (wl, bank, btag, wtag) in ((q2lw, PB[2], "pb2", "q2lw"), (q2lrw, PB[3], "pb3", "q2lrw"),
                                               (kv2lw, PB[4], "pb4", "kv2lw"), (kv2lrw, PB[5], "pb5", "kv2lrw")):
                    srcc = 0 if wtag.startswith("q") else 8
                    for i in range(4):
                        hh = 4 * g + i
                        chunk = srcc + hh // 2
                        par = 64 * (hh % 2)
                        lhsT = wl.app(par, 64)
                        rhs = qkvS.app(par, 64, o=chunk * 512, dims=[[1, 512]])
                        ct.op("T", lambda h, l=lhsT, r2=rhs, b=bank, i=i, par=par:
                              h.matmul(AP(b, 32 * i * 512, [[512, 32], [1, 512]]), l, r2,
                                       start=True, stop=True, tile_position=(par, 32 * i)),
                              r=["qkv", wtag], w=[btag])
                ct.op("V", lambda h: h.scalar_tensor_tensor(rt1.ap(), PB[2][:, :], q2lb4.ap(), cosap, AL.add, AL.mult),
                      r=["pb2", "q2lb4", "cosb", "xbrt"], w=["xbrt1"])
                ct.op("V", lambda h: h.scalar_tensor_tensor(rt2.ap(), PB[3][:, :], q2lbr4.ap(), sinap, AL.add, AL.mult),
                      r=["pb3", "q2lbr4", "sinb", "xbrt"], w=["xbrt2"])
                ct.op("V", lambda h: h.tensor_tensor(rt1.ap(), rt1.ap(), rt2.ap(), AL.add),
                      r=["xbrt1", "xbrt2"], w=["xbrt1"])
                for i in range(4):
                    lhsT = wqkw.app(32 * i, 32)
                    rhs = rt1.app(32 * i, 32, dims=[[1, 512]])
                    ct.op("T", lambda h, l=lhsT, r2=rhs, i=i:
                          h.matmul(AP(PB[6], 32 * i * 512, [[512, 32], [1, 512]]), l, r2,
                                   start=True, stop=True, tile_position=(32 * i, 32 * i)),
                          r=["xbrt1", "wqkw"], w=["pb6"])
                qdst = qT.ap((s * 4 + g) * T + pos0, [[1, 512]])
                ct.op("S", lambda h, d=qdst: h.activation(d, PB[6][:, :], AF.Identity, bias=wqkb4.ap()),
                      r=["pb6", "wqkb4"], w=["qT"])
                kdst = kT.ap((s * 4 + g) * T + pos0, [[1, 512]])
                ct.op("V", lambda h: h.scalar_tensor_tensor(rt1.ap(), PB[4][:, :], kvb4.ap(), cosap, AL.add, AL.mult),
                      r=["pb4", "kvb4", "cosb", "xbrt", "xbrt1"], w=["xbrt1"])
                ct.op("V", lambda h: h.scalar_tensor_tensor(rt2.ap(), PB[5][:, :], kvbr4.ap(), sinap, AL.add, AL.mult),
                      r=["pb5", "kvbr4", "sinb", "xbrt", "xbrt2"], w=["xbrt2"])
                ct.op("V", lambda h, d=kdst: h.tensor_tensor(d, rt1.ap(), rt2.ap(), AL.add),
                      r=["xbrt1", "xbrt2"], w=["kT"])
            for hh in range(H):
                par = 64 * (hh % 2)
                chunk = 16 + hh // 2
                bank, btag = (PB[2], "pb2") if hh % 2 == 0 else (PB[3], "pb3")
                for ksub in range(4):
                    lhsT = qkvS.app(par, 64, o=chunk * 512 + ksub * 128, dims=[[1, 128]])
                    rhs = kv2lw.app(par, 64)
                    ct.op("T", lambda h, l=lhsT, r2=rhs, b=bank, ks=ksub, par=par:
                          h.matmul(AP(b, ks * 32, [[512, 128], [1, 32]]), l, r2,
                                   start=True, stop=True, tile_position=(par, 0)),
                          r=["qkv", "kv2lw"], w=[btag])
                if j == 0:
                    vdst0 = vSB.ap(voff(s, hh, 0), [[1, 1], [1, 32]])
                    ct.op("V", lambda h, b=bank, d=vdst0: h.tensor_tensor(
                        d, AP(b, 0, [[512, 128], [32, 1], [1, 32]]), kvbias.ap(0, [[32, 1], [1, 32]]), AL.add),
                        r=[btag, "kvbias"], w=["vSB"])
                    vdst1 = vSB.ap(voff(s, hh, 1), [[33, 3], [1, 32]])
                    ct.op("V", lambda h, b=bank, d=vdst1: h.tensor_tensor(
                        d, AP(b, 32, [[512, 128], [32, 3], [1, 32]]), kvbias.ap(32, [[32, 3], [1, 32]]), AL.add),
                        r=[btag, "kvbias"], w=["vSB"])
                else:
                    vdst = vSB.ap(voff(s, hh, j * 4), [[33, 4], [1, 32]])
                    ct.op("V", lambda h, b=bank, d=vdst: h.tensor_tensor(
                        d, AP(b, 0, [[512, 128], [32, 4], [1, 32]]), kvbias.ap(0, [[32, 4], [1, 32]]), AL.add),
                        r=[btag, "kvbias"], w=["vSB"])

        for s in range(SEQ):
            for j in range(TPS):
                p0_tile(s, j)
        ct.barrier()

        ct.op("GD", lambda h: h.dma_start(
            out=cprojS.ap(), in_=AP(cproj_wT.tensor, 0, [[C, 128], [128 * C, 8], [128, 8], [1, 128]])),
            w=["cprojw"])

        def p1_tile(s, qt):
            qlo = qt * TILE
            jj = s * TPS + qt
            col0 = jj * TILE
            ct.op("Y", lambda h: h.dma_start(
                out=x0S.ap(), in_=AP(x_in.tensor, col0, [[NTOK, 128], [128 * NTOK, 8], [1, 512]])),
                r=["x0"], w=["x0"])
            nkt = 4 * qt + 4
            for half in range(2):
                for kt in range(nkt):
                    qs = 0 if (kt + 1) * 128 <= qlo else (kt * 128 - qlo)
                    epar = kt % 2
                    for gg in range(2):
                        g = half * 2 + gg
                        for i in range(4):
                            lhsT = kT.app(32 * i, 32, o=(s * 4 + g) * T + kt * 128, dims=[[1, 128]])
                            rhs = qT.app(32 * i, 32, o=(s * 4 + g) * T + qlo + qs, dims=[[1, 512 - qs]])
                            ct.op("T", lambda h, l=lhsT, r2=rhs, b=PB[i], qs=qs, i=i:
                                  h.matmul(AP(b, qs, [[512, 128], [1, 512 - qs]]), l, r2,
                                           start=True, stop=True, tile_position=(32 * i, 0)),
                                  r=["qT", "kT"], w=[f"pb{i}"])
                        for i in range(4):
                            hrel = gg * 4 + i
                            ec = epar * 4096 + hrel * 512
                            ct.op("S", lambda h, i=i, ec=ec, qs=qs: h.activation(
                                eS.ap(ec + qs, [[1, 512 - qs]]),
                                AP(PB[i], qs, [[512, 128], [1, 512 - qs]]), AF.Exp, scale=ISQL),
                                r=[f"pb{i}"], w=[f"e{epar}"])
                            if qs > 0:
                                ct.op("V", lambda h, ec=ec, qs=qs: h.memset(eS.ap(ec, [[1, qs]]), 0.0),
                                      w=[f"e{epar}"])
                            if kt * 128 >= qlo:
                                ct.op("G", lambda h, ec=ec, qs=qs: h.affine_select(
                                    out=eS.ap(ec + qs, [[1, 128]]), in_=eS.ap(ec + qs, [[1, 128]]),
                                    compare_op=AL.is_ge, fill=0.0, base=0,
                                    pattern=[[1, 128]], channel_multiplier=-1),
                                    r=[f"e{epar}"], w=[f"e{epar}"])
                    for pr in range(4):
                        for ab in range(2):
                            hh = half * 8 + 2 * pr + ab
                            hrel = hh - half * 8
                            ec = epar * 4096 + hrel * 512
                            mdim = 64 if kt == 0 else 33
                            lhsT = vSB.ap(voff(s, hh, kt), [[1, mdim]])
                            rhs = eS.ap(ec + qs, [[1, 512 - qs]])
                            ct.op("T", lambda h, l=lhsT, r2=rhs, b=PB[4 + pr], ab=ab, st=(kt == 0), qs=qs, md=mdim:
                                  h.matmul(AP(b, 64 * ab * 512 + qs, [[512, md], [1, 512 - qs]]), l, r2,
                                           start=st, stop=False, tile_position=(0, 64 * ab)),
                                  r=["vSB", f"e{epar}"], w=[f"pb{4 + pr}"])
                for pr in range(4):
                    bank, btag = PB[4 + pr], f"pb{4 + pr}"
                    ct.op("S", lambda h, b=bank, pr=pr: h.activation(
                        yrawS.ap(pr * 512, [[1, 512]]), b[:, :], AF.Identity),
                        r=[btag], w=["e0", "e1", "yraw"])
                    ct.op("GD", lambda h, pr=pr: h.dma_start(
                        out=dn.app(32 * pr, 1, dims=[[1, 512]]),
                        in_=yrawS.app(32, 1, o=pr * 512, dims=[[1, 512]])),
                        r=["yraw"], w=["dn"])
                    ct.op("GD", lambda h, pr=pr: h.dma_start(
                        out=dn.app(32 * pr, 1, o=512, dims=[[1, 512]]),
                        in_=yrawS.app(96, 1, o=pr * 512, dims=[[1, 512]])),
                        r=["yraw"], w=["dn"])
                ct.op("V", lambda h: h.reciprocal(dn.ap(), dn.ap()), r=["dn"], w=["dn"])
                ct.op("V", lambda h: h.tensor_copy(dnh.ap(), dn.ap()), r=["dn"], w=["dnh"])
                ct.op("V", lambda h: h.tensor_tensor(dnl.ap(), dn.ap(), dnh.ap(), AL.subtract),
                      r=["dn", "dnh"], w=["dnl"])
                for pr in range(4):
                    ch = half * 4 + pr
                    bank, btag = PB[4 + pr], f"pb{4 + pr}"
                    for ab in range(2):
                        for lohi, buf, bt2 in ((0, dnh, "dnh"), (1, dnl, "dnl")):
                            ct.op("T", lambda h, b=bank, pr=pr, ab=ab, b2=buf, st=(lohi == 0):
                                  h.matmul(AP(b, 64 * ab * 512, [[512, 64], [1, 512]]),
                                           onesr.app(32 * pr, 1, dims=[[1, 64]]),
                                           b2.app(32 * pr, 1, o=512 * ab, dims=[[1, 512]]),
                                           start=st, stop=(lohi == 1), tile_position=(32 * pr, 64 * ab)),
                                  r=["dnh", "dnl", "onesr", "yraw"], w=[btag])
                    ct.op("V", lambda h, b=bank, pr=pr: h.tensor_tensor(
                        yvS.ap(pr * 512, [[1, 512]]), yrawS.ap(pr * 512, [[1, 512]]), b[:, :], AL.mult),
                        r=[btag, "yraw"], w=["yv"])
                    ab_bank, ab_tag = PB[pr % 2], f"pb{pr % 2}"
                    for ab in range(2):
                        lhsT = l2ow.app(64 * ab, 32, dims=[[1, 64]])
                        rhs = yvS.app(64 * ab, 32, o=pr * 512, dims=[[1, 512]])
                        ct.op("T", lambda h, l=lhsT, r2=rhs, b=ab_bank, ab=ab:
                              h.matmul(AP(b, 64 * ab * 512, [[512, 64], [1, 512]]), l, r2,
                                       start=True, stop=True, tile_position=(64 * ab, 64 * ab)),
                              r=["yv", "l2ow"], w=[ab_tag])
                    ct.op("S", lambda h, b=ab_bank, ch=ch: h.activation(
                        attnS.ap(ch * 512, [[1, 512]]), b[:, :], AF.Identity, bias=l2ob2.ap()),
                        r=[ab_tag, "l2ob2"], w=["attn"])
            for oc in range(8):
                bank, btag = PB[2 + oc % 2], f"pb{2 + oc % 2}"
                for kc in range(8):
                    lhsT = cprojS.ap((kc * 8 + oc) * 128, [[1, 128]])
                    rhs = attnS.ap(kc * 512, [[1, 512]])
                    ct.op("T", lambda h, l=lhsT, r2=rhs, b=bank, st=(kc == 0), sp=(kc == 7):
                          h.matmul(b[:, :], l, r2, start=st, stop=sp), r=["attn", "cprojw"], w=[btag])
                xs = x0S.ap(oc * 512, [[1, 512]])
                ct.op("V", lambda h, b=bank, xs=xs, oc=oc: h.scalar_tensor_tensor(
                    xs, b[:, :], cprojb.ap(oc, [[1, 1]]), xs, AL.add, AL.add),
                    r=[btag, "x0", "cprojb"], w=["x0"])
                ct.op("S", lambda h, xs=xs: h.activation(y2buf.ap(), xs, AF.Square), r=["x0"], w=["y2b"])
                ct.op("V", lambda h, xs=xs: h.tensor_copy(ybf.ap(), xs), r=["x0"], w=["ybf"])
                ct.op("T", lambda h, st=(oc == 0), sp=(oc == 7): h.matmul(
                    AP(PB[6], 0, [[512, 1], [1, 512]]), ones1.ap(0, [[1, 1]]), ybf.ap(), start=st, stop=sp,
                    tile_position=(0, 0)), r=["ybf", "ones1"], w=["pb6"])
                ct.op("T", lambda h, st=(oc == 0), sp=(oc == 7): h.matmul(
                    AP(PB[7], 0, [[512, 1], [1, 512]]), ones1.ap(0, [[1, 1]]), y2buf.ap(), start=st, stop=sp,
                    tile_position=(0, 0)), r=["y2b", "ones1"], w=["pb7"])
            ct.op("V", lambda h: h.tensor_copy(lnw0, AP(PB[6], 0, [[512, 1], [1, 512]])),
                  r=["pb6", "strow"], w=["strow"])
            ct.op("V", lambda h: h.tensor_copy(lnw1, AP(PB[7], 0, [[512, 1], [1, 512]])),
                  r=["pb7", "dn"], w=["dn"])
            ln_coef(rms=False)
            ln_hilo_mms(lnw1, lnw0, PB[6], "pb6", PB[7], "pb7")
            x1dst = out_d if stop == 1 else x1_d
            for oc in range(8):
                xs = x0S.ap(oc * 512, [[1, 512]])
                ct.op("V", lambda h, xs=xs: h.tensor_tensor(xs, xs, PB[6][:, :], AL.mult),
                      r=["x0", "pb6"], w=["x0"])
                ct.op("V", lambda h, xs=xs: h.tensor_tensor(xs, xs, PB[7][:, :], AL.subtract),
                      r=["x0", "pb7"], w=["x0"])
                ct.op("Y", lambda h, xs=xs, oc=oc: h.dma_start(
                    out=AP(x1dst.tensor, oc * 128 * NTOK + col0, [[NTOK, 128], [1, 512]]), in_=xs),
                    r=["x0"], w=["x1_d"])

        wq_slice_fetch()
        p1_ti = 0
        for s in range(SEQ):
            for qt in range(TPS):
                p1_tile(s, qt)
                emit_wq(10)
                p1_ti += 1
                if p1_ti == 4:
                    emit_wq(len(WQB))
                    wq_ag()
        ct.barrier()

        def lnfix(k, y_src, s_src, cf_dst, x_dst, rms_extra=False, to_out=False):
            sw = [statw.ap(i * 32, [[1, 32]]) for i in range(5)]
            ct.op("Y", lambda h: h.dma_start(
                out=statw.ap(0, [[32, 2], [1, 32]]),
                in_=AP(s_src.tensor, 0, [[32, 128], [NTOK, 2], [1, 32]])), r=["statw"], w=["statw"])
            ct.op("V", lambda h: h.tensor_scalar_mul(sw[0], sw[0], 1.0 / C), r=["statw"], w=["statw"])
            ct.op("V", lambda h: h.tensor_scalar_mul(sw[1], sw[1], 1.0 / C), r=["statw"], w=["statw"])
            ct.op("V", lambda h: h.tensor_tensor(sw[2], sw[0], sw[0], AL.mult), r=["statw"], w=["statw"])
            ct.op("V", lambda h: h.tensor_tensor(sw[1], sw[1], sw[2], AL.subtract), r=["statw"], w=["statw"])
            ct.op("S", lambda h: h.activation(sw[2], sw[1], AF.Sqrt, bias=EPS_LN), r=["statw"], w=["statw"])
            ct.op("V", lambda h: h.reciprocal(sw[2], sw[2]), r=["statw"], w=["statw"])
            ct.op("V", lambda h: h.tensor_tensor(sw[3], sw[0], sw[2], AL.mult), r=["statw"], w=["statw"])
            ct.op("V", lambda h: h.tensor_scalar_mul(sw[3], sw[3], -1.0), r=["statw"], w=["statw"])
            def hilo(dst_t, row, src_sw):
                ct.op("V", lambda h: h.tensor_copy(ybf.ap(0, [[1, 32]]), src_sw), r=["statw"], w=["ybf"])
                ct.op("V", lambda h: h.tensor_tensor(strow.ap(0, [[1, 32]]), src_sw, ybf.ap(0, [[1, 32]]),
                                                     AL.subtract), r=["statw", "ybf"], w=["strow"])
                ct.op("Y", lambda h: h.dma_start(
                    out=AP(dst_t.tensor, row * NTOK, [[32, 128], [1, 32]]), in_=ybf.ap(0, [[1, 32]])),
                    r=["ybf"], w=[dst_t.tensor.name])
                ct.op("V", lambda h: h.tensor_copy(ybf.ap(0, [[1, 32]]), strow.ap(0, [[1, 32]])),
                      r=["strow"], w=["ybf"])
                ct.op("Y", lambda h: h.dma_start(
                    out=AP(dst_t.tensor, (row + 1) * NTOK, [[32, 128], [1, 32]]), in_=ybf.ap(0, [[1, 32]])),
                    r=["ybf"], w=[dst_t.tensor.name])
            hilo(cf_dst, 0, sw[2])
            hilo(cf_dst, 2, sw[3])
            if rms_extra:
                ct.op("V", lambda h: h.tensor_tensor(sw[4], sw[2], sw[2], AL.mult), r=["statw"], w=["statw"])
                ct.op("V", lambda h: h.tensor_tensor(sw[4], sw[4], sw[1], AL.mult), r=["statw"], w=["statw"])
                ct.op("S", lambda h: h.activation(sw[4], sw[4], AF.Sqrt, bias=EPS_RMS), r=["statw"], w=["statw"])
                ct.op("V", lambda h: h.reciprocal(sw[4], sw[4]), r=["statw"], w=["statw"])
                ct.op("V", lambda h: h.tensor_tensor(sw[2], sw[2], sw[4], AL.mult), r=["statw"], w=["statw"])
                ct.op("V", lambda h: h.tensor_tensor(sw[3], sw[3], sw[4], AL.mult), r=["statw"], w=["statw"])
                hilo(cfx_d, 0, sw[2])
                hilo(cfx_d, 2, sw[3])
            for jj in range(NT):
                col0 = jj * TILE
                ct.op("Y", lambda h, c0=col0: h.dma_start(
                    out=lyt.ap(), in_=AP(y_src.tensor, c0, [[NTOK, 128], [128 * NTOK, 8], [1, 512]])),
                    r=["lyt"], w=["lyt"])
                ct.op("Y", lambda h, c0=col0: h.dma_start(
                    out=lcf.app(0, 1, dims=[[512, 4], [1, 512]]),
                    in_=AP(cf_dst.tensor, c0, [[NTOK, 1], [NTOK, 4], [1, 512]])),
                    r=["lcf", cf_dst.tensor.name], w=["lcf"])
                for part, btag in ((0, "pb0"), (2, "pb1")):
                    bank = PB[0] if part == 0 else PB[1]
                    for lohi in range(2):
                        ct.op("T", lambda h, b=bank, p=part + lohi, st=(lohi == 0):
                              h.matmul(b[:, :], onesr.app(0, 1, dims=[[1, 128]]),
                                       lcf.app(0, 1, o=p * 512, dims=[[1, 512]]), start=st, stop=(lohi == 1),
                                       tile_position=(0, 0)),
                              r=["lcf", "onesr"], w=[btag])
                if rms_extra:
                    ct.op("Y", lambda h, c0=col0: h.dma_start(
                        out=lcf.app(0, 1, o=2048, dims=[[512, 4], [1, 512]]),
                        in_=AP(cfx_d.tensor, c0, [[NTOK, 1], [NTOK, 4], [1, 512]])),
                        r=["lcf2", "cfx_d"], w=["lcf2"])
                    for part, btag in ((0, "pb2"), (2, "pb3")):
                        bank = PB[2] if part == 0 else PB[3]
                        for lohi in range(2):
                            ct.op("T", lambda h, b=bank, p=part + lohi, st=(lohi == 0):
                                  h.matmul(b[:, :], onesr.app(0, 1, dims=[[1, 128]]),
                                           lcf.app(0, 1, o=2048 + p * 512, dims=[[1, 512]]), start=st, stop=(lohi == 1),
                                           tile_position=(0, 0)),
                                  r=["lcf2", "onesr"], w=[btag])
                for oc in range(8):
                    ys = lyt.ap(oc * 512, [[1, 512]])
                    if rms_extra:
                        ct.op("V", lambda h, ys=ys: h.tensor_tensor(qt_x.ap(0, [[1, 512]]), ys, PB[2][:, :], AL.mult),
                              r=["lyt", "pb2", "qtmp0"], w=["qtmp0"])
                        ct.op("V", lambda h: h.tensor_tensor(qt_u.ap(0, [[1, 512]]), qt_x.ap(0, [[1, 512]]), PB[3][:, :], AL.add),
                              r=["qtmp0", "pb3"], w=["qtmp0"])
                        ct.op("V", lambda h, oc=oc: h.tensor_scalar(
                            lxn.ap(oc * 512, [[1, 512]]), qt_u.ap(0, [[1, 512]]), normwb.ap(oc, [[1, 1]]), None, AL.mult),
                            r=["qtmp0", "normwb"], w=["lxn"])
                    ct.op("V", lambda h, ys=ys: h.tensor_tensor(ys, ys, PB[0][:, :], AL.mult),
                          r=["lyt", "pb0"], w=["lyt"])
                    ct.op("V", lambda h, ys=ys: h.tensor_tensor(ys, ys, PB[1][:, :], AL.add),
                          r=["lyt", "pb1"], w=["lyt"])
                dst = out_d if to_out else x_dst
                ct.op("Y", lambda h, c0=col0, d=dst: h.dma_start(
                    out=AP(d.tensor, c0, [[NTOK, 128], [128 * NTOK, 8], [1, 512]]), in_=lyt.ap()),
                    r=["lyt"], w=["x_out"])
                if rms_extra:
                    ct.op("Y", lambda h, c0=col0: h.dma_start(
                        out=AP(xn_d.tensor, c0, [[NTOK, 128], [128 * NTOK, 8], [1, 512]]), in_=lxn.ap()),
                        r=["lxn"], w=["xn_d"])

        if stop == 1:
            ct.emit(block, sems)
            return nc

        def p2_tile(jj):
            col0 = jj * TILE
            xt = x1tp[jj % 2]
            xtg = f"x1t{jj % 2}"
            ct.op("Y", lambda h, xt=xt: h.dma_start(
                out=xt.ap(), in_=AP(x1_d.tensor, col0, [[NTOK, 128], [128 * NTOK, 8], [1, 512]])),
                r=[xtg], w=[xtg])
            for kc in range(8):
                qstage_a(xt.ap(kc * 512, [[1, 512]]), 512, kc % 2, r_extra=[xtg])
                qstage_s(512, kc % 2)
                if kc >= 1:
                    qstage_b(xqS.ap((kc - 1) * 512, [[1, 512]]), 512, (kc - 1) % 2, w_extra=[f"xq{kc - 1}"])
            qstage_b(xqS.ap(7 * 512, [[1, 512]]), 512, 1, w_extra=["xq7"])
            for oc in range(ICH):
                par = oc % 2
                for (r2, a, e) in wg_segs(oc):
                    off_in = r2 * RK + (a - 352 * r2)
                    ct.op("Y", lambda h, par=par, oi=off_in, do=a - 128 * oc, w_=e - a: h.dma_start(
                        out=wgbuf[par].ap(do, dims=[[128, 8], [1, w_]]),
                        in_=AP(wq_g.tensor, oi, [[352, 128], [128 * 352, 8], [1, w_]])),
                        r=[f"wgb{par}", "wq_g"], w=[f"wgb{par}"])
                    ct.op("Y", lambda h, par=par, oi=off_in + WSEG, do=a - 128 * oc, w_=e - a: h.dma_start(
                        out=wubuf[par].ap(do, dims=[[128, 8], [1, w_]]),
                        in_=AP(wq_g.tensor, oi, [[352, 128], [128 * 352, 8], [1, w_]])),
                        r=[f"wub{par}", "wq_g"], w=[f"wub{par}"])
                for kc in range(8):
                    ct.op("T", lambda h, kc=kc, par=par, st=(kc == 0), sp=(kc == 7):
                          h.matmul(PB[0][:, :], wgbuf[par].ap(kc * 128, [[1, 128]]),
                                   xqS.ap(kc * 512, [[1, 512]]), start=st, stop=sp),
                          r=[f"wgb{par}", f"xq{kc}"], w=["pb0"])
                    ct.op("T", lambda h, kc=kc, par=par, st=(kc == 0), sp=(kc == 7):
                          h.matmul(PB[1][:, :], wubuf[par].ap(kc * 128, [[1, 128]]),
                                   xqS.ap(kc * 512, [[1, 512]]), start=st, stop=sp),
                          r=[f"wub{par}", f"xq{kc}"], w=["pb1"])
                ct.op("S", lambda h: h.activation(silS.ap(), PB[0][:, :], AF.Silu), r=["pb0"], w=["sil"])
                ct.op("V", lambda h, oc=oc: h.tensor_tensor(htS.ap(oc * 512, [[1, 512]]), silS.ap(),
                                                            PB[1][:, :], AL.mult),
                      r=["sil", "pb1"], w=[f"ht{oc}"])
                qstage_a(htS.ap(oc * 512, [[1, 512]]), 512, oc % 2, r_extra=[f"ht{oc}"])
                qstage_s(512, oc % 2)
                if oc >= 1:
                    qstage_b(hqS.ap((oc - 1) * 512, [[1, 512]]), 512, (oc - 1) % 2, w_extra=[f"hq{oc - 1}"])
            qstage_b(hqS.ap((ICH - 1) * 512, [[1, 512]]), 512, (ICH - 1) % 2, w_extra=[f"hq{ICH - 1}"])
            for oc in range(8):
                par = oc % 2
                bank, btag = PB[2 + par], f"pb{2 + par}"
                ct.op("Y", lambda h, oc=oc, par=par: h.dma_start(
                    out=wdbuf[par].ap(), in_=AP(wq_g.tensor, oc * RK + 2 * WSEG, [[128, 128], [128 * 128, 22], [1, 128]])),
                    r=[f"wdb{par}", "wq_g"], w=[f"wdb{par}"])
                for kc in range(ICH):
                    ct.op("T", lambda h, kc=kc, par=par, b=bank, st=(kc == 0), sp=(kc == ICH - 1):
                          h.matmul(b[:, :], wdbuf[par].ap(kc * 128, [[1, 128]]),
                                   hqS.ap(kc * 512, [[1, 512]]), start=st, stop=sp),
                          r=[f"wdb{par}", f"hq{kc}"], w=[btag])
                xs = xt.ap(oc * 512, [[1, 512]])
                ct.op("V", lambda h, b=bank, xs=xs: h.tensor_tensor(xs, xs, b[:, :], AL.add),
                      r=[btag, xtg], w=[xtg])
                ct.op("S", lambda h, xs=xs: h.activation(y2buf.ap(), xs, AF.Square), r=[xtg], w=["y2b"])
                ct.op("V", lambda h, xs=xs: h.tensor_copy(ybf.ap(), xs), r=[xtg], w=["ybf"])
                ct.op("T", lambda h, st=(oc == 0), sp=(oc == 7): h.matmul(
                    AP(PB[6], 0, [[512, 1], [1, 512]]), ones1.ap(0, [[1, 1]]), ybf.ap(), start=st, stop=sp,
                    tile_position=(0, 0)), r=["ybf", "ones1"], w=["pb6"])
                ct.op("T", lambda h, st=(oc == 0), sp=(oc == 7): h.matmul(
                    AP(PB[7], 0, [[512, 1], [1, 512]]), ones1.ap(0, [[1, 1]]), y2buf.ap(), start=st, stop=sp,
                    tile_position=(0, 0)), r=["y2b", "ones1"], w=["pb7"])
                if stop == 12:
                    ct.op("Y", lambda h, xs=xs, oc=oc: h.dma_start(
                        out=AP(y2_d.tensor, oc * 128 * NTOK + col0, [[NTOK, 128], [1, 512]]), in_=xs),
                        r=[xtg], w=["y2_d"])
            ct.op("V", lambda h: h.tensor_copy(lnw0, AP(PB[6], 0, [[512, 1], [1, 512]])),
                  r=["pb6", "strow"], w=["strow"])
            ct.op("V", lambda h: h.tensor_copy(lnw1, AP(PB[7], 0, [[512, 1], [1, 512]])),
                  r=["pb7", "dn"], w=["dn"])
            ln_coef(rms=True)
            ln_hilo_mms(lnw1, lnw0, PB[4], "pb4", PB[5], "pb5")
            ct.op("V", lambda h: h.tensor_tensor(lnw2, lnw2, lnw1, AL.mult), r=["dn"], w=["dn"])
            ct.op("V", lambda h: h.tensor_tensor(lnw2, lnw2, lnw1, AL.mult), r=["dn"], w=["dn"])
            ct.op("S", lambda h: h.activation(lnw2, lnw2, AF.Sqrt, bias=EPS_RMS), r=["dn"], w=["dn"])
            ct.op("V", lambda h: h.reciprocal(lnw2, lnw2), r=["dn"], w=["dn"])
            ct.op("V", lambda h: h.tensor_tensor(lnw1, lnw1, lnw2, AL.mult), r=["dn"], w=["dn"])
            ct.op("V", lambda h: h.tensor_tensor(lnw0, lnw0, lnw2, AL.mult), r=["strow", "dn"], w=["strow"])
            ln_hilo_mms(lnw1, lnw0, PB[6], "pb6", PB[7], "pb7")
            x2dst = out_d if stop == 2 else x2_d
            for oc in range(8):
                ys = xt.ap(oc * 512, [[1, 512]])
                ct.op("V", lambda h, ys=ys: h.tensor_tensor(qt_x.ap(0, [[1, 512]]), ys, PB[6][:, :], AL.mult),
                      r=[xtg, "pb6", "qtmp0"], w=["qtmp0"])
                ct.op("V", lambda h: h.tensor_tensor(qt_x.ap(0, [[1, 512]]), qt_x.ap(0, [[1, 512]]),
                                                     PB[7][:, :], AL.subtract),
                      r=["qtmp0", "pb7"], w=["qtmp0"])
                ct.op("V", lambda h, oc=oc: h.tensor_scalar(
                    xnb.ap(), qt_x.ap(0, [[1, 512]]), normwb.ap(oc, [[1, 1]]), None, AL.mult),
                    r=["qtmp0", "normwb"], w=["xnb"])
                ct.op("Y", lambda h, oc=oc, c0=col0: h.dma_start(
                    out=AP(xn_d.tensor, oc * 128 * NTOK + c0, [[NTOK, 128], [1, 512]]), in_=xnb.ap()),
                    r=["xnb"], w=["xn_d"])
                ct.op("V", lambda h, ys=ys: h.tensor_tensor(ys, ys, PB[4][:, :], AL.mult),
                      r=[xtg, "pb4"], w=[xtg])
                ct.op("V", lambda h, ys=ys: h.tensor_tensor(ys, ys, PB[5][:, :], AL.subtract),
                      r=[xtg, "pb5"], w=[xtg])
                ct.op("Y", lambda h, ys=ys, oc=oc: h.dma_start(
                    out=AP(x2dst.tensor, oc * 128 * NTOK + col0, [[NTOK, 128], [1, 512]]), in_=ys),
                    r=[xtg], w=["x2_d"])

        for jj in range(NT):
            p2_tile(jj)
        ct.barrier()

        if stop == 12:
            ct.op("Y", lambda h: h.dma_start(out=out_d, in_=y2_d), r=["y2_d"], w=["x_out"])
            ct.barrier()
            ct.emit(block, sems)
            return nc

        if stop == 2:
            ct.emit(block, sems)
            return nc

        def p3_scan():
            ct.op("G", lambda h: h.collective_compute(
                "AllGather", AL.bypass, replica_groups=[list(range(NCORES))],
                ins=[xn_d.opt()], outs=[xn_g.opt()]), r=["xn_d"], w=["xn_g"])
            def extract_op(h):
                cid = nc.partition_id([mybir.EngineType.Pool])
                src = AP(xn_g.tensor, cid * (128 * NTOK),
                         [[NTOK, 128], [C * NTOK, 8], [T, 2], [1, T]])
                return h.dma_start(out=xsc.ap(0, [[2 * T, 8], [T, 2], [1, T]]), in_=src)
            ct.op("GD", extract_op, r=["xn_g", "cid"], w=["xsc"])

            ct.op("V", lambda h: h.memset(stS.ap(1024, [[1, 16]]), 0.0), w=["st_i"])
            NW = T // 32
            for w in range(NW):
                bkA, tgA = PB[w % 2], f"pb{w % 2}"
                bkB, tgB = PB[4 + w % 2], f"pb{4 + w % 2}"
                ct.op("T", lambda h, w=w, b=bkA: h.matmul(
                    b[:, 0:256], bbd.ap(), xsc.ap(w * 32, [[1, 32], [T, 8]]), start=True, stop=False),
                    r=["xsc", "bbd"], w=[tgA])
                ct.op("T", lambda h, w=w, b=bkB: h.matmul(
                    b[:, 0:256], bbd.ap(), xsc.ap(w * 32 + 8 * T, [[1, 32], [T, 8]]), start=True, stop=False),
                    r=["xsc", "bbd"], w=[tgB])
                for t_in in range(32):
                    t = w * 32 + t_in
                    slot = t % 64
                    for grp, bk, tg, off in ((0, bkA, tgA, 0), (1, bkB, tgB, 8)):
                        if t == 0:
                            prev = stS.ap(1024 + off, [[1, 8]])
                            ptag = "st_i"
                        else:
                            prev = stS.ap(((t - 1) % 64) * 16 + off, [[1, 8]])
                            ptag = f"st{grp}{((t - 1) % 64) // 32}"
                        ct.op("T", lambda h, b=bk, ti=t_in, pv=prev: h.matmul(
                            AP(b, ti * 8, [[512, 128], [1, 8]]), abd.ap(), pv, start=False, stop=True),
                            r=[ptag, "abd"], w=[tg])
                        ct.op("S", lambda h, b=bk, ti=t_in, sl=slot, off=off: h.activation(
                            stS.ap(sl * 16 + off, [[1, 8]]), AP(b, ti * 8, [[512, 128], [1, 8]]), AF.Tanh),
                            r=[tg], w=[f"st{grp}{slot // 32}"])
                s0 = (w * 32) % 64
                ct.op("T", lambda h, s0=s0: h.matmul(
                    PB[2][:, :], cbd.ap(), stS.ap(s0 * 16, [[1, 512]]), start=True, stop=True),
                    r=[f"st0{s0 // 32}", f"st1{s0 // 32}", "cbd"], w=["pb2"])
                acc = mxacc[(w // 8) % 2]
                atag = f"mxa{(w // 8) % 2}"
                ct.op("V", lambda h, w=w, acc=acc: h.scalar_tensor_tensor(
                    acc.ap((w % 8) * 32, [[1, 32], [256, 16]]),
                    xsc.ap(w * 32, [[1, 32], [T, 16]]), skipb.ap(), PB[2][:, :], AL.mult, AL.add),
                    r=["pb2", "xsc", "skipb", atag], w=[atag])
                if (w + 1) % 8 == 0:
                    wf = w - 7
                    ct.op("Y", lambda h, wf=wf, acc=acc: h.dma_start(
                        out=AP(mx_d.tensor, wf * 32, [[H * T, 128], [T, 16], [1, 256]]),
                        in_=acc.ap(0, [[256, 16], [1, 256]])), r=[atag], w=["mx_d"])
            ct.op("G", lambda h: h.collective_compute(
                "AllGather", AL.bypass, replica_groups=[list(range(NCORES))],
                ins=[mx_d.opt()], outs=[mx_g.opt()]), r=["mx_d"], w=["mx_g"])

        if stop == 14:
            ct.op("GD", lambda h: h.dma_start(out=out_d, in_=xn_d), r=["xn_d"], w=["x_out"])
            ct.barrier()
            ct.emit(block, sems)
            return nc

        p3_scan()
        ct.barrier()

        if stop == 15:
            ct.op("GD", lambda h: h.dma_start(
                out=AP(out_d.tensor, 0, [[NTOK, 128], [1, NTOK]]),
                in_=AP(mx_d.tensor, 0, [[H * T, 128], [1, NTOK]])), r=["mx_d"], w=["x_out"])
            ct.barrier()
            ct.emit(block, sems)
            return nc

        def p3b():
            def extract2_op(h):
                cid = nc.partition_id([mybir.EngineType.Pool])
                src = AP(mx_g.tensor, cid * (2 * T),
                         [[H * T, 128], [128 * H * T, 8], [1, 2 * T]])
                return h.dma_start(out=mxfm.ap(0, [[NTOK, 8], [1, 2 * T]]), in_=src)
            ct.op("GD", extract2_op, r=["mx_g", "cid"], w=["mxfm"])
            ct.op("GD", lambda h: h.dma_start(
                out=woutS.ap(), in_=AP(wout_T.tensor, 0, [[C, 128], [128 * C, 8], [128, 8], [1, 128]])),
                w=["woutw"])
            for jj in range(NT):
                col0 = jj * TILE
                ct.op("Y", lambda h, c0=col0: h.dma_start(
                    out=x2t.ap(), in_=AP(x2_d.tensor, c0, [[NTOK, 128], [128 * NTOK, 8], [1, 512]])),
                    r=["x2t"], w=["x2t"])
                for oc in range(8):
                    bank, btag = PB[2 + oc % 2], f"pb{2 + oc % 2}"
                    for kc in range(8):
                        ct.op("T", lambda h, kc=kc, oc=oc, b=bank, c0=col0, st=(kc == 0), sp=(kc == 7):
                              h.matmul(b[:, :], woutS.ap((kc * 8 + oc) * 128, [[1, 128]]),
                                       mxfm.ap(kc * NTOK + c0, [[1, 512]]), start=st, stop=sp),
                              r=["mxfm", "woutw"], w=[btag])
                    xs = x2t.ap(oc * 512, [[1, 512]])
                    ct.op("V", lambda h, b=bank, xs=xs, oc=oc: h.scalar_tensor_tensor(
                        qt_x.ap(0, [[1, 512]]), b[:, :], boutb.ap(oc, [[1, 1]]), xs, AL.add, AL.add),
                        r=[btag, "x2t", "boutb", "qtmp"], w=["qtmp"])
                    ct.op("V", lambda h, xs=xs: h.tensor_tensor(xs, xs, qt_x.ap(0, [[1, 512]]), AL.add),
                        r=["x2t", "qtmp"], w=["x2t"])
                    ct.op("S", lambda h, xs=xs: h.activation(y2buf.ap(), xs, AF.Square), r=["x2t"], w=["y2b"])
                    ct.op("V", lambda h, xs=xs: h.tensor_copy(ybf.ap(), xs), r=["x2t"], w=["ybf"])
                    ct.op("T", lambda h, st=(oc == 0), sp=(oc == 7): h.matmul(
                        AP(PB[6], 0, [[512, 1], [1, 512]]), ones1.ap(0, [[1, 1]]), ybf.ap(), start=st, stop=sp,
                        tile_position=(0, 0)), r=["ybf", "ones1"], w=["pb6"])
                    ct.op("T", lambda h, st=(oc == 0), sp=(oc == 7): h.matmul(
                        AP(PB[7], 0, [[512, 1], [1, 512]]), ones1.ap(0, [[1, 1]]), y2buf.ap(), start=st, stop=sp,
                        tile_position=(0, 0)), r=["y2b", "ones1"], w=["pb7"])
                    if stop == 13:
                        ct.op("Y", lambda h, xs=xs, oc=oc, c0=col0: h.dma_start(
                            out=AP(y3_d.tensor, oc * 128 * NTOK + c0, [[NTOK, 128], [1, 512]]), in_=xs),
                            r=["x2t"], w=["y3_d"])
                ct.op("V", lambda h: h.tensor_copy(lnw0, AP(PB[6], 0, [[512, 1], [1, 512]])),
                      r=["pb6", "strow"], w=["strow"])
                ct.op("V", lambda h: h.tensor_copy(lnw1, AP(PB[7], 0, [[512, 1], [1, 512]])),
                      r=["pb7", "dn"], w=["dn"])
                ln_coef(rms=False)
                ln_hilo_mms(lnw1, lnw0, PB[6], "pb6", PB[7], "pb7")
                x3dst = out_d if stop == 3 else x3_d
                for oc in range(8):
                    xs = x2t.ap(oc * 512, [[1, 512]])
                    ct.op("V", lambda h, xs=xs: h.tensor_tensor(xs, xs, PB[6][:, :], AL.mult),
                          r=["x2t", "pb6"], w=["x2t"])
                    ct.op("V", lambda h, xs=xs: h.tensor_tensor(xs, xs, PB[7][:, :], AL.subtract),
                          r=["x2t", "pb7"], w=["x2t"])
                    ct.op("Y", lambda h, xs=xs, oc=oc, c0=col0: h.dma_start(
                        out=AP(x3dst.tensor, oc * 128 * NTOK + c0, [[NTOK, 128], [1, 512]]), in_=xs),
                        r=["x2t"], w=["x3_d"])

        p3b()
        ct.barrier()
        if stop == 13:
            ct.op("Y", lambda h: h.dma_start(out=out_d, in_=y3_d), r=["y3_d"], w=["x_out"])
            ct.barrier()
            ct.emit(block, sems)
            return nc
        if stop == 3:
            ct.emit(block, sems)
            return nc

        for wdst, wsrc, nm in ((wrS, wr_T, "wrw"), (wkS, wk_T, "wkw"), (wvS, wv_T, "wvw")):
            ct.op("GD", lambda h, d=wdst, s=wsrc: h.dma_start(
                out=d.ap(), in_=AP(s.tensor, 0, [[C, 128], [128 * C, 8], [128, 8], [1, 128]])), w=[nm])

        def p4_tile(jj):
            col0 = jj * TILE
            ct.op("Y", lambda h: h.dma_start(
                out=x3t.ap(), in_=AP(x3_d.tensor, col0, [[NTOK, 128], [128 * NTOK, 8], [1, 512]])),
                r=["x3t"], w=["x3t"])
            if jj % TPS == 0:
                ct.op("Y", lambda h: h.dma_start(
                    out=xpt.ap(1, [[512, 8], [1, 511]]),
                    in_=AP(x3_d.tensor, col0, [[NTOK, 128], [128 * NTOK, 8], [1, 511]])),
                    r=["xpt"], w=["xpt"])
                ct.op("V", lambda h: h.memset(xpt.ap(0, [[512, 8], [1, 1]]), 0.0), r=["xpt"], w=["xpt"])
            else:
                ct.op("Y", lambda h: h.dma_start(
                    out=xpt.ap(), in_=AP(x3_d.tensor, col0 - 1, [[NTOK, 128], [128 * NTOK, 8], [1, 512]])),
                    r=["xpt"], w=["xpt"])
            for oc in range(8):
                ct.op("V", lambda h, oc=oc: h.tensor_scalar(p4tmp.ap(0, [[1, 512]]), xpt.ap(oc * 512, [[1, 512]]),
                                                            omrb.ap(oc, [[1, 1]]), None, AL.mult),
                      r=["xpt", "omrb"], w=["p4t"])
                ct.op("V", lambda h, oc=oc: h.scalar_tensor_tensor(xrS.ap(oc * 512, [[1, 512]]),
                                                                   x3t.ap(oc * 512, [[1, 512]]), murb.ap(oc, [[1, 1]]),
                                                                   p4tmp.ap(0, [[1, 512]]), AL.mult, AL.add),
                      r=["x3t", "murb", "p4t"], w=["xr"])
                ct.op("V", lambda h, oc=oc: h.tensor_scalar(p4tmp.ap(0, [[1, 512]]), xpt.ap(oc * 512, [[1, 512]]),
                                                            omkb.ap(oc, [[1, 1]]), None, AL.mult),
                      r=["xpt", "omkb"], w=["p4t"])
                ct.op("V", lambda h, oc=oc: h.scalar_tensor_tensor(xkS.ap(oc * 512, [[1, 512]]),
                                                                   x3t.ap(oc * 512, [[1, 512]]), mukb.ap(oc, [[1, 1]]),
                                                                   p4tmp.ap(0, [[1, 512]]), AL.mult, AL.add),
                      r=["x3t", "mukb", "p4t"], w=["xk"])
            for oc in range(8):
                for kc in range(8):
                    ct.op("T", lambda h, kc=kc, oc=oc, st=(kc == 0), sp=(kc == 7):
                          h.matmul(PB[0][:, :], wrS.ap((kc * 8 + oc) * 128, [[1, 128]]),
                                   xrS.ap(kc * 512, [[1, 512]]), start=st, stop=sp),
                          r=["xr", "wrw"], w=["pb0"])
                ct.op("S", lambda h, oc=oc: h.activation(rbS.ap(oc * 512, [[1, 512]]), PB[0][:, :], AF.Sigmoid),
                      r=["pb0"], w=["rb"])
                for kc in range(8):
                    ct.op("T", lambda h, kc=kc, oc=oc, st=(kc == 0), sp=(kc == 7):
                          h.matmul(PB[1][:, :], wkS.ap((kc * 8 + oc) * 128, [[1, 128]]),
                                   xkS.ap(kc * 512, [[1, 512]]), start=st, stop=sp),
                          r=["xk", "wkw"], w=["pb1"])
                ct.op("S", lambda h, oc=oc: h.activation(y2buf.ap(), PB[1][:, :], AF.Relu),
                      r=["pb1"], w=["y2b"])
                ct.op("V", lambda h, oc=oc: h.tensor_tensor(kaS.ap(oc * 512, [[1, 512]]), y2buf.ap(), y2buf.ap(), AL.mult),
                      r=["y2b"], w=["ka"])
            for oc in range(8):
                bank, btag = PB[2 + oc % 2], f"pb{2 + oc % 2}"
                for kc in range(8):
                    ct.op("T", lambda h, kc=kc, oc=oc, b=bank, st=(kc == 0), sp=(kc == 7):
                          h.matmul(b[:, :], wvS.ap((kc * 8 + oc) * 128, [[1, 128]]),
                                   kaS.ap(kc * 512, [[1, 512]]), start=st, stop=sp),
                          r=["ka", "wvw"], w=[btag])
                xs = x3t.ap(oc * 512, [[1, 512]])
                ct.op("V", lambda h, b=bank, oc=oc: h.scalar_tensor_tensor(
                    p4tmp.ap(0, [[1, 512]]), b[:, :], 1.0, rbS.ap(oc * 512, [[1, 512]]), AL.mult, AL.mult),
                    r=[btag, "rb", "p4t"], w=["p4t"])
                ct.op("V", lambda h, xs=xs: h.tensor_tensor(xs, xs, p4tmp.ap(0, [[1, 512]]), AL.add),
                      r=["x3t", "p4t"], w=["x3t"])
                ct.op("S", lambda h, xs=xs: h.activation(y2buf.ap(), xs, AF.Square), r=["x3t"], w=["y2b"])
                ct.op("V", lambda h, xs=xs: h.tensor_copy(ybf.ap(), xs), r=["x3t"], w=["ybf"])
                ct.op("T", lambda h, st=(oc == 0), sp=(oc == 7): h.matmul(
                    AP(PB[6], 0, [[512, 1], [1, 512]]), ones1.ap(0, [[1, 1]]), ybf.ap(), start=st, stop=sp,
                    tile_position=(0, 0)), r=["ybf", "ones1"], w=["pb6"])
                ct.op("T", lambda h, st=(oc == 0), sp=(oc == 7): h.matmul(
                    AP(PB[7], 0, [[512, 1], [1, 512]]), ones1.ap(0, [[1, 1]]), y2buf.ap(), start=st, stop=sp,
                    tile_position=(0, 0)), r=["y2b", "ones1"], w=["pb7"])
            ct.op("V", lambda h: h.tensor_copy(lnw0, AP(PB[6], 0, [[512, 1], [1, 512]])),
                  r=["pb6", "strow"], w=["strow"])
            ct.op("V", lambda h: h.tensor_copy(lnw1, AP(PB[7], 0, [[512, 1], [1, 512]])),
                  r=["pb7", "dn"], w=["dn"])
            ln_coef(rms=False)
            ln_hilo_mms(lnw1, lnw0, PB[6], "pb6", PB[7], "pb7")
            for oc in range(8):
                xs = x3t.ap(oc * 512, [[1, 512]])
                ct.op("V", lambda h, xs=xs: h.tensor_tensor(xs, xs, PB[6][:, :], AL.mult),
                      r=["x3t", "pb6"], w=["x3t"])
                ct.op("V", lambda h, xs=xs: h.tensor_tensor(xs, xs, PB[7][:, :], AL.subtract),
                      r=["x3t", "pb7"], w=["x3t"])
                ct.op("Y", lambda h, xs=xs, oc=oc: h.dma_start(
                    out=AP(out_d.tensor, oc * 128 * NTOK + col0, [[NTOK, 128], [1, 512]]), in_=xs),
                    r=["x3t"], w=["x_out"])

        for jj in range(NT):
            p4_tile(jj)
        ct.barrier()

        ct.emit(block, sems)

    return nc


_NC_CACHE = {}


def _get_nc(stop):
    if stop not in _NC_CACHE:
        _NC_CACHE[stop] = build_nc(stop)
    return _NC_CACHE[stop]


def _rope_tables():
    inv_freq = (1.0 / 10000.0 ** (np.arange(0, LAT, 2, dtype=np.float32) / LAT)).astype(np.float32)
    t = np.arange(T, dtype=np.float32)
    freqs = np.outer(t, inv_freq)
    cos = np.repeat(np.cos(freqs), 2, axis=-1)[:, :LAT].astype(np.float32)
    sin = np.repeat(np.sin(freqs), 2, axis=-1)[:, :LAT].astype(np.float32)
    cos4 = np.tile(np.ascontiguousarray(cos.T), (4, 1))
    sin4 = np.tile(np.ascontiguousarray(sin.T), (4, 1))
    return np.ascontiguousarray(cos4), np.ascontiguousarray(sin4)


def kernel(stop=4, **inputs):
    x = inputs["L0_attn_x"]
    cos4, sin4 = _rope_tables()
    shared = {
        "cattn_wT": np.ascontiguousarray(inputs["L0_attn_c_attn_w"].T),
        "cattn_b": np.ascontiguousarray(inputs["L0_attn_c_attn_b"]),
        "q2l_wT": np.ascontiguousarray(inputs["L0_attn_q2l_w"].T),
        "q2l_b": np.ascontiguousarray(inputs["L0_attn_q2l_b"]),
        "kv2l_wT": np.ascontiguousarray(inputs["L0_attn_kv2l_w"].T),
        "kv2l_b": np.ascontiguousarray(inputs["L0_attn_kv2l_b"]),
        "l2o_wT": np.ascontiguousarray(inputs["L0_attn_l2o_w"].T),
        "l2o_b": np.ascontiguousarray(inputs["L0_attn_l2o_b"]),
        "wqk_wT": np.ascontiguousarray(inputs["L0_attn_wqk_w"].T),
        "wqk_b": np.ascontiguousarray(inputs["L0_attn_wqk_b"]),
        "cproj_wT": np.ascontiguousarray(inputs["L0_attn_cproj_w"].T),
        "cproj_b": np.ascontiguousarray(inputs["L0_attn_cproj_b"]),
        "wg_T": np.ascontiguousarray(inputs["L0_ffn_w_gate"].T),
        "wu_T": np.ascontiguousarray(inputs["L0_ffn_w_up"].T),
        "wd_T": np.ascontiguousarray(inputs["L0_ffn_w_down"].T),
        "norm_w": np.ascontiguousarray(inputs["L1_attn_norm_w"]),
        "wout_T": np.ascontiguousarray(inputs["L1_attn_w_out"].T),
        "bout_v": np.ascontiguousarray(inputs["L1_attn_b_out"]),
        "wr_T": np.ascontiguousarray(inputs["L1_ffn_W_r"].T),
        "wk_T": np.ascontiguousarray(inputs["L1_ffn_W_k"].T),
        "wv_T": np.ascontiguousarray(inputs["L1_ffn_W_v"].T),
        "mu_r": np.ascontiguousarray(inputs["L1_ffn_mu_r"]),
        "mu_k": np.ascontiguousarray(inputs["L1_ffn_mu_k"]),
        "cos4": cos4,
        "sin4": sin4,
        "altsgn": np.tile(np.array([-1.0, 1.0], dtype=np.float32), 64),
    }
    a_full = inputs["L1_attn_a"]
    b_full = inputs["L1_attn_b"]
    c_full = inputs["L1_attn_c"]
    skip_full = inputs["L1_attn_skip"]
    in_maps = []
    for c in range(NCORES):
        m = dict(shared)
        xc = x[2 * c:2 * c + 2]
        m["x_in"] = np.ascontiguousarray(xc.transpose(2, 0, 1).reshape(C, NTOK))
        m["a_sh"] = np.ascontiguousarray(a_full[2 * c:2 * c + 2])
        m["b_sh"] = np.ascontiguousarray(b_full[2 * c:2 * c + 2])
        m["c_sh"] = np.ascontiguousarray(c_full[2 * c:2 * c + 2])
        m["skip_sh"] = np.ascontiguousarray(skip_full[2 * c:2 * c + 2].reshape(128))
        m["core_id_in"] = np.array([[c]], dtype=np.int32)
        in_maps.append(m)
    nc = _get_nc(stop)
    res = run_bass_kernel_spmd(nc, in_maps, core_ids=list(range(NCORES)))
    global LAST_RES
    LAST_RES = res
    outs = []
    for c in range(NCORES):
        o = res.results[c]["out_d"]
        outs.append(o.reshape(C, SEQ, T).transpose(1, 2, 0))
    return np.ascontiguousarray(np.concatenate(outs, axis=0))


if __name__ == "__main__":
    pass


# version 12
# speedup vs baseline: 1.0044x; 1.0032x over previous
LAST_RES = None
"""Single-launch raw-Bass megakernel for mk_seq_postln_d2 on 8 TRN2 NeuronCores.

Module (depth 2, post-LN): x -> LN(x + MLA(x)) -> LN(. + fp4FFN(.)) ->
LN(. + tanh-scan mixer(.)) -> LN(. + sq-relu FFN(.)).   B=16, T=2048, C=1024.

Per core: 2 sequences (batch shard).  Feature-major stream [C -> 8x128 partitions,
4096 tokens free].  Stream spills to HBM between phases (compute-bound module, HBM
has ~35x headroom).  bf16 matmul operands / fp32 PSUM accumulation and elementwise.
LayerNorms run in dedicated LNFIX phases (isolates the sqrt activation table).
The L1 tanh-scan is head-resharded across cores via AllGather (2 heads x 16 seqs per
core) so each scan step is one 128-contraction matmul + one tanh.
fp4 quant-dequant: exact fp32 rounding ladder in DVE 32x32-transposed space.
"""
import numpy as np
import ml_dtypes
import concourse.bass as bass
import concourse.mybir as mybir
from concourse.bass_utils import run_bass_kernel_spmd

F32 = mybir.dt.float32
BF16 = mybir.dt.bfloat16
I32 = mybir.dt.int32
AL = mybir.AluOpType
AF = mybir.ActivationFunctionType
AX = mybir.AxisListType.X
AP = bass.AP

NCORES = 8
SEQ, T, C, NCH = 2, 2048, 1024, 8
H, HD, LAT = 16, 64, 32
IFF, ICH = 2816, 22
NTOK = SEQ * T
TILE = 512
NT = NTOK // TILE
TPS = T // TILE
KTN = T // 128
EPS_LN, EPS_RMS = 1e-5, 1e-4
C_HALF = float(1.5 * 2 ** 22)
C_INT = float(1.5 * 2 ** 23)
ISQL = float(LAT) ** -0.5


class Ctx:
    NY, NGD = 12, 4
    ENGS = ("T", "V", "S", "G") + tuple(f"Y{i}" for i in range(12)) + tuple(f"GD{i}" for i in range(4))

    def __init__(self, nc):
        self.nc = nc
        self.count = {e: 0 for e in self.ENGS}
        self.ops = {e: [] for e in ("T", "V", "S", "G", "Y")}
        self.last_w = {}
        self.readers = {}
        self.rr = {"Y": 0, "GD": 0}

    def op(self, eng, fn, r=(), w=()):
        q = eng in ("Y", "GD")
        if q:
            stream = "Y" if eng == "Y" else "G"
            sem = f"{eng}{self.rr[eng]}"
            self.rr[eng] = (self.rr[eng] + 1) % (self.NY if eng == "Y" else self.NGD)
        else:
            stream = eng
            sem = eng
        need = {}
        for tag in r:
            lw = self.last_w.get(tag)
            if lw:
                need[lw[0]] = max(need.get(lw[0], 0), lw[1])
        for tag in w:
            lw = self.last_w.get(tag)
            if lw and (q or lw[0] != sem):
                need[lw[0]] = max(need.get(lw[0], 0), lw[1])
            for (e2, v) in self.readers.get(tag, ()):
                if q or e2 != sem:
                    need[e2] = max(need.get(e2, 0), v)
        if q and self.count[sem] > 0:
            need[sem] = max(need.get(sem, 0), self.count[sem])
        inc = 16 if q else 1
        self.count[sem] += inc
        val = self.count[sem]
        self.ops[stream].append((need, sem, fn, val))
        for tag in w:
            self.last_w[tag] = (sem, val)
            self.readers[tag] = []
        for tag in r:
            self.readers.setdefault(tag, []).append((sem, val))

    def barrier(self):
        snap = dict(self.count)
        for stream in self.ops:
            self.ops[stream].append(("B", snap))
        self.last_w = {}
        self.readers = {}

    def emit(self, block, sems):
        def body_of(stream):
            def body(h):
                wm = {e: 0 for e in self.ENGS}
                for item in self.ops[stream]:
                    if item[0] == "B":
                        for e, v in item[1].items():
                            if v > wm[e]:
                                h.wait_ge(sems[e], v)
                                wm[e] = v
                        continue
                    need, eng, fn, val = item
                    for e, v in need.items():
                        if v > wm[e]:
                            h.wait_ge(sems[e], v)
                            wm[e] = v
                    inst = fn(h)
                    if inst is not None:
                        inst.then_inc(sems[eng], 16 if eng not in ("T", "V", "S", "G") else 1)
            return body
        block.sync(body_of("Y"))
        block.tensor(body_of("T"))
        block.vector(body_of("V"))
        block.scalar(body_of("S"))
        block.gpsimd(body_of("G"))


class Buf:
    def __init__(self, slab, slab_w, off, n, dt, slab_dt=BF16):
        self.slab, self.sw, self.off, self.dt = slab, slab_w, off, dt
        self.native = (dt == slab_dt) or (mybir.dt.size(dt) == mybir.dt.size(slab_dt))
        self.esz = 1 if self.native else 2
        self.n = n // self.esz

    def _mk(self, par_pair, o, dims):
        if dims is None:
            dims = [[1, self.n]]
        if self.native:
            a = AP(self.slab, self.off + par_pair[2] + o, [list(par_pair[:2])] + [list(d) for d in dims])
            if a.dtype != self.dt:
                a = a.bitcast(self.dt)
            return a
        nd = [[s * 2, c] for s, c in dims[:-1]]
        s_in, c_in = dims[-1]
        assert s_in == 1, f"fp32-on-bf16 innermost must be contiguous, got step {s_in}"
        nd.append([1, 2 * c_in])
        a = AP(self.slab, self.off + par_pair[2] + o * 2, [list(par_pair[:2])] + nd)
        return a.bitcast(F32)

    def ap(self, o=0, dims=None):
        return self._mk((self.sw, 128, 0), o, dims)

    def app(self, p0, np_, o=0, dims=None):
        return self._mk((self.sw, np_, p0 * self.sw), o, dims)


def build_nc(stop=4):
    nc = bass.Bass("TRN2", target_bir_lowering=False, debug=False)

    def din(name, shape, dt=F32):
        return nc.dram_tensor(name, shape, dt, kind="ExternalInput").ap()

    x_in = din("x_in", [C, NTOK])
    cattn_wT = din("cattn_wT", [C, 3 * C])
    cattn_b = din("cattn_b", [3 * C])
    q2l_wT = din("q2l_wT", [HD, LAT])
    q2l_b = din("q2l_b", [LAT])
    kv2l_wT = din("kv2l_wT", [HD, LAT])
    kv2l_b = din("kv2l_b", [LAT])
    l2o_wT = din("l2o_wT", [LAT, HD])
    l2o_b = din("l2o_b", [HD])
    wqk_wT = din("wqk_wT", [LAT, LAT])
    wqk_b = din("wqk_b", [LAT])
    cproj_wT = din("cproj_wT", [C, C])
    cproj_b = din("cproj_b", [C])
    wg_T = din("wg_T", [C, IFF])
    wu_T = din("wu_T", [C, IFF])
    wd_T = din("wd_T", [IFF, C])
    norm_w = din("norm_w", [C])
    a_sh = din("a_sh", [2, 64, 64])
    b_sh = din("b_sh", [2, 64, 64])
    c_sh = din("c_sh", [2, 64, 64])
    skip_sh = din("skip_sh", [128])
    wout_T = din("wout_T", [C, C])
    bout_v = din("bout_v", [C])
    wr_T = din("wr_T", [C, C])
    wk_T = din("wk_T", [C, C])
    wv_T = din("wv_T", [C, C])
    mu_r = din("mu_r", [C])
    mu_k = din("mu_k", [C])
    altsgn = din("altsgn", [128])
    cos4 = din("cos4", [128, T])
    sin4 = din("sin4", [128, T])
    core_id_in = din("core_id_in", [1, 1], I32)

    out_d = nc.dram_tensor("out_d", [C, NTOK], F32, kind="ExternalOutput").ap()


    def dsc(name, shape, dt=F32, **kw):
        return nc.dram_tensor(name, shape, dt, kind="Internal", **kw).ap()

    y1_d = dsc("y1_d", [C, NTOK])
    x1_d = dsc("x1_d", [C, NTOK])
    y2_d = dsc("y2_d", [C, NTOK])
    x2_d = dsc("x2_d", [C, NTOK])
    y3_d = dsc("y3_d", [C, NTOK])
    x3_d = dsc("x3_d", [C, NTOK])
    y4_d = dsc("y4_d", [C, NTOK])
    s_d = [dsc(f"s{k}_d", [2, NTOK]) for k in range(4)]
    cf_d = [dsc(f"cf{k}_d", [4, NTOK], BF16) for k in range(4)]
    cfx_d = dsc("cfx_d", [4, NTOK], BF16)
    WSEG = 1024 * 352
    WDSEG = 2816 * 128
    wgs_d = dsc("wgs_d", [C, 352])
    wus_d = dsc("wus_d", [C, 352])
    wds_d = dsc("wds_d", [IFF, 128])
    wqin_d = dsc("wqin_d", [2 * WSEG + WDSEG], BF16)
    wq_g = dsc("wq_g", [NCORES * (2 * WSEG + WDSEG)], BF16)
    wgq_d = dsc("wgq_d", [C, IFF], BF16)
    wuq_d = dsc("wuq_d", [C, IFF], BF16)
    wdq_d = dsc("wdq_d", [IFF, C], BF16)
    xn_d = dsc("xn_d", [C, NTOK], BF16)
    xn_g = dsc("xn_g", [NCORES * C, NTOK], BF16)
    mx_d = dsc("mx_d", [128, H * T], BF16)
    mx_g = dsc("mx_g", [NCORES * 128, H * T], BF16)

    ct = Ctx(nc)
    sems = {e: nc.semaphore(f"sem{e}").__enter__() for e in Ctx.ENGS}
    PF = 2300
    PW = 8000
    WB = 91648

    with (
        nc.sbuf_tensor("persf", [128, PF], F32) as persf_t,
        nc.sbuf_tensor("pers", [128, PW], BF16) as pers_t,
        nc.sbuf_tensor("arena", [128, WB], BF16) as arena_t,
        nc.psum_tensor("pb0", [128, 512], F32) as _p0,
        nc.psum_tensor("pb1", [128, 512], F32) as _p1,
        nc.psum_tensor("pb2", [128, 512], F32) as _p2,
        nc.psum_tensor("pb3", [128, 512], F32) as _p3,
        nc.psum_tensor("pb4", [128, 512], F32) as _p4,
        nc.psum_tensor("pb5", [128, 512], F32) as _p5,
        nc.psum_tensor("pb6", [128, 512], F32) as _p6,
        nc.psum_tensor("pb7", [128, 512], F32) as _p7,
        nc.Block() as block,
    ):
        pass_sems = sems
        PB = [_p0, _p1, _p2, _p3, _p4, _p5, _p6, _p7]

        def PF32(off, n, dt=F32):
            return Buf(persf_t, PF, off, n, dt, slab_dt=F32)

        def P(off, n, dt=BF16):
            return Buf(pers_t, PW, off, n, dt)

        def A(off, n, dt=BF16):
            return Buf(arena_t, WB, off, n, dt)

        cattnb = PF32(0, 24)
        cprojb = PF32(24, 8)
        boutb = PF32(32, 8)
        normwb = PF32(40, 8)
        murb = PF32(48, 8)
        mukb = PF32(56, 8)
        omrb = PF32(64, 8)
        omkb = PF32(72, 8)
        q2lb4 = PF32(80, 1)
        q2lbr4 = PF32(81, 1)
        kvb4 = PF32(82, 1)
        kvbr4 = PF32(83, 1)
        wqkb4 = PF32(84, 1)
        l2ob2 = PF32(85, 1)
        skipb = PF32(86, 1)
        cidb = PF32(87, 1, I32)
        kvrow = PF32(88, 32)
        kvbias = PF32(120, 128)
        abdf = PF32(248, 128)
        dn = PF32(376, 1024)
        strow = PF32(1400, 512)
        statw = PF32(1912, 160)
        qt_am = PF32(2072, 32)
        qt_inv = PF32(2104, 32)
        qt_s6 = PF32(2136, 32)
        altsg = PF32(2170, 1)
        constc = PF32(2168, 2)
        qtB_am = PF32(2172, 32)
        qtB_inv = PF32(2204, 32)
        qtB_s6 = PF32(2236, 32)

        nc.const_aps.register(F32, EPS_LN, constc.ap(0, [[1, 1]]))
        nc.const_aps.register(F32, EPS_RMS, constc.ap(1, [[1, 1]]))
        q2lw = P(0, 32)
        q2lrw = P(32, 32)
        kv2lw = P(64, 32)
        kv2lrw = P(96, 32)
        wqkw = P(128, 32)
        l2ow = P(160, 64)
        cosb = P(224, T)
        sinb = P(2272, T)
        abd = P(4320, 128)
        bbd = P(4448, 128)
        cbd = P(4576, 128)
        ones1 = P(4704, 2)
        onesr = P(4706, 128)
        dnh = P(5858, 1024)
        dnl = P(6882, 1024)
        y2buf = P(4834, 512)
        ybf = P(5346, 512)

        cattn = A(0, 24576)
        qT = A(24576, 16384)
        kT = A(40960, 16384)
        vSB = A(57344, 17888)
        qkvS = A(75232, 12288)
        xbS = A(87520, 4096)
        rt1 = A(87520, 512)
        rt2 = A(88544, 512)

        def voff(s2, hh2, kt2):
            return (s2 * H + hh2) * 559 + (0 if kt2 == 0 else 64 + (kt2 - 1) * 33)
        qt_x = A(75232, 1024, F32)
        qt_t = A(76256, 1024, F32)
        qt_xn = A(77280, 1024, F32)
        qt_u = A(78304, 1024, F32)
        qt_r2 = A(79328, 1024, F32)
        qtB_t = A(56832, 1024, F32)
        qtB_xn = A(57856, 1024, F32)
        qtB_u = A(58880, 1024, F32)
        qtB_r2 = A(59904, 1024, F32)
        qtB_r1 = A(60928, 512)
        qtB_a5 = A(61440, 512)
        qtB_sg = A(61952, 512)
        qtB_t2 = A(62464, 512)
        qtB_qs = A(62976, 512)
        qtB_qT = A(63488, 512)

        qt_r1 = A(80352, 512)
        qt_a5 = A(80864, 512)
        qt_sg = A(81376, 512)
        qt_t2 = A(81888, 512)
        qt_qs = A(82400, 512)
        qt_qT = A(82912, 512)
        qt_q = A(83424, 512)
        QTS = [
            {"t": qt_t, "xn": qt_xn, "u": qt_u, "r2": qt_r2, "am": qt_am, "inv": qt_inv,
             "s6": qt_s6, "r1": qt_r1, "a5": qt_a5, "sg": qt_sg, "t2": qt_t2, "qs": qt_qs, "qT": qt_qT},
            {"t": qtB_t, "xn": qtB_xn, "u": qtB_u, "r2": qtB_r2, "am": qtB_am, "inv": qtB_inv,
             "s6": qtB_s6, "r1": qtB_r1, "a5": qtB_a5, "sg": qtB_sg, "t2": qtB_t2, "qs": qtB_qs, "qT": qtB_qT},
        ]
        x0S = A(0, 8192, F32)
        cprojS = A(8192, 8192)
        eS = A(16384, 8192)
        yrawS = A(16384, 8192, F32)
        yvS = A(83936, 2048)
        attnS = A(85984, 4096)

        x1t = A(0, 8192, F32)
        x1tB = A(64000, 8192, F32)
        x1tp = [x1t, x1tB]
        xqS = A(8192, 4096)
        htS = A(12288, 22528, F32)
        hqS = A(34816, 11264)
        wgbuf = [A(46080 + i * 1024, 1024) for i in range(2)]
        wubuf = [A(48128 + i * 1024, 1024) for i in range(2)]
        wdbuf = [A(50176 + i * 2816, 2816) for i in range(2)]
        silS = A(55808, 512)
        xnb = A(56320, 512)

        lyt = A(0, 8192, F32)
        lcf = A(8192, 4096)
        lxn = A(12288, 4096)

        xsc = A(0, 32768)
        stS = A(32768, 1024 + 16)
        mxacc = [A(33808 + i * 4096, 4096) for i in range(2)]
        mxfm = A(0, 32768)
        woutS = A(32768, 8192)
        x2t = A(40960, 8192, F32)
        x2tB = A(49152, 8192, F32)
        x2tp = [x2t, x2tB]

        wrS = A(0, 8192)
        wkS = A(8192, 8192)
        wvS = A(16384, 8192)
        x3t = A(24576, 8192, F32)
        xpt = A(32768, 8192, F32)
        xrS = A(40960, 4096)
        xkS = A(45056, 4096)
        rbS = A(49152, 4096)
        kaS = A(53248, 4096)
        p4tmp = A(57344, 1024, F32)

        def init_phase():
            ct.op("Y", lambda h: h.dma_start(out=cattnb.ap(), in_=AP(cattn_b.tensor, 0, [[1, 128], [128, 24]]),
                                             allow_slow_non_contiguous=True), w=["cattnb"])
            for dst, src, nm in ((cprojb, cproj_b, "cprojb"), (boutb, bout_v, "boutb"),
                                 (normwb, norm_w, "normwb"), (murb, mu_r, "murb"), (mukb, mu_k, "mukb")):
                ct.op("Y", lambda h, d=dst, s=src: h.dma_start(out=d.ap(), in_=AP(s.tensor, 0, [[1, 128], [128, 8]]),
                                                               allow_slow_non_contiguous=True), w=[nm])
            ct.op("Y", lambda h: h.dma_start(out=skipb.ap(), in_=AP(skip_sh.tensor, 0, [[1, 128], [1, 1]])), w=["skipb"])
            ct.op("Y", lambda h: h.dma_start(out=cidb.app(0, 1, dims=[[1, 1]]), in_=core_id_in), w=["cid"])
            for half in range(2):
                ct.op("GD", lambda h, p=64 * half: h.dma_start(out=q2lw.app(p, 64), in_=q2l_wT), w=["q2lw"])
                ct.op("GD", lambda h, p=64 * half: h.dma_start(out=kv2lw.app(p, 64), in_=kv2l_wT), w=["kv2lw"])
                ct.op("GD", lambda h, p=64 * half: h.dma_start(out=l2ow.app(p, 32, dims=[[1, 64]]), in_=l2o_wT), w=["l2ow"])
            for g in range(4):
                ct.op("GD", lambda h, p=32 * g: h.dma_start(out=wqkw.app(p, 32), in_=wqk_wT), w=["wqkw"])
            ct.op("GD", lambda h: h.dma_start(out=cosb.ap(), in_=cos4), w=["cosb"])
            ct.op("GD", lambda h: h.dma_start(out=sinb.ap(), in_=sin4), w=["sinb"])
            ct.op("GD", lambda h: h.dma_start(
                out=cattn.ap(), in_=AP(cattn_wT.tensor, 0, [[3 * C, 128], [128 * 3 * C, 8], [128, 24], [1, 128]])),
                w=["cattn"])
            for g in range(4):
                ct.op("Y", lambda h, p=32 * g: h.dma_start(out=q2lb4.app(p, 32, dims=[[1, 1]]), in_=AP(q2l_b.tensor, 0, [[1, 32], [1, 1]])), w=["q2lb4"])
                ct.op("Y", lambda h, p=32 * g: h.dma_start(out=kvb4.app(p, 32, dims=[[1, 1]]), in_=AP(kv2l_b.tensor, 0, [[1, 32], [1, 1]])), w=["kvb4"])
                ct.op("Y", lambda h, p=32 * g: h.dma_start(out=wqkb4.app(p, 32, dims=[[1, 1]]), in_=AP(wqk_b.tensor, 0, [[1, 32], [1, 1]])), w=["wqkb4"])
            for half in range(2):
                ct.op("Y", lambda h, p=64 * half: h.dma_start(out=l2ob2.app(p, 64, dims=[[1, 1]]), in_=AP(l2o_b.tensor, 0, [[1, 64], [1, 1]])), w=["l2ob2"])
            ct.op("V", lambda h: h.memset(ones1.ap(), 1.0), w=["ones1"])
            ct.op("V", lambda h: h.memset(constc.ap(0, [[1, 1]]), EPS_LN), w=["constc"])
            ct.op("V", lambda h: h.memset(constc.ap(1, [[1, 1]]), EPS_RMS), w=["constc"])
            ct.op("V", lambda h: h.memset(onesr.ap(0, [[1, 128]]), 1.0), w=["onesr"])
            ct.op("V", lambda h: h.memset(dn.ap(), 1.0), w=["dn"])
            for dst, src, nm, nmr in ((q2lrw, q2lw, "q2lw", "q2lrw"), (kv2lrw, kv2lw, "kv2lw", "kv2lrw")):
                ct.op("V", lambda h, d=dst, s=src: h.tensor_scalar_mul(
                    d.ap(0, [[2, 16]]), s.ap(1, [[2, 16]]), -1.0), r=[nm], w=[nmr])
                ct.op("V", lambda h, d=dst, s=src: h.tensor_scalar_mul(
                    d.ap(1, [[2, 16]]), s.ap(0, [[2, 16]]), 1.0), r=[nm], w=[nmr])
            ct.op("Y", lambda h: h.dma_start(out=altsg.ap(), in_=AP(altsgn.tensor, 0, [[1, 128], [1, 1]])), w=["altsg"])
            pair_mask = [i + 1 if i % 2 == 0 else i - 1 for i in range(32)]
            for dst, src, nm, nmr in ((q2lbr4, q2lb4, "q2lb4", "q2lbr4"), (kvbr4, kvb4, "kvb4", "kvbr4")):
                ct.op("V", lambda h, d=dst, s=src: h.stream_shuffle(d.ap(), s.ap(), pair_mask), r=[nm], w=[nmr])
                ct.op("V", lambda h, d=dst: h.tensor_scalar(d.ap(), d.ap(), altsg.ap(), None, AL.mult),
                      r=[nmr, "altsg"], w=[nmr])
            ct.op("V", lambda h: h.transpose(
                kvrow.app(0, 32, dims=[[1, 32]]),
                AP(persf_t, kvb4.off, [[PF, 32], [0, 32]])), r=["kvb4"], w=["kvrow"])
            ct.op("V", lambda h: h.tensor_copy(dnh.app(0, 1, dims=[[1, 32]]), kvrow.app(0, 1, dims=[[1, 32]])),
                  r=["kvrow"], w=["dnh"])
            ct.op("V", lambda h: h.tensor_tensor(dnl.app(0, 1, dims=[[1, 32]]), kvrow.app(0, 1, dims=[[1, 32]]),
                                                 dnh.app(0, 1, dims=[[1, 32]]), AL.subtract),
                  r=["kvrow", "dnh"], w=["dnl"])
            for lohi, buf in ((0, dnh), (1, dnl)):
                ct.op("T", lambda h, b2=buf, st=(lohi == 0): h.matmul(
                    AP(PB[0], 0, [[512, 128], [1, 32]]), onesr.app(0, 1, dims=[[1, 128]]),
                    b2.app(0, 1, dims=[[1, 32]]), start=st, stop=(lohi == 1), tile_position=(0, 0)),
                    r=["dnh", "dnl", "onesr"], w=["pb0"])
            for rep in range(4):
                ct.op("V", lambda h, rp=rep: h.tensor_copy(
                    kvbias.ap(32 * rp, [[1, 32]]), AP(PB[0], 0, [[512, 128], [1, 32]])),
                    r=["pb0"], w=["kvbias"])
            ct.op("V", lambda h: h.memset(abdf.ap(), 0.0), w=["abdf"])
            ct.op("GD", lambda h: h.dma_start(out=abdf.app(0, 64, dims=[[1, 64]]), in_=a_sh[0]), w=["abdf"])
            ct.op("GD", lambda h: h.dma_start(out=abdf.app(64, 64, o=64, dims=[[1, 64]]), in_=a_sh[1]), w=["abdf"])
            ct.op("S", lambda h: h.activation(abd.ap(), abdf.ap(), AF.Tanh), r=["abdf"], w=["abd"])
            for wbd, wsrc, nm in ((bbd, b_sh, "bbd"), (cbd, c_sh, "cbd")):
                ct.op("V", lambda h, d=wbd: h.memset(d.ap(), 0.0), w=[nm])
                ct.op("GD", lambda h, d=wbd, s=wsrc: h.dma_start(out=d.app(0, 64, dims=[[1, 64]]), in_=s[0]), w=[nm])
                ct.op("GD", lambda h, d=wbd, s=wsrc: h.dma_start(out=d.app(64, 64, o=64, dims=[[1, 64]]), in_=s[1]), w=[nm])
            ct.op("V", lambda h: h.tensor_scalar(omrb.ap(), murb.ap(), -1.0, 1.0, AL.mult, AL.add), r=["murb"], w=["omrb"])
            ct.op("V", lambda h: h.tensor_scalar(omkb.ap(), mukb.ap(), -1.0, 1.0, AL.mult, AL.add), r=["mukb"], w=["omkb"])
            ct.op("V", lambda h: h.memset(vSB.ap(), 0.0), w=["vSB"])
            for s2 in range(SEQ):
                for hh2 in range(H):
                    base = (s2 * H + hh2) * 559
                    ct.op("V", lambda h, b2=base: h.memset(vSB.ap(b2 + 32, [[1, 1]]), 1.0), w=["vSB"])
                    ct.op("V", lambda h, b2=base: h.memset(vSB.ap(b2 + 96, [[33, 15], [1, 1]]), 1.0), w=["vSB"])

        init_phase()
        ct.barrier()

        def qstage_a(src_ap, n, qs_, r_extra=()):
            nb = n // 16
            B = QTS[qs_]
            tg = f"qtmp{qs_}"
            ct.op("V", lambda h: h.transpose(B["t"].ap(0, [[1, n]]), src_ap), r=list(r_extra) + [tg], w=[tg])
            ct.op("V", lambda h: h.tensor_reduce(B["am"].ap(0, [[1, nb]]), B["t"].ap(0, [[16, nb], [1, 16]]),
                                                 AX, AL.max, apply_absolute_value=True), r=[tg], w=[tg])
            ct.op("V", lambda h: h.tensor_scalar(B["s6"].ap(0, [[1, nb]]), B["am"].ap(0, [[1, nb]]),
                                                 1.0 / 6.0, 1e-12, AL.mult, AL.max), r=[tg], w=[tg])
            ct.op("V", lambda h: h.reciprocal(B["inv"].ap(0, [[1, nb]]), B["s6"].ap(0, [[1, nb]])), r=[tg], w=[tg])
            ct.op("V", lambda h: h.tensor_tensor(B["xn"].ap(0, [[16, nb], [1, 16]]), B["t"].ap(0, [[16, nb], [1, 16]]),
                                                 B["inv"].ap(0, [[1, nb], [0, 16]]), AL.mult), r=[tg], w=[tg])

        def qstage_s(n, qs_):
            B = QTS[qs_]
            tg = f"qtmp{qs_}"
            sg_t = f"qsg{qs_}"
            ct.op("S", lambda h: h.activation(B["u"].ap(0, [[1, n]]), B["xn"].ap(0, [[1, n]]), AF.Abs),
                  r=[tg], w=[tg])
            ct.op("S", lambda h: h.activation(B["sg"].ap(0, [[1, n]]), B["xn"].ap(0, [[1, n]]), AF.Sign), r=[tg], w=[sg_t])

        def qstage_b(dst_ap, n, qs_, w_extra=()):
            nb = n // 16
            B = QTS[qs_]
            tg = f"qtmp{qs_}"
            sg_t = f"qsg{qs_}"
            ct.op("V", lambda h: h.tensor_scalar(B["r2"].ap(0, [[1, n]]), B["u"].ap(0, [[1, n]]), 2.0, C_HALF, AL.min, AL.add),
                  r=[tg], w=[tg])
            ct.op("V", lambda h: h.tensor_scalar(B["r1"].ap(0, [[1, n]]), B["r2"].ap(0, [[1, n]]), C_HALF, None, AL.subtract),
                  r=[tg], w=[tg])
            ct.op("V", lambda h: h.tensor_scalar(B["r2"].ap(0, [[1, n]]), B["u"].ap(0, [[1, n]]), 2.0, 4.0, AL.max, AL.min),
                  r=[tg], w=[tg])
            ct.op("V", lambda h: h.tensor_scalar(B["r2"].ap(0, [[1, n]]), B["r2"].ap(0, [[1, n]]), C_INT, C_INT, AL.add, AL.subtract),
                  r=[tg], w=[tg])
            ct.op("V", lambda h: h.tensor_scalar(B["a5"].ap(0, [[1, n]]), B["u"].ap(0, [[1, n]]), 5.0, 2.0, AL.is_gt, AL.mult),
                  r=[tg], w=[tg])
            ct.op("V", lambda h: h.tensor_tensor(B["t2"].ap(0, [[1, n]]), B["r1"].ap(0, [[1, n]]), B["r2"].ap(0, [[1, n]]), AL.add),
                  r=[tg], w=[tg])
            ct.op("V", lambda h: h.tensor_tensor(B["t2"].ap(0, [[1, n]]), B["t2"].ap(0, [[1, n]]), B["a5"].ap(0, [[1, n]]), AL.add),
                  r=[tg], w=[tg])
            ct.op("V", lambda h: h.scalar_tensor_tensor(B["qs"].ap(0, [[1, n]]), B["t2"].ap(0, [[1, n]]), 2.0,
                                                        B["sg"].ap(0, [[1, n]]), AL.subtract, AL.mult),
                  r=[tg, sg_t], w=[tg])
            ct.op("V", lambda h: h.tensor_tensor(B["qT"].ap(0, [[16, nb], [1, 16]]), B["qs"].ap(0, [[16, nb], [1, 16]]),
                                                 B["s6"].ap(0, [[1, nb], [0, 16]]), AL.mult), r=[tg], w=[tg])
            ct.op("V", lambda h: h.transpose(dst_ap, B["qT"].ap(0, [[1, n]])), r=[tg], w=list(w_extra) + [tg])

        def quant_block(src_ap, dst_ap, n, r_extra=(), w_extra=(), qs_=0):
            qstage_a(src_ap, n, qs_, r_extra=r_extra)
            qstage_s(n, qs_)
            qstage_b(dst_ap, n, qs_, w_extra=w_extra)

        lnw0 = strow.app(0, 1, dims=[[1, 512]])
        lnw1 = dn.app(0, 1, o=0, dims=[[1, 512]])
        lnw2 = dn.app(0, 1, o=512, dims=[[1, 512]])
        rhi = ybf.app(0, 1, dims=[[1, 512]])
        rlo = y2buf.app(0, 1, dims=[[1, 512]])
        nhi = dnh.app(0, 1, o=0, dims=[[1, 512]])
        nlo = dnh.app(0, 1, o=512, dims=[[1, 512]])

        def ln_coef(rms=False):
            ct.op("V", lambda h: h.tensor_scalar_mul(lnw0, lnw0, 1.0 / C), r=["strow"], w=["strow"])
            ct.op("V", lambda h: h.tensor_scalar_mul(lnw1, lnw1, 1.0 / C), r=["strow", "dn"], w=["dn"])
            ct.op("V", lambda h: h.tensor_tensor(lnw2, lnw0, lnw0, AL.mult), r=["strow", "dn"], w=["dn"])
            ct.op("V", lambda h: h.tensor_tensor(lnw1, lnw1, lnw2, AL.subtract), r=["dn"], w=["dn"])
            if rms:
                ct.op("V", lambda h: h.tensor_copy(lnw2, lnw1), r=["dn"], w=["dn"])
            ct.op("S", lambda h: h.activation(lnw1, lnw1, AF.Sqrt, bias=EPS_LN), r=["dn"], w=["dn"])
            ct.op("V", lambda h: h.reciprocal(lnw1, lnw1), r=["dn"], w=["dn"])
            ct.op("V", lambda h: h.tensor_tensor(lnw0, lnw0, lnw1, AL.mult), r=["strow", "dn"], w=["strow"])

        def ln_hilo_mms(src_r, src_nb, bR, btagR, bNB, btagNB):
            ct.op("V", lambda h, s=src_r: h.tensor_copy(rhi, s), r=["strow", "dn"], w=["ybf"])
            ct.op("V", lambda h, s=src_r: h.tensor_tensor(rlo, s, rhi, AL.subtract),
                  r=["strow", "dn", "ybf"], w=["y2b"])
            ct.op("V", lambda h, s=src_nb: h.tensor_copy(nhi, s), r=["strow", "dn", "dnh"], w=["dnh"])
            ct.op("V", lambda h, s=src_nb: h.tensor_tensor(nlo, s, nhi, AL.subtract),
                  r=["strow", "dn", "dnh"], w=["dnh"])
            ct.op("T", lambda h, b=bR: h.matmul(b[:, :], onesr.app(0, 1, dims=[[1, 128]]), rhi,
                                                start=True, stop=False, tile_position=(0, 0)),
                  r=["ybf", "onesr"], w=[btagR])
            ct.op("T", lambda h, b=bR: h.matmul(b[:, :], onesr.app(0, 1, dims=[[1, 128]]), rlo,
                                                start=False, stop=True, tile_position=(0, 0)),
                  r=["y2b", "onesr"], w=[btagR])
            ct.op("T", lambda h, b=bNB: h.matmul(b[:, :], onesr.app(0, 1, dims=[[1, 128]]), nhi,
                                                 start=True, stop=False, tile_position=(0, 0)),
                  r=["dnh", "onesr"], w=[btagNB])
            ct.op("T", lambda h, b=bNB: h.matmul(b[:, :], onesr.app(0, 1, dims=[[1, 128]]), nlo,
                                                 start=False, stop=True, tile_position=(0, 0)),
                  r=["dnh", "onesr"], w=[btagNB])

        def wq_slice_fetch():
            ct.op("GD", lambda h: h.dma_start(
                out=wgs_d, in_=AP(wg_T.tensor, nc.partition_id([mybir.EngineType.Pool]) * 352,
                                  [[IFF, C], [1, 352]])), r=[], w=["wgs"])
            ct.op("GD", lambda h: h.dma_start(
                out=wus_d, in_=AP(wu_T.tensor, nc.partition_id([mybir.EngineType.Pool]) * 352,
                                  [[IFF, C], [1, 352]])), r=[], w=["wus"])
            ct.op("GD", lambda h: h.dma_start(
                out=wds_d, in_=AP(wd_T.tensor, nc.partition_id([mybir.EngineType.Pool]) * 128,
                                  [[C, IFF], [1, 128]])), r=[], w=["wds"])

        def wq_blocklist():
            out = []
            for (w_src, seg0, rows, cols, nm) in ((wgs_d, 0, C, 352, "wgs"),
                                                  (wus_d, WSEG, C, 352, "wus"),
                                                  (wds_d, 2 * WSEG, IFF, 128, "wds")):
                for rc in range(rows // 128):
                    out.append((w_src, seg0, cols, nm, rc))
            return out

        WQB = wq_blocklist()
        wqi = [0]

        def emit_wq(kcount):
            for _ in range(kcount):
                if wqi[0] >= len(WQB):
                    return
                w_src, seg0, n, nm, rc = WQB[wqi[0]]
                wqi[0] += 1
                src = AP(w_src.tensor, rc * 128 * n, [[n, 128], [1, n]])
                dst = AP(wqin_d.tensor, seg0 + rc * 128 * n, [[n, 128], [1, n]])
                ct.op("Y", lambda h, s=src, n=n: h.dma_start(out=qt_x.ap(0, [[1, n]]), in_=s), r=["qtmp0", nm], w=["qt_x"])
                quant_block(qt_x.ap(0, [[1, n]]), qt_q.ap(0, [[1, n]]), n, r_extra=["qt_x"], qs_=0)
                ct.op("Y", lambda h, d=dst, n=n: h.dma_start(out=d, in_=qt_q.ap(0, [[1, n]])), r=["qtmp0"], w=["wqin"])

        def wq_ag():
            ct.op("G", lambda h: h.collective_compute(
                "AllGather", AL.bypass, replica_groups=[list(range(NCORES))],
                ins=[wqin_d.opt()], outs=[wq_g.opt()]), r=["wqin"], w=["wq_g"])

        RK = 2 * WSEG + WDSEG

        def wg_segs(oc):
            a = 128 * oc
            b = a + 128
            segs = []
            r2 = a // 352
            while a < b:
                e = min(b, 352 * (r2 + 1))
                segs.append((r2, a, e))
                a = e
                r2 += 1
            return segs

        def p0_tile(s, j):
            jj = s * TPS + j
            col0 = jj * TILE
            pos0 = j * TILE
            ct.op("GD", lambda h: h.dma_start(
                out=xbS.ap(), in_=AP(x_in.tensor, col0, [[NTOK, 128], [128 * NTOK, 8], [1, 512]])),
                r=["xbrt"], w=["xbrt"])
            for oc in range(24):
                bank, btag = PB[oc % 2], f"pb{oc % 2}"
                for kc in range(8):
                    lhsT = cattn.ap((kc * 24 + oc) * 128, [[1, 128]])
                    rhs = xbS.ap(kc * 512, [[1, 512]])
                    ct.op("T", lambda h, l=lhsT, r2=rhs, b=bank, st=(kc == 0), sp=(kc == 7):
                          h.matmul(b[:, :], l, r2, start=st, stop=sp), r=["cattn", "xbrt"], w=[btag])
                ct.op("S", lambda h, oc=oc, b=bank: h.activation(
                    qkvS.ap(oc * 512, [[1, 512]]), b[:, :], AF.Identity, bias=cattnb.ap(oc, [[1, 1]])),
                    r=[btag, "cattnb"], w=["qkv"])
            cosap = cosb.ap(pos0, [[1, 512]])
            sinap = sinb.ap(pos0, [[1, 512]])
            for g in range(4):
                for (wl, bank, btag, wtag) in ((q2lw, PB[2], "pb2", "q2lw"), (q2lrw, PB[3], "pb3", "q2lrw"),
                                               (kv2lw, PB[4], "pb4", "kv2lw"), (kv2lrw, PB[5], "pb5", "kv2lrw")):
                    srcc = 0 if wtag.startswith("q") else 8
                    for i in range(4):
                        hh = 4 * g + i
                        chunk = srcc + hh // 2
                        par = 64 * (hh % 2)
                        lhsT = wl.app(par, 64)
                        rhs = qkvS.app(par, 64, o=chunk * 512, dims=[[1, 512]])
                        ct.op("T", lambda h, l=lhsT, r2=rhs, b=bank, i=i, par=par:
                              h.matmul(AP(b, 32 * i * 512, [[512, 32], [1, 512]]), l, r2,
                                       start=True, stop=True, tile_position=(par, 32 * i)),
                              r=["qkv", wtag], w=[btag])
                ct.op("V", lambda h: h.scalar_tensor_tensor(rt1.ap(), PB[2][:, :], q2lb4.ap(), cosap, AL.add, AL.mult),
                      r=["pb2", "q2lb4", "cosb", "xbrt"], w=["xbrt1"])
                ct.op("V", lambda h: h.scalar_tensor_tensor(rt2.ap(), PB[3][:, :], q2lbr4.ap(), sinap, AL.add, AL.mult),
                      r=["pb3", "q2lbr4", "sinb", "xbrt"], w=["xbrt2"])
                ct.op("V", lambda h: h.tensor_tensor(rt1.ap(), rt1.ap(), rt2.ap(), AL.add),
                      r=["xbrt1", "xbrt2"], w=["xbrt1"])
                for i in range(4):
                    lhsT = wqkw.app(32 * i, 32)
                    rhs = rt1.app(32 * i, 32, dims=[[1, 512]])
                    ct.op("T", lambda h, l=lhsT, r2=rhs, i=i:
                          h.matmul(AP(PB[6], 32 * i * 512, [[512, 32], [1, 512]]), l, r2,
                                   start=True, stop=True, tile_position=(32 * i, 32 * i)),
                          r=["xbrt1", "wqkw"], w=["pb6"])
                qdst = qT.ap((s * 4 + g) * T + pos0, [[1, 512]])
                ct.op("S", lambda h, d=qdst: h.activation(d, PB[6][:, :], AF.Identity, bias=wqkb4.ap()),
                      r=["pb6", "wqkb4"], w=["qT"])
                kdst = kT.ap((s * 4 + g) * T + pos0, [[1, 512]])
                ct.op("V", lambda h: h.scalar_tensor_tensor(rt1.ap(), PB[4][:, :], kvb4.ap(), cosap, AL.add, AL.mult),
                      r=["pb4", "kvb4", "cosb", "xbrt", "xbrt1"], w=["xbrt1"])
                ct.op("V", lambda h: h.scalar_tensor_tensor(rt2.ap(), PB[5][:, :], kvbr4.ap(), sinap, AL.add, AL.mult),
                      r=["pb5", "kvbr4", "sinb", "xbrt", "xbrt2"], w=["xbrt2"])
                ct.op("V", lambda h, d=kdst: h.tensor_tensor(d, rt1.ap(), rt2.ap(), AL.add),
                      r=["xbrt1", "xbrt2"], w=["kT"])
            for hh in range(H):
                par = 64 * (hh % 2)
                chunk = 16 + hh // 2
                bank, btag = (PB[2], "pb2") if hh % 2 == 0 else (PB[3], "pb3")
                for ksub in range(4):
                    lhsT = qkvS.app(par, 64, o=chunk * 512 + ksub * 128, dims=[[1, 128]])
                    rhs = kv2lw.app(par, 64)
                    ct.op("T", lambda h, l=lhsT, r2=rhs, b=bank, ks=ksub, par=par:
                          h.matmul(AP(b, ks * 32, [[512, 128], [1, 32]]), l, r2,
                                   start=True, stop=True, tile_position=(par, 0)),
                          r=["qkv", "kv2lw"], w=[btag])
                if j == 0:
                    vdst0 = vSB.ap(voff(s, hh, 0), [[1, 1], [1, 32]])
                    ct.op("V", lambda h, b=bank, d=vdst0: h.tensor_tensor(
                        d, AP(b, 0, [[512, 128], [32, 1], [1, 32]]), kvbias.ap(0, [[32, 1], [1, 32]]), AL.add),
                        r=[btag, "kvbias"], w=["vSB"])
                    vdst1 = vSB.ap(voff(s, hh, 1), [[33, 3], [1, 32]])
                    ct.op("V", lambda h, b=bank, d=vdst1: h.tensor_tensor(
                        d, AP(b, 32, [[512, 128], [32, 3], [1, 32]]), kvbias.ap(32, [[32, 3], [1, 32]]), AL.add),
                        r=[btag, "kvbias"], w=["vSB"])
                else:
                    vdst = vSB.ap(voff(s, hh, j * 4), [[33, 4], [1, 32]])
                    ct.op("V", lambda h, b=bank, d=vdst: h.tensor_tensor(
                        d, AP(b, 0, [[512, 128], [32, 4], [1, 32]]), kvbias.ap(0, [[32, 4], [1, 32]]), AL.add),
                        r=[btag, "kvbias"], w=["vSB"])

        for s in range(SEQ):
            for j in range(TPS):
                p0_tile(s, j)
        ct.barrier()

        ct.op("GD", lambda h: h.dma_start(
            out=cprojS.ap(), in_=AP(cproj_wT.tensor, 0, [[C, 128], [128 * C, 8], [128, 8], [1, 128]])),
            w=["cprojw"])

        def p1_tile(s, qt):
            qlo = qt * TILE
            jj = s * TPS + qt
            col0 = jj * TILE
            ct.op("Y", lambda h: h.dma_start(
                out=x0S.ap(), in_=AP(x_in.tensor, col0, [[NTOK, 128], [128 * NTOK, 8], [1, 512]])),
                r=["x0"], w=["x0"])
            nkt = 4 * qt + 4
            for half in range(2):
                for kt in range(nkt):
                    qs = 0 if (kt + 1) * 128 <= qlo else (kt * 128 - qlo)
                    epar = kt % 2
                    for gg in range(2):
                        g = half * 2 + gg
                        for i in range(4):
                            lhsT = kT.app(32 * i, 32, o=(s * 4 + g) * T + kt * 128, dims=[[1, 128]])
                            rhs = qT.app(32 * i, 32, o=(s * 4 + g) * T + qlo + qs, dims=[[1, 512 - qs]])
                            ct.op("T", lambda h, l=lhsT, r2=rhs, b=PB[i], qs=qs, i=i:
                                  h.matmul(AP(b, qs, [[512, 128], [1, 512 - qs]]), l, r2,
                                           start=True, stop=True, tile_position=(32 * i, 0)),
                                  r=["qT", "kT"], w=[f"pb{i}"])
                        for i in range(4):
                            hrel = gg * 4 + i
                            ec = epar * 4096 + hrel * 512
                            ct.op("S", lambda h, i=i, ec=ec, qs=qs: h.activation(
                                eS.ap(ec + qs, [[1, 512 - qs]]),
                                AP(PB[i], qs, [[512, 128], [1, 512 - qs]]), AF.Exp, scale=ISQL),
                                r=[f"pb{i}"], w=[f"e{epar}"])
                            if qs > 0:
                                ct.op("V", lambda h, ec=ec, qs=qs: h.memset(eS.ap(ec, [[1, qs]]), 0.0),
                                      w=[f"e{epar}"])
                            if kt * 128 >= qlo:
                                ct.op("G", lambda h, ec=ec, qs=qs: h.affine_select(
                                    out=eS.ap(ec + qs, [[1, 128]]), in_=eS.ap(ec + qs, [[1, 128]]),
                                    compare_op=AL.is_ge, fill=0.0, base=0,
                                    pattern=[[1, 128]], channel_multiplier=-1),
                                    r=[f"e{epar}"], w=[f"e{epar}"])
                    for pr in range(4):
                        for ab in range(2):
                            hh = half * 8 + 2 * pr + ab
                            hrel = hh - half * 8
                            ec = epar * 4096 + hrel * 512
                            mdim = 64 if kt == 0 else 33
                            lhsT = vSB.ap(voff(s, hh, kt), [[1, mdim]])
                            rhs = eS.ap(ec + qs, [[1, 512 - qs]])
                            ct.op("T", lambda h, l=lhsT, r2=rhs, b=PB[4 + pr], ab=ab, st=(kt == 0), qs=qs, md=mdim:
                                  h.matmul(AP(b, 64 * ab * 512 + qs, [[512, md], [1, 512 - qs]]), l, r2,
                                           start=st, stop=False, tile_position=(0, 64 * ab)),
                                  r=["vSB", f"e{epar}"], w=[f"pb{4 + pr}"])
                for pr in range(4):
                    bank, btag = PB[4 + pr], f"pb{4 + pr}"
                    ct.op("S", lambda h, b=bank, pr=pr: h.activation(
                        yrawS.ap(pr * 512, [[1, 512]]), b[:, :], AF.Identity),
                        r=[btag], w=["e0", "e1", "yraw"])
                    ct.op("GD", lambda h, pr=pr: h.dma_start(
                        out=dn.app(32 * pr, 1, dims=[[1, 512]]),
                        in_=yrawS.app(32, 1, o=pr * 512, dims=[[1, 512]])),
                        r=["yraw"], w=["dn"])
                    ct.op("GD", lambda h, pr=pr: h.dma_start(
                        out=dn.app(32 * pr, 1, o=512, dims=[[1, 512]]),
                        in_=yrawS.app(96, 1, o=pr * 512, dims=[[1, 512]])),
                        r=["yraw"], w=["dn"])
                ct.op("V", lambda h: h.reciprocal(dn.ap(), dn.ap()), r=["dn"], w=["dn"])
                ct.op("V", lambda h: h.tensor_copy(dnh.ap(), dn.ap()), r=["dn"], w=["dnh"])
                ct.op("V", lambda h: h.tensor_tensor(dnl.ap(), dn.ap(), dnh.ap(), AL.subtract),
                      r=["dn", "dnh"], w=["dnl"])
                for pr in range(4):
                    ch = half * 4 + pr
                    bank, btag = PB[4 + pr], f"pb{4 + pr}"
                    for ab in range(2):
                        for lohi, buf, bt2 in ((0, dnh, "dnh"), (1, dnl, "dnl")):
                            ct.op("T", lambda h, b=bank, pr=pr, ab=ab, b2=buf, st=(lohi == 0):
                                  h.matmul(AP(b, 64 * ab * 512, [[512, 64], [1, 512]]),
                                           onesr.app(32 * pr, 1, dims=[[1, 64]]),
                                           b2.app(32 * pr, 1, o=512 * ab, dims=[[1, 512]]),
                                           start=st, stop=(lohi == 1), tile_position=(32 * pr, 64 * ab)),
                                  r=["dnh", "dnl", "onesr", "yraw"], w=[btag])
                    ct.op("V", lambda h, b=bank, pr=pr: h.tensor_tensor(
                        yvS.ap(pr * 512, [[1, 512]]), yrawS.ap(pr * 512, [[1, 512]]), b[:, :], AL.mult),
                        r=[btag, "yraw"], w=["yv"])
                    ab_bank, ab_tag = PB[pr % 2], f"pb{pr % 2}"
                    for ab in range(2):
                        lhsT = l2ow.app(64 * ab, 32, dims=[[1, 64]])
                        rhs = yvS.app(64 * ab, 32, o=pr * 512, dims=[[1, 512]])
                        ct.op("T", lambda h, l=lhsT, r2=rhs, b=ab_bank, ab=ab:
                              h.matmul(AP(b, 64 * ab * 512, [[512, 64], [1, 512]]), l, r2,
                                       start=True, stop=True, tile_position=(64 * ab, 64 * ab)),
                              r=["yv", "l2ow"], w=[ab_tag])
                    ct.op("S", lambda h, b=ab_bank, ch=ch: h.activation(
                        attnS.ap(ch * 512, [[1, 512]]), b[:, :], AF.Identity, bias=l2ob2.ap()),
                        r=[ab_tag, "l2ob2"], w=["attn"])
            for oc in range(8):
                bank, btag = PB[2 + oc % 2], f"pb{2 + oc % 2}"
                for kc in range(8):
                    lhsT = cprojS.ap((kc * 8 + oc) * 128, [[1, 128]])
                    rhs = attnS.ap(kc * 512, [[1, 512]])
                    ct.op("T", lambda h, l=lhsT, r2=rhs, b=bank, st=(kc == 0), sp=(kc == 7):
                          h.matmul(b[:, :], l, r2, start=st, stop=sp), r=["attn", "cprojw"], w=[btag])
                xs = x0S.ap(oc * 512, [[1, 512]])
                ct.op("V", lambda h, b=bank, xs=xs, oc=oc: h.scalar_tensor_tensor(
                    xs, b[:, :], cprojb.ap(oc, [[1, 1]]), xs, AL.add, AL.add),
                    r=[btag, "x0", "cprojb"], w=["x0"])
                ct.op("S", lambda h, xs=xs: h.activation(y2buf.ap(), xs, AF.Square), r=["x0"], w=["y2b"])
                ct.op("V", lambda h, xs=xs: h.tensor_copy(ybf.ap(), xs), r=["x0"], w=["ybf"])
                ct.op("T", lambda h, st=(oc == 0), sp=(oc == 7): h.matmul(
                    AP(PB[6], 0, [[512, 1], [1, 512]]), ones1.ap(0, [[1, 1]]), ybf.ap(), start=st, stop=sp,
                    tile_position=(0, 0)), r=["ybf", "ones1"], w=["pb6"])
                ct.op("T", lambda h, st=(oc == 0), sp=(oc == 7): h.matmul(
                    AP(PB[7], 0, [[512, 1], [1, 512]]), ones1.ap(0, [[1, 1]]), y2buf.ap(), start=st, stop=sp,
                    tile_position=(0, 0)), r=["y2b", "ones1"], w=["pb7"])
            ct.op("V", lambda h: h.tensor_copy(lnw0, AP(PB[6], 0, [[512, 1], [1, 512]])),
                  r=["pb6", "strow"], w=["strow"])
            ct.op("V", lambda h: h.tensor_copy(lnw1, AP(PB[7], 0, [[512, 1], [1, 512]])),
                  r=["pb7", "dn"], w=["dn"])
            ln_coef(rms=False)
            ln_hilo_mms(lnw1, lnw0, PB[6], "pb6", PB[7], "pb7")
            x1dst = out_d if stop == 1 else x1_d
            for oc in range(8):
                xs = x0S.ap(oc * 512, [[1, 512]])
                ct.op("V", lambda h, xs=xs: h.tensor_tensor(xs, xs, PB[6][:, :], AL.mult),
                      r=["x0", "pb6"], w=["x0"])
                ct.op("V", lambda h, xs=xs: h.tensor_tensor(xs, xs, PB[7][:, :], AL.subtract),
                      r=["x0", "pb7"], w=["x0"])
                ct.op("Y", lambda h, xs=xs, oc=oc: h.dma_start(
                    out=AP(x1dst.tensor, oc * 128 * NTOK + col0, [[NTOK, 128], [1, 512]]), in_=xs),
                    r=["x0"], w=["x1_d"])

        wq_slice_fetch()
        p1_ti = 0
        for s in range(SEQ):
            for qt in range(TPS):
                p1_tile(s, qt)
                emit_wq(10)
                p1_ti += 1
                if p1_ti == 4:
                    emit_wq(len(WQB))
                    wq_ag()
        ct.barrier()

        def lnfix(k, y_src, s_src, cf_dst, x_dst, rms_extra=False, to_out=False):
            sw = [statw.ap(i * 32, [[1, 32]]) for i in range(5)]
            ct.op("Y", lambda h: h.dma_start(
                out=statw.ap(0, [[32, 2], [1, 32]]),
                in_=AP(s_src.tensor, 0, [[32, 128], [NTOK, 2], [1, 32]])), r=["statw"], w=["statw"])
            ct.op("V", lambda h: h.tensor_scalar_mul(sw[0], sw[0], 1.0 / C), r=["statw"], w=["statw"])
            ct.op("V", lambda h: h.tensor_scalar_mul(sw[1], sw[1], 1.0 / C), r=["statw"], w=["statw"])
            ct.op("V", lambda h: h.tensor_tensor(sw[2], sw[0], sw[0], AL.mult), r=["statw"], w=["statw"])
            ct.op("V", lambda h: h.tensor_tensor(sw[1], sw[1], sw[2], AL.subtract), r=["statw"], w=["statw"])
            ct.op("S", lambda h: h.activation(sw[2], sw[1], AF.Sqrt, bias=EPS_LN), r=["statw"], w=["statw"])
            ct.op("V", lambda h: h.reciprocal(sw[2], sw[2]), r=["statw"], w=["statw"])
            ct.op("V", lambda h: h.tensor_tensor(sw[3], sw[0], sw[2], AL.mult), r=["statw"], w=["statw"])
            ct.op("V", lambda h: h.tensor_scalar_mul(sw[3], sw[3], -1.0), r=["statw"], w=["statw"])
            def hilo(dst_t, row, src_sw):
                ct.op("V", lambda h: h.tensor_copy(ybf.ap(0, [[1, 32]]), src_sw), r=["statw"], w=["ybf"])
                ct.op("V", lambda h: h.tensor_tensor(strow.ap(0, [[1, 32]]), src_sw, ybf.ap(0, [[1, 32]]),
                                                     AL.subtract), r=["statw", "ybf"], w=["strow"])
                ct.op("Y", lambda h: h.dma_start(
                    out=AP(dst_t.tensor, row * NTOK, [[32, 128], [1, 32]]), in_=ybf.ap(0, [[1, 32]])),
                    r=["ybf"], w=[dst_t.tensor.name])
                ct.op("V", lambda h: h.tensor_copy(ybf.ap(0, [[1, 32]]), strow.ap(0, [[1, 32]])),
                      r=["strow"], w=["ybf"])
                ct.op("Y", lambda h: h.dma_start(
                    out=AP(dst_t.tensor, (row + 1) * NTOK, [[32, 128], [1, 32]]), in_=ybf.ap(0, [[1, 32]])),
                    r=["ybf"], w=[dst_t.tensor.name])
            hilo(cf_dst, 0, sw[2])
            hilo(cf_dst, 2, sw[3])
            if rms_extra:
                ct.op("V", lambda h: h.tensor_tensor(sw[4], sw[2], sw[2], AL.mult), r=["statw"], w=["statw"])
                ct.op("V", lambda h: h.tensor_tensor(sw[4], sw[4], sw[1], AL.mult), r=["statw"], w=["statw"])
                ct.op("S", lambda h: h.activation(sw[4], sw[4], AF.Sqrt, bias=EPS_RMS), r=["statw"], w=["statw"])
                ct.op("V", lambda h: h.reciprocal(sw[4], sw[4]), r=["statw"], w=["statw"])
                ct.op("V", lambda h: h.tensor_tensor(sw[2], sw[2], sw[4], AL.mult), r=["statw"], w=["statw"])
                ct.op("V", lambda h: h.tensor_tensor(sw[3], sw[3], sw[4], AL.mult), r=["statw"], w=["statw"])
                hilo(cfx_d, 0, sw[2])
                hilo(cfx_d, 2, sw[3])
            for jj in range(NT):
                col0 = jj * TILE
                ct.op("Y", lambda h, c0=col0: h.dma_start(
                    out=lyt.ap(), in_=AP(y_src.tensor, c0, [[NTOK, 128], [128 * NTOK, 8], [1, 512]])),
                    r=["lyt"], w=["lyt"])
                ct.op("Y", lambda h, c0=col0: h.dma_start(
                    out=lcf.app(0, 1, dims=[[512, 4], [1, 512]]),
                    in_=AP(cf_dst.tensor, c0, [[NTOK, 1], [NTOK, 4], [1, 512]])),
                    r=["lcf", cf_dst.tensor.name], w=["lcf"])
                for part, btag in ((0, "pb0"), (2, "pb1")):
                    bank = PB[0] if part == 0 else PB[1]
                    for lohi in range(2):
                        ct.op("T", lambda h, b=bank, p=part + lohi, st=(lohi == 0):
                              h.matmul(b[:, :], onesr.app(0, 1, dims=[[1, 128]]),
                                       lcf.app(0, 1, o=p * 512, dims=[[1, 512]]), start=st, stop=(lohi == 1),
                                       tile_position=(0, 0)),
                              r=["lcf", "onesr"], w=[btag])
                if rms_extra:
                    ct.op("Y", lambda h, c0=col0: h.dma_start(
                        out=lcf.app(0, 1, o=2048, dims=[[512, 4], [1, 512]]),
                        in_=AP(cfx_d.tensor, c0, [[NTOK, 1], [NTOK, 4], [1, 512]])),
                        r=["lcf2", "cfx_d"], w=["lcf2"])
                    for part, btag in ((0, "pb2"), (2, "pb3")):
                        bank = PB[2] if part == 0 else PB[3]
                        for lohi in range(2):
                            ct.op("T", lambda h, b=bank, p=part + lohi, st=(lohi == 0):
                                  h.matmul(b[:, :], onesr.app(0, 1, dims=[[1, 128]]),
                                           lcf.app(0, 1, o=2048 + p * 512, dims=[[1, 512]]), start=st, stop=(lohi == 1),
                                           tile_position=(0, 0)),
                                  r=["lcf2", "onesr"], w=[btag])
                for oc in range(8):
                    ys = lyt.ap(oc * 512, [[1, 512]])
                    if rms_extra:
                        ct.op("V", lambda h, ys=ys: h.tensor_tensor(qt_x.ap(0, [[1, 512]]), ys, PB[2][:, :], AL.mult),
                              r=["lyt", "pb2", "qtmp0"], w=["qtmp0"])
                        ct.op("V", lambda h: h.tensor_tensor(qt_u.ap(0, [[1, 512]]), qt_x.ap(0, [[1, 512]]), PB[3][:, :], AL.add),
                              r=["qtmp0", "pb3"], w=["qtmp0"])
                        ct.op("V", lambda h, oc=oc: h.tensor_scalar(
                            lxn.ap(oc * 512, [[1, 512]]), qt_u.ap(0, [[1, 512]]), normwb.ap(oc, [[1, 1]]), None, AL.mult),
                            r=["qtmp0", "normwb"], w=["lxn"])
                    ct.op("V", lambda h, ys=ys: h.tensor_tensor(ys, ys, PB[0][:, :], AL.mult),
                          r=["lyt", "pb0"], w=["lyt"])
                    ct.op("V", lambda h, ys=ys: h.tensor_tensor(ys, ys, PB[1][:, :], AL.add),
                          r=["lyt", "pb1"], w=["lyt"])
                dst = out_d if to_out else x_dst
                ct.op("Y", lambda h, c0=col0, d=dst: h.dma_start(
                    out=AP(d.tensor, c0, [[NTOK, 128], [128 * NTOK, 8], [1, 512]]), in_=lyt.ap()),
                    r=["lyt"], w=["x_out"])
                if rms_extra:
                    ct.op("Y", lambda h, c0=col0: h.dma_start(
                        out=AP(xn_d.tensor, c0, [[NTOK, 128], [128 * NTOK, 8], [1, 512]]), in_=lxn.ap()),
                        r=["lxn"], w=["xn_d"])

        if stop == 1:
            ct.emit(block, sems)
            return nc

        def p2_tile(jj):
            col0 = jj * TILE
            xt = x1tp[jj % 2]
            xtg = f"x1t{jj % 2}"
            ct.op("Y", lambda h, xt=xt: h.dma_start(
                out=xt.ap(), in_=AP(x1_d.tensor, col0, [[NTOK, 128], [128 * NTOK, 8], [1, 512]])),
                r=[xtg], w=[xtg])
            for kc in range(8):
                qstage_a(xt.ap(kc * 512, [[1, 512]]), 512, kc % 2, r_extra=[xtg])
                qstage_s(512, kc % 2)
                if kc >= 1:
                    qstage_b(xqS.ap((kc - 1) * 512, [[1, 512]]), 512, (kc - 1) % 2, w_extra=[f"xq{kc - 1}"])
            qstage_b(xqS.ap(7 * 512, [[1, 512]]), 512, 1, w_extra=["xq7"])
            for oc in range(ICH):
                par = oc % 2
                for (r2, a, e) in wg_segs(oc):
                    off_in = r2 * RK + (a - 352 * r2)
                    ct.op("Y", lambda h, par=par, oi=off_in, do=a - 128 * oc, w_=e - a: h.dma_start(
                        out=wgbuf[par].ap(do, dims=[[128, 8], [1, w_]]),
                        in_=AP(wq_g.tensor, oi, [[352, 128], [128 * 352, 8], [1, w_]])),
                        r=[f"wgb{par}", "wq_g"], w=[f"wgb{par}"])
                    ct.op("Y", lambda h, par=par, oi=off_in + WSEG, do=a - 128 * oc, w_=e - a: h.dma_start(
                        out=wubuf[par].ap(do, dims=[[128, 8], [1, w_]]),
                        in_=AP(wq_g.tensor, oi, [[352, 128], [128 * 352, 8], [1, w_]])),
                        r=[f"wub{par}", "wq_g"], w=[f"wub{par}"])
                for kc in range(8):
                    ct.op("T", lambda h, kc=kc, par=par, st=(kc == 0), sp=(kc == 7):
                          h.matmul(PB[0][:, :], wgbuf[par].ap(kc * 128, [[1, 128]]),
                                   xqS.ap(kc * 512, [[1, 512]]), start=st, stop=sp),
                          r=[f"wgb{par}", f"xq{kc}"], w=["pb0"])
                    ct.op("T", lambda h, kc=kc, par=par, st=(kc == 0), sp=(kc == 7):
                          h.matmul(PB[1][:, :], wubuf[par].ap(kc * 128, [[1, 128]]),
                                   xqS.ap(kc * 512, [[1, 512]]), start=st, stop=sp),
                          r=[f"wub{par}", f"xq{kc}"], w=["pb1"])
                ct.op("S", lambda h: h.activation(silS.ap(), PB[0][:, :], AF.Silu), r=["pb0"], w=["sil"])
                ct.op("V", lambda h, oc=oc: h.tensor_tensor(htS.ap(oc * 512, [[1, 512]]), silS.ap(),
                                                            PB[1][:, :], AL.mult),
                      r=["sil", "pb1"], w=[f"ht{oc}"])
                qstage_a(htS.ap(oc * 512, [[1, 512]]), 512, oc % 2, r_extra=[f"ht{oc}"])
                qstage_s(512, oc % 2)
                if oc >= 1:
                    qstage_b(hqS.ap((oc - 1) * 512, [[1, 512]]), 512, (oc - 1) % 2, w_extra=[f"hq{oc - 1}"])
            qstage_b(hqS.ap((ICH - 1) * 512, [[1, 512]]), 512, (ICH - 1) % 2, w_extra=[f"hq{ICH - 1}"])
            for oc in range(8):
                par = oc % 2
                bank, btag = PB[2 + par], f"pb{2 + par}"
                ct.op("Y", lambda h, oc=oc, par=par: h.dma_start(
                    out=wdbuf[par].ap(), in_=AP(wq_g.tensor, oc * RK + 2 * WSEG, [[128, 128], [128 * 128, 22], [1, 128]])),
                    r=[f"wdb{par}", "wq_g"], w=[f"wdb{par}"])
                for kc in range(ICH):
                    ct.op("T", lambda h, kc=kc, par=par, b=bank, st=(kc == 0), sp=(kc == ICH - 1):
                          h.matmul(b[:, :], wdbuf[par].ap(kc * 128, [[1, 128]]),
                                   hqS.ap(kc * 512, [[1, 512]]), start=st, stop=sp),
                          r=[f"wdb{par}", f"hq{kc}"], w=[btag])
                xs = xt.ap(oc * 512, [[1, 512]])
                ct.op("V", lambda h, b=bank, xs=xs: h.tensor_tensor(xs, xs, b[:, :], AL.add),
                      r=[btag, xtg], w=[xtg])
                ct.op("S", lambda h, xs=xs: h.activation(y2buf.ap(), xs, AF.Square), r=[xtg], w=["y2b"])
                ct.op("V", lambda h, xs=xs: h.tensor_copy(ybf.ap(), xs), r=[xtg], w=["ybf"])
                ct.op("T", lambda h, st=(oc == 0), sp=(oc == 7): h.matmul(
                    AP(PB[6], 0, [[512, 1], [1, 512]]), ones1.ap(0, [[1, 1]]), ybf.ap(), start=st, stop=sp,
                    tile_position=(0, 0)), r=["ybf", "ones1"], w=["pb6"])
                ct.op("T", lambda h, st=(oc == 0), sp=(oc == 7): h.matmul(
                    AP(PB[7], 0, [[512, 1], [1, 512]]), ones1.ap(0, [[1, 1]]), y2buf.ap(), start=st, stop=sp,
                    tile_position=(0, 0)), r=["y2b", "ones1"], w=["pb7"])
                if stop == 12:
                    ct.op("Y", lambda h, xs=xs, oc=oc: h.dma_start(
                        out=AP(y2_d.tensor, oc * 128 * NTOK + col0, [[NTOK, 128], [1, 512]]), in_=xs),
                        r=[xtg], w=["y2_d"])
            ct.op("V", lambda h: h.tensor_copy(lnw0, AP(PB[6], 0, [[512, 1], [1, 512]])),
                  r=["pb6", "strow"], w=["strow"])
            ct.op("V", lambda h: h.tensor_copy(lnw1, AP(PB[7], 0, [[512, 1], [1, 512]])),
                  r=["pb7", "dn"], w=["dn"])
            ln_coef(rms=True)
            ln_hilo_mms(lnw1, lnw0, PB[4], "pb4", PB[5], "pb5")
            ct.op("V", lambda h: h.tensor_tensor(lnw2, lnw2, lnw1, AL.mult), r=["dn"], w=["dn"])
            ct.op("V", lambda h: h.tensor_tensor(lnw2, lnw2, lnw1, AL.mult), r=["dn"], w=["dn"])
            ct.op("S", lambda h: h.activation(lnw2, lnw2, AF.Sqrt, bias=EPS_RMS), r=["dn"], w=["dn"])
            ct.op("V", lambda h: h.reciprocal(lnw2, lnw2), r=["dn"], w=["dn"])
            ct.op("V", lambda h: h.tensor_tensor(lnw1, lnw1, lnw2, AL.mult), r=["dn"], w=["dn"])
            ct.op("V", lambda h: h.tensor_tensor(lnw0, lnw0, lnw2, AL.mult), r=["strow", "dn"], w=["strow"])
            ln_hilo_mms(lnw1, lnw0, PB[6], "pb6", PB[7], "pb7")
            x2dst = out_d if stop == 2 else x2_d
            for oc in range(8):
                ys = xt.ap(oc * 512, [[1, 512]])
                ct.op("V", lambda h, ys=ys: h.tensor_tensor(qt_x.ap(0, [[1, 512]]), ys, PB[6][:, :], AL.mult),
                      r=[xtg, "pb6", "qtmp0"], w=["qtmp0"])
                ct.op("V", lambda h: h.tensor_tensor(qt_x.ap(0, [[1, 512]]), qt_x.ap(0, [[1, 512]]),
                                                     PB[7][:, :], AL.subtract),
                      r=["qtmp0", "pb7"], w=["qtmp0"])
                ct.op("V", lambda h, oc=oc: h.tensor_scalar(
                    xnb.ap(), qt_x.ap(0, [[1, 512]]), normwb.ap(oc, [[1, 1]]), None, AL.mult),
                    r=["qtmp0", "normwb"], w=["xnb"])
                ct.op("Y", lambda h, oc=oc, c0=col0: h.dma_start(
                    out=AP(xn_d.tensor, oc * 128 * NTOK + c0, [[NTOK, 128], [1, 512]]), in_=xnb.ap()),
                    r=["xnb"], w=["xn_d"])
                ct.op("V", lambda h, ys=ys: h.tensor_tensor(ys, ys, PB[4][:, :], AL.mult),
                      r=[xtg, "pb4"], w=[xtg])
                ct.op("V", lambda h, ys=ys: h.tensor_tensor(ys, ys, PB[5][:, :], AL.subtract),
                      r=[xtg, "pb5"], w=[xtg])
                ct.op("Y", lambda h, ys=ys, oc=oc: h.dma_start(
                    out=AP(x2dst.tensor, oc * 128 * NTOK + col0, [[NTOK, 128], [1, 512]]), in_=ys),
                    r=[xtg], w=["x2_d"])

        for jj in range(NT):
            p2_tile(jj)
        ct.barrier()

        if stop == 12:
            ct.op("Y", lambda h: h.dma_start(out=out_d, in_=y2_d), r=["y2_d"], w=["x_out"])
            ct.barrier()
            ct.emit(block, sems)
            return nc

        if stop == 2:
            ct.emit(block, sems)
            return nc

        def p3_scan():
            ct.op("G", lambda h: h.collective_compute(
                "AllGather", AL.bypass, replica_groups=[list(range(NCORES))],
                ins=[xn_d.opt()], outs=[xn_g.opt()]), r=["xn_d"], w=["xn_g"])
            def extract_op(h):
                cid = nc.partition_id([mybir.EngineType.Pool])
                src = AP(xn_g.tensor, cid * (128 * NTOK),
                         [[NTOK, 128], [C * NTOK, 8], [T, 2], [1, T]])
                return h.dma_start(out=xsc.ap(0, [[2 * T, 8], [T, 2], [1, T]]), in_=src)
            ct.op("GD", extract_op, r=["xn_g", "cid"], w=["xsc"])

            ct.op("V", lambda h: h.memset(stS.ap(1024, [[1, 16]]), 0.0), w=["st_i"])
            NW = T // 32
            for w in range(NW):
                bkA, tgA = PB[w % 2], f"pb{w % 2}"
                bkB, tgB = PB[4 + w % 2], f"pb{4 + w % 2}"
                ct.op("T", lambda h, w=w, b=bkA: h.matmul(
                    b[:, 0:256], bbd.ap(), xsc.ap(w * 32, [[1, 32], [T, 8]]), start=True, stop=False),
                    r=["xsc", "bbd"], w=[tgA])
                ct.op("T", lambda h, w=w, b=bkB: h.matmul(
                    b[:, 0:256], bbd.ap(), xsc.ap(w * 32 + 8 * T, [[1, 32], [T, 8]]), start=True, stop=False),
                    r=["xsc", "bbd"], w=[tgB])
                for t_in in range(32):
                    t = w * 32 + t_in
                    slot = t % 64
                    for grp, bk, tg, off in ((0, bkA, tgA, 0), (1, bkB, tgB, 8)):
                        if t == 0:
                            prev = stS.ap(1024 + off, [[1, 8]])
                            ptag = "st_i"
                        else:
                            prev = stS.ap(((t - 1) % 64) * 16 + off, [[1, 8]])
                            ptag = f"st{grp}{((t - 1) % 64) // 32}"
                        ct.op("T", lambda h, b=bk, ti=t_in, pv=prev: h.matmul(
                            AP(b, ti * 8, [[512, 128], [1, 8]]), abd.ap(), pv, start=False, stop=True),
                            r=[ptag, "abd"], w=[tg])
                        ct.op("S", lambda h, b=bk, ti=t_in, sl=slot, off=off: h.activation(
                            stS.ap(sl * 16 + off, [[1, 8]]), AP(b, ti * 8, [[512, 128], [1, 8]]), AF.Tanh),
                            r=[tg], w=[f"st{grp}{slot // 32}"])
                s0 = (w * 32) % 64
                ct.op("T", lambda h, s0=s0: h.matmul(
                    PB[2][:, :], cbd.ap(), stS.ap(s0 * 16, [[1, 512]]), start=True, stop=True),
                    r=[f"st0{s0 // 32}", f"st1{s0 // 32}", "cbd"], w=["pb2"])
                acc = mxacc[(w // 8) % 2]
                atag = f"mxa{(w // 8) % 2}"
                ct.op("V", lambda h, w=w, acc=acc: h.scalar_tensor_tensor(
                    acc.ap((w % 8) * 32, [[1, 32], [256, 16]]),
                    xsc.ap(w * 32, [[1, 32], [T, 16]]), skipb.ap(), PB[2][:, :], AL.mult, AL.add),
                    r=["pb2", "xsc", "skipb", atag], w=[atag])
                if (w + 1) % 8 == 0:
                    wf = w - 7
                    ct.op("Y", lambda h, wf=wf, acc=acc: h.dma_start(
                        out=AP(mx_d.tensor, wf * 32, [[H * T, 128], [T, 16], [1, 256]]),
                        in_=acc.ap(0, [[256, 16], [1, 256]])), r=[atag], w=["mx_d"])
            ct.op("G", lambda h: h.collective_compute(
                "AllGather", AL.bypass, replica_groups=[list(range(NCORES))],
                ins=[mx_d.opt()], outs=[mx_g.opt()]), r=["mx_d"], w=["mx_g"])

        if stop == 14:
            ct.op("GD", lambda h: h.dma_start(out=out_d, in_=xn_d), r=["xn_d"], w=["x_out"])
            ct.barrier()
            ct.emit(block, sems)
            return nc

        p3_scan()
        ct.barrier()

        if stop == 15:
            ct.op("GD", lambda h: h.dma_start(
                out=AP(out_d.tensor, 0, [[NTOK, 128], [1, NTOK]]),
                in_=AP(mx_d.tensor, 0, [[H * T, 128], [1, NTOK]])), r=["mx_d"], w=["x_out"])
            ct.barrier()
            ct.emit(block, sems)
            return nc

        def p3b():
            def extract2_op(h):
                cid = nc.partition_id([mybir.EngineType.Pool])
                src = AP(mx_g.tensor, cid * (2 * T),
                         [[H * T, 128], [128 * H * T, 8], [1, 2 * T]])
                return h.dma_start(out=mxfm.ap(0, [[NTOK, 8], [1, 2 * T]]), in_=src)
            ct.op("GD", extract2_op, r=["mx_g", "cid"], w=["mxfm"])
            ct.op("GD", lambda h: h.dma_start(
                out=woutS.ap(), in_=AP(wout_T.tensor, 0, [[C, 128], [128 * C, 8], [128, 8], [1, 128]])),
                w=["woutw"])
            for jj in range(NT):
                col0 = jj * TILE
                xt2 = x2tp[jj % 2]
                xtg2 = f"x2t{jj % 2}"
                ct.op("Y", lambda h, c0=col0, xt2=xt2: h.dma_start(
                    out=xt2.ap(), in_=AP(x2_d.tensor, c0, [[NTOK, 128], [128 * NTOK, 8], [1, 512]])),
                    r=[xtg2], w=[xtg2])
                for oc in range(8):
                    bank, btag = PB[2 + oc % 2], f"pb{2 + oc % 2}"
                    for kc in range(8):
                        ct.op("T", lambda h, kc=kc, oc=oc, b=bank, c0=col0, st=(kc == 0), sp=(kc == 7):
                              h.matmul(b[:, :], woutS.ap((kc * 8 + oc) * 128, [[1, 128]]),
                                       mxfm.ap(kc * NTOK + c0, [[1, 512]]), start=st, stop=sp),
                              r=["mxfm", "woutw"], w=[btag])
                    xs = xt2.ap(oc * 512, [[1, 512]])
                    ct.op("V", lambda h, b=bank, xs=xs, oc=oc: h.scalar_tensor_tensor(
                        qt_x.ap(0, [[1, 512]]), b[:, :], boutb.ap(oc, [[1, 1]]), xs, AL.add, AL.add),
                        r=[btag, xtg2, "boutb", "qtmp"], w=["qtmp"])
                    ct.op("V", lambda h, xs=xs: h.tensor_tensor(xs, xs, qt_x.ap(0, [[1, 512]]), AL.add),
                        r=[xtg2, "qtmp"], w=[xtg2])
                    ct.op("S", lambda h, xs=xs: h.activation(y2buf.ap(), xs, AF.Square), r=[xtg2], w=["y2b"])
                    ct.op("V", lambda h, xs=xs: h.tensor_copy(ybf.ap(), xs), r=[xtg2], w=["ybf"])
                    ct.op("T", lambda h, st=(oc == 0), sp=(oc == 7): h.matmul(
                        AP(PB[6], 0, [[512, 1], [1, 512]]), ones1.ap(0, [[1, 1]]), ybf.ap(), start=st, stop=sp,
                        tile_position=(0, 0)), r=["ybf", "ones1"], w=["pb6"])
                    ct.op("T", lambda h, st=(oc == 0), sp=(oc == 7): h.matmul(
                        AP(PB[7], 0, [[512, 1], [1, 512]]), ones1.ap(0, [[1, 1]]), y2buf.ap(), start=st, stop=sp,
                        tile_position=(0, 0)), r=["y2b", "ones1"], w=["pb7"])
                    if stop == 13:
                        ct.op("Y", lambda h, xs=xs, oc=oc, c0=col0: h.dma_start(
                            out=AP(y3_d.tensor, oc * 128 * NTOK + c0, [[NTOK, 128], [1, 512]]), in_=xs),
                            r=[xtg2], w=["y3_d"])
                ct.op("V", lambda h: h.tensor_copy(lnw0, AP(PB[6], 0, [[512, 1], [1, 512]])),
                      r=["pb6", "strow"], w=["strow"])
                ct.op("V", lambda h: h.tensor_copy(lnw1, AP(PB[7], 0, [[512, 1], [1, 512]])),
                      r=["pb7", "dn"], w=["dn"])
                ln_coef(rms=False)
                ln_hilo_mms(lnw1, lnw0, PB[6], "pb6", PB[7], "pb7")
                x3dst = out_d if stop == 3 else x3_d
                for oc in range(8):
                    xs = xt2.ap(oc * 512, [[1, 512]])
                    ct.op("V", lambda h, xs=xs: h.tensor_tensor(xs, xs, PB[6][:, :], AL.mult),
                          r=[xtg2, "pb6"], w=[xtg2])
                    ct.op("V", lambda h, xs=xs: h.tensor_tensor(xs, xs, PB[7][:, :], AL.subtract),
                          r=[xtg2, "pb7"], w=[xtg2])
                    ct.op("Y", lambda h, xs=xs, oc=oc, c0=col0: h.dma_start(
                        out=AP(x3dst.tensor, oc * 128 * NTOK + c0, [[NTOK, 128], [1, 512]]), in_=xs),
                        r=[xtg2], w=["x3_d"])

        p3b()
        ct.barrier()
        if stop == 13:
            ct.op("Y", lambda h: h.dma_start(out=out_d, in_=y3_d), r=["y3_d"], w=["x_out"])
            ct.barrier()
            ct.emit(block, sems)
            return nc
        if stop == 3:
            ct.emit(block, sems)
            return nc

        for wdst, wsrc, nm in ((wrS, wr_T, "wrw"), (wkS, wk_T, "wkw"), (wvS, wv_T, "wvw")):
            ct.op("GD", lambda h, d=wdst, s=wsrc: h.dma_start(
                out=d.ap(), in_=AP(s.tensor, 0, [[C, 128], [128 * C, 8], [128, 8], [1, 128]])), w=[nm])

        def p4_tile(jj):
            col0 = jj * TILE
            ct.op("Y", lambda h: h.dma_start(
                out=x3t.ap(), in_=AP(x3_d.tensor, col0, [[NTOK, 128], [128 * NTOK, 8], [1, 512]])),
                r=["x3t"], w=["x3t"])
            if jj % TPS == 0:
                ct.op("Y", lambda h: h.dma_start(
                    out=xpt.ap(1, [[512, 8], [1, 511]]),
                    in_=AP(x3_d.tensor, col0, [[NTOK, 128], [128 * NTOK, 8], [1, 511]])),
                    r=["xpt"], w=["xpt"])
                ct.op("V", lambda h: h.memset(xpt.ap(0, [[512, 8], [1, 1]]), 0.0), r=["xpt"], w=["xpt"])
            else:
                ct.op("Y", lambda h: h.dma_start(
                    out=xpt.ap(), in_=AP(x3_d.tensor, col0 - 1, [[NTOK, 128], [128 * NTOK, 8], [1, 512]])),
                    r=["xpt"], w=["xpt"])
            for oc in range(8):
                ct.op("V", lambda h, oc=oc: h.tensor_scalar(p4tmp.ap(0, [[1, 512]]), xpt.ap(oc * 512, [[1, 512]]),
                                                            omrb.ap(oc, [[1, 1]]), None, AL.mult),
                      r=["xpt", "omrb"], w=["p4t"])
                ct.op("V", lambda h, oc=oc: h.scalar_tensor_tensor(xrS.ap(oc * 512, [[1, 512]]),
                                                                   x3t.ap(oc * 512, [[1, 512]]), murb.ap(oc, [[1, 1]]),
                                                                   p4tmp.ap(0, [[1, 512]]), AL.mult, AL.add),
                      r=["x3t", "murb", "p4t"], w=["xr"])
                ct.op("V", lambda h, oc=oc: h.tensor_scalar(p4tmp.ap(0, [[1, 512]]), xpt.ap(oc * 512, [[1, 512]]),
                                                            omkb.ap(oc, [[1, 1]]), None, AL.mult),
                      r=["xpt", "omkb"], w=["p4t"])
                ct.op("V", lambda h, oc=oc: h.scalar_tensor_tensor(xkS.ap(oc * 512, [[1, 512]]),
                                                                   x3t.ap(oc * 512, [[1, 512]]), mukb.ap(oc, [[1, 1]]),
                                                                   p4tmp.ap(0, [[1, 512]]), AL.mult, AL.add),
                      r=["x3t", "mukb", "p4t"], w=["xk"])
            for oc in range(8):
                for kc in range(8):
                    ct.op("T", lambda h, kc=kc, oc=oc, st=(kc == 0), sp=(kc == 7):
                          h.matmul(PB[0][:, :], wrS.ap((kc * 8 + oc) * 128, [[1, 128]]),
                                   xrS.ap(kc * 512, [[1, 512]]), start=st, stop=sp),
                          r=["xr", "wrw"], w=["pb0"])
                ct.op("S", lambda h, oc=oc: h.activation(rbS.ap(oc * 512, [[1, 512]]), PB[0][:, :], AF.Sigmoid),
                      r=["pb0"], w=["rb"])
                for kc in range(8):
                    ct.op("T", lambda h, kc=kc, oc=oc, st=(kc == 0), sp=(kc == 7):
                          h.matmul(PB[1][:, :], wkS.ap((kc * 8 + oc) * 128, [[1, 128]]),
                                   xkS.ap(kc * 512, [[1, 512]]), start=st, stop=sp),
                          r=["xk", "wkw"], w=["pb1"])
                ct.op("S", lambda h, oc=oc: h.activation(y2buf.ap(), PB[1][:, :], AF.Relu),
                      r=["pb1"], w=["y2b"])
                ct.op("V", lambda h, oc=oc: h.tensor_tensor(kaS.ap(oc * 512, [[1, 512]]), y2buf.ap(), y2buf.ap(), AL.mult),
                      r=["y2b"], w=["ka"])
            for oc in range(8):
                bank, btag = PB[2 + oc % 2], f"pb{2 + oc % 2}"
                for kc in range(8):
                    ct.op("T", lambda h, kc=kc, oc=oc, b=bank, st=(kc == 0), sp=(kc == 7):
                          h.matmul(b[:, :], wvS.ap((kc * 8 + oc) * 128, [[1, 128]]),
                                   kaS.ap(kc * 512, [[1, 512]]), start=st, stop=sp),
                          r=["ka", "wvw"], w=[btag])
                xs = x3t.ap(oc * 512, [[1, 512]])
                ct.op("V", lambda h, b=bank, oc=oc: h.scalar_tensor_tensor(
                    p4tmp.ap(0, [[1, 512]]), b[:, :], 1.0, rbS.ap(oc * 512, [[1, 512]]), AL.mult, AL.mult),
                    r=[btag, "rb", "p4t"], w=["p4t"])
                ct.op("V", lambda h, xs=xs: h.tensor_tensor(xs, xs, p4tmp.ap(0, [[1, 512]]), AL.add),
                      r=["x3t", "p4t"], w=["x3t"])
                ct.op("S", lambda h, xs=xs: h.activation(y2buf.ap(), xs, AF.Square), r=["x3t"], w=["y2b"])
                ct.op("V", lambda h, xs=xs: h.tensor_copy(ybf.ap(), xs), r=["x3t"], w=["ybf"])
                ct.op("T", lambda h, st=(oc == 0), sp=(oc == 7): h.matmul(
                    AP(PB[6], 0, [[512, 1], [1, 512]]), ones1.ap(0, [[1, 1]]), ybf.ap(), start=st, stop=sp,
                    tile_position=(0, 0)), r=["ybf", "ones1"], w=["pb6"])
                ct.op("T", lambda h, st=(oc == 0), sp=(oc == 7): h.matmul(
                    AP(PB[7], 0, [[512, 1], [1, 512]]), ones1.ap(0, [[1, 1]]), y2buf.ap(), start=st, stop=sp,
                    tile_position=(0, 0)), r=["y2b", "ones1"], w=["pb7"])
            ct.op("V", lambda h: h.tensor_copy(lnw0, AP(PB[6], 0, [[512, 1], [1, 512]])),
                  r=["pb6", "strow"], w=["strow"])
            ct.op("V", lambda h: h.tensor_copy(lnw1, AP(PB[7], 0, [[512, 1], [1, 512]])),
                  r=["pb7", "dn"], w=["dn"])
            ln_coef(rms=False)
            ln_hilo_mms(lnw1, lnw0, PB[6], "pb6", PB[7], "pb7")
            for oc in range(8):
                xs = x3t.ap(oc * 512, [[1, 512]])
                ct.op("V", lambda h, xs=xs: h.tensor_tensor(xs, xs, PB[6][:, :], AL.mult),
                      r=["x3t", "pb6"], w=["x3t"])
                ct.op("V", lambda h, xs=xs: h.tensor_tensor(xs, xs, PB[7][:, :], AL.subtract),
                      r=["x3t", "pb7"], w=["x3t"])
                ct.op("Y", lambda h, xs=xs, oc=oc: h.dma_start(
                    out=AP(out_d.tensor, oc * 128 * NTOK + col0, [[NTOK, 128], [1, 512]]), in_=xs),
                    r=["x3t"], w=["x_out"])

        for jj in range(NT):
            p4_tile(jj)
        ct.barrier()

        ct.emit(block, sems)

    return nc


_NC_CACHE = {}


def _get_nc(stop):
    if stop not in _NC_CACHE:
        _NC_CACHE[stop] = build_nc(stop)
    return _NC_CACHE[stop]


def _rope_tables():
    inv_freq = (1.0 / 10000.0 ** (np.arange(0, LAT, 2, dtype=np.float32) / LAT)).astype(np.float32)
    t = np.arange(T, dtype=np.float32)
    freqs = np.outer(t, inv_freq)
    cos = np.repeat(np.cos(freqs), 2, axis=-1)[:, :LAT].astype(np.float32)
    sin = np.repeat(np.sin(freqs), 2, axis=-1)[:, :LAT].astype(np.float32)
    cos4 = np.tile(np.ascontiguousarray(cos.T), (4, 1))
    sin4 = np.tile(np.ascontiguousarray(sin.T), (4, 1))
    return np.ascontiguousarray(cos4), np.ascontiguousarray(sin4)


def kernel(stop=4, **inputs):
    x = inputs["L0_attn_x"]
    cos4, sin4 = _rope_tables()
    shared = {
        "cattn_wT": np.ascontiguousarray(inputs["L0_attn_c_attn_w"].T),
        "cattn_b": np.ascontiguousarray(inputs["L0_attn_c_attn_b"]),
        "q2l_wT": np.ascontiguousarray(inputs["L0_attn_q2l_w"].T),
        "q2l_b": np.ascontiguousarray(inputs["L0_attn_q2l_b"]),
        "kv2l_wT": np.ascontiguousarray(inputs["L0_attn_kv2l_w"].T),
        "kv2l_b": np.ascontiguousarray(inputs["L0_attn_kv2l_b"]),
        "l2o_wT": np.ascontiguousarray(inputs["L0_attn_l2o_w"].T),
        "l2o_b": np.ascontiguousarray(inputs["L0_attn_l2o_b"]),
        "wqk_wT": np.ascontiguousarray(inputs["L0_attn_wqk_w"].T),
        "wqk_b": np.ascontiguousarray(inputs["L0_attn_wqk_b"]),
        "cproj_wT": np.ascontiguousarray(inputs["L0_attn_cproj_w"].T),
        "cproj_b": np.ascontiguousarray(inputs["L0_attn_cproj_b"]),
        "wg_T": np.ascontiguousarray(inputs["L0_ffn_w_gate"].T),
        "wu_T": np.ascontiguousarray(inputs["L0_ffn_w_up"].T),
        "wd_T": np.ascontiguousarray(inputs["L0_ffn_w_down"].T),
        "norm_w": np.ascontiguousarray(inputs["L1_attn_norm_w"]),
        "wout_T": np.ascontiguousarray(inputs["L1_attn_w_out"].T),
        "bout_v": np.ascontiguousarray(inputs["L1_attn_b_out"]),
        "wr_T": np.ascontiguousarray(inputs["L1_ffn_W_r"].T),
        "wk_T": np.ascontiguousarray(inputs["L1_ffn_W_k"].T),
        "wv_T": np.ascontiguousarray(inputs["L1_ffn_W_v"].T),
        "mu_r": np.ascontiguousarray(inputs["L1_ffn_mu_r"]),
        "mu_k": np.ascontiguousarray(inputs["L1_ffn_mu_k"]),
        "cos4": cos4,
        "sin4": sin4,
        "altsgn": np.tile(np.array([-1.0, 1.0], dtype=np.float32), 64),
    }
    a_full = inputs["L1_attn_a"]
    b_full = inputs["L1_attn_b"]
    c_full = inputs["L1_attn_c"]
    skip_full = inputs["L1_attn_skip"]
    in_maps = []
    for c in range(NCORES):
        m = dict(shared)
        xc = x[2 * c:2 * c + 2]
        m["x_in"] = np.ascontiguousarray(xc.transpose(2, 0, 1).reshape(C, NTOK))
        m["a_sh"] = np.ascontiguousarray(a_full[2 * c:2 * c + 2])
        m["b_sh"] = np.ascontiguousarray(b_full[2 * c:2 * c + 2])
        m["c_sh"] = np.ascontiguousarray(c_full[2 * c:2 * c + 2])
        m["skip_sh"] = np.ascontiguousarray(skip_full[2 * c:2 * c + 2].reshape(128))
        m["core_id_in"] = np.array([[c]], dtype=np.int32)
        in_maps.append(m)
    nc = _get_nc(stop)
    res = run_bass_kernel_spmd(nc, in_maps, core_ids=list(range(NCORES)))
    global LAST_RES
    LAST_RES = res
    outs = []
    for c in range(NCORES):
        o = res.results[c]["out_d"]
        outs.append(o.reshape(C, SEQ, T).transpose(1, 2, 0))
    return np.ascontiguousarray(np.concatenate(outs, axis=0))


if __name__ == "__main__":
    pass


# version 13 (final)
# speedup vs baseline: 1.0428x; 1.0382x over previous
LAST_RES = None
"""Single-launch raw-Bass megakernel for mk_seq_postln_d2 on 8 TRN2 NeuronCores.

Module (depth 2, post-LN): x -> LN(x + MLA(x)) -> LN(. + fp4FFN(.)) ->
LN(. + tanh-scan mixer(.)) -> LN(. + sq-relu FFN(.)).   B=16, T=2048, C=1024.

Per core: 2 sequences (batch shard).  Feature-major stream [C -> 8x128 partitions,
4096 tokens free].  Stream spills to HBM between phases (compute-bound module, HBM
has ~35x headroom).  bf16 matmul operands / fp32 PSUM accumulation and elementwise.
LayerNorms run in dedicated LNFIX phases (isolates the sqrt activation table).
The L1 tanh-scan is head-resharded across cores via AllGather (2 heads x 16 seqs per
core) so each scan step is one 128-contraction matmul + one tanh.
fp4 quant-dequant: exact fp32 rounding ladder in DVE 32x32-transposed space.
"""
import numpy as np
import ml_dtypes
import concourse.bass as bass
import concourse.mybir as mybir
from concourse.bass_utils import run_bass_kernel_spmd

F32 = mybir.dt.float32
BF16 = mybir.dt.bfloat16
I32 = mybir.dt.int32
AL = mybir.AluOpType
AF = mybir.ActivationFunctionType
AX = mybir.AxisListType.X
AP = bass.AP

NCORES = 8
SEQ, T, C, NCH = 2, 2048, 1024, 8
H, HD, LAT = 16, 64, 32
IFF, ICH = 2816, 22
NTOK = SEQ * T
TILE = 512
NT = NTOK // TILE
TPS = T // TILE
KTN = T // 128
EPS_LN, EPS_RMS = 1e-5, 1e-4
C_HALF = float(1.5 * 2 ** 22)
C_INT = float(1.5 * 2 ** 23)
ISQL = float(LAT) ** -0.5


class Ctx:
    NY, NGD = 12, 4
    ENGS = ("T", "V", "S", "G") + tuple(f"Y{i}" for i in range(12)) + tuple(f"GD{i}" for i in range(4))

    def __init__(self, nc):
        self.nc = nc
        self.count = {e: 0 for e in self.ENGS}
        self.ops = {e: [] for e in ("T", "V", "S", "G", "Y")}
        self.last_w = {}
        self.readers = {}
        self.rr = {"Y": 0, "GD": 0}

    def op(self, eng, fn, r=(), w=()):
        q = eng in ("Y", "GD")
        if q:
            stream = "Y" if eng == "Y" else "G"
            sem = f"{eng}{self.rr[eng]}"
            self.rr[eng] = (self.rr[eng] + 1) % (self.NY if eng == "Y" else self.NGD)
        else:
            stream = eng
            sem = eng
        need = {}
        for tag in r:
            lw = self.last_w.get(tag)
            if lw:
                need[lw[0]] = max(need.get(lw[0], 0), lw[1])
        for tag in w:
            lw = self.last_w.get(tag)
            if lw and (q or lw[0] != sem):
                need[lw[0]] = max(need.get(lw[0], 0), lw[1])
            for (e2, v) in self.readers.get(tag, ()):
                if q or e2 != sem:
                    need[e2] = max(need.get(e2, 0), v)
        if q and self.count[sem] > 0:
            need[sem] = max(need.get(sem, 0), self.count[sem])
        inc = 16 if q else 1
        self.count[sem] += inc
        val = self.count[sem]
        self.ops[stream].append((need, sem, fn, val))
        for tag in w:
            self.last_w[tag] = (sem, val)
            self.readers[tag] = []
        for tag in r:
            self.readers.setdefault(tag, []).append((sem, val))

    def barrier(self):
        snap = dict(self.count)
        for stream in self.ops:
            self.ops[stream].append(("B", snap))
        self.last_w = {}
        self.readers = {}

    def emit(self, block, sems):
        def body_of(stream):
            def body(h):
                wm = {e: 0 for e in self.ENGS}
                for item in self.ops[stream]:
                    if item[0] == "B":
                        for e, v in item[1].items():
                            if v > wm[e]:
                                h.wait_ge(sems[e], v)
                                wm[e] = v
                        continue
                    need, eng, fn, val = item
                    for e, v in need.items():
                        if v > wm[e]:
                            h.wait_ge(sems[e], v)
                            wm[e] = v
                    inst = fn(h)
                    if inst is not None:
                        inst.then_inc(sems[eng], 16 if eng not in ("T", "V", "S", "G") else 1)
            return body
        block.sync(body_of("Y"))
        block.tensor(body_of("T"))
        block.vector(body_of("V"))
        block.scalar(body_of("S"))
        block.gpsimd(body_of("G"))


class Buf:
    def __init__(self, slab, slab_w, off, n, dt, slab_dt=BF16):
        self.slab, self.sw, self.off, self.dt = slab, slab_w, off, dt
        self.native = (dt == slab_dt) or (mybir.dt.size(dt) == mybir.dt.size(slab_dt))
        self.esz = 1 if self.native else 2
        self.n = n // self.esz

    def _mk(self, par_pair, o, dims):
        if dims is None:
            dims = [[1, self.n]]
        if self.native:
            a = AP(self.slab, self.off + par_pair[2] + o, [list(par_pair[:2])] + [list(d) for d in dims])
            if a.dtype != self.dt:
                a = a.bitcast(self.dt)
            return a
        nd = [[s * 2, c] for s, c in dims[:-1]]
        s_in, c_in = dims[-1]
        assert s_in == 1, f"fp32-on-bf16 innermost must be contiguous, got step {s_in}"
        nd.append([1, 2 * c_in])
        a = AP(self.slab, self.off + par_pair[2] + o * 2, [list(par_pair[:2])] + nd)
        return a.bitcast(F32)

    def ap(self, o=0, dims=None):
        return self._mk((self.sw, 128, 0), o, dims)

    def app(self, p0, np_, o=0, dims=None):
        return self._mk((self.sw, np_, p0 * self.sw), o, dims)


def build_nc(stop=4):
    nc = bass.Bass("TRN2", target_bir_lowering=False, debug=False)

    def din(name, shape, dt=F32):
        return nc.dram_tensor(name, shape, dt, kind="ExternalInput").ap()

    x_in = din("x_in", [C, NTOK])
    cattn_wT = din("cattn_wT", [C, 3 * C])
    cattn_b = din("cattn_b", [3 * C])
    q2l_wT = din("q2l_wT", [HD, LAT])
    q2l_b = din("q2l_b", [LAT])
    kv2l_wT = din("kv2l_wT", [HD, LAT])
    kv2l_b = din("kv2l_b", [LAT])
    l2o_wT = din("l2o_wT", [LAT, HD])
    l2o_b = din("l2o_b", [HD])
    wqk_wT = din("wqk_wT", [LAT, LAT])
    wqk_b = din("wqk_b", [LAT])
    cproj_wT = din("cproj_wT", [C, C])
    cproj_b = din("cproj_b", [C])
    wg_T = din("wg_T", [C, IFF])
    wu_T = din("wu_T", [C, IFF])
    wd_T = din("wd_T", [IFF, C])
    norm_w = din("norm_w", [C])
    a_sh = din("a_sh", [2, 64, 64])
    b_sh = din("b_sh", [2, 64, 64])
    c_sh = din("c_sh", [2, 64, 64])
    skip_sh = din("skip_sh", [128])
    wout_T = din("wout_T", [C, C])
    bout_v = din("bout_v", [C])
    wr_T = din("wr_T", [C, C])
    wk_T = din("wk_T", [C, C])
    wv_T = din("wv_T", [C, C])
    mu_r = din("mu_r", [C])
    mu_k = din("mu_k", [C])
    altsgn = din("altsgn", [128])
    cos4 = din("cos4", [128, T])
    sin4 = din("sin4", [128, T])
    core_id_in = din("core_id_in", [1, 1], I32)

    out_d = nc.dram_tensor("out_d", [C, NTOK], F32, kind="ExternalOutput").ap()


    def dsc(name, shape, dt=F32, **kw):
        return nc.dram_tensor(name, shape, dt, kind="Internal", **kw).ap()

    y1_d = dsc("y1_d", [C, NTOK])
    x1_d = dsc("x1_d", [C, NTOK])
    y2_d = dsc("y2_d", [C, NTOK])
    x2_d = dsc("x2_d", [C, NTOK])
    y3_d = dsc("y3_d", [C, NTOK])
    x3_d = dsc("x3_d", [C, NTOK])
    y4_d = dsc("y4_d", [C, NTOK])
    s_d = [dsc(f"s{k}_d", [2, NTOK]) for k in range(4)]
    cf_d = [dsc(f"cf{k}_d", [4, NTOK], BF16) for k in range(4)]
    cfx_d = dsc("cfx_d", [4, NTOK], BF16)
    WSEG = 1024 * 352
    WDSEG = 2816 * 128
    wgs_d = dsc("wgs_d", [C, 352])
    wus_d = dsc("wus_d", [C, 352])
    wds_d = dsc("wds_d", [IFF, 128])
    wqin_d = dsc("wqin_d", [2 * WSEG + WDSEG], BF16)
    wq_g = dsc("wq_g", [NCORES * (2 * WSEG + WDSEG)], BF16)
    wgq_d = dsc("wgq_d", [C, IFF], BF16)
    wuq_d = dsc("wuq_d", [C, IFF], BF16)
    wdq_d = dsc("wdq_d", [IFF, C], BF16)
    xn_d = dsc("xn_d", [C, NTOK], BF16)
    xn_g = dsc("xn_g", [NCORES * C, NTOK], BF16)
    mx_d = dsc("mx_d", [128, H * T], BF16)
    mx_g = dsc("mx_g", [NCORES * 128, H * T], BF16)

    ct = Ctx(nc)
    sems = {e: nc.semaphore(f"sem{e}").__enter__() for e in Ctx.ENGS}
    PF = 2300
    PW = 8000
    WB = 91648

    with (
        nc.sbuf_tensor("persf", [128, PF], F32) as persf_t,
        nc.sbuf_tensor("pers", [128, PW], BF16) as pers_t,
        nc.sbuf_tensor("arena", [128, WB], BF16) as arena_t,
        nc.psum_tensor("pb0", [128, 512], F32) as _p0,
        nc.psum_tensor("pb1", [128, 512], F32) as _p1,
        nc.psum_tensor("pb2", [128, 512], F32) as _p2,
        nc.psum_tensor("pb3", [128, 512], F32) as _p3,
        nc.psum_tensor("pb4", [128, 512], F32) as _p4,
        nc.psum_tensor("pb5", [128, 512], F32) as _p5,
        nc.psum_tensor("pb6", [128, 512], F32) as _p6,
        nc.psum_tensor("pb7", [128, 512], F32) as _p7,
        nc.Block() as block,
    ):
        pass_sems = sems
        PB = [_p0, _p1, _p2, _p3, _p4, _p5, _p6, _p7]

        def PF32(off, n, dt=F32):
            return Buf(persf_t, PF, off, n, dt, slab_dt=F32)

        def P(off, n, dt=BF16):
            return Buf(pers_t, PW, off, n, dt)

        def A(off, n, dt=BF16):
            return Buf(arena_t, WB, off, n, dt)

        cattnb = PF32(0, 24)
        cprojb = PF32(24, 8)
        boutb = PF32(32, 8)
        normwb = PF32(40, 8)
        murb = PF32(48, 8)
        mukb = PF32(56, 8)
        omrb = PF32(64, 8)
        omkb = PF32(72, 8)
        q2lb4 = PF32(80, 1)
        q2lbr4 = PF32(81, 1)
        kvb4 = PF32(82, 1)
        kvbr4 = PF32(83, 1)
        wqkb4 = PF32(84, 1)
        l2ob2 = PF32(85, 1)
        skipb = PF32(86, 1)
        cidb = PF32(87, 1, I32)
        kvrow = PF32(88, 32)
        kvbias = PF32(120, 128)
        abdf = PF32(248, 128)
        dn = PF32(376, 1024)
        strow = PF32(1400, 512)
        statw = PF32(1912, 160)
        qt_am = PF32(2072, 32)
        qt_inv = PF32(2104, 32)
        qt_s6 = PF32(2136, 32)
        altsg = PF32(2170, 1)
        constc = PF32(2168, 2)
        qtB_am = PF32(2172, 32)
        qtB_inv = PF32(2204, 32)
        qtB_s6 = PF32(2236, 32)

        nc.const_aps.register(F32, EPS_LN, constc.ap(0, [[1, 1]]))
        nc.const_aps.register(F32, EPS_RMS, constc.ap(1, [[1, 1]]))
        q2lw = P(0, 32)
        q2lrw = P(32, 32)
        kv2lw = P(64, 32)
        kv2lrw = P(96, 32)
        wqkw = P(128, 32)
        l2ow = P(160, 64)
        cosb = P(224, T)
        sinb = P(2272, T)
        abd = P(4320, 128)
        bbd = P(4448, 128)
        cbd = P(4576, 128)
        ones1 = P(4704, 2)
        onesr = P(4706, 128)
        dnh = P(5858, 1024)
        dnl = P(6882, 1024)
        y2buf = P(4834, 512)
        ybf = P(5346, 512)

        cattn = A(0, 24576)
        qT = A(24576, 16384)
        kT = A(40960, 16384)
        vSB = A(57344, 17888)
        qkvS = A(75232, 12288)
        xbS = A(87520, 4096)
        rt1 = A(87520, 512)
        rt2 = A(88544, 512)

        def voff(s2, hh2, kt2):
            return (s2 * H + hh2) * 559 + (0 if kt2 == 0 else 64 + (kt2 - 1) * 33)
        qt_x = A(75232, 1024, F32)
        qt_t = A(76256, 1024, F32)
        qt_xn = A(77280, 1024, F32)
        qt_u = A(78304, 1024, F32)
        qt_r2 = A(79328, 1024, F32)
        qtB_t = A(56832, 1024, F32)
        qtB_xn = A(57856, 1024, F32)
        qtB_u = A(58880, 1024, F32)
        qtB_r2 = A(59904, 1024, F32)
        qtB_r1 = A(60928, 512)
        qtB_a5 = A(61440, 512)
        qtB_sg = A(61952, 512)
        qtB_t2 = A(62464, 512)
        qtB_qs = A(62976, 512)
        qtB_qT = A(63488, 512)

        qt_r1 = A(80352, 512)
        qt_a5 = A(80864, 512)
        qt_sg = A(81376, 512)
        qt_t2 = A(81888, 512)
        qt_qs = A(82400, 512)
        qt_qT = A(82912, 512)
        qt_q = A(83424, 512)
        QTS = [
            {"t": qt_t, "xn": qt_xn, "u": qt_u, "r2": qt_r2, "am": qt_am, "inv": qt_inv,
             "s6": qt_s6, "r1": qt_r1, "a5": qt_a5, "sg": qt_sg, "t2": qt_t2, "qs": qt_qs, "qT": qt_qT},
            {"t": qtB_t, "xn": qtB_xn, "u": qtB_u, "r2": qtB_r2, "am": qtB_am, "inv": qtB_inv,
             "s6": qtB_s6, "r1": qtB_r1, "a5": qtB_a5, "sg": qtB_sg, "t2": qtB_t2, "qs": qtB_qs, "qT": qtB_qT},
        ]
        x0S = A(0, 8192, F32)
        cprojS = A(8192, 8192)
        eS = A(16384, 8192)
        yrawS = A(16384, 8192, F32)
        yvS = A(83936, 2048)
        attnS = A(85984, 4096)

        x1t = A(0, 8192, F32)
        x1tB = A(64000, 8192, F32)
        x1tp = [x1t, x1tB]
        xqS = A(8192, 4096)
        htS = A(12288, 22528, F32)
        hqS = A(34816, 11264)
        wgbuf = [A(46080 + i * 1024, 1024) for i in range(2)]
        wubuf = [A(48128 + i * 1024, 1024) for i in range(2)]
        wdbuf = [A(50176 + i * 2816, 2816) for i in range(2)]
        silS = A(55808, 512)
        xnb = A(56320, 512)

        lyt = A(0, 8192, F32)
        lcf = A(8192, 4096)
        lxn = A(12288, 4096)

        xsc = A(0, 32768)
        stS = A(32768, 1024 + 16)
        mxacc = [A(33808 + i * 4096, 4096) for i in range(2)]
        mxfm = A(0, 32768)
        woutS = A(32768, 8192)
        x2t = A(40960, 8192, F32)

        wrS = A(0, 8192)
        wkS = A(8192, 8192)
        wvS = A(16384, 8192)
        x3t = A(24576, 8192, F32)
        xpt = A(32768, 8192, F32)
        xrS = A(40960, 4096)
        xkS = A(45056, 4096)
        rbS = A(49152, 4096)
        kaS = A(53248, 4096)
        p4tmp = A(57344, 1024, F32)

        def init_phase():
            ct.op("Y", lambda h: h.dma_start(out=cattnb.ap(), in_=AP(cattn_b.tensor, 0, [[1, 128], [128, 24]]),
                                             allow_slow_non_contiguous=True), w=["cattnb"])
            for dst, src, nm in ((cprojb, cproj_b, "cprojb"), (boutb, bout_v, "boutb"),
                                 (normwb, norm_w, "normwb"), (murb, mu_r, "murb"), (mukb, mu_k, "mukb")):
                ct.op("Y", lambda h, d=dst, s=src: h.dma_start(out=d.ap(), in_=AP(s.tensor, 0, [[1, 128], [128, 8]]),
                                                               allow_slow_non_contiguous=True), w=[nm])
            ct.op("Y", lambda h: h.dma_start(out=skipb.ap(), in_=AP(skip_sh.tensor, 0, [[1, 128], [1, 1]])), w=["skipb"])
            ct.op("Y", lambda h: h.dma_start(out=cidb.app(0, 1, dims=[[1, 1]]), in_=core_id_in), w=["cid"])
            for half in range(2):
                ct.op("GD", lambda h, p=64 * half: h.dma_start(out=q2lw.app(p, 64), in_=q2l_wT), w=["q2lw"])
                ct.op("GD", lambda h, p=64 * half: h.dma_start(out=kv2lw.app(p, 64), in_=kv2l_wT), w=["kv2lw"])
                ct.op("GD", lambda h, p=64 * half: h.dma_start(out=l2ow.app(p, 32, dims=[[1, 64]]), in_=l2o_wT), w=["l2ow"])
            for g in range(4):
                ct.op("GD", lambda h, p=32 * g: h.dma_start(out=wqkw.app(p, 32), in_=wqk_wT), w=["wqkw"])
            ct.op("GD", lambda h: h.dma_start(out=cosb.ap(), in_=cos4), w=["cosb"])
            ct.op("GD", lambda h: h.dma_start(out=sinb.ap(), in_=sin4), w=["sinb"])
            ct.op("GD", lambda h: h.dma_start(
                out=cattn.ap(), in_=AP(cattn_wT.tensor, 0, [[3 * C, 128], [128 * 3 * C, 8], [128, 24], [1, 128]])),
                w=["cattn"])
            for g in range(4):
                ct.op("Y", lambda h, p=32 * g: h.dma_start(out=q2lb4.app(p, 32, dims=[[1, 1]]), in_=AP(q2l_b.tensor, 0, [[1, 32], [1, 1]])), w=["q2lb4"])
                ct.op("Y", lambda h, p=32 * g: h.dma_start(out=kvb4.app(p, 32, dims=[[1, 1]]), in_=AP(kv2l_b.tensor, 0, [[1, 32], [1, 1]])), w=["kvb4"])
                ct.op("Y", lambda h, p=32 * g: h.dma_start(out=wqkb4.app(p, 32, dims=[[1, 1]]), in_=AP(wqk_b.tensor, 0, [[1, 32], [1, 1]])), w=["wqkb4"])
            for half in range(2):
                ct.op("Y", lambda h, p=64 * half: h.dma_start(out=l2ob2.app(p, 64, dims=[[1, 1]]), in_=AP(l2o_b.tensor, 0, [[1, 64], [1, 1]])), w=["l2ob2"])
            ct.op("V", lambda h: h.memset(ones1.ap(), 1.0), w=["ones1"])
            ct.op("V", lambda h: h.memset(constc.ap(0, [[1, 1]]), EPS_LN), w=["constc"])
            ct.op("V", lambda h: h.memset(constc.ap(1, [[1, 1]]), EPS_RMS), w=["constc"])
            ct.op("V", lambda h: h.memset(onesr.ap(0, [[1, 128]]), 1.0), w=["onesr"])
            ct.op("V", lambda h: h.memset(dn.ap(), 1.0), w=["dn"])
            for dst, src, nm, nmr in ((q2lrw, q2lw, "q2lw", "q2lrw"), (kv2lrw, kv2lw, "kv2lw", "kv2lrw")):
                ct.op("V", lambda h, d=dst, s=src: h.tensor_scalar_mul(
                    d.ap(0, [[2, 16]]), s.ap(1, [[2, 16]]), -1.0), r=[nm], w=[nmr])
                ct.op("V", lambda h, d=dst, s=src: h.tensor_scalar_mul(
                    d.ap(1, [[2, 16]]), s.ap(0, [[2, 16]]), 1.0), r=[nm], w=[nmr])
            ct.op("Y", lambda h: h.dma_start(out=altsg.ap(), in_=AP(altsgn.tensor, 0, [[1, 128], [1, 1]])), w=["altsg"])
            pair_mask = [i + 1 if i % 2 == 0 else i - 1 for i in range(32)]
            for dst, src, nm, nmr in ((q2lbr4, q2lb4, "q2lb4", "q2lbr4"), (kvbr4, kvb4, "kvb4", "kvbr4")):
                ct.op("V", lambda h, d=dst, s=src: h.stream_shuffle(d.ap(), s.ap(), pair_mask), r=[nm], w=[nmr])
                ct.op("V", lambda h, d=dst: h.tensor_scalar(d.ap(), d.ap(), altsg.ap(), None, AL.mult),
                      r=[nmr, "altsg"], w=[nmr])
            ct.op("V", lambda h: h.transpose(
                kvrow.app(0, 32, dims=[[1, 32]]),
                AP(persf_t, kvb4.off, [[PF, 32], [0, 32]])), r=["kvb4"], w=["kvrow"])
            ct.op("V", lambda h: h.tensor_copy(dnh.app(0, 1, dims=[[1, 32]]), kvrow.app(0, 1, dims=[[1, 32]])),
                  r=["kvrow"], w=["dnh"])
            ct.op("V", lambda h: h.tensor_tensor(dnl.app(0, 1, dims=[[1, 32]]), kvrow.app(0, 1, dims=[[1, 32]]),
                                                 dnh.app(0, 1, dims=[[1, 32]]), AL.subtract),
                  r=["kvrow", "dnh"], w=["dnl"])
            for lohi, buf in ((0, dnh), (1, dnl)):
                ct.op("T", lambda h, b2=buf, st=(lohi == 0): h.matmul(
                    AP(PB[0], 0, [[512, 128], [1, 32]]), onesr.app(0, 1, dims=[[1, 128]]),
                    b2.app(0, 1, dims=[[1, 32]]), start=st, stop=(lohi == 1), tile_position=(0, 0)),
                    r=["dnh", "dnl", "onesr"], w=["pb0"])
            for rep in range(4):
                ct.op("V", lambda h, rp=rep: h.tensor_copy(
                    kvbias.ap(32 * rp, [[1, 32]]), AP(PB[0], 0, [[512, 128], [1, 32]])),
                    r=["pb0"], w=["kvbias"])
            ct.op("V", lambda h: h.memset(abdf.ap(), 0.0), w=["abdf"])
            ct.op("GD", lambda h: h.dma_start(out=abdf.app(0, 64, dims=[[1, 64]]), in_=a_sh[0]), w=["abdf"])
            ct.op("GD", lambda h: h.dma_start(out=abdf.app(64, 64, o=64, dims=[[1, 64]]), in_=a_sh[1]), w=["abdf"])
            ct.op("S", lambda h: h.activation(abd.ap(), abdf.ap(), AF.Tanh), r=["abdf"], w=["abd"])
            for wbd, wsrc, nm in ((bbd, b_sh, "bbd"), (cbd, c_sh, "cbd")):
                ct.op("V", lambda h, d=wbd: h.memset(d.ap(), 0.0), w=[nm])
                ct.op("GD", lambda h, d=wbd, s=wsrc: h.dma_start(out=d.app(0, 64, dims=[[1, 64]]), in_=s[0]), w=[nm])
                ct.op("GD", lambda h, d=wbd, s=wsrc: h.dma_start(out=d.app(64, 64, o=64, dims=[[1, 64]]), in_=s[1]), w=[nm])
            ct.op("V", lambda h: h.tensor_scalar(omrb.ap(), murb.ap(), -1.0, 1.0, AL.mult, AL.add), r=["murb"], w=["omrb"])
            ct.op("V", lambda h: h.tensor_scalar(omkb.ap(), mukb.ap(), -1.0, 1.0, AL.mult, AL.add), r=["mukb"], w=["omkb"])
            ct.op("V", lambda h: h.memset(vSB.ap(), 0.0), w=["vSB"])
            for s2 in range(SEQ):
                for hh2 in range(H):
                    base = (s2 * H + hh2) * 559
                    ct.op("V", lambda h, b2=base: h.memset(vSB.ap(b2 + 32, [[1, 1]]), 1.0), w=["vSB"])
                    ct.op("V", lambda h, b2=base: h.memset(vSB.ap(b2 + 96, [[33, 15], [1, 1]]), 1.0), w=["vSB"])

        init_phase()
        ct.barrier()

        def qstage_a(src_ap, n, qs_, r_extra=()):
            nb = n // 16
            B = QTS[qs_]
            tg = f"qtmp{qs_}"
            ct.op("V", lambda h: h.transpose(B["t"].ap(0, [[1, n]]), src_ap), r=list(r_extra) + [tg], w=[tg])
            ct.op("V", lambda h: h.tensor_reduce(B["am"].ap(0, [[1, nb]]), B["t"].ap(0, [[16, nb], [1, 16]]),
                                                 AX, AL.max, apply_absolute_value=True), r=[tg], w=[tg])
            ct.op("V", lambda h: h.tensor_scalar(B["s6"].ap(0, [[1, nb]]), B["am"].ap(0, [[1, nb]]),
                                                 1.0 / 6.0, 1e-12, AL.mult, AL.max), r=[tg], w=[tg])
            ct.op("V", lambda h: h.reciprocal(B["inv"].ap(0, [[1, nb]]), B["s6"].ap(0, [[1, nb]])), r=[tg], w=[tg])
            ct.op("V", lambda h: h.tensor_tensor(B["xn"].ap(0, [[16, nb], [1, 16]]), B["t"].ap(0, [[16, nb], [1, 16]]),
                                                 B["inv"].ap(0, [[1, nb], [0, 16]]), AL.mult), r=[tg], w=[tg])

        def qstage_s(n, qs_):
            B = QTS[qs_]
            tg = f"qtmp{qs_}"
            sg_t = f"qsg{qs_}"
            ct.op("S", lambda h: h.activation(B["u"].ap(0, [[1, n]]), B["xn"].ap(0, [[1, n]]), AF.Abs),
                  r=[tg], w=[tg])
            ct.op("S", lambda h: h.activation(B["sg"].ap(0, [[1, n]]), B["xn"].ap(0, [[1, n]]), AF.Sign), r=[tg], w=[sg_t])

        def qstage_b(dst_ap, n, qs_, w_extra=()):
            nb = n // 16
            B = QTS[qs_]
            tg = f"qtmp{qs_}"
            sg_t = f"qsg{qs_}"
            ct.op("V", lambda h: h.tensor_scalar(B["r2"].ap(0, [[1, n]]), B["u"].ap(0, [[1, n]]), 2.0, C_HALF, AL.min, AL.add),
                  r=[tg], w=[tg])
            ct.op("V", lambda h: h.tensor_scalar(B["r1"].ap(0, [[1, n]]), B["r2"].ap(0, [[1, n]]), C_HALF, None, AL.subtract),
                  r=[tg], w=[tg])
            ct.op("V", lambda h: h.tensor_scalar(B["r2"].ap(0, [[1, n]]), B["u"].ap(0, [[1, n]]), 2.0, 4.0, AL.max, AL.min),
                  r=[tg], w=[tg])
            ct.op("V", lambda h: h.tensor_scalar(B["r2"].ap(0, [[1, n]]), B["r2"].ap(0, [[1, n]]), C_INT, C_INT, AL.add, AL.subtract),
                  r=[tg], w=[tg])
            ct.op("V", lambda h: h.tensor_scalar(B["a5"].ap(0, [[1, n]]), B["u"].ap(0, [[1, n]]), 5.0, 2.0, AL.is_gt, AL.mult),
                  r=[tg], w=[tg])
            ct.op("V", lambda h: h.tensor_tensor(B["t2"].ap(0, [[1, n]]), B["r1"].ap(0, [[1, n]]), B["r2"].ap(0, [[1, n]]), AL.add),
                  r=[tg], w=[tg])
            ct.op("V", lambda h: h.tensor_tensor(B["t2"].ap(0, [[1, n]]), B["t2"].ap(0, [[1, n]]), B["a5"].ap(0, [[1, n]]), AL.add),
                  r=[tg], w=[tg])
            ct.op("V", lambda h: h.scalar_tensor_tensor(B["qs"].ap(0, [[1, n]]), B["t2"].ap(0, [[1, n]]), 2.0,
                                                        B["sg"].ap(0, [[1, n]]), AL.subtract, AL.mult),
                  r=[tg, sg_t], w=[tg])
            ct.op("V", lambda h: h.tensor_tensor(B["qT"].ap(0, [[16, nb], [1, 16]]), B["qs"].ap(0, [[16, nb], [1, 16]]),
                                                 B["s6"].ap(0, [[1, nb], [0, 16]]), AL.mult), r=[tg], w=[tg])
            ct.op("V", lambda h: h.transpose(dst_ap, B["qT"].ap(0, [[1, n]])), r=[tg], w=list(w_extra) + [tg])

        def quant_block(src_ap, dst_ap, n, r_extra=(), w_extra=(), qs_=0):
            qstage_a(src_ap, n, qs_, r_extra=r_extra)
            qstage_s(n, qs_)
            qstage_b(dst_ap, n, qs_, w_extra=w_extra)

        lnw0 = strow.app(0, 1, dims=[[1, 512]])
        lnw1 = dn.app(0, 1, o=0, dims=[[1, 512]])
        lnw2 = dn.app(0, 1, o=512, dims=[[1, 512]])
        rhi = ybf.app(0, 1, dims=[[1, 512]])
        rlo = y2buf.app(0, 1, dims=[[1, 512]])
        nhi = dnh.app(0, 1, o=0, dims=[[1, 512]])
        nlo = dnh.app(0, 1, o=512, dims=[[1, 512]])

        def ln_coef(rms=False):
            ct.op("V", lambda h: h.tensor_scalar_mul(lnw0, lnw0, 1.0 / C), r=["strow"], w=["strow"])
            ct.op("V", lambda h: h.tensor_scalar_mul(lnw1, lnw1, 1.0 / C), r=["strow", "dn"], w=["dn"])
            ct.op("V", lambda h: h.tensor_tensor(lnw2, lnw0, lnw0, AL.mult), r=["strow", "dn"], w=["dn"])
            ct.op("V", lambda h: h.tensor_tensor(lnw1, lnw1, lnw2, AL.subtract), r=["dn"], w=["dn"])
            if rms:
                ct.op("V", lambda h: h.tensor_copy(lnw2, lnw1), r=["dn"], w=["dn"])
            ct.op("S", lambda h: h.activation(lnw1, lnw1, AF.Sqrt, bias=EPS_LN), r=["dn"], w=["dn"])
            ct.op("V", lambda h: h.reciprocal(lnw1, lnw1), r=["dn"], w=["dn"])
            ct.op("V", lambda h: h.tensor_tensor(lnw0, lnw0, lnw1, AL.mult), r=["strow", "dn"], w=["strow"])

        def ln_hilo_mms(src_r, src_nb, bR, btagR, bNB, btagNB):
            ct.op("V", lambda h, s=src_r: h.tensor_copy(rhi, s), r=["strow", "dn"], w=["ybf"])
            ct.op("V", lambda h, s=src_r: h.tensor_tensor(rlo, s, rhi, AL.subtract),
                  r=["strow", "dn", "ybf"], w=["y2b"])
            ct.op("V", lambda h, s=src_nb: h.tensor_copy(nhi, s), r=["strow", "dn", "dnh"], w=["dnh"])
            ct.op("V", lambda h, s=src_nb: h.tensor_tensor(nlo, s, nhi, AL.subtract),
                  r=["strow", "dn", "dnh"], w=["dnh"])
            ct.op("T", lambda h, b=bR: h.matmul(b[:, :], onesr.app(0, 1, dims=[[1, 128]]), rhi,
                                                start=True, stop=False, tile_position=(0, 0)),
                  r=["ybf", "onesr"], w=[btagR])
            ct.op("T", lambda h, b=bR: h.matmul(b[:, :], onesr.app(0, 1, dims=[[1, 128]]), rlo,
                                                start=False, stop=True, tile_position=(0, 0)),
                  r=["y2b", "onesr"], w=[btagR])
            ct.op("T", lambda h, b=bNB: h.matmul(b[:, :], onesr.app(0, 1, dims=[[1, 128]]), nhi,
                                                 start=True, stop=False, tile_position=(0, 0)),
                  r=["dnh", "onesr"], w=[btagNB])
            ct.op("T", lambda h, b=bNB: h.matmul(b[:, :], onesr.app(0, 1, dims=[[1, 128]]), nlo,
                                                 start=False, stop=True, tile_position=(0, 0)),
                  r=["dnh", "onesr"], w=[btagNB])

        def wq_slice_fetch():
            ct.op("GD", lambda h: h.dma_start(
                out=wgs_d, in_=AP(wg_T.tensor, nc.partition_id([mybir.EngineType.Pool]) * 352,
                                  [[IFF, C], [1, 352]])), r=[], w=["wgs"])
            ct.op("GD", lambda h: h.dma_start(
                out=wus_d, in_=AP(wu_T.tensor, nc.partition_id([mybir.EngineType.Pool]) * 352,
                                  [[IFF, C], [1, 352]])), r=[], w=["wus"])
            ct.op("GD", lambda h: h.dma_start(
                out=wds_d, in_=AP(wd_T.tensor, nc.partition_id([mybir.EngineType.Pool]) * 128,
                                  [[C, IFF], [1, 128]])), r=[], w=["wds"])

        def wq_blocklist():
            out = []
            for (w_src, seg0, rows, cols, nm) in ((wgs_d, 0, C, 352, "wgs"),
                                                  (wus_d, WSEG, C, 352, "wus"),
                                                  (wds_d, 2 * WSEG, IFF, 128, "wds")):
                for rc in range(rows // 128):
                    out.append((w_src, seg0, cols, nm, rc))
            return out

        WQB = wq_blocklist()
        wqi = [0]

        def emit_wq(kcount):
            for _ in range(kcount):
                if wqi[0] >= len(WQB):
                    return
                w_src, seg0, n, nm, rc = WQB[wqi[0]]
                wqi[0] += 1
                src = AP(w_src.tensor, rc * 128 * n, [[n, 128], [1, n]])
                dst = AP(wqin_d.tensor, seg0 + rc * 128 * n, [[n, 128], [1, n]])
                ct.op("Y", lambda h, s=src, n=n: h.dma_start(out=qt_x.ap(0, [[1, n]]), in_=s), r=["qtmp0", nm], w=["qt_x"])
                quant_block(qt_x.ap(0, [[1, n]]), qt_q.ap(0, [[1, n]]), n, r_extra=["qt_x"], qs_=0)
                ct.op("Y", lambda h, d=dst, n=n: h.dma_start(out=d, in_=qt_q.ap(0, [[1, n]])), r=["qtmp0"], w=["wqin"])

        def wq_ag():
            ct.op("G", lambda h: h.collective_compute(
                "AllGather", AL.bypass, replica_groups=[list(range(NCORES))],
                ins=[wqin_d.opt()], outs=[wq_g.opt()]), r=["wqin"], w=["wq_g"])

        RK = 2 * WSEG + WDSEG

        def wg_segs(oc):
            a = 128 * oc
            b = a + 128
            segs = []
            r2 = a // 352
            while a < b:
                e = min(b, 352 * (r2 + 1))
                segs.append((r2, a, e))
                a = e
                r2 += 1
            return segs

        def p0_tile(s, j):
            jj = s * TPS + j
            col0 = jj * TILE
            pos0 = j * TILE
            ct.op("GD", lambda h: h.dma_start(
                out=xbS.ap(), in_=AP(x_in.tensor, col0, [[NTOK, 128], [128 * NTOK, 8], [1, 512]])),
                r=["xbrt"], w=["xbrt"])
            for oc in range(24):
                bank, btag = PB[oc % 2], f"pb{oc % 2}"
                for kc in range(8):
                    lhsT = cattn.ap((kc * 24 + oc) * 128, [[1, 128]])
                    rhs = xbS.ap(kc * 512, [[1, 512]])
                    ct.op("T", lambda h, l=lhsT, r2=rhs, b=bank, st=(kc == 0), sp=(kc == 7):
                          h.matmul(b[:, :], l, r2, start=st, stop=sp), r=["cattn", "xbrt"], w=[btag])
                ct.op("S", lambda h, oc=oc, b=bank: h.activation(
                    qkvS.ap(oc * 512, [[1, 512]]), b[:, :], AF.Identity, bias=cattnb.ap(oc, [[1, 1]])),
                    r=[btag, "cattnb"], w=["qkv"])
            cosap = cosb.ap(pos0, [[1, 512]])
            sinap = sinb.ap(pos0, [[1, 512]])
            for g in range(4):
                for (wl, bank, btag, wtag) in ((q2lw, PB[2], "pb2", "q2lw"), (q2lrw, PB[3], "pb3", "q2lrw"),
                                               (kv2lw, PB[4], "pb4", "kv2lw"), (kv2lrw, PB[5], "pb5", "kv2lrw")):
                    srcc = 0 if wtag.startswith("q") else 8
                    for i in range(4):
                        hh = 4 * g + i
                        chunk = srcc + hh // 2
                        par = 64 * (hh % 2)
                        lhsT = wl.app(par, 64)
                        rhs = qkvS.app(par, 64, o=chunk * 512, dims=[[1, 512]])
                        ct.op("T", lambda h, l=lhsT, r2=rhs, b=bank, i=i, par=par:
                              h.matmul(AP(b, 32 * i * 512, [[512, 32], [1, 512]]), l, r2,
                                       start=True, stop=True, tile_position=(par, 32 * i)),
                              r=["qkv", wtag], w=[btag])
                ct.op("V", lambda h: h.scalar_tensor_tensor(rt1.ap(), PB[2][:, :], q2lb4.ap(), cosap, AL.add, AL.mult),
                      r=["pb2", "q2lb4", "cosb", "xbrt"], w=["xbrt1"])
                ct.op("V", lambda h: h.scalar_tensor_tensor(rt2.ap(), PB[3][:, :], q2lbr4.ap(), sinap, AL.add, AL.mult),
                      r=["pb3", "q2lbr4", "sinb", "xbrt"], w=["xbrt2"])
                ct.op("V", lambda h: h.tensor_tensor(rt1.ap(), rt1.ap(), rt2.ap(), AL.add),
                      r=["xbrt1", "xbrt2"], w=["xbrt1"])
                for i in range(4):
                    lhsT = wqkw.app(32 * i, 32)
                    rhs = rt1.app(32 * i, 32, dims=[[1, 512]])
                    ct.op("T", lambda h, l=lhsT, r2=rhs, i=i:
                          h.matmul(AP(PB[6], 32 * i * 512, [[512, 32], [1, 512]]), l, r2,
                                   start=True, stop=True, tile_position=(32 * i, 32 * i)),
                          r=["xbrt1", "wqkw"], w=["pb6"])
                qdst = qT.ap((s * 4 + g) * T + pos0, [[1, 512]])
                ct.op("S", lambda h, d=qdst: h.activation(d, PB[6][:, :], AF.Identity, bias=wqkb4.ap()),
                      r=["pb6", "wqkb4"], w=["qT"])
                kdst = kT.ap((s * 4 + g) * T + pos0, [[1, 512]])
                ct.op("V", lambda h: h.scalar_tensor_tensor(rt1.ap(), PB[4][:, :], kvb4.ap(), cosap, AL.add, AL.mult),
                      r=["pb4", "kvb4", "cosb", "xbrt", "xbrt1"], w=["xbrt1"])
                ct.op("V", lambda h: h.scalar_tensor_tensor(rt2.ap(), PB[5][:, :], kvbr4.ap(), sinap, AL.add, AL.mult),
                      r=["pb5", "kvbr4", "sinb", "xbrt", "xbrt2"], w=["xbrt2"])
                ct.op("V", lambda h, d=kdst: h.tensor_tensor(d, rt1.ap(), rt2.ap(), AL.add),
                      r=["xbrt1", "xbrt2"], w=["kT"])
            for hh in range(H):
                par = 64 * (hh % 2)
                chunk = 16 + hh // 2
                bank, btag = (PB[2], "pb2") if hh % 2 == 0 else (PB[3], "pb3")
                for ksub in range(4):
                    lhsT = qkvS.app(par, 64, o=chunk * 512 + ksub * 128, dims=[[1, 128]])
                    rhs = kv2lw.app(par, 64)
                    ct.op("T", lambda h, l=lhsT, r2=rhs, b=bank, ks=ksub, par=par:
                          h.matmul(AP(b, ks * 32, [[512, 128], [1, 32]]), l, r2,
                                   start=True, stop=True, tile_position=(par, 0)),
                          r=["qkv", "kv2lw"], w=[btag])
                if j == 0:
                    vdst0 = vSB.ap(voff(s, hh, 0), [[1, 1], [1, 32]])
                    ct.op("V", lambda h, b=bank, d=vdst0: h.tensor_tensor(
                        d, AP(b, 0, [[512, 128], [32, 1], [1, 32]]), kvbias.ap(0, [[32, 1], [1, 32]]), AL.add),
                        r=[btag, "kvbias"], w=["vSB"])
                    vdst1 = vSB.ap(voff(s, hh, 1), [[33, 3], [1, 32]])
                    ct.op("V", lambda h, b=bank, d=vdst1: h.tensor_tensor(
                        d, AP(b, 32, [[512, 128], [32, 3], [1, 32]]), kvbias.ap(32, [[32, 3], [1, 32]]), AL.add),
                        r=[btag, "kvbias"], w=["vSB"])
                else:
                    vdst = vSB.ap(voff(s, hh, j * 4), [[33, 4], [1, 32]])
                    ct.op("V", lambda h, b=bank, d=vdst: h.tensor_tensor(
                        d, AP(b, 0, [[512, 128], [32, 4], [1, 32]]), kvbias.ap(0, [[32, 4], [1, 32]]), AL.add),
                        r=[btag, "kvbias"], w=["vSB"])

        for s in range(SEQ):
            for j in range(TPS):
                p0_tile(s, j)
        ct.barrier()

        ct.op("GD", lambda h: h.dma_start(
            out=cprojS.ap(), in_=AP(cproj_wT.tensor, 0, [[C, 128], [128 * C, 8], [128, 8], [1, 128]])),
            w=["cprojw"])

        def p1_tile(s, qt):
            qlo = qt * TILE
            jj = s * TPS + qt
            col0 = jj * TILE
            ct.op("Y", lambda h: h.dma_start(
                out=x0S.ap(), in_=AP(x_in.tensor, col0, [[NTOK, 128], [128 * NTOK, 8], [1, 512]])),
                r=["x0"], w=["x0"])
            nkt = 4 * qt + 4
            for half in range(2):
                for kt in range(nkt):
                    qs = 0 if (kt + 1) * 128 <= qlo else (kt * 128 - qlo)
                    epar = kt % 2
                    for gg in range(2):
                        g = half * 2 + gg
                        for i in range(4):
                            lhsT = kT.app(32 * i, 32, o=(s * 4 + g) * T + kt * 128, dims=[[1, 128]])
                            rhs = qT.app(32 * i, 32, o=(s * 4 + g) * T + qlo + qs, dims=[[1, 512 - qs]])
                            ct.op("T", lambda h, l=lhsT, r2=rhs, b=PB[i], qs=qs, i=i:
                                  h.matmul(AP(b, qs, [[512, 128], [1, 512 - qs]]), l, r2,
                                           start=True, stop=True, tile_position=(32 * i, 0)),
                                  r=["qT", "kT"], w=[f"pb{i}"])
                        for i in range(4):
                            hrel = gg * 4 + i
                            ec = epar * 4096 + hrel * 512
                            ct.op("S", lambda h, i=i, ec=ec, qs=qs: h.activation(
                                eS.ap(ec + qs, [[1, 512 - qs]]),
                                AP(PB[i], qs, [[512, 128], [1, 512 - qs]]), AF.Exp, scale=ISQL),
                                r=[f"pb{i}"], w=[f"e{epar}"])
                            if qs > 0:
                                ct.op("V", lambda h, ec=ec, qs=qs: h.memset(eS.ap(ec, [[1, qs]]), 0.0),
                                      w=[f"e{epar}"])
                            if kt * 128 >= qlo:
                                ct.op("G", lambda h, ec=ec, qs=qs: h.affine_select(
                                    out=eS.ap(ec + qs, [[1, 128]]), in_=eS.ap(ec + qs, [[1, 128]]),
                                    compare_op=AL.is_ge, fill=0.0, base=0,
                                    pattern=[[1, 128]], channel_multiplier=-1),
                                    r=[f"e{epar}"], w=[f"e{epar}"])
                    for pr in range(4):
                        for ab in range(2):
                            hh = half * 8 + 2 * pr + ab
                            hrel = hh - half * 8
                            ec = epar * 4096 + hrel * 512
                            mdim = 64 if kt == 0 else 33
                            lhsT = vSB.ap(voff(s, hh, kt), [[1, mdim]])
                            rhs = eS.ap(ec + qs, [[1, 512 - qs]])
                            ct.op("T", lambda h, l=lhsT, r2=rhs, b=PB[4 + pr], ab=ab, st=(kt == 0), qs=qs, md=mdim:
                                  h.matmul(AP(b, 64 * ab * 512 + qs, [[512, md], [1, 512 - qs]]), l, r2,
                                           start=st, stop=False, tile_position=(0, 64 * ab)),
                                  r=["vSB", f"e{epar}"], w=[f"pb{4 + pr}"])
                for pr in range(4):
                    bank, btag = PB[4 + pr], f"pb{4 + pr}"
                    ct.op("S", lambda h, b=bank, pr=pr: h.activation(
                        yrawS.ap(pr * 512, [[1, 512]]), b[:, :], AF.Identity),
                        r=[btag], w=["e0", "e1", "yraw"])
                ct.op("GD", lambda h: h.dma_start(
                    out=AP(persf_t, dn.off, [[32 * PF, 4], [1, 512]]),
                    in_=yrawS.app(32, 1, dims=[[512, 4], [1, 512]])),
                    r=["yraw"], w=["dn"])
                ct.op("GD", lambda h: h.dma_start(
                    out=AP(persf_t, dn.off + 512, [[32 * PF, 4], [1, 512]]),
                    in_=yrawS.app(96, 1, dims=[[512, 4], [1, 512]])),
                    r=["yraw"], w=["dn"])
                ct.op("V", lambda h: h.reciprocal(dn.ap(), dn.ap()), r=["dn"], w=["dn"])
                ct.op("V", lambda h: h.tensor_copy(dnh.ap(), dn.ap()), r=["dn"], w=["dnh"])
                ct.op("V", lambda h: h.tensor_tensor(dnl.ap(), dn.ap(), dnh.ap(), AL.subtract),
                      r=["dn", "dnh"], w=["dnl"])
                for pr in range(4):
                    ch = half * 4 + pr
                    bank, btag = PB[4 + pr], f"pb{4 + pr}"
                    for ab in range(2):
                        for lohi, buf, bt2 in ((0, dnh, "dnh"), (1, dnl, "dnl")):
                            ct.op("T", lambda h, b=bank, pr=pr, ab=ab, b2=buf, st=(lohi == 0):
                                  h.matmul(AP(b, 64 * ab * 512, [[512, 64], [1, 512]]),
                                           onesr.app(32 * pr, 1, dims=[[1, 64]]),
                                           b2.app(32 * pr, 1, o=512 * ab, dims=[[1, 512]]),
                                           start=st, stop=(lohi == 1), tile_position=(32 * pr, 64 * ab)),
                                  r=["dnh", "dnl", "onesr", "yraw"], w=[btag])
                    ct.op("V", lambda h, b=bank, pr=pr: h.tensor_tensor(
                        yvS.ap(pr * 512, [[1, 512]]), yrawS.ap(pr * 512, [[1, 512]]), b[:, :], AL.mult),
                        r=[btag, "yraw"], w=["yv"])
                    ab_bank, ab_tag = PB[pr % 2], f"pb{pr % 2}"
                    for ab in range(2):
                        lhsT = l2ow.app(64 * ab, 32, dims=[[1, 64]])
                        rhs = yvS.app(64 * ab, 32, o=pr * 512, dims=[[1, 512]])
                        ct.op("T", lambda h, l=lhsT, r2=rhs, b=ab_bank, ab=ab:
                              h.matmul(AP(b, 64 * ab * 512, [[512, 64], [1, 512]]), l, r2,
                                       start=True, stop=True, tile_position=(64 * ab, 64 * ab)),
                              r=["yv", "l2ow"], w=[ab_tag])
                    ct.op("S", lambda h, b=ab_bank, ch=ch: h.activation(
                        attnS.ap(ch * 512, [[1, 512]]), b[:, :], AF.Identity, bias=l2ob2.ap()),
                        r=[ab_tag, "l2ob2"], w=["attn"])
            for oc in range(8):
                bank, btag = PB[2 + oc % 2], f"pb{2 + oc % 2}"
                for kc in range(8):
                    lhsT = cprojS.ap((kc * 8 + oc) * 128, [[1, 128]])
                    rhs = attnS.ap(kc * 512, [[1, 512]])
                    ct.op("T", lambda h, l=lhsT, r2=rhs, b=bank, st=(kc == 0), sp=(kc == 7):
                          h.matmul(b[:, :], l, r2, start=st, stop=sp), r=["attn", "cprojw"], w=[btag])
                xs = x0S.ap(oc * 512, [[1, 512]])
                ct.op("V", lambda h, b=bank, xs=xs, oc=oc: h.scalar_tensor_tensor(
                    xs, b[:, :], cprojb.ap(oc, [[1, 1]]), xs, AL.add, AL.add),
                    r=[btag, "x0", "cprojb"], w=["x0"])
                ct.op("S", lambda h, xs=xs: h.activation(y2buf.ap(), xs, AF.Square), r=["x0"], w=["y2b"])
                ct.op("V", lambda h, xs=xs: h.tensor_copy(ybf.ap(), xs), r=["x0"], w=["ybf"])
                ct.op("T", lambda h, st=(oc == 0), sp=(oc == 7): h.matmul(
                    AP(PB[6], 0, [[512, 1], [1, 512]]), ones1.ap(0, [[1, 1]]), ybf.ap(), start=st, stop=sp,
                    tile_position=(0, 0)), r=["ybf", "ones1"], w=["pb6"])
                ct.op("T", lambda h, st=(oc == 0), sp=(oc == 7): h.matmul(
                    AP(PB[7], 0, [[512, 1], [1, 512]]), ones1.ap(0, [[1, 1]]), y2buf.ap(), start=st, stop=sp,
                    tile_position=(0, 0)), r=["y2b", "ones1"], w=["pb7"])
            ct.op("V", lambda h: h.tensor_copy(lnw0, AP(PB[6], 0, [[512, 1], [1, 512]])),
                  r=["pb6", "strow"], w=["strow"])
            ct.op("V", lambda h: h.tensor_copy(lnw1, AP(PB[7], 0, [[512, 1], [1, 512]])),
                  r=["pb7", "dn"], w=["dn"])
            ln_coef(rms=False)
            ln_hilo_mms(lnw1, lnw0, PB[6], "pb6", PB[7], "pb7")
            x1dst = out_d if stop == 1 else x1_d
            for oc in range(8):
                xs = x0S.ap(oc * 512, [[1, 512]])
                ct.op("V", lambda h, xs=xs: h.tensor_tensor(xs, xs, PB[6][:, :], AL.mult),
                      r=["x0", "pb6"], w=["x0"])
                ct.op("V", lambda h, xs=xs: h.tensor_tensor(xs, xs, PB[7][:, :], AL.subtract),
                      r=["x0", "pb7"], w=["x0"])
                ct.op("Y", lambda h, xs=xs, oc=oc: h.dma_start(
                    out=AP(x1dst.tensor, oc * 128 * NTOK + col0, [[NTOK, 128], [1, 512]]), in_=xs),
                    r=["x0"], w=["x1_d"])

        wq_slice_fetch()
        p1_ti = 0
        for s in range(SEQ):
            for qt in range(TPS):
                p1_tile(s, qt)
                emit_wq(10)
                p1_ti += 1
                if p1_ti == 4:
                    emit_wq(len(WQB))
                    wq_ag()
        ct.barrier()

        def lnfix(k, y_src, s_src, cf_dst, x_dst, rms_extra=False, to_out=False):
            sw = [statw.ap(i * 32, [[1, 32]]) for i in range(5)]
            ct.op("Y", lambda h: h.dma_start(
                out=statw.ap(0, [[32, 2], [1, 32]]),
                in_=AP(s_src.tensor, 0, [[32, 128], [NTOK, 2], [1, 32]])), r=["statw"], w=["statw"])
            ct.op("V", lambda h: h.tensor_scalar_mul(sw[0], sw[0], 1.0 / C), r=["statw"], w=["statw"])
            ct.op("V", lambda h: h.tensor_scalar_mul(sw[1], sw[1], 1.0 / C), r=["statw"], w=["statw"])
            ct.op("V", lambda h: h.tensor_tensor(sw[2], sw[0], sw[0], AL.mult), r=["statw"], w=["statw"])
            ct.op("V", lambda h: h.tensor_tensor(sw[1], sw[1], sw[2], AL.subtract), r=["statw"], w=["statw"])
            ct.op("S", lambda h: h.activation(sw[2], sw[1], AF.Sqrt, bias=EPS_LN), r=["statw"], w=["statw"])
            ct.op("V", lambda h: h.reciprocal(sw[2], sw[2]), r=["statw"], w=["statw"])
            ct.op("V", lambda h: h.tensor_tensor(sw[3], sw[0], sw[2], AL.mult), r=["statw"], w=["statw"])
            ct.op("V", lambda h: h.tensor_scalar_mul(sw[3], sw[3], -1.0), r=["statw"], w=["statw"])
            def hilo(dst_t, row, src_sw):
                ct.op("V", lambda h: h.tensor_copy(ybf.ap(0, [[1, 32]]), src_sw), r=["statw"], w=["ybf"])
                ct.op("V", lambda h: h.tensor_tensor(strow.ap(0, [[1, 32]]), src_sw, ybf.ap(0, [[1, 32]]),
                                                     AL.subtract), r=["statw", "ybf"], w=["strow"])
                ct.op("Y", lambda h: h.dma_start(
                    out=AP(dst_t.tensor, row * NTOK, [[32, 128], [1, 32]]), in_=ybf.ap(0, [[1, 32]])),
                    r=["ybf"], w=[dst_t.tensor.name])
                ct.op("V", lambda h: h.tensor_copy(ybf.ap(0, [[1, 32]]), strow.ap(0, [[1, 32]])),
                      r=["strow"], w=["ybf"])
                ct.op("Y", lambda h: h.dma_start(
                    out=AP(dst_t.tensor, (row + 1) * NTOK, [[32, 128], [1, 32]]), in_=ybf.ap(0, [[1, 32]])),
                    r=["ybf"], w=[dst_t.tensor.name])
            hilo(cf_dst, 0, sw[2])
            hilo(cf_dst, 2, sw[3])
            if rms_extra:
                ct.op("V", lambda h: h.tensor_tensor(sw[4], sw[2], sw[2], AL.mult), r=["statw"], w=["statw"])
                ct.op("V", lambda h: h.tensor_tensor(sw[4], sw[4], sw[1], AL.mult), r=["statw"], w=["statw"])
                ct.op("S", lambda h: h.activation(sw[4], sw[4], AF.Sqrt, bias=EPS_RMS), r=["statw"], w=["statw"])
                ct.op("V", lambda h: h.reciprocal(sw[4], sw[4]), r=["statw"], w=["statw"])
                ct.op("V", lambda h: h.tensor_tensor(sw[2], sw[2], sw[4], AL.mult), r=["statw"], w=["statw"])
                ct.op("V", lambda h: h.tensor_tensor(sw[3], sw[3], sw[4], AL.mult), r=["statw"], w=["statw"])
                hilo(cfx_d, 0, sw[2])
                hilo(cfx_d, 2, sw[3])
            for jj in range(NT):
                col0 = jj * TILE
                ct.op("Y", lambda h, c0=col0: h.dma_start(
                    out=lyt.ap(), in_=AP(y_src.tensor, c0, [[NTOK, 128], [128 * NTOK, 8], [1, 512]])),
                    r=["lyt"], w=["lyt"])
                ct.op("Y", lambda h, c0=col0: h.dma_start(
                    out=lcf.app(0, 1, dims=[[512, 4], [1, 512]]),
                    in_=AP(cf_dst.tensor, c0, [[NTOK, 1], [NTOK, 4], [1, 512]])),
                    r=["lcf", cf_dst.tensor.name], w=["lcf"])
                for part, btag in ((0, "pb0"), (2, "pb1")):
                    bank = PB[0] if part == 0 else PB[1]
                    for lohi in range(2):
                        ct.op("T", lambda h, b=bank, p=part + lohi, st=(lohi == 0):
                              h.matmul(b[:, :], onesr.app(0, 1, dims=[[1, 128]]),
                                       lcf.app(0, 1, o=p * 512, dims=[[1, 512]]), start=st, stop=(lohi == 1),
                                       tile_position=(0, 0)),
                              r=["lcf", "onesr"], w=[btag])
                if rms_extra:
                    ct.op("Y", lambda h, c0=col0: h.dma_start(
                        out=lcf.app(0, 1, o=2048, dims=[[512, 4], [1, 512]]),
                        in_=AP(cfx_d.tensor, c0, [[NTOK, 1], [NTOK, 4], [1, 512]])),
                        r=["lcf2", "cfx_d"], w=["lcf2"])
                    for part, btag in ((0, "pb2"), (2, "pb3")):
                        bank = PB[2] if part == 0 else PB[3]
                        for lohi in range(2):
                            ct.op("T", lambda h, b=bank, p=part + lohi, st=(lohi == 0):
                                  h.matmul(b[:, :], onesr.app(0, 1, dims=[[1, 128]]),
                                           lcf.app(0, 1, o=2048 + p * 512, dims=[[1, 512]]), start=st, stop=(lohi == 1),
                                           tile_position=(0, 0)),
                                  r=["lcf2", "onesr"], w=[btag])
                for oc in range(8):
                    ys = lyt.ap(oc * 512, [[1, 512]])
                    if rms_extra:
                        ct.op("V", lambda h, ys=ys: h.tensor_tensor(qt_x.ap(0, [[1, 512]]), ys, PB[2][:, :], AL.mult),
                              r=["lyt", "pb2", "qtmp0"], w=["qtmp0"])
                        ct.op("V", lambda h: h.tensor_tensor(qt_u.ap(0, [[1, 512]]), qt_x.ap(0, [[1, 512]]), PB[3][:, :], AL.add),
                              r=["qtmp0", "pb3"], w=["qtmp0"])
                        ct.op("V", lambda h, oc=oc: h.tensor_scalar(
                            lxn.ap(oc * 512, [[1, 512]]), qt_u.ap(0, [[1, 512]]), normwb.ap(oc, [[1, 1]]), None, AL.mult),
                            r=["qtmp0", "normwb"], w=["lxn"])
                    ct.op("V", lambda h, ys=ys: h.tensor_tensor(ys, ys, PB[0][:, :], AL.mult),
                          r=["lyt", "pb0"], w=["lyt"])
                    ct.op("V", lambda h, ys=ys: h.tensor_tensor(ys, ys, PB[1][:, :], AL.add),
                          r=["lyt", "pb1"], w=["lyt"])
                dst = out_d if to_out else x_dst
                ct.op("Y", lambda h, c0=col0, d=dst: h.dma_start(
                    out=AP(d.tensor, c0, [[NTOK, 128], [128 * NTOK, 8], [1, 512]]), in_=lyt.ap()),
                    r=["lyt"], w=["x_out"])
                if rms_extra:
                    ct.op("Y", lambda h, c0=col0: h.dma_start(
                        out=AP(xn_d.tensor, c0, [[NTOK, 128], [128 * NTOK, 8], [1, 512]]), in_=lxn.ap()),
                        r=["lxn"], w=["xn_d"])

        if stop == 1:
            ct.emit(block, sems)
            return nc

        def p2_tile(jj):
            col0 = jj * TILE
            xt = x1tp[jj % 2]
            xtg = f"x1t{jj % 2}"
            ct.op("Y", lambda h, xt=xt: h.dma_start(
                out=xt.ap(), in_=AP(x1_d.tensor, col0, [[NTOK, 128], [128 * NTOK, 8], [1, 512]])),
                r=[xtg], w=[xtg])
            for kc in range(8):
                qstage_a(xt.ap(kc * 512, [[1, 512]]), 512, kc % 2, r_extra=[xtg])
                qstage_s(512, kc % 2)
                if kc >= 1:
                    qstage_b(xqS.ap((kc - 1) * 512, [[1, 512]]), 512, (kc - 1) % 2, w_extra=[f"xq{kc - 1}"])
            qstage_b(xqS.ap(7 * 512, [[1, 512]]), 512, 1, w_extra=["xq7"])
            for oc in range(ICH):
                par = oc % 2
                for (r2, a, e) in wg_segs(oc):
                    off_in = r2 * RK + (a - 352 * r2)
                    ct.op("Y", lambda h, par=par, oi=off_in, do=a - 128 * oc, w_=e - a: h.dma_start(
                        out=wgbuf[par].ap(do, dims=[[128, 8], [1, w_]]),
                        in_=AP(wq_g.tensor, oi, [[352, 128], [128 * 352, 8], [1, w_]])),
                        r=[f"wgb{par}", "wq_g"], w=[f"wgb{par}"])
                    ct.op("Y", lambda h, par=par, oi=off_in + WSEG, do=a - 128 * oc, w_=e - a: h.dma_start(
                        out=wubuf[par].ap(do, dims=[[128, 8], [1, w_]]),
                        in_=AP(wq_g.tensor, oi, [[352, 128], [128 * 352, 8], [1, w_]])),
                        r=[f"wub{par}", "wq_g"], w=[f"wub{par}"])
                for kc in range(8):
                    ct.op("T", lambda h, kc=kc, par=par, st=(kc == 0), sp=(kc == 7):
                          h.matmul(PB[0][:, :], wgbuf[par].ap(kc * 128, [[1, 128]]),
                                   xqS.ap(kc * 512, [[1, 512]]), start=st, stop=sp),
                          r=[f"wgb{par}", f"xq{kc}"], w=["pb0"])
                    ct.op("T", lambda h, kc=kc, par=par, st=(kc == 0), sp=(kc == 7):
                          h.matmul(PB[1][:, :], wubuf[par].ap(kc * 128, [[1, 128]]),
                                   xqS.ap(kc * 512, [[1, 512]]), start=st, stop=sp),
                          r=[f"wub{par}", f"xq{kc}"], w=["pb1"])
                ct.op("S", lambda h: h.activation(silS.ap(), PB[0][:, :], AF.Silu), r=["pb0"], w=["sil"])
                ct.op("V", lambda h, oc=oc: h.tensor_tensor(htS.ap(oc * 512, [[1, 512]]), silS.ap(),
                                                            PB[1][:, :], AL.mult),
                      r=["sil", "pb1"], w=[f"ht{oc}"])
                qstage_a(htS.ap(oc * 512, [[1, 512]]), 512, oc % 2, r_extra=[f"ht{oc}"])
                qstage_s(512, oc % 2)
                if oc >= 1:
                    qstage_b(hqS.ap((oc - 1) * 512, [[1, 512]]), 512, (oc - 1) % 2, w_extra=[f"hq{oc - 1}"])
            qstage_b(hqS.ap((ICH - 1) * 512, [[1, 512]]), 512, (ICH - 1) % 2, w_extra=[f"hq{ICH - 1}"])
            for oc in range(8):
                par = oc % 2
                bank, btag = PB[2 + par], f"pb{2 + par}"
                ct.op("Y", lambda h, oc=oc, par=par: h.dma_start(
                    out=wdbuf[par].ap(), in_=AP(wq_g.tensor, oc * RK + 2 * WSEG, [[128, 128], [128 * 128, 22], [1, 128]])),
                    r=[f"wdb{par}", "wq_g"], w=[f"wdb{par}"])
                for kc in range(ICH):
                    ct.op("T", lambda h, kc=kc, par=par, b=bank, st=(kc == 0), sp=(kc == ICH - 1):
                          h.matmul(b[:, :], wdbuf[par].ap(kc * 128, [[1, 128]]),
                                   hqS.ap(kc * 512, [[1, 512]]), start=st, stop=sp),
                          r=[f"wdb{par}", f"hq{kc}"], w=[btag])
                xs = xt.ap(oc * 512, [[1, 512]])
                ct.op("V", lambda h, b=bank, xs=xs: h.tensor_tensor(xs, xs, b[:, :], AL.add),
                      r=[btag, xtg], w=[xtg])
                ct.op("S", lambda h, xs=xs: h.activation(y2buf.ap(), xs, AF.Square), r=[xtg], w=["y2b"])
                ct.op("V", lambda h, xs=xs: h.tensor_copy(ybf.ap(), xs), r=[xtg], w=["ybf"])
                ct.op("T", lambda h, st=(oc == 0), sp=(oc == 7): h.matmul(
                    AP(PB[6], 0, [[512, 1], [1, 512]]), ones1.ap(0, [[1, 1]]), ybf.ap(), start=st, stop=sp,
                    tile_position=(0, 0)), r=["ybf", "ones1"], w=["pb6"])
                ct.op("T", lambda h, st=(oc == 0), sp=(oc == 7): h.matmul(
                    AP(PB[7], 0, [[512, 1], [1, 512]]), ones1.ap(0, [[1, 1]]), y2buf.ap(), start=st, stop=sp,
                    tile_position=(0, 0)), r=["y2b", "ones1"], w=["pb7"])
                if stop == 12:
                    ct.op("Y", lambda h, xs=xs, oc=oc: h.dma_start(
                        out=AP(y2_d.tensor, oc * 128 * NTOK + col0, [[NTOK, 128], [1, 512]]), in_=xs),
                        r=[xtg], w=["y2_d"])
            ct.op("V", lambda h: h.tensor_copy(lnw0, AP(PB[6], 0, [[512, 1], [1, 512]])),
                  r=["pb6", "strow"], w=["strow"])
            ct.op("V", lambda h: h.tensor_copy(lnw1, AP(PB[7], 0, [[512, 1], [1, 512]])),
                  r=["pb7", "dn"], w=["dn"])
            ln_coef(rms=True)
            ln_hilo_mms(lnw1, lnw0, PB[4], "pb4", PB[5], "pb5")
            ct.op("V", lambda h: h.tensor_tensor(lnw2, lnw2, lnw1, AL.mult), r=["dn"], w=["dn"])
            ct.op("V", lambda h: h.tensor_tensor(lnw2, lnw2, lnw1, AL.mult), r=["dn"], w=["dn"])
            ct.op("S", lambda h: h.activation(lnw2, lnw2, AF.Sqrt, bias=EPS_RMS), r=["dn"], w=["dn"])
            ct.op("V", lambda h: h.reciprocal(lnw2, lnw2), r=["dn"], w=["dn"])
            ct.op("V", lambda h: h.tensor_tensor(lnw1, lnw1, lnw2, AL.mult), r=["dn"], w=["dn"])
            ct.op("V", lambda h: h.tensor_tensor(lnw0, lnw0, lnw2, AL.mult), r=["strow", "dn"], w=["strow"])
            ln_hilo_mms(lnw1, lnw0, PB[6], "pb6", PB[7], "pb7")
            x2dst = out_d if stop == 2 else x2_d
            for oc in range(8):
                ys = xt.ap(oc * 512, [[1, 512]])
                ct.op("V", lambda h, ys=ys: h.tensor_tensor(qt_x.ap(0, [[1, 512]]), ys, PB[6][:, :], AL.mult),
                      r=[xtg, "pb6", "qtmp0"], w=["qtmp0"])
                ct.op("V", lambda h: h.tensor_tensor(qt_x.ap(0, [[1, 512]]), qt_x.ap(0, [[1, 512]]),
                                                     PB[7][:, :], AL.subtract),
                      r=["qtmp0", "pb7"], w=["qtmp0"])
                ct.op("V", lambda h, oc=oc: h.tensor_scalar(
                    xnb.ap(), qt_x.ap(0, [[1, 512]]), normwb.ap(oc, [[1, 1]]), None, AL.mult),
                    r=["qtmp0", "normwb"], w=["xnb"])
                ct.op("Y", lambda h, oc=oc, c0=col0: h.dma_start(
                    out=AP(xn_d.tensor, oc * 128 * NTOK + c0, [[NTOK, 128], [1, 512]]), in_=xnb.ap()),
                    r=["xnb"], w=["xn_d"])
                ct.op("V", lambda h, ys=ys: h.tensor_tensor(ys, ys, PB[4][:, :], AL.mult),
                      r=[xtg, "pb4"], w=[xtg])
                ct.op("V", lambda h, ys=ys: h.tensor_tensor(ys, ys, PB[5][:, :], AL.subtract),
                      r=[xtg, "pb5"], w=[xtg])
                ct.op("Y", lambda h, ys=ys, oc=oc: h.dma_start(
                    out=AP(x2dst.tensor, oc * 128 * NTOK + col0, [[NTOK, 128], [1, 512]]), in_=ys),
                    r=[xtg], w=["x2_d"])

        for jj in range(NT):
            p2_tile(jj)
        ct.barrier()

        if stop == 12:
            ct.op("Y", lambda h: h.dma_start(out=out_d, in_=y2_d), r=["y2_d"], w=["x_out"])
            ct.barrier()
            ct.emit(block, sems)
            return nc

        if stop == 2:
            ct.emit(block, sems)
            return nc

        def p3_scan():
            ct.op("G", lambda h: h.collective_compute(
                "AllGather", AL.bypass, replica_groups=[list(range(NCORES))],
                ins=[xn_d.opt()], outs=[xn_g.opt()]), r=["xn_d"], w=["xn_g"])
            def extract_op(h):
                cid = nc.partition_id([mybir.EngineType.Pool])
                src = AP(xn_g.tensor, cid * (128 * NTOK),
                         [[NTOK, 128], [C * NTOK, 8], [T, 2], [1, T]])
                return h.dma_start(out=xsc.ap(0, [[2 * T, 8], [T, 2], [1, T]]), in_=src)
            ct.op("GD", extract_op, r=["xn_g", "cid"], w=["xsc"])

            ct.op("V", lambda h: h.memset(stS.ap(1024, [[1, 16]]), 0.0), w=["st_i"])
            NW = T // 32
            for w in range(NW):
                bkA, tgA = PB[w % 2], f"pb{w % 2}"
                bkB, tgB = PB[4 + w % 2], f"pb{4 + w % 2}"
                ct.op("T", lambda h, w=w, b=bkA: h.matmul(
                    b[:, 0:256], bbd.ap(), xsc.ap(w * 32, [[1, 32], [T, 8]]), start=True, stop=False),
                    r=["xsc", "bbd"], w=[tgA])
                ct.op("T", lambda h, w=w, b=bkB: h.matmul(
                    b[:, 0:256], bbd.ap(), xsc.ap(w * 32 + 8 * T, [[1, 32], [T, 8]]), start=True, stop=False),
                    r=["xsc", "bbd"], w=[tgB])
                for t_in in range(32):
                    t = w * 32 + t_in
                    slot = t % 64
                    for grp, bk, tg, off in ((0, bkA, tgA, 0), (1, bkB, tgB, 8)):
                        if t == 0:
                            prev = stS.ap(1024 + off, [[1, 8]])
                            ptag = "st_i"
                        else:
                            prev = stS.ap(((t - 1) % 64) * 16 + off, [[1, 8]])
                            ptag = f"st{grp}{((t - 1) % 64) // 32}"
                        ct.op("T", lambda h, b=bk, ti=t_in, pv=prev: h.matmul(
                            AP(b, ti * 8, [[512, 128], [1, 8]]), abd.ap(), pv, start=False, stop=True),
                            r=[ptag, "abd"], w=[tg])
                        ct.op("S", lambda h, b=bk, ti=t_in, sl=slot, off=off: h.activation(
                            stS.ap(sl * 16 + off, [[1, 8]]), AP(b, ti * 8, [[512, 128], [1, 8]]), AF.Tanh),
                            r=[tg], w=[f"st{grp}{slot // 32}"])
                s0 = (w * 32) % 64
                ct.op("T", lambda h, s0=s0: h.matmul(
                    PB[2][:, :], cbd.ap(), stS.ap(s0 * 16, [[1, 512]]), start=True, stop=True),
                    r=[f"st0{s0 // 32}", f"st1{s0 // 32}", "cbd"], w=["pb2"])
                acc = mxacc[(w // 8) % 2]
                atag = f"mxa{(w // 8) % 2}"
                ct.op("V", lambda h, w=w, acc=acc: h.scalar_tensor_tensor(
                    acc.ap((w % 8) * 32, [[1, 32], [256, 16]]),
                    xsc.ap(w * 32, [[1, 32], [T, 16]]), skipb.ap(), PB[2][:, :], AL.mult, AL.add),
                    r=["pb2", "xsc", "skipb", atag], w=[atag])
                if (w + 1) % 8 == 0:
                    wf = w - 7
                    ct.op("Y", lambda h, wf=wf, acc=acc: h.dma_start(
                        out=AP(mx_d.tensor, wf * 32, [[H * T, 128], [T, 16], [1, 256]]),
                        in_=acc.ap(0, [[256, 16], [1, 256]])), r=[atag], w=["mx_d"])
            ct.op("G", lambda h: h.collective_compute(
                "AllGather", AL.bypass, replica_groups=[list(range(NCORES))],
                ins=[mx_d.opt()], outs=[mx_g.opt()]), r=["mx_d"], w=["mx_g"])

        if stop == 14:
            ct.op("GD", lambda h: h.dma_start(out=out_d, in_=xn_d), r=["xn_d"], w=["x_out"])
            ct.barrier()
            ct.emit(block, sems)
            return nc

        p3_scan()
        ct.barrier()

        if stop == 15:
            ct.op("GD", lambda h: h.dma_start(
                out=AP(out_d.tensor, 0, [[NTOK, 128], [1, NTOK]]),
                in_=AP(mx_d.tensor, 0, [[H * T, 128], [1, NTOK]])), r=["mx_d"], w=["x_out"])
            ct.barrier()
            ct.emit(block, sems)
            return nc

        def p3b():
            def extract2_op(h):
                cid = nc.partition_id([mybir.EngineType.Pool])
                src = AP(mx_g.tensor, cid * (2 * T),
                         [[H * T, 128], [128 * H * T, 8], [1, 2 * T]])
                return h.dma_start(out=mxfm.ap(0, [[NTOK, 8], [1, 2 * T]]), in_=src)
            ct.op("GD", extract2_op, r=["mx_g", "cid"], w=["mxfm"])
            ct.op("GD", lambda h: h.dma_start(
                out=woutS.ap(), in_=AP(wout_T.tensor, 0, [[C, 128], [128 * C, 8], [128, 8], [1, 128]])),
                w=["woutw"])
            for jj in range(NT):
                col0 = jj * TILE
                ct.op("Y", lambda h, c0=col0: h.dma_start(
                    out=x2t.ap(), in_=AP(x2_d.tensor, c0, [[NTOK, 128], [128 * NTOK, 8], [1, 512]])),
                    r=["x2t"], w=["x2t"])
                for oc in range(8):
                    bank, btag = PB[2 + oc % 2], f"pb{2 + oc % 2}"
                    for kc in range(8):
                        ct.op("T", lambda h, kc=kc, oc=oc, b=bank, c0=col0, st=(kc == 0), sp=(kc == 7):
                              h.matmul(b[:, :], woutS.ap((kc * 8 + oc) * 128, [[1, 128]]),
                                       mxfm.ap(kc * NTOK + c0, [[1, 512]]), start=st, stop=sp),
                              r=["mxfm", "woutw"], w=[btag])
                    xs = x2t.ap(oc * 512, [[1, 512]])
                    ct.op("V", lambda h, b=bank, xs=xs, oc=oc: h.scalar_tensor_tensor(
                        qt_x.ap(0, [[1, 512]]), b[:, :], boutb.ap(oc, [[1, 1]]), xs, AL.add, AL.add),
                        r=[btag, "x2t", "boutb", "qtmp"], w=["qtmp"])
                    ct.op("V", lambda h, xs=xs: h.tensor_tensor(xs, xs, qt_x.ap(0, [[1, 512]]), AL.add),
                        r=["x2t", "qtmp"], w=["x2t"])
                    ct.op("S", lambda h, xs=xs: h.activation(y2buf.ap(), xs, AF.Square), r=["x2t"], w=["y2b"])
                    ct.op("V", lambda h, xs=xs: h.tensor_copy(ybf.ap(), xs), r=["x2t"], w=["ybf"])
                    ct.op("T", lambda h, st=(oc == 0), sp=(oc == 7): h.matmul(
                        AP(PB[6], 0, [[512, 1], [1, 512]]), ones1.ap(0, [[1, 1]]), ybf.ap(), start=st, stop=sp,
                        tile_position=(0, 0)), r=["ybf", "ones1"], w=["pb6"])
                    ct.op("T", lambda h, st=(oc == 0), sp=(oc == 7): h.matmul(
                        AP(PB[7], 0, [[512, 1], [1, 512]]), ones1.ap(0, [[1, 1]]), y2buf.ap(), start=st, stop=sp,
                        tile_position=(0, 0)), r=["y2b", "ones1"], w=["pb7"])
                    ct.op("Y", lambda h, xs=xs, oc=oc, c0=col0: h.dma_start(
                        out=AP(y3_d.tensor, oc * 128 * NTOK + c0, [[NTOK, 128], [1, 512]]), in_=xs),
                        r=["x2t"], w=["y3_d"])
                ct.op("V", lambda h: h.tensor_copy(strow.app(0, 1, dims=[[1, 512]]), AP(PB[6], 0, [[512, 1], [1, 512]])),
                      r=["pb6"], w=["strow"])
                ct.op("V", lambda h: h.tensor_copy(strow.app(32, 1, dims=[[1, 512]]), AP(PB[7], 0, [[512, 1], [1, 512]])),
                      r=["pb7"], w=["strow"])
                ct.op("Y", lambda h, c0=col0: h.dma_start(
                    out=AP(s_d[2].tensor, c0, [[NTOK, 1], [1, 512]]),
                    in_=strow.app(0, 1, dims=[[1, 512]])), r=["strow"], w=["s3_d"])
                ct.op("Y", lambda h, c0=col0: h.dma_start(
                    out=AP(s_d[2].tensor, NTOK + (c0), [[NTOK, 1], [1, 512]]),
                    in_=strow.app(32, 1, dims=[[1, 512]])), r=["strow"], w=["s3_d"])

        p3b()
        ct.barrier()
        if stop == 13:
            ct.op("Y", lambda h: h.dma_start(out=out_d, in_=y3_d), r=["y3_d"], w=["x_out"])
            ct.barrier()
            ct.emit(block, sems)
            return nc
        lnfix(2, y3_d, s_d[2], cf_d[2], x3_d, to_out=(stop == 3))
        ct.barrier()
        if stop == 3:
            ct.emit(block, sems)
            return nc

        for wdst, wsrc, nm in ((wrS, wr_T, "wrw"), (wkS, wk_T, "wkw"), (wvS, wv_T, "wvw")):
            ct.op("GD", lambda h, d=wdst, s=wsrc: h.dma_start(
                out=d.ap(), in_=AP(s.tensor, 0, [[C, 128], [128 * C, 8], [128, 8], [1, 128]])), w=[nm])

        def p4_tile(jj):
            col0 = jj * TILE
            ct.op("Y", lambda h: h.dma_start(
                out=x3t.ap(), in_=AP(x3_d.tensor, col0, [[NTOK, 128], [128 * NTOK, 8], [1, 512]])),
                r=["x3t"], w=["x3t"])
            if jj % TPS == 0:
                ct.op("Y", lambda h: h.dma_start(
                    out=xpt.ap(1, [[512, 8], [1, 511]]),
                    in_=AP(x3_d.tensor, col0, [[NTOK, 128], [128 * NTOK, 8], [1, 511]])),
                    r=["xpt"], w=["xpt"])
                ct.op("V", lambda h: h.memset(xpt.ap(0, [[512, 8], [1, 1]]), 0.0), r=["xpt"], w=["xpt"])
            else:
                ct.op("Y", lambda h: h.dma_start(
                    out=xpt.ap(), in_=AP(x3_d.tensor, col0 - 1, [[NTOK, 128], [128 * NTOK, 8], [1, 512]])),
                    r=["xpt"], w=["xpt"])
            for oc in range(8):
                ct.op("V", lambda h, oc=oc: h.tensor_scalar(p4tmp.ap(0, [[1, 512]]), xpt.ap(oc * 512, [[1, 512]]),
                                                            omrb.ap(oc, [[1, 1]]), None, AL.mult),
                      r=["xpt", "omrb"], w=["p4t"])
                ct.op("V", lambda h, oc=oc: h.scalar_tensor_tensor(xrS.ap(oc * 512, [[1, 512]]),
                                                                   x3t.ap(oc * 512, [[1, 512]]), murb.ap(oc, [[1, 1]]),
                                                                   p4tmp.ap(0, [[1, 512]]), AL.mult, AL.add),
                      r=["x3t", "murb", "p4t"], w=["xr"])
                ct.op("V", lambda h, oc=oc: h.tensor_scalar(p4tmp.ap(0, [[1, 512]]), xpt.ap(oc * 512, [[1, 512]]),
                                                            omkb.ap(oc, [[1, 1]]), None, AL.mult),
                      r=["xpt", "omkb"], w=["p4t"])
                ct.op("V", lambda h, oc=oc: h.scalar_tensor_tensor(xkS.ap(oc * 512, [[1, 512]]),
                                                                   x3t.ap(oc * 512, [[1, 512]]), mukb.ap(oc, [[1, 1]]),
                                                                   p4tmp.ap(0, [[1, 512]]), AL.mult, AL.add),
                      r=["x3t", "mukb", "p4t"], w=["xk"])
            for oc in range(8):
                for kc in range(8):
                    ct.op("T", lambda h, kc=kc, oc=oc, st=(kc == 0), sp=(kc == 7):
                          h.matmul(PB[0][:, :], wrS.ap((kc * 8 + oc) * 128, [[1, 128]]),
                                   xrS.ap(kc * 512, [[1, 512]]), start=st, stop=sp),
                          r=["xr", "wrw"], w=["pb0"])
                ct.op("S", lambda h, oc=oc: h.activation(rbS.ap(oc * 512, [[1, 512]]), PB[0][:, :], AF.Sigmoid),
                      r=["pb0"], w=["rb"])
                for kc in range(8):
                    ct.op("T", lambda h, kc=kc, oc=oc, st=(kc == 0), sp=(kc == 7):
                          h.matmul(PB[1][:, :], wkS.ap((kc * 8 + oc) * 128, [[1, 128]]),
                                   xkS.ap(kc * 512, [[1, 512]]), start=st, stop=sp),
                          r=["xk", "wkw"], w=["pb1"])
                ct.op("S", lambda h, oc=oc: h.activation(y2buf.ap(), PB[1][:, :], AF.Relu),
                      r=["pb1"], w=["y2b"])
                ct.op("V", lambda h, oc=oc: h.tensor_tensor(kaS.ap(oc * 512, [[1, 512]]), y2buf.ap(), y2buf.ap(), AL.mult),
                      r=["y2b"], w=["ka"])
            for oc in range(8):
                bank, btag = PB[2 + oc % 2], f"pb{2 + oc % 2}"
                for kc in range(8):
                    ct.op("T", lambda h, kc=kc, oc=oc, b=bank, st=(kc == 0), sp=(kc == 7):
                          h.matmul(b[:, :], wvS.ap((kc * 8 + oc) * 128, [[1, 128]]),
                                   kaS.ap(kc * 512, [[1, 512]]), start=st, stop=sp),
                          r=["ka", "wvw"], w=[btag])
                xs = x3t.ap(oc * 512, [[1, 512]])
                ct.op("V", lambda h, b=bank, oc=oc: h.scalar_tensor_tensor(
                    p4tmp.ap(0, [[1, 512]]), b[:, :], 1.0, rbS.ap(oc * 512, [[1, 512]]), AL.mult, AL.mult),
                    r=[btag, "rb", "p4t"], w=["p4t"])
                ct.op("V", lambda h, xs=xs: h.tensor_tensor(xs, xs, p4tmp.ap(0, [[1, 512]]), AL.add),
                      r=["x3t", "p4t"], w=["x3t"])
                ct.op("S", lambda h, xs=xs: h.activation(y2buf.ap(), xs, AF.Square), r=["x3t"], w=["y2b"])
                ct.op("V", lambda h, xs=xs: h.tensor_copy(ybf.ap(), xs), r=["x3t"], w=["ybf"])
                ct.op("T", lambda h, st=(oc == 0), sp=(oc == 7): h.matmul(
                    AP(PB[6], 0, [[512, 1], [1, 512]]), ones1.ap(0, [[1, 1]]), ybf.ap(), start=st, stop=sp,
                    tile_position=(0, 0)), r=["ybf", "ones1"], w=["pb6"])
                ct.op("T", lambda h, st=(oc == 0), sp=(oc == 7): h.matmul(
                    AP(PB[7], 0, [[512, 1], [1, 512]]), ones1.ap(0, [[1, 1]]), y2buf.ap(), start=st, stop=sp,
                    tile_position=(0, 0)), r=["y2b", "ones1"], w=["pb7"])
                ct.op("Y", lambda h, xs=xs, oc=oc: h.dma_start(
                    out=AP(y4_d.tensor, oc * 128 * NTOK + col0, [[NTOK, 128], [1, 512]]), in_=xs),
                    r=["x3t"], w=["y4_d"])
            ct.op("V", lambda h: h.tensor_copy(strow.app(0, 1, dims=[[1, 512]]), AP(PB[6], 0, [[512, 1], [1, 512]])),
                  r=["pb6"], w=["strow"])
            ct.op("V", lambda h: h.tensor_copy(strow.app(32, 1, dims=[[1, 512]]), AP(PB[7], 0, [[512, 1], [1, 512]])),
                  r=["pb7"], w=["strow"])
            ct.op("Y", lambda h, c0=col0: h.dma_start(
                out=AP(s_d[3].tensor, c0, [[NTOK, 1], [1, 512]]),
                in_=strow.app(0, 1, dims=[[1, 512]])), r=["strow"], w=["s4_d"])
            ct.op("Y", lambda h, c0=col0: h.dma_start(
                out=AP(s_d[3].tensor, NTOK + (c0), [[NTOK, 1], [1, 512]]),
                in_=strow.app(32, 1, dims=[[1, 512]])), r=["strow"], w=["s4_d"])

        for jj in range(NT):
            p4_tile(jj)
        ct.barrier()
        lnfix(3, y4_d, s_d[3], cf_d[3], None, to_out=True)
        ct.barrier()

        ct.emit(block, sems)

    return nc


_NC_CACHE = {}


def _get_nc(stop):
    if stop not in _NC_CACHE:
        _NC_CACHE[stop] = build_nc(stop)
    return _NC_CACHE[stop]


def _rope_tables():
    inv_freq = (1.0 / 10000.0 ** (np.arange(0, LAT, 2, dtype=np.float32) / LAT)).astype(np.float32)
    t = np.arange(T, dtype=np.float32)
    freqs = np.outer(t, inv_freq)
    cos = np.repeat(np.cos(freqs), 2, axis=-1)[:, :LAT].astype(np.float32)
    sin = np.repeat(np.sin(freqs), 2, axis=-1)[:, :LAT].astype(np.float32)
    cos4 = np.tile(np.ascontiguousarray(cos.T), (4, 1))
    sin4 = np.tile(np.ascontiguousarray(sin.T), (4, 1))
    return np.ascontiguousarray(cos4), np.ascontiguousarray(sin4)


def kernel(stop=4, **inputs):
    x = inputs["L0_attn_x"]
    cos4, sin4 = _rope_tables()
    shared = {
        "cattn_wT": np.ascontiguousarray(inputs["L0_attn_c_attn_w"].T),
        "cattn_b": np.ascontiguousarray(inputs["L0_attn_c_attn_b"]),
        "q2l_wT": np.ascontiguousarray(inputs["L0_attn_q2l_w"].T),
        "q2l_b": np.ascontiguousarray(inputs["L0_attn_q2l_b"]),
        "kv2l_wT": np.ascontiguousarray(inputs["L0_attn_kv2l_w"].T),
        "kv2l_b": np.ascontiguousarray(inputs["L0_attn_kv2l_b"]),
        "l2o_wT": np.ascontiguousarray(inputs["L0_attn_l2o_w"].T),
        "l2o_b": np.ascontiguousarray(inputs["L0_attn_l2o_b"]),
        "wqk_wT": np.ascontiguousarray(inputs["L0_attn_wqk_w"].T),
        "wqk_b": np.ascontiguousarray(inputs["L0_attn_wqk_b"]),
        "cproj_wT": np.ascontiguousarray(inputs["L0_attn_cproj_w"].T),
        "cproj_b": np.ascontiguousarray(inputs["L0_attn_cproj_b"]),
        "wg_T": np.ascontiguousarray(inputs["L0_ffn_w_gate"].T),
        "wu_T": np.ascontiguousarray(inputs["L0_ffn_w_up"].T),
        "wd_T": np.ascontiguousarray(inputs["L0_ffn_w_down"].T),
        "norm_w": np.ascontiguousarray(inputs["L1_attn_norm_w"]),
        "wout_T": np.ascontiguousarray(inputs["L1_attn_w_out"].T),
        "bout_v": np.ascontiguousarray(inputs["L1_attn_b_out"]),
        "wr_T": np.ascontiguousarray(inputs["L1_ffn_W_r"].T),
        "wk_T": np.ascontiguousarray(inputs["L1_ffn_W_k"].T),
        "wv_T": np.ascontiguousarray(inputs["L1_ffn_W_v"].T),
        "mu_r": np.ascontiguousarray(inputs["L1_ffn_mu_r"]),
        "mu_k": np.ascontiguousarray(inputs["L1_ffn_mu_k"]),
        "cos4": cos4,
        "sin4": sin4,
        "altsgn": np.tile(np.array([-1.0, 1.0], dtype=np.float32), 64),
    }
    a_full = inputs["L1_attn_a"]
    b_full = inputs["L1_attn_b"]
    c_full = inputs["L1_attn_c"]
    skip_full = inputs["L1_attn_skip"]
    in_maps = []
    for c in range(NCORES):
        m = dict(shared)
        xc = x[2 * c:2 * c + 2]
        m["x_in"] = np.ascontiguousarray(xc.transpose(2, 0, 1).reshape(C, NTOK))
        m["a_sh"] = np.ascontiguousarray(a_full[2 * c:2 * c + 2])
        m["b_sh"] = np.ascontiguousarray(b_full[2 * c:2 * c + 2])
        m["c_sh"] = np.ascontiguousarray(c_full[2 * c:2 * c + 2])
        m["skip_sh"] = np.ascontiguousarray(skip_full[2 * c:2 * c + 2].reshape(128))
        m["core_id_in"] = np.array([[c]], dtype=np.int32)
        in_maps.append(m)
    nc = _get_nc(stop)
    res = run_bass_kernel_spmd(nc, in_maps, core_ids=list(range(NCORES)))
    global LAST_RES
    LAST_RES = res
    outs = []
    for c in range(NCORES):
        o = res.results[c]["out_d"]
        outs.append(o.reshape(C, SEQ, T).transpose(1, 2, 0))
    return np.ascontiguousarray(np.concatenate(outs, axis=0))


if __name__ == "__main__":
    pass


# version 14
# speedup vs baseline: 1.0748x; 1.0307x over previous
LAST_RES = None
"""Single-launch raw-Bass megakernel for mk_seq_postln_d2 on 8 TRN2 NeuronCores.

Module (depth 2, post-LN): x -> LN(x + MLA(x)) -> LN(. + fp4FFN(.)) ->
LN(. + tanh-scan mixer(.)) -> LN(. + sq-relu FFN(.)).   B=16, T=2048, C=1024.

Per core: 2 sequences (batch shard).  Feature-major stream [C -> 8x128 partitions,
4096 tokens free].  Stream spills to HBM between phases (compute-bound module, HBM
has ~35x headroom).  bf16 matmul operands / fp32 PSUM accumulation and elementwise.
LayerNorms run in dedicated LNFIX phases (isolates the sqrt activation table).
The L1 tanh-scan is head-resharded across cores via AllGather (2 heads x 16 seqs per
core) so each scan step is one 128-contraction matmul + one tanh.
fp4 quant-dequant: exact fp32 rounding ladder in DVE 32x32-transposed space.
"""
import numpy as np
import ml_dtypes
import concourse.bass as bass
import concourse.mybir as mybir
from concourse.bass_utils import run_bass_kernel_spmd

F32 = mybir.dt.float32
BF16 = mybir.dt.bfloat16
I32 = mybir.dt.int32
AL = mybir.AluOpType
AF = mybir.ActivationFunctionType
AX = mybir.AxisListType.X
AP = bass.AP

NCORES = 8
SEQ, T, C, NCH = 2, 2048, 1024, 8
H, HD, LAT = 16, 64, 32
IFF, ICH = 2816, 22
NTOK = SEQ * T
TILE = 512
NT = NTOK // TILE
TPS = T // TILE
KTN = T // 128
EPS_LN, EPS_RMS = 1e-5, 1e-4
C_HALF = float(1.5 * 2 ** 22)
C_INT = float(1.5 * 2 ** 23)
ISQL = float(LAT) ** -0.5


class Ctx:
    NY, NGD = 12, 4
    ENGS = ("T", "V", "S", "G") + tuple(f"Y{i}" for i in range(12)) + tuple(f"GD{i}" for i in range(4))

    def __init__(self, nc):
        self.nc = nc
        self.count = {e: 0 for e in self.ENGS}
        self.ops = {e: [] for e in ("T", "V", "S", "G", "Y")}
        self.last_w = {}
        self.readers = {}
        self.rr = {"Y": 0, "GD": 0}

    def op(self, eng, fn, r=(), w=()):
        q = eng in ("Y", "GD")
        if q:
            stream = "Y" if eng == "Y" else "G"
            sem = f"{eng}{self.rr[eng]}"
            self.rr[eng] = (self.rr[eng] + 1) % (self.NY if eng == "Y" else self.NGD)
        else:
            stream = eng
            sem = eng
        need = {}
        for tag in r:
            lw = self.last_w.get(tag)
            if lw:
                need[lw[0]] = max(need.get(lw[0], 0), lw[1])
        for tag in w:
            lw = self.last_w.get(tag)
            if lw and (q or lw[0] != sem):
                need[lw[0]] = max(need.get(lw[0], 0), lw[1])
            for (e2, v) in self.readers.get(tag, ()):
                if q or e2 != sem:
                    need[e2] = max(need.get(e2, 0), v)
        if q and self.count[sem] > 0:
            need[sem] = max(need.get(sem, 0), self.count[sem])
        inc = 16 if q else 1
        self.count[sem] += inc
        val = self.count[sem]
        self.ops[stream].append((need, sem, fn, val))
        for tag in w:
            self.last_w[tag] = (sem, val)
            self.readers[tag] = []
        for tag in r:
            self.readers.setdefault(tag, []).append((sem, val))

    def barrier(self):
        snap = dict(self.count)
        for stream in self.ops:
            self.ops[stream].append(("B", snap))
        self.last_w = {}
        self.readers = {}

    def emit(self, block, sems):
        def body_of(stream):
            def body(h):
                wm = {e: 0 for e in self.ENGS}
                for item in self.ops[stream]:
                    if item[0] == "B":
                        for e, v in item[1].items():
                            if v > wm[e]:
                                h.wait_ge(sems[e], v)
                                wm[e] = v
                        continue
                    need, eng, fn, val = item
                    for e, v in need.items():
                        if v > wm[e]:
                            h.wait_ge(sems[e], v)
                            wm[e] = v
                    inst = fn(h)
                    if inst is not None:
                        inst.then_inc(sems[eng], 16 if eng not in ("T", "V", "S", "G") else 1)
            return body
        block.sync(body_of("Y"))
        block.tensor(body_of("T"))
        block.vector(body_of("V"))
        block.scalar(body_of("S"))
        block.gpsimd(body_of("G"))


class Buf:
    def __init__(self, slab, slab_w, off, n, dt, slab_dt=BF16):
        self.slab, self.sw, self.off, self.dt = slab, slab_w, off, dt
        self.native = (dt == slab_dt) or (mybir.dt.size(dt) == mybir.dt.size(slab_dt))
        self.esz = 1 if self.native else 2
        self.n = n // self.esz

    def _mk(self, par_pair, o, dims):
        if dims is None:
            dims = [[1, self.n]]
        if self.native:
            a = AP(self.slab, self.off + par_pair[2] + o, [list(par_pair[:2])] + [list(d) for d in dims])
            if a.dtype != self.dt:
                a = a.bitcast(self.dt)
            return a
        nd = [[s * 2, c] for s, c in dims[:-1]]
        s_in, c_in = dims[-1]
        assert s_in == 1, f"fp32-on-bf16 innermost must be contiguous, got step {s_in}"
        nd.append([1, 2 * c_in])
        a = AP(self.slab, self.off + par_pair[2] + o * 2, [list(par_pair[:2])] + nd)
        return a.bitcast(F32)

    def ap(self, o=0, dims=None):
        return self._mk((self.sw, 128, 0), o, dims)

    def app(self, p0, np_, o=0, dims=None):
        return self._mk((self.sw, np_, p0 * self.sw), o, dims)


def build_nc(stop=4):
    nc = bass.Bass("TRN2", target_bir_lowering=False, debug=False)

    def din(name, shape, dt=F32):
        return nc.dram_tensor(name, shape, dt, kind="ExternalInput").ap()

    x_in = din("x_in", [C, NTOK])
    cattn_wT = din("cattn_wT", [C, 3 * C])
    cattn_b = din("cattn_b", [3 * C])
    q2l_wT = din("q2l_wT", [HD, LAT])
    q2l_b = din("q2l_b", [LAT])
    kv2l_wT = din("kv2l_wT", [HD, LAT])
    kv2l_b = din("kv2l_b", [LAT])
    l2o_wT = din("l2o_wT", [LAT, HD])
    l2o_b = din("l2o_b", [HD])
    wqk_wT = din("wqk_wT", [LAT, LAT])
    wqk_b = din("wqk_b", [LAT])
    cproj_wT = din("cproj_wT", [C, C])
    cproj_b = din("cproj_b", [C])
    wg_T = din("wg_T", [C, IFF])
    wu_T = din("wu_T", [C, IFF])
    wd_T = din("wd_T", [IFF, C])
    norm_w = din("norm_w", [C])
    a_sh = din("a_sh", [2, 64, 64])
    b_sh = din("b_sh", [2, 64, 64])
    c_sh = din("c_sh", [2, 64, 64])
    skip_sh = din("skip_sh", [128])
    wout_T = din("wout_T", [C, C])
    bout_v = din("bout_v", [C])
    wr_T = din("wr_T", [C, C])
    wk_T = din("wk_T", [C, C])
    wv_T = din("wv_T", [C, C])
    mu_r = din("mu_r", [C])
    mu_k = din("mu_k", [C])
    altsgn = din("altsgn", [128])
    cos4 = din("cos4", [128, T])
    sin4 = din("sin4", [128, T])
    core_id_in = din("core_id_in", [1, 1], I32)

    out_d = nc.dram_tensor("out_d", [C, NTOK], F32, kind="ExternalOutput").ap()


    def dsc(name, shape, dt=F32, **kw):
        return nc.dram_tensor(name, shape, dt, kind="Internal", **kw).ap()

    y1_d = dsc("y1_d", [C, NTOK])
    x1_d = dsc("x1_d", [C, NTOK])
    y2_d = dsc("y2_d", [C, NTOK])
    x2_d = dsc("x2_d", [C, NTOK])
    y3_d = dsc("y3_d", [C, NTOK])
    x3_d = dsc("x3_d", [C, NTOK])
    y4_d = dsc("y4_d", [C, NTOK])
    s_d = [dsc(f"s{k}_d", [2, NTOK]) for k in range(4)]
    cf_d = [dsc(f"cf{k}_d", [4, NTOK], BF16) for k in range(4)]
    cfx_d = dsc("cfx_d", [4, NTOK], BF16)
    WSEG = 1024 * 352
    WDSEG = 2816 * 128
    wgs_d = dsc("wgs_d", [C, 352])
    wus_d = dsc("wus_d", [C, 352])
    wds_d = dsc("wds_d", [IFF, 128])
    wqin_d = dsc("wqin_d", [2 * WSEG + WDSEG], BF16)
    wq_g = dsc("wq_g", [NCORES * (2 * WSEG + WDSEG)], BF16, addr_space="Shared")
    wgq_d = dsc("wgq_d", [C, IFF], BF16)
    wuq_d = dsc("wuq_d", [C, IFF], BF16)
    wdq_d = dsc("wdq_d", [IFF, C], BF16)
    xn_d = dsc("xn_d", [C, NTOK], BF16)
    xn_g = dsc("xn_g", [NCORES * C, NTOK], BF16, addr_space="Shared")
    mx_d = dsc("mx_d", [128, H * T], BF16)
    mx_g = dsc("mx_g", [NCORES * 128, H * T], BF16, addr_space="Shared")

    ct = Ctx(nc)
    sems = {e: nc.semaphore(f"sem{e}").__enter__() for e in Ctx.ENGS}
    PF = 2300
    PW = 8000
    WB = 91648

    with (
        nc.sbuf_tensor("persf", [128, PF], F32) as persf_t,
        nc.sbuf_tensor("pers", [128, PW], BF16) as pers_t,
        nc.sbuf_tensor("arena", [128, WB], BF16) as arena_t,
        nc.psum_tensor("pb0", [128, 512], F32) as _p0,
        nc.psum_tensor("pb1", [128, 512], F32) as _p1,
        nc.psum_tensor("pb2", [128, 512], F32) as _p2,
        nc.psum_tensor("pb3", [128, 512], F32) as _p3,
        nc.psum_tensor("pb4", [128, 512], F32) as _p4,
        nc.psum_tensor("pb5", [128, 512], F32) as _p5,
        nc.psum_tensor("pb6", [128, 512], F32) as _p6,
        nc.psum_tensor("pb7", [128, 512], F32) as _p7,
        nc.Block() as block,
    ):
        pass_sems = sems
        PB = [_p0, _p1, _p2, _p3, _p4, _p5, _p6, _p7]

        def PF32(off, n, dt=F32):
            return Buf(persf_t, PF, off, n, dt, slab_dt=F32)

        def P(off, n, dt=BF16):
            return Buf(pers_t, PW, off, n, dt)

        def A(off, n, dt=BF16):
            return Buf(arena_t, WB, off, n, dt)

        cattnb = PF32(0, 24)
        cprojb = PF32(24, 8)
        boutb = PF32(32, 8)
        normwb = PF32(40, 8)
        murb = PF32(48, 8)
        mukb = PF32(56, 8)
        omrb = PF32(64, 8)
        omkb = PF32(72, 8)
        q2lb4 = PF32(80, 1)
        q2lbr4 = PF32(81, 1)
        kvb4 = PF32(82, 1)
        kvbr4 = PF32(83, 1)
        wqkb4 = PF32(84, 1)
        l2ob2 = PF32(85, 1)
        skipb = PF32(86, 1)
        cidb = PF32(87, 1, I32)
        kvrow = PF32(88, 32)
        kvbias = PF32(120, 128)
        abdf = PF32(248, 128)
        dn = PF32(376, 1024)
        strow = PF32(1400, 512)
        statw = PF32(1912, 160)
        qt_am = PF32(2072, 32)
        qt_inv = PF32(2104, 32)
        qt_s6 = PF32(2136, 32)
        altsg = PF32(2170, 1)
        constc = PF32(2168, 2)
        qtB_am = PF32(2172, 32)
        qtB_inv = PF32(2204, 32)
        qtB_s6 = PF32(2236, 32)

        nc.const_aps.register(F32, EPS_LN, constc.ap(0, [[1, 1]]))
        nc.const_aps.register(F32, EPS_RMS, constc.ap(1, [[1, 1]]))
        q2lw = P(0, 32)
        q2lrw = P(32, 32)
        kv2lw = P(64, 32)
        kv2lrw = P(96, 32)
        wqkw = P(128, 32)
        l2ow = P(160, 64)
        cosb = P(224, T)
        sinb = P(2272, T)
        abd = P(4320, 128)
        bbd = P(4448, 128)
        cbd = P(4576, 128)
        ones1 = P(4704, 2)
        onesr = P(4706, 128)
        dnh = P(5858, 1024)
        dnl = P(6882, 1024)
        y2buf = P(4834, 512)
        ybf = P(5346, 512)

        cattn = A(0, 24576)
        qT = A(24576, 16384)
        kT = A(40960, 16384)
        vSB = A(57344, 17888)
        qkvS = A(75232, 12288)
        xbS = A(87520, 4096)
        rt1 = A(87520, 512)
        rt2 = A(88544, 512)

        def voff(s2, hh2, kt2):
            return (s2 * H + hh2) * 559 + (0 if kt2 == 0 else 64 + (kt2 - 1) * 33)
        qt_x = A(75232, 1024, F32)
        qt_t = A(76256, 1024, F32)
        qt_xn = A(77280, 1024, F32)
        qt_u = A(78304, 1024, F32)
        qt_r2 = A(79328, 1024, F32)
        qtB_t = A(56832, 1024, F32)
        qtB_xn = A(57856, 1024, F32)
        qtB_u = A(58880, 1024, F32)
        qtB_r2 = A(59904, 1024, F32)
        qtB_r1 = A(60928, 512)
        qtB_a5 = A(61440, 512)
        qtB_sg = A(61952, 512)
        qtB_t2 = A(62464, 512)
        qtB_qs = A(62976, 512)
        qtB_qT = A(63488, 512)

        qt_r1 = A(80352, 512)
        qt_a5 = A(80864, 512)
        qt_sg = A(81376, 512)
        qt_t2 = A(81888, 512)
        qt_qs = A(82400, 512)
        qt_qT = A(82912, 512)
        qt_q = A(83424, 512)
        QTS = [
            {"t": qt_t, "xn": qt_xn, "u": qt_u, "r2": qt_r2, "am": qt_am, "inv": qt_inv,
             "s6": qt_s6, "r1": qt_r1, "a5": qt_a5, "sg": qt_sg, "t2": qt_t2, "qs": qt_qs, "qT": qt_qT},
            {"t": qtB_t, "xn": qtB_xn, "u": qtB_u, "r2": qtB_r2, "am": qtB_am, "inv": qtB_inv,
             "s6": qtB_s6, "r1": qtB_r1, "a5": qtB_a5, "sg": qtB_sg, "t2": qtB_t2, "qs": qtB_qs, "qT": qtB_qT},
        ]
        x0S = A(0, 8192, F32)
        cprojS = A(8192, 8192)
        eS = A(16384, 8192)
        yrawS = A(16384, 8192, F32)
        yvS = A(83936, 2048)
        attnS = A(85984, 4096)

        x1t = A(0, 8192, F32)
        x1tB = A(64000, 8192, F32)
        x1tp = [x1t, x1tB]
        xqS = A(8192, 4096)
        htS = A(12288, 22528, F32)
        hqS = A(34816, 11264)
        wgbuf = [A(46080 + i * 1024, 1024) for i in range(2)]
        wubuf = [A(48128 + i * 1024, 1024) for i in range(2)]
        wdbuf = [A(50176 + i * 2816, 2816) for i in range(2)]
        silS = A(55808, 512)
        xnb = A(56320, 512)

        lyt = A(0, 8192, F32)
        lcf = A(8192, 4096)
        lxn = A(12288, 4096)

        xsc = A(0, 32768)
        stS = A(32768, 1024 + 16)
        mxacc = [A(33808 + i * 4096, 4096) for i in range(2)]
        mxfm = A(0, 32768)
        woutS = A(32768, 8192)
        x2t = A(40960, 8192, F32)

        wrS = A(0, 8192)
        wkS = A(8192, 8192)
        wvS = A(16384, 8192)
        x3t = A(24576, 8192, F32)
        xpt = A(32768, 8192, F32)
        xrS = A(40960, 4096)
        xkS = A(45056, 4096)
        rbS = A(49152, 4096)
        kaS = A(53248, 4096)
        p4tmp = A(57344, 1024, F32)

        def init_phase():
            ct.op("Y", lambda h: h.dma_start(out=cattnb.ap(), in_=AP(cattn_b.tensor, 0, [[1, 128], [128, 24]]),
                                             allow_slow_non_contiguous=True), w=["cattnb"])
            for dst, src, nm in ((cprojb, cproj_b, "cprojb"), (boutb, bout_v, "boutb"),
                                 (normwb, norm_w, "normwb"), (murb, mu_r, "murb"), (mukb, mu_k, "mukb")):
                ct.op("Y", lambda h, d=dst, s=src: h.dma_start(out=d.ap(), in_=AP(s.tensor, 0, [[1, 128], [128, 8]]),
                                                               allow_slow_non_contiguous=True), w=[nm])
            ct.op("Y", lambda h: h.dma_start(out=skipb.ap(), in_=AP(skip_sh.tensor, 0, [[1, 128], [1, 1]])), w=["skipb"])
            ct.op("Y", lambda h: h.dma_start(out=cidb.app(0, 1, dims=[[1, 1]]), in_=core_id_in), w=["cid"])
            for half in range(2):
                ct.op("GD", lambda h, p=64 * half: h.dma_start(out=q2lw.app(p, 64), in_=q2l_wT), w=["q2lw"])
                ct.op("GD", lambda h, p=64 * half: h.dma_start(out=kv2lw.app(p, 64), in_=kv2l_wT), w=["kv2lw"])
                ct.op("GD", lambda h, p=64 * half: h.dma_start(out=l2ow.app(p, 32, dims=[[1, 64]]), in_=l2o_wT), w=["l2ow"])
            for g in range(4):
                ct.op("GD", lambda h, p=32 * g: h.dma_start(out=wqkw.app(p, 32), in_=wqk_wT), w=["wqkw"])
            ct.op("GD", lambda h: h.dma_start(out=cosb.ap(), in_=cos4), w=["cosb"])
            ct.op("GD", lambda h: h.dma_start(out=sinb.ap(), in_=sin4), w=["sinb"])
            ct.op("GD", lambda h: h.dma_start(
                out=cattn.ap(), in_=AP(cattn_wT.tensor, 0, [[3 * C, 128], [128 * 3 * C, 8], [128, 24], [1, 128]])),
                w=["cattn"])
            for g in range(4):
                ct.op("Y", lambda h, p=32 * g: h.dma_start(out=q2lb4.app(p, 32, dims=[[1, 1]]), in_=AP(q2l_b.tensor, 0, [[1, 32], [1, 1]])), w=["q2lb4"])
                ct.op("Y", lambda h, p=32 * g: h.dma_start(out=kvb4.app(p, 32, dims=[[1, 1]]), in_=AP(kv2l_b.tensor, 0, [[1, 32], [1, 1]])), w=["kvb4"])
                ct.op("Y", lambda h, p=32 * g: h.dma_start(out=wqkb4.app(p, 32, dims=[[1, 1]]), in_=AP(wqk_b.tensor, 0, [[1, 32], [1, 1]])), w=["wqkb4"])
            for half in range(2):
                ct.op("Y", lambda h, p=64 * half: h.dma_start(out=l2ob2.app(p, 64, dims=[[1, 1]]), in_=AP(l2o_b.tensor, 0, [[1, 64], [1, 1]])), w=["l2ob2"])
            ct.op("V", lambda h: h.memset(ones1.ap(), 1.0), w=["ones1"])
            ct.op("V", lambda h: h.memset(constc.ap(0, [[1, 1]]), EPS_LN), w=["constc"])
            ct.op("V", lambda h: h.memset(constc.ap(1, [[1, 1]]), EPS_RMS), w=["constc"])
            ct.op("V", lambda h: h.memset(onesr.ap(0, [[1, 128]]), 1.0), w=["onesr"])
            ct.op("V", lambda h: h.memset(dn.ap(), 1.0), w=["dn"])
            for dst, src, nm, nmr in ((q2lrw, q2lw, "q2lw", "q2lrw"), (kv2lrw, kv2lw, "kv2lw", "kv2lrw")):
                ct.op("V", lambda h, d=dst, s=src: h.tensor_scalar_mul(
                    d.ap(0, [[2, 16]]), s.ap(1, [[2, 16]]), -1.0), r=[nm], w=[nmr])
                ct.op("V", lambda h, d=dst, s=src: h.tensor_scalar_mul(
                    d.ap(1, [[2, 16]]), s.ap(0, [[2, 16]]), 1.0), r=[nm], w=[nmr])
            ct.op("Y", lambda h: h.dma_start(out=altsg.ap(), in_=AP(altsgn.tensor, 0, [[1, 128], [1, 1]])), w=["altsg"])
            pair_mask = [i + 1 if i % 2 == 0 else i - 1 for i in range(32)]
            for dst, src, nm, nmr in ((q2lbr4, q2lb4, "q2lb4", "q2lbr4"), (kvbr4, kvb4, "kvb4", "kvbr4")):
                ct.op("V", lambda h, d=dst, s=src: h.stream_shuffle(d.ap(), s.ap(), pair_mask), r=[nm], w=[nmr])
                ct.op("V", lambda h, d=dst: h.tensor_scalar(d.ap(), d.ap(), altsg.ap(), None, AL.mult),
                      r=[nmr, "altsg"], w=[nmr])
            ct.op("V", lambda h: h.transpose(
                kvrow.app(0, 32, dims=[[1, 32]]),
                AP(persf_t, kvb4.off, [[PF, 32], [0, 32]])), r=["kvb4"], w=["kvrow"])
            ct.op("V", lambda h: h.tensor_copy(dnh.app(0, 1, dims=[[1, 32]]), kvrow.app(0, 1, dims=[[1, 32]])),
                  r=["kvrow"], w=["dnh"])
            ct.op("V", lambda h: h.tensor_tensor(dnl.app(0, 1, dims=[[1, 32]]), kvrow.app(0, 1, dims=[[1, 32]]),
                                                 dnh.app(0, 1, dims=[[1, 32]]), AL.subtract),
                  r=["kvrow", "dnh"], w=["dnl"])
            for lohi, buf in ((0, dnh), (1, dnl)):
                ct.op("T", lambda h, b2=buf, st=(lohi == 0): h.matmul(
                    AP(PB[0], 0, [[512, 128], [1, 32]]), onesr.app(0, 1, dims=[[1, 128]]),
                    b2.app(0, 1, dims=[[1, 32]]), start=st, stop=(lohi == 1), tile_position=(0, 0)),
                    r=["dnh", "dnl", "onesr"], w=["pb0"])
            for rep in range(4):
                ct.op("V", lambda h, rp=rep: h.tensor_copy(
                    kvbias.ap(32 * rp, [[1, 32]]), AP(PB[0], 0, [[512, 128], [1, 32]])),
                    r=["pb0"], w=["kvbias"])
            ct.op("V", lambda h: h.memset(abdf.ap(), 0.0), w=["abdf"])
            ct.op("GD", lambda h: h.dma_start(out=abdf.app(0, 64, dims=[[1, 64]]), in_=a_sh[0]), w=["abdf"])
            ct.op("GD", lambda h: h.dma_start(out=abdf.app(64, 64, o=64, dims=[[1, 64]]), in_=a_sh[1]), w=["abdf"])
            ct.op("S", lambda h: h.activation(abd.ap(), abdf.ap(), AF.Tanh), r=["abdf"], w=["abd"])
            for wbd, wsrc, nm in ((bbd, b_sh, "bbd"), (cbd, c_sh, "cbd")):
                ct.op("V", lambda h, d=wbd: h.memset(d.ap(), 0.0), w=[nm])
                ct.op("GD", lambda h, d=wbd, s=wsrc: h.dma_start(out=d.app(0, 64, dims=[[1, 64]]), in_=s[0]), w=[nm])
                ct.op("GD", lambda h, d=wbd, s=wsrc: h.dma_start(out=d.app(64, 64, o=64, dims=[[1, 64]]), in_=s[1]), w=[nm])
            ct.op("V", lambda h: h.tensor_scalar(omrb.ap(), murb.ap(), -1.0, 1.0, AL.mult, AL.add), r=["murb"], w=["omrb"])
            ct.op("V", lambda h: h.tensor_scalar(omkb.ap(), mukb.ap(), -1.0, 1.0, AL.mult, AL.add), r=["mukb"], w=["omkb"])
            ct.op("V", lambda h: h.memset(vSB.ap(), 0.0), w=["vSB"])
            for s2 in range(SEQ):
                for hh2 in range(H):
                    base = (s2 * H + hh2) * 559
                    ct.op("V", lambda h, b2=base: h.memset(vSB.ap(b2 + 32, [[1, 1]]), 1.0), w=["vSB"])
                    ct.op("V", lambda h, b2=base: h.memset(vSB.ap(b2 + 96, [[33, 15], [1, 1]]), 1.0), w=["vSB"])

        init_phase()
        ct.barrier()

        def qstage_a(src_ap, n, qs_, r_extra=()):
            nb = n // 16
            B = QTS[qs_]
            tg = f"qtmp{qs_}"
            ct.op("V", lambda h: h.transpose(B["t"].ap(0, [[1, n]]), src_ap), r=list(r_extra) + [tg], w=[tg])
            ct.op("V", lambda h: h.tensor_reduce(B["am"].ap(0, [[1, nb]]), B["t"].ap(0, [[16, nb], [1, 16]]),
                                                 AX, AL.max, apply_absolute_value=True), r=[tg], w=[tg])
            ct.op("V", lambda h: h.tensor_scalar(B["s6"].ap(0, [[1, nb]]), B["am"].ap(0, [[1, nb]]),
                                                 1.0 / 6.0, 1e-12, AL.mult, AL.max), r=[tg], w=[tg])
            ct.op("V", lambda h: h.reciprocal(B["inv"].ap(0, [[1, nb]]), B["s6"].ap(0, [[1, nb]])), r=[tg], w=[tg])
            ct.op("V", lambda h: h.tensor_tensor(B["xn"].ap(0, [[16, nb], [1, 16]]), B["t"].ap(0, [[16, nb], [1, 16]]),
                                                 B["inv"].ap(0, [[1, nb], [0, 16]]), AL.mult), r=[tg], w=[tg])

        def qstage_s(n, qs_):
            B = QTS[qs_]
            tg = f"qtmp{qs_}"
            sg_t = f"qsg{qs_}"
            ct.op("S", lambda h: h.activation(B["u"].ap(0, [[1, n]]), B["xn"].ap(0, [[1, n]]), AF.Abs),
                  r=[tg], w=[tg])
            ct.op("S", lambda h: h.activation(B["sg"].ap(0, [[1, n]]), B["xn"].ap(0, [[1, n]]), AF.Sign), r=[tg], w=[sg_t])

        def qstage_b(dst_ap, n, qs_, w_extra=()):
            nb = n // 16
            B = QTS[qs_]
            tg = f"qtmp{qs_}"
            sg_t = f"qsg{qs_}"
            ct.op("V", lambda h: h.tensor_scalar(B["r2"].ap(0, [[1, n]]), B["u"].ap(0, [[1, n]]), 2.0, C_HALF, AL.min, AL.add),
                  r=[tg], w=[tg])
            ct.op("V", lambda h: h.tensor_scalar(B["r1"].ap(0, [[1, n]]), B["r2"].ap(0, [[1, n]]), C_HALF, None, AL.subtract),
                  r=[tg], w=[tg])
            ct.op("V", lambda h: h.tensor_scalar(B["r2"].ap(0, [[1, n]]), B["u"].ap(0, [[1, n]]), 2.0, 4.0, AL.max, AL.min),
                  r=[tg], w=[tg])
            ct.op("V", lambda h: h.tensor_scalar(B["r2"].ap(0, [[1, n]]), B["r2"].ap(0, [[1, n]]), C_INT, C_INT, AL.add, AL.subtract),
                  r=[tg], w=[tg])
            ct.op("V", lambda h: h.tensor_scalar(B["a5"].ap(0, [[1, n]]), B["u"].ap(0, [[1, n]]), 5.0, 2.0, AL.is_gt, AL.mult),
                  r=[tg], w=[tg])
            ct.op("V", lambda h: h.tensor_tensor(B["t2"].ap(0, [[1, n]]), B["r1"].ap(0, [[1, n]]), B["r2"].ap(0, [[1, n]]), AL.add),
                  r=[tg], w=[tg])
            ct.op("V", lambda h: h.tensor_tensor(B["t2"].ap(0, [[1, n]]), B["t2"].ap(0, [[1, n]]), B["a5"].ap(0, [[1, n]]), AL.add),
                  r=[tg], w=[tg])
            ct.op("V", lambda h: h.scalar_tensor_tensor(B["qs"].ap(0, [[1, n]]), B["t2"].ap(0, [[1, n]]), 2.0,
                                                        B["sg"].ap(0, [[1, n]]), AL.subtract, AL.mult),
                  r=[tg, sg_t], w=[tg])
            ct.op("V", lambda h: h.tensor_tensor(B["qT"].ap(0, [[16, nb], [1, 16]]), B["qs"].ap(0, [[16, nb], [1, 16]]),
                                                 B["s6"].ap(0, [[1, nb], [0, 16]]), AL.mult), r=[tg], w=[tg])
            ct.op("V", lambda h: h.transpose(dst_ap, B["qT"].ap(0, [[1, n]])), r=[tg], w=list(w_extra) + [tg])

        def quant_block(src_ap, dst_ap, n, r_extra=(), w_extra=(), qs_=0):
            qstage_a(src_ap, n, qs_, r_extra=r_extra)
            qstage_s(n, qs_)
            qstage_b(dst_ap, n, qs_, w_extra=w_extra)

        lnw0 = strow.app(0, 1, dims=[[1, 512]])
        lnw1 = dn.app(0, 1, o=0, dims=[[1, 512]])
        lnw2 = dn.app(0, 1, o=512, dims=[[1, 512]])
        rhi = ybf.app(0, 1, dims=[[1, 512]])
        rlo = y2buf.app(0, 1, dims=[[1, 512]])
        nhi = dnh.app(0, 1, o=0, dims=[[1, 512]])
        nlo = dnh.app(0, 1, o=512, dims=[[1, 512]])

        def ln_coef(rms=False):
            ct.op("V", lambda h: h.tensor_scalar_mul(lnw0, lnw0, 1.0 / C), r=["strow"], w=["strow"])
            ct.op("V", lambda h: h.tensor_scalar_mul(lnw1, lnw1, 1.0 / C), r=["strow", "dn"], w=["dn"])
            ct.op("V", lambda h: h.tensor_tensor(lnw2, lnw0, lnw0, AL.mult), r=["strow", "dn"], w=["dn"])
            ct.op("V", lambda h: h.tensor_tensor(lnw1, lnw1, lnw2, AL.subtract), r=["dn"], w=["dn"])
            if rms:
                ct.op("V", lambda h: h.tensor_copy(lnw2, lnw1), r=["dn"], w=["dn"])
            ct.op("S", lambda h: h.activation(lnw1, lnw1, AF.Sqrt, bias=EPS_LN), r=["dn"], w=["dn"])
            ct.op("V", lambda h: h.reciprocal(lnw1, lnw1), r=["dn"], w=["dn"])
            ct.op("V", lambda h: h.tensor_tensor(lnw0, lnw0, lnw1, AL.mult), r=["strow", "dn"], w=["strow"])

        def ln_hilo_mms(src_r, src_nb, bR, btagR, bNB, btagNB):
            ct.op("V", lambda h, s=src_r: h.tensor_copy(rhi, s), r=["strow", "dn"], w=["ybf"])
            ct.op("V", lambda h, s=src_r: h.tensor_tensor(rlo, s, rhi, AL.subtract),
                  r=["strow", "dn", "ybf"], w=["y2b"])
            ct.op("V", lambda h, s=src_nb: h.tensor_copy(nhi, s), r=["strow", "dn", "dnh"], w=["dnh"])
            ct.op("V", lambda h, s=src_nb: h.tensor_tensor(nlo, s, nhi, AL.subtract),
                  r=["strow", "dn", "dnh"], w=["dnh"])
            ct.op("T", lambda h, b=bR: h.matmul(b[:, :], onesr.app(0, 1, dims=[[1, 128]]), rhi,
                                                start=True, stop=False, tile_position=(0, 0)),
                  r=["ybf", "onesr"], w=[btagR])
            ct.op("T", lambda h, b=bR: h.matmul(b[:, :], onesr.app(0, 1, dims=[[1, 128]]), rlo,
                                                start=False, stop=True, tile_position=(0, 0)),
                  r=["y2b", "onesr"], w=[btagR])
            ct.op("T", lambda h, b=bNB: h.matmul(b[:, :], onesr.app(0, 1, dims=[[1, 128]]), nhi,
                                                 start=True, stop=False, tile_position=(0, 0)),
                  r=["dnh", "onesr"], w=[btagNB])
            ct.op("T", lambda h, b=bNB: h.matmul(b[:, :], onesr.app(0, 1, dims=[[1, 128]]), nlo,
                                                 start=False, stop=True, tile_position=(0, 0)),
                  r=["dnh", "onesr"], w=[btagNB])

        def wq_slice_fetch():
            ct.op("GD", lambda h: h.dma_start(
                out=wgs_d, in_=AP(wg_T.tensor, nc.partition_id([mybir.EngineType.Pool]) * 352,
                                  [[IFF, C], [1, 352]])), r=[], w=["wgs"])
            ct.op("GD", lambda h: h.dma_start(
                out=wus_d, in_=AP(wu_T.tensor, nc.partition_id([mybir.EngineType.Pool]) * 352,
                                  [[IFF, C], [1, 352]])), r=[], w=["wus"])
            ct.op("GD", lambda h: h.dma_start(
                out=wds_d, in_=AP(wd_T.tensor, nc.partition_id([mybir.EngineType.Pool]) * 128,
                                  [[C, IFF], [1, 128]])), r=[], w=["wds"])

        def wq_blocklist():
            out = []
            for (w_src, seg0, rows, cols, nm) in ((wgs_d, 0, C, 352, "wgs"),
                                                  (wus_d, WSEG, C, 352, "wus"),
                                                  (wds_d, 2 * WSEG, IFF, 128, "wds")):
                for rc in range(rows // 128):
                    out.append((w_src, seg0, cols, nm, rc))
            return out

        WQB = wq_blocklist()
        wqi = [0]

        def emit_wq(kcount):
            for _ in range(kcount):
                if wqi[0] >= len(WQB):
                    return
                w_src, seg0, n, nm, rc = WQB[wqi[0]]
                wqi[0] += 1
                src = AP(w_src.tensor, rc * 128 * n, [[n, 128], [1, n]])
                dst = AP(wqin_d.tensor, seg0 + rc * 128 * n, [[n, 128], [1, n]])
                ct.op("Y", lambda h, s=src, n=n: h.dma_start(out=qt_x.ap(0, [[1, n]]), in_=s), r=["qtmp0", nm], w=["qt_x"])
                quant_block(qt_x.ap(0, [[1, n]]), qt_q.ap(0, [[1, n]]), n, r_extra=["qt_x"], qs_=0)
                ct.op("Y", lambda h, d=dst, n=n: h.dma_start(out=d, in_=qt_q.ap(0, [[1, n]])), r=["qtmp0"], w=["wqin"])

        def wq_ag():
            ct.op("G", lambda h: h.collective_compute(
                "AllGather", AL.bypass, replica_groups=[list(range(NCORES))],
                ins=[wqin_d.opt()], outs=[wq_g.opt()]), r=["wqin"], w=["wq_g"])

        RK = 2 * WSEG + WDSEG

        def wg_segs(oc):
            a = 128 * oc
            b = a + 128
            segs = []
            r2 = a // 352
            while a < b:
                e = min(b, 352 * (r2 + 1))
                segs.append((r2, a, e))
                a = e
                r2 += 1
            return segs

        def p0_tile(s, j):
            jj = s * TPS + j
            col0 = jj * TILE
            pos0 = j * TILE
            ct.op("GD", lambda h: h.dma_start(
                out=xbS.ap(), in_=AP(x_in.tensor, col0, [[NTOK, 128], [128 * NTOK, 8], [1, 512]])),
                r=["xbrt"], w=["xbrt"])
            for oc in range(24):
                bank, btag = PB[oc % 2], f"pb{oc % 2}"
                for kc in range(8):
                    lhsT = cattn.ap((kc * 24 + oc) * 128, [[1, 128]])
                    rhs = xbS.ap(kc * 512, [[1, 512]])
                    ct.op("T", lambda h, l=lhsT, r2=rhs, b=bank, st=(kc == 0), sp=(kc == 7):
                          h.matmul(b[:, :], l, r2, start=st, stop=sp), r=["cattn", "xbrt"], w=[btag])
                ct.op("S", lambda h, oc=oc, b=bank: h.activation(
                    qkvS.ap(oc * 512, [[1, 512]]), b[:, :], AF.Identity, bias=cattnb.ap(oc, [[1, 1]])),
                    r=[btag, "cattnb"], w=["qkv"])
            cosap = cosb.ap(pos0, [[1, 512]])
            sinap = sinb.ap(pos0, [[1, 512]])
            for g in range(4):
                for (wl, bank, btag, wtag) in ((q2lw, PB[2], "pb2", "q2lw"), (q2lrw, PB[3], "pb3", "q2lrw"),
                                               (kv2lw, PB[4], "pb4", "kv2lw"), (kv2lrw, PB[5], "pb5", "kv2lrw")):
                    srcc = 0 if wtag.startswith("q") else 8
                    for i in range(4):
                        hh = 4 * g + i
                        chunk = srcc + hh // 2
                        par = 64 * (hh % 2)
                        lhsT = wl.app(par, 64)
                        rhs = qkvS.app(par, 64, o=chunk * 512, dims=[[1, 512]])
                        ct.op("T", lambda h, l=lhsT, r2=rhs, b=bank, i=i, par=par:
                              h.matmul(AP(b, 32 * i * 512, [[512, 32], [1, 512]]), l, r2,
                                       start=True, stop=True, tile_position=(par, 32 * i)),
                              r=["qkv", wtag], w=[btag])
                ct.op("V", lambda h: h.scalar_tensor_tensor(rt1.ap(), PB[2][:, :], q2lb4.ap(), cosap, AL.add, AL.mult),
                      r=["pb2", "q2lb4", "cosb", "xbrt"], w=["xbrt1"])
                ct.op("V", lambda h: h.scalar_tensor_tensor(rt2.ap(), PB[3][:, :], q2lbr4.ap(), sinap, AL.add, AL.mult),
                      r=["pb3", "q2lbr4", "sinb", "xbrt"], w=["xbrt2"])
                ct.op("V", lambda h: h.tensor_tensor(rt1.ap(), rt1.ap(), rt2.ap(), AL.add),
                      r=["xbrt1", "xbrt2"], w=["xbrt1"])
                for i in range(4):
                    lhsT = wqkw.app(32 * i, 32)
                    rhs = rt1.app(32 * i, 32, dims=[[1, 512]])
                    ct.op("T", lambda h, l=lhsT, r2=rhs, i=i:
                          h.matmul(AP(PB[6], 32 * i * 512, [[512, 32], [1, 512]]), l, r2,
                                   start=True, stop=True, tile_position=(32 * i, 32 * i)),
                          r=["xbrt1", "wqkw"], w=["pb6"])
                qdst = qT.ap((s * 4 + g) * T + pos0, [[1, 512]])
                ct.op("S", lambda h, d=qdst: h.activation(d, PB[6][:, :], AF.Identity, bias=wqkb4.ap()),
                      r=["pb6", "wqkb4"], w=["qT"])
                kdst = kT.ap((s * 4 + g) * T + pos0, [[1, 512]])
                ct.op("V", lambda h: h.scalar_tensor_tensor(rt1.ap(), PB[4][:, :], kvb4.ap(), cosap, AL.add, AL.mult),
                      r=["pb4", "kvb4", "cosb", "xbrt", "xbrt1"], w=["xbrt1"])
                ct.op("V", lambda h: h.scalar_tensor_tensor(rt2.ap(), PB[5][:, :], kvbr4.ap(), sinap, AL.add, AL.mult),
                      r=["pb5", "kvbr4", "sinb", "xbrt", "xbrt2"], w=["xbrt2"])
                ct.op("V", lambda h, d=kdst: h.tensor_tensor(d, rt1.ap(), rt2.ap(), AL.add),
                      r=["xbrt1", "xbrt2"], w=["kT"])
            for hh in range(H):
                par = 64 * (hh % 2)
                chunk = 16 + hh // 2
                bank, btag = (PB[2], "pb2") if hh % 2 == 0 else (PB[3], "pb3")
                for ksub in range(4):
                    lhsT = qkvS.app(par, 64, o=chunk * 512 + ksub * 128, dims=[[1, 128]])
                    rhs = kv2lw.app(par, 64)
                    ct.op("T", lambda h, l=lhsT, r2=rhs, b=bank, ks=ksub, par=par:
                          h.matmul(AP(b, ks * 32, [[512, 128], [1, 32]]), l, r2,
                                   start=True, stop=True, tile_position=(par, 0)),
                          r=["qkv", "kv2lw"], w=[btag])
                if j == 0:
                    vdst0 = vSB.ap(voff(s, hh, 0), [[1, 1], [1, 32]])
                    ct.op("V", lambda h, b=bank, d=vdst0: h.tensor_tensor(
                        d, AP(b, 0, [[512, 128], [32, 1], [1, 32]]), kvbias.ap(0, [[32, 1], [1, 32]]), AL.add),
                        r=[btag, "kvbias"], w=["vSB"])
                    vdst1 = vSB.ap(voff(s, hh, 1), [[33, 3], [1, 32]])
                    ct.op("V", lambda h, b=bank, d=vdst1: h.tensor_tensor(
                        d, AP(b, 32, [[512, 128], [32, 3], [1, 32]]), kvbias.ap(32, [[32, 3], [1, 32]]), AL.add),
                        r=[btag, "kvbias"], w=["vSB"])
                else:
                    vdst = vSB.ap(voff(s, hh, j * 4), [[33, 4], [1, 32]])
                    ct.op("V", lambda h, b=bank, d=vdst: h.tensor_tensor(
                        d, AP(b, 0, [[512, 128], [32, 4], [1, 32]]), kvbias.ap(0, [[32, 4], [1, 32]]), AL.add),
                        r=[btag, "kvbias"], w=["vSB"])

        for s in range(SEQ):
            for j in range(TPS):
                p0_tile(s, j)
        ct.barrier()

        ct.op("GD", lambda h: h.dma_start(
            out=cprojS.ap(), in_=AP(cproj_wT.tensor, 0, [[C, 128], [128 * C, 8], [128, 8], [1, 128]])),
            w=["cprojw"])

        def p1_tile(s, qt):
            qlo = qt * TILE
            jj = s * TPS + qt
            col0 = jj * TILE
            ct.op("Y", lambda h: h.dma_start(
                out=x0S.ap(), in_=AP(x_in.tensor, col0, [[NTOK, 128], [128 * NTOK, 8], [1, 512]])),
                r=["x0"], w=["x0"])
            nkt = 4 * qt + 4
            for half in range(2):
                for kt in range(nkt):
                    qs = 0 if (kt + 1) * 128 <= qlo else (kt * 128 - qlo)
                    epar = kt % 2
                    for gg in range(2):
                        g = half * 2 + gg
                        for i in range(4):
                            lhsT = kT.app(32 * i, 32, o=(s * 4 + g) * T + kt * 128, dims=[[1, 128]])
                            rhs = qT.app(32 * i, 32, o=(s * 4 + g) * T + qlo + qs, dims=[[1, 512 - qs]])
                            ct.op("T", lambda h, l=lhsT, r2=rhs, b=PB[i], qs=qs, i=i:
                                  h.matmul(AP(b, qs, [[512, 128], [1, 512 - qs]]), l, r2,
                                           start=True, stop=True, tile_position=(32 * i, 0)),
                                  r=["qT", "kT"], w=[f"pb{i}"])
                        for i in range(4):
                            hrel = gg * 4 + i
                            ec = epar * 4096 + hrel * 512
                            ct.op("S", lambda h, i=i, ec=ec, qs=qs: h.activation(
                                eS.ap(ec + qs, [[1, 512 - qs]]),
                                AP(PB[i], qs, [[512, 128], [1, 512 - qs]]), AF.Exp, scale=ISQL),
                                r=[f"pb{i}"], w=[f"e{epar}"])
                            if qs > 0:
                                ct.op("V", lambda h, ec=ec, qs=qs: h.memset(eS.ap(ec, [[1, qs]]), 0.0),
                                      w=[f"e{epar}"])
                            if kt * 128 >= qlo:
                                ct.op("G", lambda h, ec=ec, qs=qs: h.affine_select(
                                    out=eS.ap(ec + qs, [[1, 128]]), in_=eS.ap(ec + qs, [[1, 128]]),
                                    compare_op=AL.is_ge, fill=0.0, base=0,
                                    pattern=[[1, 128]], channel_multiplier=-1),
                                    r=[f"e{epar}"], w=[f"e{epar}"])
                    for pr in range(4):
                        for ab in range(2):
                            hh = half * 8 + 2 * pr + ab
                            hrel = hh - half * 8
                            ec = epar * 4096 + hrel * 512
                            mdim = 64 if kt == 0 else 33
                            lhsT = vSB.ap(voff(s, hh, kt), [[1, mdim]])
                            rhs = eS.ap(ec + qs, [[1, 512 - qs]])
                            ct.op("T", lambda h, l=lhsT, r2=rhs, b=PB[4 + pr], ab=ab, st=(kt == 0), qs=qs, md=mdim:
                                  h.matmul(AP(b, 64 * ab * 512 + qs, [[512, md], [1, 512 - qs]]), l, r2,
                                           start=st, stop=False, tile_position=(0, 64 * ab)),
                                  r=["vSB", f"e{epar}"], w=[f"pb{4 + pr}"])
                for pr in range(4):
                    bank, btag = PB[4 + pr], f"pb{4 + pr}"
                    ct.op("S", lambda h, b=bank, pr=pr: h.activation(
                        yrawS.ap(pr * 512, [[1, 512]]), b[:, :], AF.Identity),
                        r=[btag], w=["e0", "e1", "yraw"])
                ct.op("GD", lambda h: h.dma_start(
                    out=AP(persf_t, dn.off, [[32 * PF, 4], [1, 512]]),
                    in_=yrawS.app(32, 1, dims=[[512, 4], [1, 512]])),
                    r=["yraw"], w=["dn"])
                ct.op("GD", lambda h: h.dma_start(
                    out=AP(persf_t, dn.off + 512, [[32 * PF, 4], [1, 512]]),
                    in_=yrawS.app(96, 1, dims=[[512, 4], [1, 512]])),
                    r=["yraw"], w=["dn"])
                ct.op("V", lambda h: h.reciprocal(dn.ap(), dn.ap()), r=["dn"], w=["dn"])
                ct.op("V", lambda h: h.tensor_copy(dnh.ap(), dn.ap()), r=["dn"], w=["dnh"])
                ct.op("V", lambda h: h.tensor_tensor(dnl.ap(), dn.ap(), dnh.ap(), AL.subtract),
                      r=["dn", "dnh"], w=["dnl"])
                for pr in range(4):
                    ch = half * 4 + pr
                    bank, btag = PB[4 + pr], f"pb{4 + pr}"
                    for ab in range(2):
                        for lohi, buf, bt2 in ((0, dnh, "dnh"), (1, dnl, "dnl")):
                            ct.op("T", lambda h, b=bank, pr=pr, ab=ab, b2=buf, st=(lohi == 0):
                                  h.matmul(AP(b, 64 * ab * 512, [[512, 64], [1, 512]]),
                                           onesr.app(32 * pr, 1, dims=[[1, 64]]),
                                           b2.app(32 * pr, 1, o=512 * ab, dims=[[1, 512]]),
                                           start=st, stop=(lohi == 1), tile_position=(32 * pr, 64 * ab)),
                                  r=["dnh", "dnl", "onesr", "yraw"], w=[btag])
                    ct.op("V", lambda h, b=bank, pr=pr: h.tensor_tensor(
                        yvS.ap(pr * 512, [[1, 512]]), yrawS.ap(pr * 512, [[1, 512]]), b[:, :], AL.mult),
                        r=[btag, "yraw"], w=["yv"])
                    ab_bank, ab_tag = PB[pr % 2], f"pb{pr % 2}"
                    for ab in range(2):
                        lhsT = l2ow.app(64 * ab, 32, dims=[[1, 64]])
                        rhs = yvS.app(64 * ab, 32, o=pr * 512, dims=[[1, 512]])
                        ct.op("T", lambda h, l=lhsT, r2=rhs, b=ab_bank, ab=ab:
                              h.matmul(AP(b, 64 * ab * 512, [[512, 64], [1, 512]]), l, r2,
                                       start=True, stop=True, tile_position=(64 * ab, 64 * ab)),
                              r=["yv", "l2ow"], w=[ab_tag])
                    ct.op("S", lambda h, b=ab_bank, ch=ch: h.activation(
                        attnS.ap(ch * 512, [[1, 512]]), b[:, :], AF.Identity, bias=l2ob2.ap()),
                        r=[ab_tag, "l2ob2"], w=["attn"])
            for oc in range(8):
                bank, btag = PB[2 + oc % 2], f"pb{2 + oc % 2}"
                for kc in range(8):
                    lhsT = cprojS.ap((kc * 8 + oc) * 128, [[1, 128]])
                    rhs = attnS.ap(kc * 512, [[1, 512]])
                    ct.op("T", lambda h, l=lhsT, r2=rhs, b=bank, st=(kc == 0), sp=(kc == 7):
                          h.matmul(b[:, :], l, r2, start=st, stop=sp), r=["attn", "cprojw"], w=[btag])
                xs = x0S.ap(oc * 512, [[1, 512]])
                ct.op("V", lambda h, b=bank, xs=xs, oc=oc: h.scalar_tensor_tensor(
                    xs, b[:, :], cprojb.ap(oc, [[1, 1]]), xs, AL.add, AL.add),
                    r=[btag, "x0", "cprojb"], w=["x0"])
                ct.op("S", lambda h, xs=xs: h.activation(y2buf.ap(), xs, AF.Square), r=["x0"], w=["y2b"])
                ct.op("V", lambda h, xs=xs: h.tensor_copy(ybf.ap(), xs), r=["x0"], w=["ybf"])
                ct.op("T", lambda h, st=(oc == 0), sp=(oc == 7): h.matmul(
                    AP(PB[6], 0, [[512, 1], [1, 512]]), ones1.ap(0, [[1, 1]]), ybf.ap(), start=st, stop=sp,
                    tile_position=(0, 0)), r=["ybf", "ones1"], w=["pb6"])
                ct.op("T", lambda h, st=(oc == 0), sp=(oc == 7): h.matmul(
                    AP(PB[7], 0, [[512, 1], [1, 512]]), ones1.ap(0, [[1, 1]]), y2buf.ap(), start=st, stop=sp,
                    tile_position=(0, 0)), r=["y2b", "ones1"], w=["pb7"])
            ct.op("V", lambda h: h.tensor_copy(lnw0, AP(PB[6], 0, [[512, 1], [1, 512]])),
                  r=["pb6", "strow"], w=["strow"])
            ct.op("V", lambda h: h.tensor_copy(lnw1, AP(PB[7], 0, [[512, 1], [1, 512]])),
                  r=["pb7", "dn"], w=["dn"])
            ln_coef(rms=False)
            ln_hilo_mms(lnw1, lnw0, PB[6], "pb6", PB[7], "pb7")
            x1dst = out_d if stop == 1 else x1_d
            for oc in range(8):
                xs = x0S.ap(oc * 512, [[1, 512]])
                ct.op("V", lambda h, xs=xs: h.tensor_tensor(xs, xs, PB[6][:, :], AL.mult),
                      r=["x0", "pb6"], w=["x0"])
                ct.op("V", lambda h, xs=xs: h.tensor_tensor(xs, xs, PB[7][:, :], AL.subtract),
                      r=["x0", "pb7"], w=["x0"])
                ct.op("Y", lambda h, xs=xs, oc=oc: h.dma_start(
                    out=AP(x1dst.tensor, oc * 128 * NTOK + col0, [[NTOK, 128], [1, 512]]), in_=xs),
                    r=["x0"], w=["x1_d"])

        wq_slice_fetch()
        p1_ti = 0
        for s in range(SEQ):
            for qt in range(TPS):
                p1_tile(s, qt)
                emit_wq(10)
                p1_ti += 1
                if p1_ti == 4:
                    emit_wq(len(WQB))
                    wq_ag()
        ct.barrier()

        def lnfix(k, y_src, s_src, cf_dst, x_dst, rms_extra=False, to_out=False):
            sw = [statw.ap(i * 32, [[1, 32]]) for i in range(5)]
            ct.op("Y", lambda h: h.dma_start(
                out=statw.ap(0, [[32, 2], [1, 32]]),
                in_=AP(s_src.tensor, 0, [[32, 128], [NTOK, 2], [1, 32]])), r=["statw"], w=["statw"])
            ct.op("V", lambda h: h.tensor_scalar_mul(sw[0], sw[0], 1.0 / C), r=["statw"], w=["statw"])
            ct.op("V", lambda h: h.tensor_scalar_mul(sw[1], sw[1], 1.0 / C), r=["statw"], w=["statw"])
            ct.op("V", lambda h: h.tensor_tensor(sw[2], sw[0], sw[0], AL.mult), r=["statw"], w=["statw"])
            ct.op("V", lambda h: h.tensor_tensor(sw[1], sw[1], sw[2], AL.subtract), r=["statw"], w=["statw"])
            ct.op("S", lambda h: h.activation(sw[2], sw[1], AF.Sqrt, bias=EPS_LN), r=["statw"], w=["statw"])
            ct.op("V", lambda h: h.reciprocal(sw[2], sw[2]), r=["statw"], w=["statw"])
            ct.op("V", lambda h: h.tensor_tensor(sw[3], sw[0], sw[2], AL.mult), r=["statw"], w=["statw"])
            ct.op("V", lambda h: h.tensor_scalar_mul(sw[3], sw[3], -1.0), r=["statw"], w=["statw"])
            def hilo(dst_t, row, src_sw):
                ct.op("V", lambda h: h.tensor_copy(ybf.ap(0, [[1, 32]]), src_sw), r=["statw"], w=["ybf"])
                ct.op("V", lambda h: h.tensor_tensor(strow.ap(0, [[1, 32]]), src_sw, ybf.ap(0, [[1, 32]]),
                                                     AL.subtract), r=["statw", "ybf"], w=["strow"])
                ct.op("Y", lambda h: h.dma_start(
                    out=AP(dst_t.tensor, row * NTOK, [[32, 128], [1, 32]]), in_=ybf.ap(0, [[1, 32]])),
                    r=["ybf"], w=[dst_t.tensor.name])
                ct.op("V", lambda h: h.tensor_copy(ybf.ap(0, [[1, 32]]), strow.ap(0, [[1, 32]])),
                      r=["strow"], w=["ybf"])
                ct.op("Y", lambda h: h.dma_start(
                    out=AP(dst_t.tensor, (row + 1) * NTOK, [[32, 128], [1, 32]]), in_=ybf.ap(0, [[1, 32]])),
                    r=["ybf"], w=[dst_t.tensor.name])
            hilo(cf_dst, 0, sw[2])
            hilo(cf_dst, 2, sw[3])
            if rms_extra:
                ct.op("V", lambda h: h.tensor_tensor(sw[4], sw[2], sw[2], AL.mult), r=["statw"], w=["statw"])
                ct.op("V", lambda h: h.tensor_tensor(sw[4], sw[4], sw[1], AL.mult), r=["statw"], w=["statw"])
                ct.op("S", lambda h: h.activation(sw[4], sw[4], AF.Sqrt, bias=EPS_RMS), r=["statw"], w=["statw"])
                ct.op("V", lambda h: h.reciprocal(sw[4], sw[4]), r=["statw"], w=["statw"])
                ct.op("V", lambda h: h.tensor_tensor(sw[2], sw[2], sw[4], AL.mult), r=["statw"], w=["statw"])
                ct.op("V", lambda h: h.tensor_tensor(sw[3], sw[3], sw[4], AL.mult), r=["statw"], w=["statw"])
                hilo(cfx_d, 0, sw[2])
                hilo(cfx_d, 2, sw[3])
            for jj in range(NT):
                col0 = jj * TILE
                ct.op("Y", lambda h, c0=col0: h.dma_start(
                    out=lyt.ap(), in_=AP(y_src.tensor, c0, [[NTOK, 128], [128 * NTOK, 8], [1, 512]])),
                    r=["lyt"], w=["lyt"])
                ct.op("Y", lambda h, c0=col0: h.dma_start(
                    out=lcf.app(0, 1, dims=[[512, 4], [1, 512]]),
                    in_=AP(cf_dst.tensor, c0, [[NTOK, 1], [NTOK, 4], [1, 512]])),
                    r=["lcf", cf_dst.tensor.name], w=["lcf"])
                for part, btag in ((0, "pb0"), (2, "pb1")):
                    bank = PB[0] if part == 0 else PB[1]
                    for lohi in range(2):
                        ct.op("T", lambda h, b=bank, p=part + lohi, st=(lohi == 0):
                              h.matmul(b[:, :], onesr.app(0, 1, dims=[[1, 128]]),
                                       lcf.app(0, 1, o=p * 512, dims=[[1, 512]]), start=st, stop=(lohi == 1),
                                       tile_position=(0, 0)),
                              r=["lcf", "onesr"], w=[btag])
                if rms_extra:
                    ct.op("Y", lambda h, c0=col0: h.dma_start(
                        out=lcf.app(0, 1, o=2048, dims=[[512, 4], [1, 512]]),
                        in_=AP(cfx_d.tensor, c0, [[NTOK, 1], [NTOK, 4], [1, 512]])),
                        r=["lcf2", "cfx_d"], w=["lcf2"])
                    for part, btag in ((0, "pb2"), (2, "pb3")):
                        bank = PB[2] if part == 0 else PB[3]
                        for lohi in range(2):
                            ct.op("T", lambda h, b=bank, p=part + lohi, st=(lohi == 0):
                                  h.matmul(b[:, :], onesr.app(0, 1, dims=[[1, 128]]),
                                           lcf.app(0, 1, o=2048 + p * 512, dims=[[1, 512]]), start=st, stop=(lohi == 1),
                                           tile_position=(0, 0)),
                                  r=["lcf2", "onesr"], w=[btag])
                for oc in range(8):
                    ys = lyt.ap(oc * 512, [[1, 512]])
                    if rms_extra:
                        ct.op("V", lambda h, ys=ys: h.tensor_tensor(qt_x.ap(0, [[1, 512]]), ys, PB[2][:, :], AL.mult),
                              r=["lyt", "pb2", "qtmp0"], w=["qtmp0"])
                        ct.op("V", lambda h: h.tensor_tensor(qt_u.ap(0, [[1, 512]]), qt_x.ap(0, [[1, 512]]), PB[3][:, :], AL.add),
                              r=["qtmp0", "pb3"], w=["qtmp0"])
                        ct.op("V", lambda h, oc=oc: h.tensor_scalar(
                            lxn.ap(oc * 512, [[1, 512]]), qt_u.ap(0, [[1, 512]]), normwb.ap(oc, [[1, 1]]), None, AL.mult),
                            r=["qtmp0", "normwb"], w=["lxn"])
                    ct.op("V", lambda h, ys=ys: h.tensor_tensor(ys, ys, PB[0][:, :], AL.mult),
                          r=["lyt", "pb0"], w=["lyt"])
                    ct.op("V", lambda h, ys=ys: h.tensor_tensor(ys, ys, PB[1][:, :], AL.add),
                          r=["lyt", "pb1"], w=["lyt"])
                dst = out_d if to_out else x_dst
                ct.op("Y", lambda h, c0=col0, d=dst: h.dma_start(
                    out=AP(d.tensor, c0, [[NTOK, 128], [128 * NTOK, 8], [1, 512]]), in_=lyt.ap()),
                    r=["lyt"], w=["x_out"])
                if rms_extra:
                    ct.op("Y", lambda h, c0=col0: h.dma_start(
                        out=AP(xn_d.tensor, c0, [[NTOK, 128], [128 * NTOK, 8], [1, 512]]), in_=lxn.ap()),
                        r=["lxn"], w=["xn_d"])

        if stop == 1:
            ct.emit(block, sems)
            return nc

        def p2_tile(jj):
            col0 = jj * TILE
            xt = x1tp[jj % 2]
            xtg = f"x1t{jj % 2}"
            ct.op("Y", lambda h, xt=xt: h.dma_start(
                out=xt.ap(), in_=AP(x1_d.tensor, col0, [[NTOK, 128], [128 * NTOK, 8], [1, 512]])),
                r=[xtg], w=[xtg])
            for kc in range(8):
                qstage_a(xt.ap(kc * 512, [[1, 512]]), 512, kc % 2, r_extra=[xtg])
                qstage_s(512, kc % 2)
                if kc >= 1:
                    qstage_b(xqS.ap((kc - 1) * 512, [[1, 512]]), 512, (kc - 1) % 2, w_extra=[f"xq{kc - 1}"])
            qstage_b(xqS.ap(7 * 512, [[1, 512]]), 512, 1, w_extra=["xq7"])
            for oc in range(ICH):
                par = oc % 2
                for (r2, a, e) in wg_segs(oc):
                    off_in = r2 * RK + (a - 352 * r2)
                    ct.op("Y", lambda h, par=par, oi=off_in, do=a - 128 * oc, w_=e - a: h.dma_start(
                        out=wgbuf[par].ap(do, dims=[[128, 8], [1, w_]]),
                        in_=AP(wq_g.tensor, oi, [[352, 128], [128 * 352, 8], [1, w_]])),
                        r=[f"wgb{par}", "wq_g"], w=[f"wgb{par}"])
                    ct.op("Y", lambda h, par=par, oi=off_in + WSEG, do=a - 128 * oc, w_=e - a: h.dma_start(
                        out=wubuf[par].ap(do, dims=[[128, 8], [1, w_]]),
                        in_=AP(wq_g.tensor, oi, [[352, 128], [128 * 352, 8], [1, w_]])),
                        r=[f"wub{par}", "wq_g"], w=[f"wub{par}"])
                for kc in range(8):
                    ct.op("T", lambda h, kc=kc, par=par, st=(kc == 0), sp=(kc == 7):
                          h.matmul(PB[0][:, :], wgbuf[par].ap(kc * 128, [[1, 128]]),
                                   xqS.ap(kc * 512, [[1, 512]]), start=st, stop=sp),
                          r=[f"wgb{par}", f"xq{kc}"], w=["pb0"])
                    ct.op("T", lambda h, kc=kc, par=par, st=(kc == 0), sp=(kc == 7):
                          h.matmul(PB[1][:, :], wubuf[par].ap(kc * 128, [[1, 128]]),
                                   xqS.ap(kc * 512, [[1, 512]]), start=st, stop=sp),
                          r=[f"wub{par}", f"xq{kc}"], w=["pb1"])
                ct.op("S", lambda h: h.activation(silS.ap(), PB[0][:, :], AF.Silu), r=["pb0"], w=["sil"])
                ct.op("V", lambda h, oc=oc: h.tensor_tensor(htS.ap(oc * 512, [[1, 512]]), silS.ap(),
                                                            PB[1][:, :], AL.mult),
                      r=["sil", "pb1"], w=[f"ht{oc}"])
                qstage_a(htS.ap(oc * 512, [[1, 512]]), 512, oc % 2, r_extra=[f"ht{oc}"])
                qstage_s(512, oc % 2)
                if oc >= 1:
                    qstage_b(hqS.ap((oc - 1) * 512, [[1, 512]]), 512, (oc - 1) % 2, w_extra=[f"hq{oc - 1}"])
            qstage_b(hqS.ap((ICH - 1) * 512, [[1, 512]]), 512, (ICH - 1) % 2, w_extra=[f"hq{ICH - 1}"])
            for oc in range(8):
                par = oc % 2
                bank, btag = PB[2 + par], f"pb{2 + par}"
                ct.op("Y", lambda h, oc=oc, par=par: h.dma_start(
                    out=wdbuf[par].ap(), in_=AP(wq_g.tensor, oc * RK + 2 * WSEG, [[128, 128], [128 * 128, 22], [1, 128]])),
                    r=[f"wdb{par}", "wq_g"], w=[f"wdb{par}"])
                for kc in range(ICH):
                    ct.op("T", lambda h, kc=kc, par=par, b=bank, st=(kc == 0), sp=(kc == ICH - 1):
                          h.matmul(b[:, :], wdbuf[par].ap(kc * 128, [[1, 128]]),
                                   hqS.ap(kc * 512, [[1, 512]]), start=st, stop=sp),
                          r=[f"wdb{par}", f"hq{kc}"], w=[btag])
                xs = xt.ap(oc * 512, [[1, 512]])
                ct.op("V", lambda h, b=bank, xs=xs: h.tensor_tensor(xs, xs, b[:, :], AL.add),
                      r=[btag, xtg], w=[xtg])
                ct.op("S", lambda h, xs=xs: h.activation(y2buf.ap(), xs, AF.Square), r=[xtg], w=["y2b"])
                ct.op("V", lambda h, xs=xs: h.tensor_copy(ybf.ap(), xs), r=[xtg], w=["ybf"])
                ct.op("T", lambda h, st=(oc == 0), sp=(oc == 7): h.matmul(
                    AP(PB[6], 0, [[512, 1], [1, 512]]), ones1.ap(0, [[1, 1]]), ybf.ap(), start=st, stop=sp,
                    tile_position=(0, 0)), r=["ybf", "ones1"], w=["pb6"])
                ct.op("T", lambda h, st=(oc == 0), sp=(oc == 7): h.matmul(
                    AP(PB[7], 0, [[512, 1], [1, 512]]), ones1.ap(0, [[1, 1]]), y2buf.ap(), start=st, stop=sp,
                    tile_position=(0, 0)), r=["y2b", "ones1"], w=["pb7"])
                if stop == 12:
                    ct.op("Y", lambda h, xs=xs, oc=oc: h.dma_start(
                        out=AP(y2_d.tensor, oc * 128 * NTOK + col0, [[NTOK, 128], [1, 512]]), in_=xs),
                        r=[xtg], w=["y2_d"])
            ct.op("V", lambda h: h.tensor_copy(lnw0, AP(PB[6], 0, [[512, 1], [1, 512]])),
                  r=["pb6", "strow"], w=["strow"])
            ct.op("V", lambda h: h.tensor_copy(lnw1, AP(PB[7], 0, [[512, 1], [1, 512]])),
                  r=["pb7", "dn"], w=["dn"])
            ln_coef(rms=True)
            ln_hilo_mms(lnw1, lnw0, PB[4], "pb4", PB[5], "pb5")
            ct.op("V", lambda h: h.tensor_tensor(lnw2, lnw2, lnw1, AL.mult), r=["dn"], w=["dn"])
            ct.op("V", lambda h: h.tensor_tensor(lnw2, lnw2, lnw1, AL.mult), r=["dn"], w=["dn"])
            ct.op("S", lambda h: h.activation(lnw2, lnw2, AF.Sqrt, bias=EPS_RMS), r=["dn"], w=["dn"])
            ct.op("V", lambda h: h.reciprocal(lnw2, lnw2), r=["dn"], w=["dn"])
            ct.op("V", lambda h: h.tensor_tensor(lnw1, lnw1, lnw2, AL.mult), r=["dn"], w=["dn"])
            ct.op("V", lambda h: h.tensor_tensor(lnw0, lnw0, lnw2, AL.mult), r=["strow", "dn"], w=["strow"])
            ln_hilo_mms(lnw1, lnw0, PB[6], "pb6", PB[7], "pb7")
            x2dst = out_d if stop == 2 else x2_d
            for oc in range(8):
                ys = xt.ap(oc * 512, [[1, 512]])
                ct.op("V", lambda h, ys=ys: h.tensor_tensor(qt_x.ap(0, [[1, 512]]), ys, PB[6][:, :], AL.mult),
                      r=[xtg, "pb6", "qtmp0"], w=["qtmp0"])
                ct.op("V", lambda h: h.tensor_tensor(qt_x.ap(0, [[1, 512]]), qt_x.ap(0, [[1, 512]]),
                                                     PB[7][:, :], AL.subtract),
                      r=["qtmp0", "pb7"], w=["qtmp0"])
                ct.op("V", lambda h, oc=oc: h.tensor_scalar(
                    xnb.ap(), qt_x.ap(0, [[1, 512]]), normwb.ap(oc, [[1, 1]]), None, AL.mult),
                    r=["qtmp0", "normwb"], w=["xnb"])
                ct.op("Y", lambda h, oc=oc, c0=col0: h.dma_start(
                    out=AP(xn_d.tensor, oc * 128 * NTOK + c0, [[NTOK, 128], [1, 512]]), in_=xnb.ap()),
                    r=["xnb"], w=["xn_d"])
                ct.op("V", lambda h, ys=ys: h.tensor_tensor(ys, ys, PB[4][:, :], AL.mult),
                      r=[xtg, "pb4"], w=[xtg])
                ct.op("V", lambda h, ys=ys: h.tensor_tensor(ys, ys, PB[5][:, :], AL.subtract),
                      r=[xtg, "pb5"], w=[xtg])
                ct.op("Y", lambda h, ys=ys, oc=oc: h.dma_start(
                    out=AP(x2dst.tensor, oc * 128 * NTOK + col0, [[NTOK, 128], [1, 512]]), in_=ys),
                    r=[xtg], w=["x2_d"])

        for jj in range(NT):
            p2_tile(jj)
        ct.barrier()

        if stop == 12:
            ct.op("Y", lambda h: h.dma_start(out=out_d, in_=y2_d), r=["y2_d"], w=["x_out"])
            ct.barrier()
            ct.emit(block, sems)
            return nc

        if stop == 2:
            ct.emit(block, sems)
            return nc

        def p3_scan():
            ct.op("G", lambda h: h.collective_compute(
                "AllGather", AL.bypass, replica_groups=[list(range(NCORES))],
                ins=[xn_d.opt()], outs=[xn_g.opt()]), r=["xn_d"], w=["xn_g"])
            def extract_op(h):
                cid = nc.partition_id([mybir.EngineType.Pool])
                src = AP(xn_g.tensor, cid * (128 * NTOK),
                         [[NTOK, 128], [C * NTOK, 8], [T, 2], [1, T]])
                return h.dma_start(out=xsc.ap(0, [[2 * T, 8], [T, 2], [1, T]]), in_=src)
            ct.op("GD", extract_op, r=["xn_g", "cid"], w=["xsc"])

            ct.op("V", lambda h: h.memset(stS.ap(1024, [[1, 16]]), 0.0), w=["st_i"])
            NW = T // 32
            for w in range(NW):
                bkA, tgA = PB[w % 2], f"pb{w % 2}"
                bkB, tgB = PB[4 + w % 2], f"pb{4 + w % 2}"
                ct.op("T", lambda h, w=w, b=bkA: h.matmul(
                    b[:, 0:256], bbd.ap(), xsc.ap(w * 32, [[1, 32], [T, 8]]), start=True, stop=False),
                    r=["xsc", "bbd"], w=[tgA])
                ct.op("T", lambda h, w=w, b=bkB: h.matmul(
                    b[:, 0:256], bbd.ap(), xsc.ap(w * 32 + 8 * T, [[1, 32], [T, 8]]), start=True, stop=False),
                    r=["xsc", "bbd"], w=[tgB])
                for t_in in range(32):
                    t = w * 32 + t_in
                    slot = t % 64
                    for grp, bk, tg, off in ((0, bkA, tgA, 0), (1, bkB, tgB, 8)):
                        if t == 0:
                            prev = stS.ap(1024 + off, [[1, 8]])
                            ptag = "st_i"
                        else:
                            prev = stS.ap(((t - 1) % 64) * 16 + off, [[1, 8]])
                            ptag = f"st{grp}{((t - 1) % 64) // 32}"
                        ct.op("T", lambda h, b=bk, ti=t_in, pv=prev: h.matmul(
                            AP(b, ti * 8, [[512, 128], [1, 8]]), abd.ap(), pv, start=False, stop=True),
                            r=[ptag, "abd"], w=[tg])
                        ct.op("S", lambda h, b=bk, ti=t_in, sl=slot, off=off: h.activation(
                            stS.ap(sl * 16 + off, [[1, 8]]), AP(b, ti * 8, [[512, 128], [1, 8]]), AF.Tanh),
                            r=[tg], w=[f"st{grp}{slot // 32}"])
                s0 = (w * 32) % 64
                ct.op("T", lambda h, s0=s0: h.matmul(
                    PB[2][:, :], cbd.ap(), stS.ap(s0 * 16, [[1, 512]]), start=True, stop=True),
                    r=[f"st0{s0 // 32}", f"st1{s0 // 32}", "cbd"], w=["pb2"])
                acc = mxacc[(w // 8) % 2]
                atag = f"mxa{(w // 8) % 2}"
                ct.op("V", lambda h, w=w, acc=acc: h.scalar_tensor_tensor(
                    acc.ap((w % 8) * 32, [[1, 32], [256, 16]]),
                    xsc.ap(w * 32, [[1, 32], [T, 16]]), skipb.ap(), PB[2][:, :], AL.mult, AL.add),
                    r=["pb2", "xsc", "skipb", atag], w=[atag])
                if (w + 1) % 8 == 0:
                    wf = w - 7
                    ct.op("Y", lambda h, wf=wf, acc=acc: h.dma_start(
                        out=AP(mx_d.tensor, wf * 32, [[H * T, 128], [T, 16], [1, 256]]),
                        in_=acc.ap(0, [[256, 16], [1, 256]])), r=[atag], w=["mx_d"])
            ct.op("G", lambda h: h.collective_compute(
                "AllGather", AL.bypass, replica_groups=[list(range(NCORES))],
                ins=[mx_d.opt()], outs=[mx_g.opt()]), r=["mx_d"], w=["mx_g"])

        if stop == 14:
            ct.op("GD", lambda h: h.dma_start(out=out_d, in_=xn_d), r=["xn_d"], w=["x_out"])
            ct.barrier()
            ct.emit(block, sems)
            return nc

        p3_scan()
        ct.barrier()

        if stop == 15:
            ct.op("GD", lambda h: h.dma_start(
                out=AP(out_d.tensor, 0, [[NTOK, 128], [1, NTOK]]),
                in_=AP(mx_d.tensor, 0, [[H * T, 128], [1, NTOK]])), r=["mx_d"], w=["x_out"])
            ct.barrier()
            ct.emit(block, sems)
            return nc

        def p3b():
            def extract2_op(h):
                cid = nc.partition_id([mybir.EngineType.Pool])
                src = AP(mx_g.tensor, cid * (2 * T),
                         [[H * T, 128], [128 * H * T, 8], [1, 2 * T]])
                return h.dma_start(out=mxfm.ap(0, [[NTOK, 8], [1, 2 * T]]), in_=src)
            ct.op("GD", extract2_op, r=["mx_g", "cid"], w=["mxfm"])
            ct.op("GD", lambda h: h.dma_start(
                out=woutS.ap(), in_=AP(wout_T.tensor, 0, [[C, 128], [128 * C, 8], [128, 8], [1, 128]])),
                w=["woutw"])
            for jj in range(NT):
                col0 = jj * TILE
                ct.op("Y", lambda h, c0=col0: h.dma_start(
                    out=x2t.ap(), in_=AP(x2_d.tensor, c0, [[NTOK, 128], [128 * NTOK, 8], [1, 512]])),
                    r=["x2t"], w=["x2t"])
                for oc in range(8):
                    bank, btag = PB[2 + oc % 2], f"pb{2 + oc % 2}"
                    for kc in range(8):
                        ct.op("T", lambda h, kc=kc, oc=oc, b=bank, c0=col0, st=(kc == 0), sp=(kc == 7):
                              h.matmul(b[:, :], woutS.ap((kc * 8 + oc) * 128, [[1, 128]]),
                                       mxfm.ap(kc * NTOK + c0, [[1, 512]]), start=st, stop=sp),
                              r=["mxfm", "woutw"], w=[btag])
                    xs = x2t.ap(oc * 512, [[1, 512]])
                    ct.op("V", lambda h, b=bank, xs=xs, oc=oc: h.scalar_tensor_tensor(
                        qt_x.ap(0, [[1, 512]]), b[:, :], boutb.ap(oc, [[1, 1]]), xs, AL.add, AL.add),
                        r=[btag, "x2t", "boutb", "qtmp"], w=["qtmp"])
                    ct.op("V", lambda h, xs=xs: h.tensor_tensor(xs, xs, qt_x.ap(0, [[1, 512]]), AL.add),
                        r=["x2t", "qtmp"], w=["x2t"])
                    ct.op("S", lambda h, xs=xs: h.activation(y2buf.ap(), xs, AF.Square), r=["x2t"], w=["y2b"])
                    ct.op("V", lambda h, xs=xs: h.tensor_copy(ybf.ap(), xs), r=["x2t"], w=["ybf"])
                    ct.op("T", lambda h, st=(oc == 0), sp=(oc == 7): h.matmul(
                        AP(PB[6], 0, [[512, 1], [1, 512]]), ones1.ap(0, [[1, 1]]), ybf.ap(), start=st, stop=sp,
                        tile_position=(0, 0)), r=["ybf", "ones1"], w=["pb6"])
                    ct.op("T", lambda h, st=(oc == 0), sp=(oc == 7): h.matmul(
                        AP(PB[7], 0, [[512, 1], [1, 512]]), ones1.ap(0, [[1, 1]]), y2buf.ap(), start=st, stop=sp,
                        tile_position=(0, 0)), r=["y2b", "ones1"], w=["pb7"])
                    ct.op("Y", lambda h, xs=xs, oc=oc, c0=col0: h.dma_start(
                        out=AP(y3_d.tensor, oc * 128 * NTOK + c0, [[NTOK, 128], [1, 512]]), in_=xs),
                        r=["x2t"], w=["y3_d"])
                ct.op("V", lambda h: h.tensor_copy(strow.app(0, 1, dims=[[1, 512]]), AP(PB[6], 0, [[512, 1], [1, 512]])),
                      r=["pb6"], w=["strow"])
                ct.op("V", lambda h: h.tensor_copy(strow.app(32, 1, dims=[[1, 512]]), AP(PB[7], 0, [[512, 1], [1, 512]])),
                      r=["pb7"], w=["strow"])
                ct.op("Y", lambda h, c0=col0: h.dma_start(
                    out=AP(s_d[2].tensor, c0, [[NTOK, 1], [1, 512]]),
                    in_=strow.app(0, 1, dims=[[1, 512]])), r=["strow"], w=["s3_d"])
                ct.op("Y", lambda h, c0=col0: h.dma_start(
                    out=AP(s_d[2].tensor, NTOK + (c0), [[NTOK, 1], [1, 512]]),
                    in_=strow.app(32, 1, dims=[[1, 512]])), r=["strow"], w=["s3_d"])

        p3b()
        ct.barrier()
        if stop == 13:
            ct.op("Y", lambda h: h.dma_start(out=out_d, in_=y3_d), r=["y3_d"], w=["x_out"])
            ct.barrier()
            ct.emit(block, sems)
            return nc
        lnfix(2, y3_d, s_d[2], cf_d[2], x3_d, to_out=(stop == 3))
        ct.barrier()
        if stop == 3:
            ct.emit(block, sems)
            return nc

        for wdst, wsrc, nm in ((wrS, wr_T, "wrw"), (wkS, wk_T, "wkw"), (wvS, wv_T, "wvw")):
            ct.op("GD", lambda h, d=wdst, s=wsrc: h.dma_start(
                out=d.ap(), in_=AP(s.tensor, 0, [[C, 128], [128 * C, 8], [128, 8], [1, 128]])), w=[nm])

        def p4_tile(jj):
            col0 = jj * TILE
            ct.op("Y", lambda h: h.dma_start(
                out=x3t.ap(), in_=AP(x3_d.tensor, col0, [[NTOK, 128], [128 * NTOK, 8], [1, 512]])),
                r=["x3t"], w=["x3t"])
            if jj % TPS == 0:
                ct.op("Y", lambda h: h.dma_start(
                    out=xpt.ap(1, [[512, 8], [1, 511]]),
                    in_=AP(x3_d.tensor, col0, [[NTOK, 128], [128 * NTOK, 8], [1, 511]])),
                    r=["xpt"], w=["xpt"])
                ct.op("V", lambda h: h.memset(xpt.ap(0, [[512, 8], [1, 1]]), 0.0), r=["xpt"], w=["xpt"])
            else:
                ct.op("Y", lambda h: h.dma_start(
                    out=xpt.ap(), in_=AP(x3_d.tensor, col0 - 1, [[NTOK, 128], [128 * NTOK, 8], [1, 512]])),
                    r=["xpt"], w=["xpt"])
            for oc in range(8):
                ct.op("V", lambda h, oc=oc: h.tensor_scalar(p4tmp.ap(0, [[1, 512]]), xpt.ap(oc * 512, [[1, 512]]),
                                                            omrb.ap(oc, [[1, 1]]), None, AL.mult),
                      r=["xpt", "omrb"], w=["p4t"])
                ct.op("V", lambda h, oc=oc: h.scalar_tensor_tensor(xrS.ap(oc * 512, [[1, 512]]),
                                                                   x3t.ap(oc * 512, [[1, 512]]), murb.ap(oc, [[1, 1]]),
                                                                   p4tmp.ap(0, [[1, 512]]), AL.mult, AL.add),
                      r=["x3t", "murb", "p4t"], w=["xr"])
                ct.op("V", lambda h, oc=oc: h.tensor_scalar(p4tmp.ap(0, [[1, 512]]), xpt.ap(oc * 512, [[1, 512]]),
                                                            omkb.ap(oc, [[1, 1]]), None, AL.mult),
                      r=["xpt", "omkb"], w=["p4t"])
                ct.op("V", lambda h, oc=oc: h.scalar_tensor_tensor(xkS.ap(oc * 512, [[1, 512]]),
                                                                   x3t.ap(oc * 512, [[1, 512]]), mukb.ap(oc, [[1, 1]]),
                                                                   p4tmp.ap(0, [[1, 512]]), AL.mult, AL.add),
                      r=["x3t", "mukb", "p4t"], w=["xk"])
            for oc in range(8):
                for kc in range(8):
                    ct.op("T", lambda h, kc=kc, oc=oc, st=(kc == 0), sp=(kc == 7):
                          h.matmul(PB[0][:, :], wrS.ap((kc * 8 + oc) * 128, [[1, 128]]),
                                   xrS.ap(kc * 512, [[1, 512]]), start=st, stop=sp),
                          r=["xr", "wrw"], w=["pb0"])
                ct.op("S", lambda h, oc=oc: h.activation(rbS.ap(oc * 512, [[1, 512]]), PB[0][:, :], AF.Sigmoid),
                      r=["pb0"], w=["rb"])
                for kc in range(8):
                    ct.op("T", lambda h, kc=kc, oc=oc, st=(kc == 0), sp=(kc == 7):
                          h.matmul(PB[1][:, :], wkS.ap((kc * 8 + oc) * 128, [[1, 128]]),
                                   xkS.ap(kc * 512, [[1, 512]]), start=st, stop=sp),
                          r=["xk", "wkw"], w=["pb1"])
                ct.op("S", lambda h, oc=oc: h.activation(y2buf.ap(), PB[1][:, :], AF.Relu),
                      r=["pb1"], w=["y2b"])
                ct.op("V", lambda h, oc=oc: h.tensor_tensor(kaS.ap(oc * 512, [[1, 512]]), y2buf.ap(), y2buf.ap(), AL.mult),
                      r=["y2b"], w=["ka"])
            for oc in range(8):
                bank, btag = PB[2 + oc % 2], f"pb{2 + oc % 2}"
                for kc in range(8):
                    ct.op("T", lambda h, kc=kc, oc=oc, b=bank, st=(kc == 0), sp=(kc == 7):
                          h.matmul(b[:, :], wvS.ap((kc * 8 + oc) * 128, [[1, 128]]),
                                   kaS.ap(kc * 512, [[1, 512]]), start=st, stop=sp),
                          r=["ka", "wvw"], w=[btag])
                xs = x3t.ap(oc * 512, [[1, 512]])
                ct.op("V", lambda h, b=bank, oc=oc: h.scalar_tensor_tensor(
                    p4tmp.ap(0, [[1, 512]]), b[:, :], 1.0, rbS.ap(oc * 512, [[1, 512]]), AL.mult, AL.mult),
                    r=[btag, "rb", "p4t"], w=["p4t"])
                ct.op("V", lambda h, xs=xs: h.tensor_tensor(xs, xs, p4tmp.ap(0, [[1, 512]]), AL.add),
                      r=["x3t", "p4t"], w=["x3t"])
                ct.op("S", lambda h, xs=xs: h.activation(y2buf.ap(), xs, AF.Square), r=["x3t"], w=["y2b"])
                ct.op("V", lambda h, xs=xs: h.tensor_copy(ybf.ap(), xs), r=["x3t"], w=["ybf"])
                ct.op("T", lambda h, st=(oc == 0), sp=(oc == 7): h.matmul(
                    AP(PB[6], 0, [[512, 1], [1, 512]]), ones1.ap(0, [[1, 1]]), ybf.ap(), start=st, stop=sp,
                    tile_position=(0, 0)), r=["ybf", "ones1"], w=["pb6"])
                ct.op("T", lambda h, st=(oc == 0), sp=(oc == 7): h.matmul(
                    AP(PB[7], 0, [[512, 1], [1, 512]]), ones1.ap(0, [[1, 1]]), y2buf.ap(), start=st, stop=sp,
                    tile_position=(0, 0)), r=["y2b", "ones1"], w=["pb7"])
                ct.op("Y", lambda h, xs=xs, oc=oc: h.dma_start(
                    out=AP(y4_d.tensor, oc * 128 * NTOK + col0, [[NTOK, 128], [1, 512]]), in_=xs),
                    r=["x3t"], w=["y4_d"])
            ct.op("V", lambda h: h.tensor_copy(strow.app(0, 1, dims=[[1, 512]]), AP(PB[6], 0, [[512, 1], [1, 512]])),
                  r=["pb6"], w=["strow"])
            ct.op("V", lambda h: h.tensor_copy(strow.app(32, 1, dims=[[1, 512]]), AP(PB[7], 0, [[512, 1], [1, 512]])),
                  r=["pb7"], w=["strow"])
            ct.op("Y", lambda h, c0=col0: h.dma_start(
                out=AP(s_d[3].tensor, c0, [[NTOK, 1], [1, 512]]),
                in_=strow.app(0, 1, dims=[[1, 512]])), r=["strow"], w=["s4_d"])
            ct.op("Y", lambda h, c0=col0: h.dma_start(
                out=AP(s_d[3].tensor, NTOK + (c0), [[NTOK, 1], [1, 512]]),
                in_=strow.app(32, 1, dims=[[1, 512]])), r=["strow"], w=["s4_d"])

        for jj in range(NT):
            p4_tile(jj)
        ct.barrier()
        lnfix(3, y4_d, s_d[3], cf_d[3], None, to_out=True)
        ct.barrier()

        ct.emit(block, sems)

    return nc


_NC_CACHE = {}


def _get_nc(stop):
    if stop not in _NC_CACHE:
        _NC_CACHE[stop] = build_nc(stop)
    return _NC_CACHE[stop]


def _rope_tables():
    inv_freq = (1.0 / 10000.0 ** (np.arange(0, LAT, 2, dtype=np.float32) / LAT)).astype(np.float32)
    t = np.arange(T, dtype=np.float32)
    freqs = np.outer(t, inv_freq)
    cos = np.repeat(np.cos(freqs), 2, axis=-1)[:, :LAT].astype(np.float32)
    sin = np.repeat(np.sin(freqs), 2, axis=-1)[:, :LAT].astype(np.float32)
    cos4 = np.tile(np.ascontiguousarray(cos.T), (4, 1))
    sin4 = np.tile(np.ascontiguousarray(sin.T), (4, 1))
    return np.ascontiguousarray(cos4), np.ascontiguousarray(sin4)


def kernel(stop=4, **inputs):
    x = inputs["L0_attn_x"]
    cos4, sin4 = _rope_tables()
    shared = {
        "cattn_wT": np.ascontiguousarray(inputs["L0_attn_c_attn_w"].T),
        "cattn_b": np.ascontiguousarray(inputs["L0_attn_c_attn_b"]),
        "q2l_wT": np.ascontiguousarray(inputs["L0_attn_q2l_w"].T),
        "q2l_b": np.ascontiguousarray(inputs["L0_attn_q2l_b"]),
        "kv2l_wT": np.ascontiguousarray(inputs["L0_attn_kv2l_w"].T),
        "kv2l_b": np.ascontiguousarray(inputs["L0_attn_kv2l_b"]),
        "l2o_wT": np.ascontiguousarray(inputs["L0_attn_l2o_w"].T),
        "l2o_b": np.ascontiguousarray(inputs["L0_attn_l2o_b"]),
        "wqk_wT": np.ascontiguousarray(inputs["L0_attn_wqk_w"].T),
        "wqk_b": np.ascontiguousarray(inputs["L0_attn_wqk_b"]),
        "cproj_wT": np.ascontiguousarray(inputs["L0_attn_cproj_w"].T),
        "cproj_b": np.ascontiguousarray(inputs["L0_attn_cproj_b"]),
        "wg_T": np.ascontiguousarray(inputs["L0_ffn_w_gate"].T),
        "wu_T": np.ascontiguousarray(inputs["L0_ffn_w_up"].T),
        "wd_T": np.ascontiguousarray(inputs["L0_ffn_w_down"].T),
        "norm_w": np.ascontiguousarray(inputs["L1_attn_norm_w"]),
        "wout_T": np.ascontiguousarray(inputs["L1_attn_w_out"].T),
        "bout_v": np.ascontiguousarray(inputs["L1_attn_b_out"]),
        "wr_T": np.ascontiguousarray(inputs["L1_ffn_W_r"].T),
        "wk_T": np.ascontiguousarray(inputs["L1_ffn_W_k"].T),
        "wv_T": np.ascontiguousarray(inputs["L1_ffn_W_v"].T),
        "mu_r": np.ascontiguousarray(inputs["L1_ffn_mu_r"]),
        "mu_k": np.ascontiguousarray(inputs["L1_ffn_mu_k"]),
        "cos4": cos4,
        "sin4": sin4,
        "altsgn": np.tile(np.array([-1.0, 1.0], dtype=np.float32), 64),
    }
    a_full = inputs["L1_attn_a"]
    b_full = inputs["L1_attn_b"]
    c_full = inputs["L1_attn_c"]
    skip_full = inputs["L1_attn_skip"]
    in_maps = []
    for c in range(NCORES):
        m = dict(shared)
        xc = x[2 * c:2 * c + 2]
        m["x_in"] = np.ascontiguousarray(xc.transpose(2, 0, 1).reshape(C, NTOK))
        m["a_sh"] = np.ascontiguousarray(a_full[2 * c:2 * c + 2])
        m["b_sh"] = np.ascontiguousarray(b_full[2 * c:2 * c + 2])
        m["c_sh"] = np.ascontiguousarray(c_full[2 * c:2 * c + 2])
        m["skip_sh"] = np.ascontiguousarray(skip_full[2 * c:2 * c + 2].reshape(128))
        m["core_id_in"] = np.array([[c]], dtype=np.int32)
        in_maps.append(m)
    nc = _get_nc(stop)
    res = run_bass_kernel_spmd(nc, in_maps, core_ids=list(range(NCORES)))
    global LAST_RES
    LAST_RES = res
    outs = []
    for c in range(NCORES):
        o = res.results[c]["out_d"]
        outs.append(o.reshape(C, SEQ, T).transpose(1, 2, 0))
    return np.ascontiguousarray(np.concatenate(outs, axis=0))


if __name__ == "__main__":
    pass
